# version 01
# baseline (speedup 1.0000x reference)
import numpy as np
import ml_dtypes
from contextlib import ExitStack

import concourse.bass as bass
import concourse.mybir as mybir
from concourse.alu_op_type import AluOpType as OP
from concourse.bass_utils import run_bass_kernel_spmd

F32 = mybir.dt.float32
BF16 = mybir.dt.bfloat16
AF = mybir.ActivationFunctionType
AX = mybir.AxisListType

D = 2048
NT = 4096
NO = 1024
CHK = 256
NCH = 16
NCH_OWN = 4
EPS = 1e-5
C7 = float(np.float32(1.0) / np.float32(7.0))
INV2048 = float(np.float32(1.0) / np.float32(2048.0))

ENGINES = ("tensor", "vector", "scalar", "sync", "gpsimd")
KB = 1024


class Prog:
    def __init__(self, nc, sems):
        self.nc = nc
        self.S = sems
        self.q = {e: [] for e in ENGINES}
        self.cnt = {k: 0 for k in sems}
        self.autogap = False
        self.gap_fn = None

    def emit(self, eng, fn):
        self.q[eng].append(fn)
        if eng == "vector" and self.gap_fn is not None:
            self.q[eng].append(self.gap_fn)

    def bump(self, sem, by=1):
        self.cnt[sem] += by
        return self.cnt[sem]

    def wait(self, eng, sem, val):
        nc, h = self.nc, self.S[sem]
        self.emit(eng, lambda: getattr(nc, eng).wait_ge(h, val))


def build(upto="full", taps=()):
    nc = bass.Bass("TRN2", target_bir_lowering=False)

    def dri(n, s, dt=F32):
        return nc.dram_tensor(n, s, dt, kind="ExternalInput").ap()

    x_d = dri("x", [NT, D])
    w_qkvo = {"q": dri("wq", [D, D]), "k": dri("wk", [D, D]),
              "v": dri("wv", [D, D]), "o": dri("wo", [D, D])}
    b_qkvo = {"q": dri("bq", [D]), "k": dri("bk", [D]),
              "v": dri("bv", [D]), "o": dri("bo", [D])}
    gate_d = dri("gate", [8, D])
    ebias_d = dri("ebias", [8])
    w1_d = dri("w1", [8, 1024, D])
    w2_d = dri("w2", [8, D, 1024])
    w3_d = dri("w3", [8, 1024, D])
    sw1_d = dri("sw1", [1024, D])
    sw2_d = dri("sw2", [D, 1024])
    sw3_d = dri("sw3", [1024, D])
    cvec_d = dri("cvec", [D])
    wmod_d = dri("wmod", [D, 6 * D])
    eye_d = dri("eyef", [128, 128])
    eyeb_d = nc.dram_tensor("eyeb", [128, 128], BF16, kind="ExternalInput").ap()

    y_d = nc.dram_tensor("y", [NO, D], F32, kind="ExternalOutput").ap()

    K_dram = nc.dram_tensor("K_dram", [D, NT], F32).ap()
    Vt_dram = nc.dram_tensor("Vt_dram", [NT, D], F32).ap()
    Q_dram = nc.dram_tensor("Q_dram", [D, NO], F32).ap()
    oe_dram = nc.dram_tensor("oe_dram", [8, D, NO], F32).ap()
    x2_dram = nc.dram_tensor("x2_dram", [NO, D], F32).ap()

    tap_d = {}
    for tname, tshape, tdt in taps:
        tap_d[tname] = nc.dram_tensor("tap_" + tname, list(tshape), tdt,
                                      kind="ExternalOutput").ap()

    AR = 192 * KB
    arena = nc.alloc_sbuf_tensor("arena", [128, AR], mybir.dt.uint8)
    base = nc.lookup_mloc(arena).addr

    def at(name, shape, dt, off):
        assert off + shape[1] * (4 if dt == F32 else 2 if dt == BF16 else 1) <= AR
        return nc.alloc_sbuf_tensor_at(name, shape, dt, offset=base + off).ap()

    R1, R2, R3 = 0, 64 * KB, 128 * KB
    Ip = at("Ip", [128, 16 * NT], BF16, R1)
    ctx_fm = at("ctx_fm", [128, 16 * NO], F32, R1)
    x2_sb = at("x2", [128, 8 * D], F32, R1)
    xb_sb = at("xb", [128, 16 * NO], BF16, R1)
    wexp = at("wexp", [128, 16 * KB], BF16, R1 + 32 * KB)
    ssl = at("ssl", [128, D], F32, R1)
    swt = at("swt", [128, D], F32, R1 + 8 * KB)
    s1s_sl = at("s1s", [128, 1024], F32, R1 + 16 * KB)
    hh_s = at("hhs", [128, 8 * KB], F32, R1 + 20 * KB)
    q4 = [at(f"q4_{i}", [128, 1024], F32, R2 + i * 4 * KB) for i in range(2)]
    kcb = [at(f"kcb{i}", [128, 256], F32, R2 + 8 * KB + i * KB) for i in range(2)]
    v65 = [at(f"v65{i}", [128, 260], F32, R2 + 10 * KB + i * 1056) for i in range(2)]
    probs = [at(f"prb{i}", [128, 512], F32, R2 + 13 * KB + i * 2 * KB)
             for i in range(8)]
    rsum65 = at("rsum65", [65, 4 * 512], F32, R2 + 29 * KB)
    rsum0 = at("rsum0", [1, 4 * 512], F32, R2 + 37 * KB)
    rq0 = at("rq0", [1, 4 * 512], F32, R2 + 45 * KB)
    rsb = at("rsb", [64, 512], F32, R2 + 53 * KB)
    ostg = at("ostg", [64, 512], F32, R2 + 55 * KB)
    I2 = at("I2", [128, 16 * NO], BF16, R2)
    pd_scr = at("pd_scr", [128, NO], F32, R2 + 32 * KB)
    pd_scr2 = at("pd_scr2", [128, NO], F32, R2 + 36 * KB)
    h2_fm = at("h2f", [128, 16 * NO], F32, R2)
    oe_buf = [at(f"oeb{i}", [128, NO], F32, R2 + i * 4 * KB) for i in range(2)]
    fin_scr = at("fin_scr", [128, NO], F32, R2 + 8 * KB)
    wcol = [at(f"wcol{i}", [128, D], F32, R3 + i * 8 * KB) for i in range(2)]
    xt_buf = [at(f"xt{i}", [128, D], F32, R3 + 16 * KB + i * 8 * KB) for i in range(2)]
    qchunk = at("qchunk", [128, 16 * CHK], F32, R3 + 16 * KB)
    hfm = at("hfm", [128, 16 * CHK], F32, R3 + 32 * KB)
    lad_thr = at("lad_thr", [128, D], F32, R3 + 48 * KB)
    lad_bb = at("lad_bb", [128, D], F32, R3 + 56 * KB)
    scr2k = at("scr2k", [128, D], F32, R3 + 56 * KB)
    wts_f = at("wts_f", [128, D], F32, R3 + 16 * KB)
    iwt_sl = at("iwt", [128, D], BF16, R3 + 24 * KB)
    krow = at("krow", [128, NT], F32, R3 + 32 * KB)
    mw_sl = at("mw_sl", [128, D], BF16, R3 + 36 * KB)
    s1_sb = at("s1", [128, 8 * KB], BF16, R3)
    hh_sb = at("hh", [128, 8 * KB], BF16, R3 + 16 * KB)
    oestg = at("oestg", [128, NO], F32, R3 + 32 * KB)
    out_moe = at("om", [128, 16 * NO], F32, R3)

    def sb(n, s, dt=F32):
        return nc.alloc_sbuf_tensor(n, s, dt).ap()

    eye = sb("eye", [128, 128])
    eye16 = sb("eye16", [128, 128], BF16)
    ones1 = sb("ones1", [1, 128])
    cvec = sb("cvecs", [128, 16])
    scv = sb("scv", [128, 16])
    modv = sb("modv", [128, 96])
    ca_v = sb("cav", [128, 16])
    ga_v = sb("gav", [128, 16])
    cb_v = sb("cbv", [128, 16])
    gb_v = sb("gbv", [128, 16])
    t5a = sb("t5a", [128, 16]); t5b = sb("t5b", [128, 16]); t5c = sb("t5c", [128, 16])
    chb = sb("chb", [128, 16 * NCH])
    tm896 = sb("tm896", [128, NCH])
    chb2 = sb("chb2", [128, 16 * 16])
    tm896o = sb("tm896o", [128, NCH_OWN])
    mxb = sb("mxb", [128, 16]); mnb = sb("mnb", [128, 16])
    cmaxb = sb("cmaxb", [128, 16]); p2b = sb("p2b", [128, 16])
    rscaleb = sb("rscaleb", [128, 16])
    tmcol = sb("tmcol", [128, 1]); tmrow = sb("tmrow", [1, 128])
    tmsc = sb("tmsc", [1, 1]); tmb = sb("tmb", [128, 1])
    thrj = sb("thrj", [128, 1]); tm7b = sb("tm7b", [128, 1])
    lnsm = sb("lnsm", [128, 1]); lnm = sb("lnm", [128, 1])
    lnv2 = sb("lnv2", [128, 1]); lna = sb("lna", [128, 1])
    lnsq = sb("lnsq", [128, 1]); lnr0 = sb("lnr0", [128, 1])
    lnt = sb("lnt", [128, 1]); lnr = sb("lnr", [128, 1])
    swv = sb("swv", [128, 1]); rswv = sb("rswv", [128, 1])
    swcv = sb("swcv", [128, 16]); cbbv = sb("cbbv", [128, 16])
    bias_sl = sb("bias_sl", [128, 1])
    gate_sb = at("gate_sb", [8, D], F32, R3 + 40 * KB)
    gtT = sb("gtT", [128, 16 * 8])
    scoresT = at("scoresT", [8, 1024], F32, R3 + 48 * KB)
    sc_tok = sb("sc_tok", [128, 8 * 8])
    bs_tok = sb("bs_tok", [128, 8])
    ebias_r = sb("ebias_r", [1, 8])
    ebias_b = sb("ebias_b", [128, 8])
    cnt1 = sb("cnt1", [128, 1]); cnt2 = sb("cnt2", [128, 1])
    cmp8 = sb("cmp8", [128, 8])
    selc = sb("selc", [128, 1])
    wt_tok = sb("wt_tok", [128, 8 * 8])
    wt_fm = at("wt_fm", [8, 1024], F32, R3 + 52 * KB)
    wtrow0 = at("wtrow0", [1, 1024], F32, R3 + 56 * KB)
    wtb_sb = at("wtb", [128, 1024], F32, R3 + 60 * KB)
    h1b_st = sb("h1b_st", [128, 1024], BF16)
    h3b_st = sb("h3b_st", [128, 1024], BF16)

    psum = nc.alloc_psum_tensor("psum", [128, 4096], F32).ap()
    B = [psum[:, 512 * i: 512 * (i + 1)] for i in range(8)]
    psum16 = psum.bitcast(BF16)
    B16 = [psum16[:, 1024 * i: 1024 * (i + 1)] for i in range(8)]

    vg_a = sb("vg_a", [128, 64])
    vg_b = sb("vg_b", [128, 64])

    sem_names = ["c0", "wa", "wb", "bs", "xa", "xb", "ka", "kb", "qa", "qb",
                 "gg", "sa", "st", "t", "v", "a", "g", "gw", "goa", "gob", "tap"]
    es = ExitStack()
    sems = {n: es.enter_context(nc.semaphore(n)) for n in sem_names}
    P = Prog(nc, sems)
    P.gap_fn = lambda: nc.vector.drain()

    def tap(name, src_ap):
        if name not in tap_d:
            return
        dst = tap_d[name]
        if P.cnt["tap"]:
            P.wait("gpsimd", "tap", P.cnt["tap"])
        P.emit("gpsimd", lambda: nc.gpsimd.dma_start(out=dst, in_=src_ap)
               .then_inc(sems["tap"], 16))
        c = P.bump("tap", 16)
        P.wait("vector", "tap", c)
        P.wait("scalar", "tap", c)

    def tap_wait(eng_sem_pairs):
        for semn, val in eng_sem_pairs:
            if tap_d:
                P.wait("gpsimd", semn, val)

    def emit_ladder(buf, width):
        v = nc.vector
        thr, bb = lad_thr[:, 0:width], lad_bb[:, 0:width]
        P.emit("vector", lambda: v.tensor_scalar(bb, buf, -0.5, None, OP.is_ge))
        P.emit("vector", lambda: v.tensor_scalar(thr, bb, 8.0, -4.5, OP.mult, OP.add))
        P.emit("vector", lambda: v.tensor_tensor(bb, buf, thr, OP.is_ge))
        P.emit("vector", lambda: v.scalar_tensor_tensor(thr, bb, 4.0, thr, OP.mult, OP.add))
        P.emit("vector", lambda: v.scalar_tensor_tensor(bb, thr, -2.0, buf, OP.add, OP.is_le))
        P.emit("vector", lambda: v.scalar_tensor_tensor(thr, bb, 2.0, thr, OP.mult, OP.add))
        P.emit("vector", lambda: v.scalar_tensor_tensor(bb, thr, -3.0, buf, OP.add, OP.is_le))
        P.emit("vector", lambda: v.scalar_tensor_tensor(buf, thr, -3.5, bb, OP.add, OP.add))

    for dst, src in [(eye, eye_d), (eye16, eyeb_d)]:
        P.emit("sync", lambda dst=dst, src=src:
               nc.sync.dma_start(out=dst, in_=src).then_inc(sems["c0"], 16))
        P.bump("c0", 16)
    P.emit("vector", lambda: nc.vector.memset(ones1, 1.0)
           .then_inc(sems["v"], 1))
    v_const = P.bump("v")

    P.emit("sync", lambda: nc.sync.dma_start(
        out=cvec, in_=cvec_d.rearrange("(j p) -> p j", p=128),
        allow_slow_non_contiguous=True).then_inc(sems["c0"], 16))
    v_ld = P.bump("c0", 16)
    P.wait("scalar", "c0", v_ld)
    P.emit("scalar", lambda: nc.scalar.preload_act_table(AF.Silu))
    P.emit("scalar", lambda: nc.scalar.activation(scv, cvec, AF.Silu)
           .then_inc(sems["a"], 1))
    v_scv = P.bump("a")

    mod_mm_done = {}
    for m in range(96):
        buf = wcol[m % 2]
        wsem = "wa" if m % 2 == 0 else "wb"
        src = wmod_d[:, 128 * m:128 * (m + 1)].rearrange("(k p) j -> p k j", p=128)
        dst3 = buf.rearrange("p (k j) -> p k j", j=128)
        if m >= 2:
            P.wait("sync", "t", mod_mm_done[m - 2])
        P.emit("sync", lambda dst3=dst3, src=src, wsem=wsem:
               nc.sync.dma_start(out=dst3, in_=src).then_inc(sems[wsem], 16))
        v_ld = P.bump(wsem, 16)
        P.wait("tensor", wsem, v_ld)
        if m == 0:
            P.wait("tensor", "a", v_scv)
        bufr = buf.rearrange("p (k j) -> p k j", j=128)
        for kc in range(16):
            lhsT, rhs = bufr[:, kc], scv[:, kc:kc + 1]
            if kc == 15:
                P.emit("tensor", lambda lhsT=lhsT, rhs=rhs, m=m:
                       nc.tensor.matmul(B[7][:, m:m + 1], lhsT, rhs,
                                        start=False, stop=True).then_inc(sems["t"], 1))
                mod_mm_done[m] = P.bump("t")
            else:
                P.emit("tensor", lambda lhsT=lhsT, rhs=rhs, m=m, kc=kc:
                       nc.tensor.matmul(B[7][:, m:m + 1], lhsT, rhs,
                                        start=(kc == 0), stop=False))
    P.wait("vector", "t", mod_mm_done[31])
    P.emit("vector", lambda: nc.vector.tensor_copy(modv[:, 0:16], B[7][:, 0:16]))
    P.emit("vector", lambda: nc.vector.tensor_scalar(
        ca_v, B[7][:, 16:32], 1.0, None, OP.add).then_inc(sems["v"], 1))
    v_mod1 = P.bump("v")

    def emit_clamp(src_cols, dst):
        v = nc.vector
        P.autogap = True
        P.emit("vector", lambda: v.tensor_scalar(t5a, src_cols, 0.25, None, OP.max))
        P.emit("vector", lambda: v.tensor_scalar(t5b, src_cols, -0.25, None, OP.min))
        P.emit("vector", lambda: v.tensor_scalar(t5c, src_cols, 0.0, None, OP.is_ge))
        P.emit("vector", lambda: v.tensor_tensor(t5a, t5a, t5b, OP.subtract))
        P.emit("vector", lambda: v.tensor_tensor(t5a, t5c, t5a, OP.mult))
        P.emit("vector", lambda: v.tensor_tensor(dst, t5a, t5b, OP.add))
        P.autogap = False

    def emit_mod_part2():
        P.wait("vector", "t", mod_mm_done[95])
        P.emit("vector", lambda: nc.vector.tensor_copy(modv[:, 16:96], B[7][:, 16:96]))
        emit_clamp(modv[:, 32:48], ga_v)
        P.emit("vector", lambda: nc.vector.tensor_scalar(
            cb_v, modv[:, 64:80], 1.0, None, OP.add))
        emit_clamp(modv[:, 80:96], gb_v)
        P.emit("vector", lambda: nc.vector.engine_nop().then_inc(sems["v"], 1))
        return P.bump("v")

    if upto == "s1":
        v_mod2 = emit_mod_part2()
        if tap_d:
            tap_wait([("v", v_mod2)])
            tap("modv", modv)
            tap("cav", ca_v)
            tap("gav", ga_v)
            tap("cbv", cb_v)
            tap("gbv", gb_v)
        _finalize(nc, P, sems, es)
        return nc

    P.emit("scalar", lambda: nc.scalar.preload_act_table(AF.Sqrt))

    def emit_ln_tile(xin):
        v = nc.vector
        P.autogap = True
        P.emit("vector", lambda: v.tensor_reduce(lnsm, xin, axis=AX.X, op=OP.add))
        P.emit("vector", lambda: v.tensor_scalar(lnm, lnsm, INV2048, None, OP.mult))
        P.emit("vector", lambda: v.tensor_scalar(xin, xin, lnm, None, OP.subtract))
        P.emit("vector", lambda: v.tensor_tensor(scr2k, xin, xin, OP.mult))
        P.emit("vector", lambda: v.tensor_reduce(lnv2, scr2k, axis=AX.X, op=OP.add))
        P.emit("vector", lambda: v.tensor_scalar(
            lna, lnv2, INV2048, EPS, OP.mult, OP.add).then_inc(sems["v"], 1))
        vv = P.bump("v")
        P.wait("scalar", "v", vv)
        P.emit("scalar", lambda: nc.scalar.activation(lnsq, lna, AF.Sqrt)
               .then_inc(sems["a"], 1))
        aa = P.bump("a")
        P.wait("vector", "a", aa)
        P.emit("vector", lambda: v.reciprocal(lnr0, lnsq))
        P.emit("vector", lambda: v.tensor_tensor(lnt, lnr0, lnr0, OP.mult))
        P.emit("vector", lambda: v.tensor_tensor(lnt, lnt, lna, OP.mult))
        P.emit("vector", lambda: v.tensor_scalar(lnt, lnt, -0.5, 1.5, OP.mult, OP.add))
        P.emit("vector", lambda: v.tensor_tensor(lnr, lnr0, lnt, OP.mult))
        P.emit("vector", lambda: v.tensor_scalar(xin, xin, lnr, None, OP.mult))
        P.autogap = False

    def emit_quant_chunk(src_fm, chb_dst, tm_dst, ip_dst):
        v = nc.vector
        P.autogap = True
        for pc in range(16):
            s = src_fm(pc)
            P.emit("vector", lambda s=s, pc=pc: v.tensor_reduce(
                mxb[:, pc:pc + 1], s, axis=AX.X, op=OP.max))
            P.emit("vector", lambda s=s, pc=pc: v.tensor_reduce(
                mnb[:, pc:pc + 1], s, axis=AX.X, op=OP.min))
        P.emit("vector", lambda: v.tensor_tensor(t5a, mxb, mnb, OP.add))
        P.emit("vector", lambda: v.tensor_scalar(t5b, t5a, 0.5, None, OP.mult))
        for pc in range(16):
            d = chb_dst(pc)
            P.emit("vector", lambda d=d, pc=pc: v.tensor_copy(d, t5b[:, pc:pc + 1]))
        P.emit("vector", lambda: v.tensor_tensor(t5a, mxb, t5b, OP.subtract))
        P.emit("vector", lambda: v.tensor_tensor(t5c, t5b, mnb, OP.subtract))
        P.emit("vector", lambda: v.tensor_tensor(cmaxb, t5a, t5c, OP.max))
        P.emit("vector", lambda: v.tensor_reduce(
            tmcol, cmaxb, axis=AX.X, op=OP.max).then_inc(sems["v"], 1))
        vv = P.bump("v")
        P.wait("tensor", "v", vv)
        P.emit("tensor", lambda: nc.tensor.transpose(B[6][0:1, 0:128], tmcol, eye)
               .then_inc(sems["t"], 1))
        tt_ = P.bump("t")
        P.wait("vector", "t", tt_)
        P.emit("vector", lambda: v.tensor_copy(tmrow, B[6][0:1, 0:128]))
        P.emit("vector", lambda: v.tensor_reduce(
            tmsc, tmrow, axis=AX.X, op=OP.max).then_inc(sems["v"], 1))
        vv = P.bump("v")
        P.wait("tensor", "v", vv)
        P.emit("tensor", lambda: nc.tensor.matmul(
            B[6][:, 256:257], ones1, tmsc, start=True, stop=True)
            .then_inc(sems["t"], 1))
        tt_ = P.bump("t")
        P.wait("vector", "t", tt_)
        P.emit("vector", lambda: v.tensor_copy(tmb, B[6][:, 256:257]))
        P.emit("vector", lambda: v.tensor_scalar(tm7b, tmb, C7, None, OP.mult))
        P.emit("vector", lambda: v.tensor_scalar(tm_dst, tm7b, 2.0 ** -7, None, OP.mult))
        for j in range(7):
            P.emit("vector", lambda j=j: v.tensor_scalar(
                thrj, tmb, 2.0 ** (j - 7), None, OP.mult))
            P.emit("vector", lambda: v.tensor_scalar(t5a, cmaxb, thrj, None, OP.is_gt))
            if j == 0:
                P.emit("vector", lambda: v.tensor_scalar(p2b, t5a, 1.0, None, OP.add))
            else:
                P.emit("vector", lambda: v.tensor_scalar(t5a, t5a, 1.0, None, OP.add))
                P.emit("vector", lambda: v.tensor_tensor(p2b, p2b, t5a, OP.mult))
        P.emit("vector", lambda: v.tensor_scalar(t5a, p2b, tm7b, 2.0 ** -7,
                                                 OP.mult, OP.mult))
        P.emit("vector", lambda: v.tensor_scalar(t5a, t5a, 1e-9, None, OP.max))
        P.emit("vector", lambda: v.reciprocal(rscaleb, t5a))
        P.autogap = False
        for pc in range(16):
            s = src_fm(pc)
            cb_ = chb_dst(pc)
            P.emit("vector", lambda s=s, cb_=cb_, pc=pc: v.tensor_scalar(
                qchunk[:, pc * CHK:(pc + 1) * CHK], s, cb_, rscaleb[:, pc:pc + 1],
                OP.subtract, OP.mult))
        emit_ladder(qchunk[:, 0:2048], 2048)
        emit_ladder(qchunk[:, 2048:4096], 2048)
        for pc in range(16):
            d = ip_dst(pc)
            P.emit("vector", lambda d=d, pc=pc: v.tensor_scalar(
                d, qchunk[:, pc * CHK:(pc + 1) * CHK], p2b[:, pc:pc + 1], None,
                OP.mult))
        P.emit("vector", lambda: nc.vector.engine_nop().then_inc(sems["v"], 1))
        return P.bump("v")

    v_qdone = {}
    xtr_done = {}
    for c in range(NCH):
        for half in range(2):
            tix = 2 * c + half
            buf = xt_buf[half]
            xsem = "xa" if half == 0 else "xb"
            if c >= 1 and half == 0:
                P.wait("scalar", "v", v_qdone[c - 1])
            P.emit("scalar", lambda buf=buf, tix=tix, xsem=xsem:
                   nc.scalar.dma_start(
                       out=buf, in_=x_d[128 * tix:128 * (tix + 1), :])
                   .then_inc(sems[xsem], 16))
            v_x = P.bump(xsem, 16)
            P.wait("vector", xsem, v_x)
            emit_ln_tile(buf)
            P.emit("vector", lambda: nc.vector.engine_nop().then_inc(sems["v"], 1))
            v_xh = P.bump("v")
            P.wait("tensor", "v", v_xh)
            if c == 0 and half == 0:
                P.wait("tensor", "v", v_mod1)
            for pcg in range(4):
                for pc in range(4 * pcg, 4 * pcg + 4):
                    bank = B[pc % 4]
                    fin = pc % 4 == 3
                    if fin:
                        P.emit("tensor", lambda pc=pc, bank=bank, buf=buf:
                               nc.tensor.transpose(bank[:, 0:128],
                                                   buf[:, 128 * pc:128 * (pc + 1)],
                                                   eye).then_inc(sems["t"], 1))
                        t_tr = P.bump("t")
                    else:
                        P.emit("tensor", lambda pc=pc, bank=bank, buf=buf:
                               nc.tensor.transpose(bank[:, 0:128],
                                                   buf[:, 128 * pc:128 * (pc + 1)],
                                                   eye))
                P.wait("vector", "t", t_tr)
                for pc in range(4 * pcg, 4 * pcg + 4):
                    bank = B[pc % 4]
                    dst = hfm[:, pc * CHK + half * 128: pc * CHK + half * 128 + 128]
                    P.emit("vector", lambda pc=pc, bank=bank, dst=dst:
                           nc.vector.tensor_scalar(
                               dst, bank[:, 0:128], ca_v[:, pc:pc + 1],
                               modv[:, pc:pc + 1], OP.mult, OP.add))
                P.emit("vector", lambda: nc.vector.engine_nop().then_inc(sems["v"], 1))
                v_ev = P.bump("v")
                P.wait("tensor", "v", v_ev)
        v_qdone[c] = emit_quant_chunk(
            src_fm=lambda pc: hfm[:, pc * CHK:(pc + 1) * CHK],
            chb_dst=lambda pc, c=c: chb[:, pc * 16 + c: pc * 16 + c + 1],
            tm_dst=tm896[:, c:c + 1],
            ip_dst=lambda pc, c=c: Ip[:, pc * NT + c * CHK: pc * NT + (c + 1) * CHK],
        )
        if tap_d and c == 0:
            tap_wait([("v", v_qdone[0])])
            tap("hfm0", hfm)
            tap("qchunk0", qchunk)

    v_quant_all = v_qdone[NCH - 1]
    v_mod2 = emit_mod_part2()
    if tap_d:
        tap_wait([("v", v_quant_all), ("v", v_mod2)])
        tap("modv", modv)
        tap("gav", ga_v)
        tap("chb", chb)
        tap("tm896", tm896)
        tap("Ip0", Ip[:, 0:NT])
    if upto == "s2":
        _finalize(nc, P, sems, es)
        return nc

    wcol_last_t = {0: None, 1: None}
    proj_bank_v = {0: None, 1: None}
    wts_last_t = [None]
    bias_last_v = [None]

    def emit_wstream(kind, ip_src, ip_blk, tm_src, chb_src, nch, ntok):
        w_da = w_qkvo[kind.lower()]
        b_da = b_qkvo[kind.lower()]
        ntt = ntok // 512
        o_rows_v = []
        for jt in range(16):
            wsem = "wa" if jt % 2 == 0 else "wb"
            if wcol_last_t[jt % 2] is not None:
                P.wait("sync", "t", wcol_last_t[jt % 2])
            P.emit("sync", lambda jt=jt, w_da=w_da, wsem=wsem: nc.sync.dma_start(
                out=wcol[jt % 2], in_=w_da[128 * jt:128 * (jt + 1), :])
                .then_inc(sems[wsem], 16))
            v_w = P.bump(wsem, 16)
            if bias_last_v[0] is not None:
                P.wait("sync", "v", bias_last_v[0])
            P.emit("sync", lambda jt=jt, b_da=b_da: nc.sync.dma_start(
                out=bias_sl,
                in_=b_da.rearrange("(a p) -> p a", p=128)[:, jt:jt + 1],
                allow_slow_non_contiguous=True).then_inc(sems["bs"], 16))
            v_b = P.bump("bs", 16)
            v = nc.vector
            P.wait("vector", wsem, v_w)
            P.wait("vector", "bs", v_b)
            if wts_last_t[0] is not None:
                P.wait("vector", "t", wts_last_t[0])
            wbuf = wcol[jt % 2]
            P.autogap = True
            P.emit("vector", lambda wbuf=wbuf: v.tensor_reduce(
                swv, wbuf, axis=AX.X, op=OP.max, apply_absolute_value=True))
            P.emit("vector", lambda: v.tensor_scalar(swv, swv, C7, 1e-9, OP.mult, OP.max))
            P.emit("vector", lambda: v.reciprocal(rswv, swv))
            P.autogap = False
            P.emit("vector", lambda wbuf=wbuf: v.tensor_scalar(
                wts_f, wbuf, rswv, None, OP.mult))
            emit_ladder(wts_f, D)
            P.emit("vector", lambda: v.tensor_copy(iwt_sl, wts_f)
               .then_inc(sems["v"], 1))
            v_iw = P.bump("v")
            P.wait("tensor", "v", v_iw)
            for kc in range(16):
                bk = 4 + (kc % 2)
                P.emit("tensor", lambda kc=kc, bk=bk:
                       nc.tensor.transpose(B16[bk][:, 0:128],
                                           iwt_sl[:, 128 * kc:128 * (kc + 1)], eye16))
                P.emit("tensor", lambda kc=kc, bk=bk, wbuf=wbuf:
                       nc.tensor.transpose(B[bk][:, 128:256],
                                           wbuf[:, 128 * kc:128 * (kc + 1)], eye)
                       .then_inc(sems["t"], 1))
                t_f = P.bump("t")
                if kc == 15:
                    wcol_last_t[jt % 2] = t_f
                P.wait("vector", "t", t_f)
                P.emit("vector", lambda kc=kc, bk=bk:
                       v.tensor_copy(iwt_sl[:, 128 * kc:128 * (kc + 1)],
                                     B16[bk][:, 0:128]))
                P.emit("vector", lambda kc=kc, bk=bk:
                       v.tensor_copy(wts_f[:, 128 * kc:128 * (kc + 1)],
                                     B[bk][:, 128:256]).then_inc(sems["v"], 1))
                v_tr = P.bump("v")
                P.wait("tensor", "v", v_tr)
            for kc in range(16):
                lhsT = wts_f[:, 128 * kc:128 * (kc + 1)]
                rhs = chb_src[:, kc * 16: kc * 16 + nch]
                if kc == 15:
                    P.emit("tensor", lambda lhsT=lhsT, rhs=rhs, nch=nch:
                           nc.tensor.matmul(B[6][:, 0:nch], lhsT, rhs,
                                            start=False, stop=True)
                           .then_inc(sems["t"], 1))
                    t_cbb = P.bump("t")
                else:
                    P.emit("tensor", lambda lhsT=lhsT, rhs=rhs, kc=kc, nch=nch:
                           nc.tensor.matmul(B[6][:, 0:nch], lhsT, rhs,
                                            start=(kc == 0), stop=False))
            wts_last_t[0] = t_cbb
            P.wait("vector", "t", t_cbb)
            P.autogap = True
            if kind == "Q":
                P.emit("vector", lambda nch=nch: v.tensor_scalar(
                    cbbv[:, 0:nch], B[6][:, 0:nch], bias_sl, 0.125, OP.add, OP.mult))
                P.emit("vector", lambda nch=nch: v.tensor_scalar(
                    swcv[:, 0:nch], tm_src[:, 0:nch], swv, 0.125, OP.mult, OP.mult))
            else:
                P.emit("vector", lambda nch=nch: v.tensor_scalar(
                    cbbv[:, 0:nch], B[6][:, 0:nch], bias_sl, None, OP.add))
                P.emit("vector", lambda nch=nch: v.tensor_scalar(
                    swcv[:, 0:nch], tm_src[:, 0:nch], swv, None, OP.mult))
            P.autogap = False
            P.emit("vector", lambda: nc.vector.engine_nop().then_inc(sems["v"], 1))
            v_sc = P.bump("v")
            bias_last_v[0] = v_sc
            P.wait("tensor", "v", v_sc)
            for tt in range(ntt):
                bank = tt % 2
                if proj_bank_v[bank] is not None:
                    P.wait("tensor", "v", proj_bank_v[bank])
                for kc in range(16):
                    lhsT = iwt_sl[:, 128 * kc:128 * (kc + 1)]
                    rhs = ip_src[:, kc * ip_blk + tt * 512: kc * ip_blk + tt * 512 + 512]
                    if kc == 15:
                        P.emit("tensor", lambda lhsT=lhsT, rhs=rhs, bank=bank:
                               nc.tensor.matmul(B[bank], lhsT, rhs,
                                                start=False, stop=True)
                               .then_inc(sems["t"], 1))
                        t_p = P.bump("t")
                    else:
                        P.emit("tensor", lambda lhsT=lhsT, rhs=rhs, kc=kc, bank=bank:
                               nc.tensor.matmul(B[bank], lhsT, rhs,
                                                start=(kc == 0), stop=False))
                P.wait("vector", "t", t_p)
                for cc in range(2):
                    ci = tt * 2 + cc
                    dst = krow[:, tt * 512 + cc * 256: tt * 512 + cc * 256 + 256]
                    P.emit("vector", lambda dst=dst, cc=cc, ci=ci, bank=bank:
                           v.tensor_scalar(dst, B[bank][:, cc * 256:(cc + 1) * 256],
                                           swcv[:, ci:ci + 1], cbbv[:, ci:ci + 1],
                                           OP.mult, OP.add))
                P.emit("vector", lambda: nc.vector.engine_nop().then_inc(sems["v"], 1))
                proj_bank_v[bank] = P.bump("v")
            v_row = P.cnt["v"]
            if kind == "K":
                P.wait("sync", "v", v_row)
                P.emit("sync", lambda jt=jt: nc.sync.dma_start(
                    out=K_dram[128 * jt:128 * (jt + 1), :], in_=krow)
                    .then_inc(sems["st"], 16))
                st_c = P.bump("st", 16)
                P.wait("vector", "st", st_c)
            elif kind == "Q":
                P.wait("sync", "v", v_row)
                P.emit("sync", lambda jt=jt: nc.sync.dma_start(
                    out=Q_dram[128 * jt:128 * (jt + 1), :], in_=krow[:, 0:NO])
                    .then_inc(sems["st"], 16))
                st_c = P.bump("st", 16)
                P.wait("vector", "st", st_c)
            elif kind == "O":
                P.wait("vector", "v", v_row)
                o_rows_v.append(v_row)
                _pd_consume(jt)
            elif kind == "V":
                P.wait("tensor", "v", v_row)
                for kc in range(32):
                    bk = 2 + (kc % 2)
                    P.emit("tensor", lambda kc=kc, bk=bk:
                           nc.tensor.transpose(B[bk][:, 0:128],
                                               krow[:, 128 * kc:128 * (kc + 1)], eye)
                           .then_inc(sems["t"], 1))
                    t_v = P.bump("t")
                    P.wait("vector", "t", t_v)
                    P.emit("vector", lambda kc=kc, bk=bk:
                           v.tensor_copy(krow[:, 128 * kc:128 * (kc + 1)],
                                         B[bk][:, 0:128]).then_inc(sems["v"], 1))
                    v_vt = P.bump("v")
                    P.wait("tensor", "v", v_vt)
                P.wait("sync", "v", P.cnt["v"])
                dstap = Vt_dram.rearrange("(c p) (a j) -> p c a j", p=128, j=128)[:, :, jt]
                srcap = krow.rearrange("p (c j) -> p c j", j=128)
                P.emit("sync", lambda dstap=dstap, srcap=srcap: nc.sync.dma_start(
                    out=dstap, in_=srcap).then_inc(sems["st"], 16))
                st_c = P.bump("st", 16)
                P.wait("vector", "st", st_c)
        return o_rows_v

    wcol_last_t[0] = mod_mm_done[94]
    wcol_last_t[1] = mod_mm_done[95]
    emit_wstream("K", Ip, NT, tm896, chb, NCH, NT)
    emit_wstream("V", Ip, NT, tm896, chb, NCH, NT)
    emit_wstream("Q", Ip, NT, tm896, chb, NCH_OWN, NO)
    v_kvq_store = P.cnt["st"]

    if tap_d:
        tap_wait([("st", v_kvq_store)])
        tap("Krow0", K_dram[0:128, :])
        tap("Vt0", Vt_dram[0:128, :])
        tap("Q0", Q_dram[0:128, :])
    if upto == "qkv":
        _finalize(nc, P, sems, es)
        return nc

    P.emit("scalar", lambda: nc.scalar.preload_act_table(AF.Exp))
    P.wait("tensor", "st", v_kvq_store)
    P.wait("sync", "st", v_kvq_store)
    P.wait("vector", "st", v_kvq_store)
    for i in range(2):
        vv = v65[i].rearrange("p (i j) -> p i j", j=65)[:, :, 64:65]
        P.emit("vector", lambda vv=vv: nc.vector.memset(vv, 1.0))
    P.emit("vector", lambda: nc.vector.engine_nop().then_inc(sems["v"], 1))
    v_ones_pc = P.bump("v")

    t_pv = {}
    t_qk = {}
    a_exp = {}
    t_last_qk_of_grp = {}
    v_epi = {}
    for qh in range(2):
        for grp in range(8):
            gi = qh * 8 + grp
            qb = q4[gi % 2]
            qsem = "qa" if gi % 2 == 0 else "qb"
            if gi >= 2:
                P.wait("sync", "t", t_last_qk_of_grp[gi - 2])
            P.emit("sync", lambda qb=qb, qh=qh, grp=grp, qsem=qsem:
                   nc.sync.dma_start(
                       out=qb.rearrange("p (a t) -> p a t", a=2),
                       in_=Q_dram[256 * grp:256 * (grp + 1), 512 * qh:512 * (qh + 1)]
                       .rearrange("(a p) t -> p a t", p=128))
                   .then_inc(sems[qsem], 16))
            v_q = P.bump(qsem, 16)
            for kc in range(32):
                kb, vb = kcb[kc % 2], v65[kc % 2]
                if kc >= 2:
                    P.wait("sync", "t", t_pv[(gi, kc - 2)])
                elif gi >= 1:
                    P.wait("sync", "t", t_pv[(gi - 1, 30 + kc)])
                ksem = "ka" if kc % 2 == 0 else "kb"
                P.emit("sync", lambda kb=kb, grp=grp, kc=kc, ksem=ksem:
                       nc.sync.dma_start(
                           out=kb.rearrange("p (a t) -> p a t", a=2),
                           in_=K_dram[256 * grp:256 * (grp + 1), 128 * kc:128 * (kc + 1)]
                           .rearrange("(a p) t -> p a t", p=128))
                       .then_inc(sems[ksem], 16))
                P.bump(ksem, 16)
                P.emit("sync", lambda vb=vb, grp=grp, kc=kc, ksem=ksem:
                       nc.sync.dma_start(
                           out=vb.rearrange("p (i j) -> p i j", j=65)[:, :, 0:64],
                           in_=Vt_dram[128 * kc:128 * (kc + 1), 256 * grp:256 * (grp + 1)]
                           .rearrange("p (i j) -> p i j", j=64))
                       .then_inc(sems[ksem], 16))
                v_v = P.bump(ksem, 16)
                P.wait("tensor", ksem, v_v)
                if kc == 0:
                    P.wait("tensor", qsem, v_q)
                    if gi == 0:
                        P.wait("tensor", "v", v_ones_pc)
                    if gi >= 1:
                        P.wait("tensor", "v", v_epi[gi - 1])
                if kc >= 1:
                    P.wait("tensor", "a", a_exp[(gi, kc - 1)])
                for pr in range(2):
                    for o in range(2):
                        h_i = 2 * pr + o
                        bank = B[h_i]
                        lhsT = kb[64 * o:64 * o + 64, 128 * pr:128 * (pr + 1)]
                        rhs = qb[64 * o:64 * o + 64, 512 * pr:512 * (pr + 1)]
                        fin = h_i == 3
                        if fin:
                            P.emit("tensor", lambda bank=bank, lhsT=lhsT, rhs=rhs, o=o:
                                   nc.tensor.matmul(bank, lhsT, rhs, start=True,
                                                    stop=True,
                                                    tile_position=(64 * o, 0))
                                   .then_inc(sems["t"], 1))
                            t_qk[(gi, kc)] = P.bump("t")
                        else:
                            P.emit("tensor", lambda bank=bank, lhsT=lhsT, rhs=rhs, o=o:
                                   nc.tensor.matmul(bank, lhsT, rhs, start=True,
                                                    stop=True,
                                                    tile_position=(64 * o, 0)))
                if kc == 31:
                    t_last_qk_of_grp[gi] = t_qk[(gi, kc)]
                if kc >= 1:
                    for h_i in range(4):
                        pb = probs[((kc - 1) % 2) * 4 + h_i]
                        lhsT = v65[(kc - 1) % 2][:, 65 * h_i:65 * h_i + 65]
                        fin = h_i == 3
                        if fin:
                            P.emit("tensor", lambda lhsT=lhsT, pb=pb, h_i=h_i, kc=kc:
                                   nc.tensor.matmul(B[4 + h_i][0:65, :], lhsT, pb,
                                                    start=(kc - 1 == 0), stop=(kc - 1 == 31))
                                   .then_inc(sems["t"], 1))
                            t_pv[(gi, kc - 1)] = P.bump("t")
                        else:
                            P.emit("tensor", lambda lhsT=lhsT, pb=pb, h_i=h_i, kc=kc:
                                   nc.tensor.matmul(B[4 + h_i][0:65, :], lhsT, pb,
                                                    start=(kc - 1 == 0), stop=(kc - 1 == 31)))
                P.wait("scalar", "t", t_qk[(gi, kc)])
                if kc >= 2:
                    P.wait("scalar", "t", t_pv[(gi, kc - 2)])
                for h_i in range(4):
                    pb = probs[(kc % 2) * 4 + h_i]
                    fin = h_i == 3
                    if fin:
                        P.emit("scalar", lambda pb=pb, h_i=h_i: nc.scalar.activation(
                            pb, B[h_i], AF.Exp).then_inc(sems["a"], 1))
                        a_exp[(gi, kc)] = P.bump("a")
                    else:
                        P.emit("scalar", lambda pb=pb, h_i=h_i: nc.scalar.activation(
                            pb, B[h_i], AF.Exp))
            P.wait("tensor", "a", a_exp[(gi, 31)])
            for h_i in range(4):
                pb = probs[(31 % 2) * 4 + h_i]
                lhsT = v65[31 % 2][:, 65 * h_i:65 * h_i + 65]
                if h_i == 3:
                    P.emit("tensor", lambda lhsT=lhsT, pb=pb, h_i=h_i:
                           nc.tensor.matmul(B[4 + h_i][0:65, :], lhsT, pb,
                                            start=False, stop=True)
                           .then_inc(sems["t"], 1))
                    t_pv[(gi, 31)] = P.bump("t")
                else:
                    P.emit("tensor", lambda lhsT=lhsT, pb=pb, h_i=h_i:
                           nc.tensor.matmul(B[4 + h_i][0:65, :], lhsT, pb,
                                            start=False, stop=True))
            P.wait("vector", "t", t_pv[(gi, 31)])
            for h_i in range(4):
                P.emit("vector", lambda h_i=h_i: nc.vector.tensor_copy(
                    rsum65[64:65, 512 * h_i:512 * (h_i + 1)], B[4 + h_i][64:65, :]))
            P.emit("vector", lambda: nc.vector.engine_nop().then_inc(sems["v"], 1))
            v_s4 = P.bump("v")
            P.wait("gpsimd", "v", v_s4)
            P.emit("gpsimd", lambda: nc.gpsimd.dma_start(
                out=rsum0, in_=rsum65[64:65, :]).then_inc(sems["g"], 16))
            g_r = P.bump("g", 16)
            P.wait("vector", "g", g_r)
            P.emit("vector", lambda: nc.vector.reciprocal(rq0, rsum0)
                   .then_inc(sems["v"], 1))
            v_rq = P.bump("v")
            P.wait("tensor", "v", v_rq)
            for h_i in range(4):
                H = 4 * grp + h_i
                pcH, half = H // 2, H % 2
                P.emit("tensor", lambda h_i=h_i: nc.tensor.matmul(
                    B[h_i][0:64, :], ones1[0:1, 0:64],
                    rq0[0:1, 512 * h_i:512 * (h_i + 1)], start=True,
                    stop=True).then_inc(sems["t"], 1))
                t_b = P.bump("t")
                P.wait("vector", "t", t_b)
                P.emit("vector", lambda h_i=h_i: nc.vector.tensor_copy(
                    rsb, B[h_i][0:64, :]))
                if half == 0:
                    dst = ctx_fm[0:64, pcH * NO + qh * 512: pcH * NO + qh * 512 + 512]
                    P.emit("vector", lambda dst=dst, h_i=h_i: nc.vector.tensor_tensor(
                        dst, B[4 + h_i][0:64, :], rsb, OP.mult)
                        .then_inc(sems["v"], 1))
                    P.bump("v")
                else:
                    P.emit("vector", lambda h_i=h_i: nc.vector.tensor_tensor(
                        ostg, B[4 + h_i][0:64, :], rsb, OP.mult)
                        .then_inc(sems["v"], 1))
                    v_o = P.bump("v")
                    P.wait("gpsimd", "v", v_o)
                    dst = ctx_fm[64:128, pcH * NO + qh * 512: pcH * NO + qh * 512 + 512]
                    P.emit("gpsimd", lambda dst=dst: nc.gpsimd.dma_start(
                        out=dst, in_=ostg).then_inc(sems["g"], 16))
                    g_c = P.bump("g", 16)
                    P.wait("vector", "g", g_c)
            P.emit("vector", lambda: nc.vector.engine_nop().then_inc(sems["v"], 1))
            v_epi[gi] = P.bump("v")
    v_attn_done_v = P.cnt["v"]
    g_attn_done = P.cnt["g"]

    if tap_d:
        tap_wait([("v", v_attn_done_v), ("g", g_attn_done)])
        tap("ctx", ctx_fm)
    if upto == "attn":
        _finalize(nc, P, sems, es)
        return nc

    P.wait("vector", "g", g_attn_done)
    for c in range(NCH_OWN):
        v_qdone2 = emit_quant_chunk(
            src_fm=lambda pc, c=c: ctx_fm[:, pc * NO + c * CHK: pc * NO + (c + 1) * CHK],
            chb_dst=lambda pc, c=c: chb2[:, pc * 16 + c: pc * 16 + c + 1],
            tm_dst=tm896o[:, c:c + 1],
            ip_dst=lambda pc, c=c: I2[:, pc * NO + c * CHK: pc * NO + (c + 1) * CHK],
        )
    if tap_d:
        tap_wait([("v", v_qdone2)])
        tap("I2", I2[:, 0:NO])
        tap("chb2", chb2)
    P.wait("scalar", "v", v_qdone2)
    for t in range(8):
        P.emit("scalar", lambda t=t: nc.scalar.dma_start(
            out=x2_sb[:, t * D:(t + 1) * D], in_=x_d[128 * t:128 * (t + 1), :])
            .then_inc(sems["xa"], 16))
        P.bump("xa", 16)
    v_x2ld = P.cnt["xa"]
    P.wait("vector", "xa", v_x2ld)
    P.wait("tensor", "xa", v_x2ld)

    def _pd_consume(jt):
        v = nc.vector
        P.emit("vector", lambda jt=jt: v.tensor_scalar(
            pd_scr, krow[:, 0:NO], ga_v[:, jt:jt + 1], None, OP.mult)
            .then_inc(sems["v"], 1))
        v_g = P.bump("v")
        P.wait("tensor", "v", v_g)
        for tt in range(8):
            bk = 2 + tt % 2
            P.emit("tensor", lambda tt=tt, bk=bk: nc.tensor.transpose(
                B[bk][:, 0:128], pd_scr[:, 128 * tt:128 * (tt + 1)], eye)
                .then_inc(sems["t"], 1))
            t_t = P.bump("t")
            P.wait("vector", "t", t_t)
            dst = x2_sb[:, tt * D + jt * 128: tt * D + jt * 128 + 128]
            P.emit("vector", lambda dst=dst, bk=bk: v.tensor_tensor(
                dst, B[bk][:, 0:128], dst, OP.add).then_inc(sems["v"], 1))
            v_a2 = P.bump("v")
            P.wait("tensor", "v", v_a2)

    emit_wstream("O", I2, NO, tm896o, chb2, NCH_OWN, NO)
    v_x2_done = P.cnt["v"]
    if tap_d:
        tap_wait([("v", v_x2_done)])
        tap("x2", x2_sb)
    if upto == "x2":
        _finalize(nc, P, sems, es)
        return nc

    P.wait("sync", "v", v_x2_done)
    for t in range(8):
        P.emit("sync", lambda t=t: nc.sync.dma_start(
            out=x2_dram[128 * t:128 * (t + 1), :], in_=x2_sb[:, t * D:(t + 1) * D])
            .then_inc(sems["st"], 16))
        P.bump("st", 16)
    st_x2 = P.cnt["st"]
    P.emit("scalar", lambda: nc.scalar.preload_act_table(AF.Sqrt))
    P.wait("vector", "v", v_x2_done)
    for t in range(8):
        xt = wcol[0]
        P.emit("vector", lambda t=t, xt=xt: nc.vector.tensor_copy(
            xt, x2_sb[:, t * D:(t + 1) * D]))
        emit_ln_tile(xt)
        P.emit("vector", lambda: nc.vector.engine_nop().then_inc(sems["v"], 1))
        v_xh = P.bump("v")
        P.wait("tensor", "v", v_xh)
        for pcg in range(4):
            for pc in range(4 * pcg, 4 * pcg + 4):
                bank = B[pc % 4]
                fin = pc % 4 == 3
                if fin:
                    P.emit("tensor", lambda pc=pc, bank=bank, xt=xt:
                           nc.tensor.transpose(bank[:, 0:128],
                                               xt[:, 128 * pc:128 * (pc + 1)], eye)
                           .then_inc(sems["t"], 1))
                    t_tr = P.bump("t")
                else:
                    P.emit("tensor", lambda pc=pc, bank=bank, xt=xt:
                           nc.tensor.transpose(bank[:, 0:128],
                                               xt[:, 128 * pc:128 * (pc + 1)], eye))
            P.wait("vector", "t", t_tr)
            for pc in range(4 * pcg, 4 * pcg + 4):
                bank = B[pc % 4]
                dst = h2_fm[:, pc * NO + t * 128: pc * NO + t * 128 + 128]
                P.emit("vector", lambda pc=pc, bank=bank, dst=dst:
                       nc.vector.tensor_scalar(
                           dst, bank[:, 0:128], cb_v[:, pc:pc + 1],
                           modv[:, 48 + pc:48 + pc + 1], OP.mult, OP.add))
            P.emit("vector", lambda: nc.vector.engine_nop().then_inc(sems["v"], 1))
            v_ev = P.bump("v")
            P.wait("tensor", "v", v_ev)
    v_h2 = P.cnt["v"]
    if tap_d:
        tap_wait([("v", v_h2)])
        tap("h2", h2_fm)
    if upto == "h2":
        _finalize(nc, P, sems, es)
        return nc

    P.emit("sync", lambda: nc.sync.dma_start(out=gate_sb, in_=gate_d)
           .then_inc(sems["gg"], 16))
    P.emit("sync", lambda: nc.sync.dma_start(
        out=ebias_r, in_=ebias_d.rearrange("(a e) -> a e", a=1))
        .then_inc(sems["gg"], 16))
    v_g8 = P.bump("gg", 32)
    P.wait("tensor", "gg", v_g8)
    P.wait("tensor", "v", v_h2)
    for kc in range(16):
        bk = 2 + kc % 2
        P.emit("tensor", lambda kc=kc, bk=bk: nc.tensor.transpose(
            B[bk][:, 0:8], gate_sb[:, 128 * kc:128 * (kc + 1)], eye[0:8, 0:8])
            .then_inc(sems["t"], 1))
        t_g = P.bump("t")
        P.wait("vector", "t", t_g)
        P.emit("vector", lambda kc=kc, bk=bk: nc.vector.tensor_copy(
            gtT[:, 8 * kc:8 * (kc + 1)], B[bk][:, 0:8])
            .then_inc(sems["v"], 1))
        v_gt = P.bump("v")
        P.wait("tensor", "v", v_gt)
    P.emit("tensor", lambda: nc.tensor.matmul(
        B[6][:, 0:8], ones1, ebias_r, start=True, stop=True).then_inc(sems["t"], 1))
    t_eb = P.bump("t")
    P.wait("vector", "t", t_eb)
    P.emit("vector", lambda: nc.vector.tensor_copy(ebias_b, B[6][:, 0:8]))
    P.emit("scalar", lambda: nc.scalar.preload_act_table(AF.Sigmoid))
    for tt in range(2):
        bank = tt % 2
        for kc in range(16):
            lhsT = gtT[:, 8 * kc:8 * (kc + 1)]
            rhs = h2_fm[:, kc * NO + tt * 512: kc * NO + tt * 512 + 512]
            if kc == 15:
                P.emit("tensor", lambda lhsT=lhsT, rhs=rhs, bank=bank:
                       nc.tensor.matmul(B[bank][0:8, :], lhsT, rhs,
                                        start=False, stop=True)
                       .then_inc(sems["t"], 1))
                t_gm = P.bump("t")
            else:
                P.emit("tensor", lambda lhsT=lhsT, rhs=rhs, kc=kc, bank=bank:
                       nc.tensor.matmul(B[bank][0:8, :], lhsT, rhs,
                                        start=(kc == 0), stop=False))
        P.wait("scalar", "t", t_gm)
        P.emit("scalar", lambda tt=tt, bank=bank: nc.scalar.activation(
            scoresT[:, 512 * tt:512 * (tt + 1)], B[bank][0:8, :], AF.Sigmoid)
            .then_inc(sems["a"], 1))
        P.bump("a")
    a_sc = P.cnt["a"]
    P.wait("tensor", "a", a_sc)
    for tti in range(8):
        bk = 2 + tti % 2
        P.emit("tensor", lambda tti=tti, bk=bk: nc.tensor.transpose(
            B[bk][:, 0:8], scoresT[:, 128 * tti:128 * (tti + 1)], eye[0:8, 0:8])
            .then_inc(sems["t"], 1))
        t_s = P.bump("t")
        P.wait("vector", "t", t_s)
        P.emit("vector", lambda tti=tti, bk=bk: nc.vector.tensor_copy(
            sc_tok[:, 8 * tti:8 * (tti + 1)], B[bk][:, 0:8]))
        P.emit("vector", lambda tti=tti: nc.vector.tensor_tensor(
            bs_tok, sc_tok[:, 8 * tti:8 * (tti + 1)], ebias_b, OP.add))
        P.emit("vector", lambda: nc.vector.engine_nop().then_inc(sems["v"], 1))
        v_bs = P.bump("v")
        P.wait("tensor", "v", v_bs)
        v = nc.vector
        P.autogap = True
        for e in range(8):
            be = bs_tok[:, e:e + 1]
            P.emit("vector", lambda be=be: v.tensor_scalar(
                cmp8, bs_tok, be, None, OP.is_gt))
            P.emit("vector", lambda: v.tensor_reduce(cnt1, cmp8, axis=AX.X, op=OP.add))
            if e > 0:
                P.emit("vector", lambda be=be: v.tensor_scalar(
                    cmp8, bs_tok, be, None, OP.is_equal))
                P.emit("vector", lambda e=e: v.tensor_reduce(
                    cnt2, cmp8[:, 0:e], axis=AX.X, op=OP.add))
                P.emit("vector", lambda: v.tensor_tensor(cnt1, cnt1, cnt2, OP.add))
            P.emit("vector", lambda: v.tensor_scalar(selc, cnt1, 1.5, None, OP.is_lt))
            P.emit("vector", lambda tti=tti, e=e: v.tensor_tensor(
                wt_tok[:, 8 * tti + e: 8 * tti + e + 1], selc,
                sc_tok[:, 8 * tti + e: 8 * tti + e + 1], OP.mult))
        P.autogap = False
        P.emit("vector", lambda: nc.vector.engine_nop().then_inc(sems["v"], 1))
        v_wt = P.bump("v")
        P.wait("tensor", "v", v_wt)
        P.emit("tensor", lambda tti=tti, bk=bk: nc.tensor.transpose(
            B[bk][0:8, 0:128], wt_tok[:, 8 * tti:8 * (tti + 1)], eye)
            .then_inc(sems["t"], 1))
        t_w = P.bump("t")
        P.wait("vector", "t", t_w)
        P.emit("vector", lambda tti=tti, bk=bk: nc.vector.tensor_copy(
            wt_fm[:, 128 * tti:128 * (tti + 1)], B[bk][0:8, 0:128])
            .then_inc(sems["v"], 1))
        v_wf = P.bump("v")
        P.wait("tensor", "v", v_wf)
    v_masks = P.cnt["v"]
    if tap_d:
        tap_wait([("v", v_masks)])
        tap("wt_fm", wt_fm)

    P.emit("scalar", lambda: nc.scalar.preload_act_table(AF.Silu))

    def stream_expert_w(w_slice_src, n_slices, dst_col, kc_count, first_wait=None):
        inner = 128 * kc_count
        for hs in range(n_slices):
            P.emit("gpsimd", lambda hs=hs, inner=inner: nc.gpsimd.dma_start(
                out=mw_sl[:, 0:inner], in_=w_slice_src(hs)).then_inc(sems["g"], 16))
            g_w = P.bump("g", 16)
            P.wait("tensor", "g", g_w)
            if first_wait is not None and hs == 0:
                P.wait("tensor", "v", first_wait)
            for kc in range(kc_count):
                bk = 6 + kc % 2
                fin = kc == kc_count - 1
                P.emit("tensor", lambda kc=kc, bk=bk: nc.tensor.transpose(
                    B16[bk][:, 0:128], mw_sl[:, 128 * kc:128 * (kc + 1)], eye16)
                    .then_inc(sems["t"], 1))
                t_t = P.bump("t")
                P.wait("vector", "t", t_t)
                dst = dst_col(kc, hs)
                P.emit("vector", lambda dst=dst, bk=bk: nc.vector.tensor_copy(
                    dst, B16[bk][:, 0:128]).then_inc(sems["v"], 1))
                v_t = P.bump("v")
                P.wait("tensor", "v", v_t)
            P.wait("gpsimd", "t", P.cnt["t"])
        return P.cnt["v"]

    P.wait("vector", "st", st_x2)
    v_xb_prev = None
    t_wtb_prev = None
    for e in range(8):
        P.wait("gpsimd", "v", v_masks)
        if t_wtb_prev is not None:
            P.wait("gpsimd", "t", t_wtb_prev)
        P.emit("gpsimd", lambda e=e: nc.gpsimd.dma_start(
            out=wtrow0, in_=wt_fm[e:e + 1, :]).then_inc(sems["gw"], 16))
        g_wr = P.bump("gw", 16)
        P.wait("tensor", "gw", g_wr)
        P.wait("tensor", "v", v_masks)
        if v_xb_prev is not None:
            P.wait("tensor", "v", v_xb_prev)
        for tt in range(2):
            P.emit("tensor", lambda e=e, tt=tt: nc.tensor.matmul(
                B[4 + tt][:, 0:512], ones1, wtrow0[0:1, 512 * tt:512 * (tt + 1)],
                start=True, stop=True).then_inc(sems["t"], 1))
            t_wb = P.bump("t")
            t_wtb_prev = t_wb
            P.wait("vector", "t", t_wb)
            P.emit("vector", lambda tt=tt: nc.vector.tensor_copy(
                wtb_sb[:, 512 * tt:512 * (tt + 1)], B[4 + tt][:, 0:512]))
        P.emit("vector", lambda: nc.vector.engine_nop().then_inc(sems["v"], 1))
        v_wtb = P.bump("v")
        for pc in range(16):
            P.emit("vector", lambda pc=pc: nc.vector.tensor_tensor(
                xb_sb[:, pc * NO:(pc + 1) * NO], h2_fm[:, pc * NO:(pc + 1) * NO],
                wtb_sb, OP.mult))
        P.emit("vector", lambda: nc.vector.engine_nop().then_inc(sems["v"], 1))
        v_xb = P.bump("v")
        v_xb_prev = v_xb
        v_w1 = stream_expert_w(
            lambda hs, e=e: w1_d[e, 128 * hs:128 * (hs + 1), :],
            8, lambda kc, hs: wexp[:, kc * KB + hs * 128: kc * KB + hs * 128 + 128],
            16)
        P.wait("tensor", "v", max(v_w1, v_xb))
        a_s1_last = None
        for hs in range(8):
            for tt in range(2):
                bank = tt % 2
                for kc in range(16):
                    lhsT = wexp[:, kc * KB + hs * 128: kc * KB + hs * 128 + 128]
                    rhs = xb_sb[:, kc * NO + tt * 512: kc * NO + tt * 512 + 512]
                    if kc == 15:
                        P.emit("tensor", lambda lhsT=lhsT, rhs=rhs, bank=bank:
                               nc.tensor.matmul(B[bank], lhsT, rhs, start=False,
                                                stop=True).then_inc(sems["t"], 1))
                        t_h1 = P.bump("t")
                    else:
                        P.emit("tensor", lambda lhsT=lhsT, rhs=rhs, kc=kc, bank=bank:
                               nc.tensor.matmul(B[bank], lhsT, rhs,
                                                start=(kc == 0), stop=False))
                P.wait("vector", "t", t_h1)
                if a_s1_last is not None and tt == 0:
                    P.wait("vector", "a", a_s1_last)
                P.emit("vector", lambda tt=tt, bank=bank: nc.vector.tensor_copy(
                    h1b_st[:, 512 * tt:512 * (tt + 1)], B[bank]).then_inc(sems["v"], 1))
                v_h1b = P.bump("v")
                P.wait("tensor", "v", v_h1b)
            P.wait("scalar", "v", v_h1b)
            P.emit("scalar", lambda hs=hs: nc.scalar.activation(
                s1_sb[:, hs * KB:(hs + 1) * KB], h1b_st, AF.Silu)
                .then_inc(sems["a"], 1))
            a_s1_last = P.bump("a")
        a_s1 = P.cnt["a"]
        v_w3 = stream_expert_w(
            lambda hs, e=e: w3_d[e, 128 * hs:128 * (hs + 1), :],
            8, lambda kc, hs: wexp[:, kc * KB + hs * 128: kc * KB + hs * 128 + 128],
            16, first_wait=P.cnt["v"])
        P.wait("tensor", "v", v_w3)
        P.wait("vector", "a", a_s1)
        for hs in range(8):
            for tt in range(2):
                bank = tt % 2
                for kc in range(16):
                    lhsT = wexp[:, kc * KB + hs * 128: kc * KB + hs * 128 + 128]
                    rhs = xb_sb[:, kc * NO + tt * 512: kc * NO + tt * 512 + 512]
                    if kc == 15:
                        P.emit("tensor", lambda lhsT=lhsT, rhs=rhs, bank=bank:
                               nc.tensor.matmul(B[bank], lhsT, rhs, start=False,
                                                stop=True).then_inc(sems["t"], 1))
                        t_h3 = P.bump("t")
                    else:
                        P.emit("tensor", lambda lhsT=lhsT, rhs=rhs, kc=kc, bank=bank:
                               nc.tensor.matmul(B[bank], lhsT, rhs,
                                                start=(kc == 0), stop=False))
                P.wait("vector", "t", t_h3)
                P.emit("vector", lambda tt=tt, bank=bank: nc.vector.tensor_copy(
                    h3b_st[:, 512 * tt:512 * (tt + 1)], B[bank]))
            P.emit("vector", lambda hs=hs: nc.vector.tensor_tensor(
                hh_sb[:, hs * KB:(hs + 1) * KB], s1_sb[:, hs * KB:(hs + 1) * KB],
                h3b_st, OP.mult).then_inc(sems["v"], 1))
            v_hh = P.bump("v")
            P.wait("tensor", "v", v_hh)
        v_w2 = stream_expert_w(
            lambda hs, e=e: w2_d[e, 128 * hs:128 * (hs + 1), :],
            16, lambda kc, hs: wexp[:, kc * 2 * KB + hs * 128: kc * 2 * KB + hs * 128 + 128],
            8, first_wait=P.cnt["v"])
        P.wait("tensor", "v", v_w2)
        for dt_ in range(16):
            for tt in range(2):
                bank = tt % 2
                for hc in range(8):
                    lhsT = wexp[:, hc * 2 * KB + dt_ * 128: hc * 2 * KB + dt_ * 128 + 128]
                    rhs = hh_sb[:, hc * KB + tt * 512: hc * KB + tt * 512 + 512]
                    if hc == 7:
                        P.emit("tensor", lambda lhsT=lhsT, rhs=rhs, bank=bank:
                               nc.tensor.matmul(B[bank], lhsT, rhs, start=False,
                                                stop=True).then_inc(sems["t"], 1))
                        t_oe = P.bump("t")
                    else:
                        P.emit("tensor", lambda lhsT=lhsT, rhs=rhs, hc=hc, bank=bank:
                               nc.tensor.matmul(B[bank], lhsT, rhs,
                                                start=(hc == 0), stop=False))
                P.wait("vector", "t", t_oe)
                P.emit("vector", lambda tt=tt, bank=bank: nc.vector.tensor_copy(
                    oestg[:, 512 * tt:512 * (tt + 1)], B[bank]).then_inc(sems["v"], 1))
                v_oe = P.bump("v")
                P.wait("tensor", "v", v_oe)
            P.wait("sync", "v", P.cnt["v"])
            P.emit("sync", lambda e=e, dt_=dt_: nc.sync.dma_start(
                out=oe_dram[e, 128 * dt_:128 * (dt_ + 1), :], in_=oestg)
                .then_inc(sems["st"], 16))
            st_oe = P.bump("st", 16)
            P.wait("vector", "st", st_oe)
    st_experts = P.cnt["st"]

    if upto == "experts":
        _finalize(nc, P, sems, es)
        return nc

    P.emit("scalar", lambda: nc.scalar.preload_act_table(AF.Silu))
    a_w1_done = None
    for phase in ("w1", "w3"):
        if phase == "w3":
            a_w1_done = P.cnt["a"]
            P.wait("vector", "a", a_w1_done)
        for hs in range(8):
            wd = sw1_d if phase == "w1" else sw3_d
            P.emit("sync", lambda wd=wd, hs=hs: nc.sync.dma_start(
                out=ssl, in_=wd[128 * hs:128 * (hs + 1), :]).then_inc(sems["sa"], 16))
            v_s = P.bump("sa", 16)
            P.wait("tensor", "sa", v_s)
            for kc in range(16):
                bk = 6 + kc % 2
                P.emit("tensor", lambda kc=kc, bk=bk: nc.tensor.transpose(
                    B[bk][:, 0:128], ssl[:, 128 * kc:128 * (kc + 1)], eye)
                    .then_inc(sems["t"], 1))
                t_t = P.bump("t")
                P.wait("vector", "t", t_t)
                P.emit("vector", lambda kc=kc, bk=bk: nc.vector.tensor_copy(
                    swt[:, 128 * kc:128 * (kc + 1)], B[bk][:, 0:128])
                    .then_inc(sems["v"], 1))
                v_t = P.bump("v")
                P.wait("tensor", "v", v_t)
            P.wait("sync", "t", P.cnt["t"])
            for tt in range(2):
                bank = tt % 2
                for kc in range(16):
                    lhsT = swt[:, 128 * kc:128 * (kc + 1)]
                    rhs = h2_fm[:, kc * NO + tt * 512: kc * NO + tt * 512 + 512]
                    if kc == 15:
                        P.emit("tensor", lambda lhsT=lhsT, rhs=rhs, bank=bank:
                               nc.tensor.matmul(B[bank], lhsT, rhs, start=False,
                                                stop=True).then_inc(sems["t"], 1))
                        t_h = P.bump("t")
                    else:
                        P.emit("tensor", lambda lhsT=lhsT, rhs=rhs, kc=kc, bank=bank:
                               nc.tensor.matmul(B[bank], lhsT, rhs,
                                                start=(kc == 0), stop=False))
                if phase == "w1":
                    P.wait("scalar", "t", t_h)
                    P.emit("scalar", lambda hs=hs, tt=tt, bank=bank:
                           nc.scalar.activation(
                               hh_s[:, hs * KB + 512 * tt: hs * KB + 512 * tt + 512],
                               B[bank], AF.Silu).then_inc(sems["a"], 1))
                    a_h = P.bump("a")
                    P.wait("tensor", "a", a_h)
                else:
                    P.wait("vector", "t", t_h)
                    dst = hh_s[:, hs * KB + 512 * tt: hs * KB + 512 * tt + 512]
                    P.emit("vector", lambda dst=dst, bank=bank:
                           nc.vector.tensor_tensor(dst, B[bank], dst, OP.mult)
                           .then_inc(sems["v"], 1))
                    v_hh = P.bump("v")
                    P.wait("tensor", "v", v_hh)

    for ds in range(16):
        P.emit("sync", lambda ds=ds: nc.sync.dma_start(
            out=ssl[:, 0:1024], in_=sw2_d[128 * ds:128 * (ds + 1), :])
            .then_inc(sems["sa"], 16))
        v_s = P.bump("sa", 16)
        P.wait("tensor", "sa", v_s)
        for hc in range(8):
            bk = 6 + hc % 2
            P.emit("tensor", lambda hc=hc, bk=bk: nc.tensor.transpose(
                B[bk][:, 0:128], ssl[:, 128 * hc:128 * (hc + 1)], eye)
                .then_inc(sems["t"], 1))
            t_t = P.bump("t")
            P.wait("vector", "t", t_t)
            P.emit("vector", lambda hc=hc, bk=bk: nc.vector.tensor_copy(
                swt[:, 128 * hc:128 * (hc + 1)], B[bk][:, 0:128])
                .then_inc(sems["v"], 1))
            v_t = P.bump("v")
            P.wait("tensor", "v", v_t)
        P.wait("sync", "t", P.cnt["t"])
        for tt in range(2):
            bank = tt % 2
            for hc in range(8):
                lhsT = swt[:, 128 * hc:128 * (hc + 1)]
                rhs = hh_s[:, hc * KB + tt * 512: hc * KB + tt * 512 + 512]
                if hc == 7:
                    P.emit("tensor", lambda lhsT=lhsT, rhs=rhs, bank=bank:
                           nc.tensor.matmul(B[bank], lhsT, rhs, start=False,
                                            stop=True).then_inc(sems["t"], 1))
                    t_o = P.bump("t")
                else:
                    P.emit("tensor", lambda lhsT=lhsT, rhs=rhs, hc=hc, bank=bank:
                           nc.tensor.matmul(B[bank], lhsT, rhs,
                                            start=(hc == 0), stop=False))
            P.wait("vector", "t", t_o)
            dst = out_moe[:, ds * NO + 512 * tt: ds * NO + 512 * tt + 512]
            P.emit("vector", lambda dst=dst, bank=bank: nc.vector.tensor_copy(
                dst, B[bank]).then_inc(sems["v"], 1))
            v_o = P.bump("v")
            P.wait("tensor", "v", v_o)
    v_shared = P.cnt["v"]

    P.wait("scalar", "v", v_shared)
    for t in range(8):
        P.emit("scalar", lambda t=t: nc.scalar.dma_start(
            out=x2_sb[:, t * D:(t + 1) * D], in_=x2_dram[128 * t:128 * (t + 1), :])
            .then_inc(sems["xa"], 16))
        P.bump("xa", 16)
    v_x2r = P.cnt["xa"]
    P.wait("gpsimd", "st", st_experts)
    P.wait("gpsimd", "v", v_shared)
    g_oe = {}
    for e in range(8):
        for jt in range(16):
            i = e * 16 + jt
            ob = oe_buf[i % 2]
            osem = "goa" if i % 2 == 0 else "gob"
            if i >= 2:
                P.wait("gpsimd", "v", g_oe[i - 2])
            P.emit("gpsimd", lambda e=e, jt=jt, ob=ob, osem=osem:
                   nc.gpsimd.dma_start(
                       out=ob, in_=oe_dram[e, 128 * jt:128 * (jt + 1), :])
                   .then_inc(sems[osem], 16))
            g_l = P.bump(osem, 16)
            P.wait("vector", osem, g_l)
            dst = out_moe[:, jt * NO:(jt + 1) * NO]
            P.emit("vector", lambda dst=dst, ob=ob: nc.vector.tensor_tensor(
                dst, dst, ob, OP.add).then_inc(sems["v"], 1))
            g_oe[i] = P.bump("v")
    v_moe = P.cnt["v"]

    P.wait("vector", "xa", v_x2r)
    P.wait("tensor", "xa", v_x2r)
    for jt in range(16):
        P.emit("vector", lambda jt=jt: nc.vector.tensor_scalar(
            fin_scr, out_moe[:, jt * NO:(jt + 1) * NO], gb_v[:, jt:jt + 1],
            None, OP.mult).then_inc(sems["v"], 1))
        v_f = P.bump("v")
        P.wait("tensor", "v", v_f)
        for tt in range(8):
            bk = 2 + tt % 2
            P.emit("tensor", lambda tt=tt, bk=bk: nc.tensor.transpose(
                B[bk][:, 0:128], fin_scr[:, 128 * tt:128 * (tt + 1)], eye)
                .then_inc(sems["t"], 1))
            t_t = P.bump("t")
            P.wait("vector", "t", t_t)
            dst = x2_sb[:, tt * D + jt * 128: tt * D + jt * 128 + 128]
            P.emit("vector", lambda dst=dst, bk=bk: nc.vector.tensor_tensor(
                dst, B[bk][:, 0:128], dst, OP.add).then_inc(sems["v"], 1))
            v_a = P.bump("v")
            P.wait("tensor", "v", v_a)
    v_y = P.cnt["v"]
    P.wait("sync", "v", v_y)
    for t in range(8):
        P.emit("sync", lambda t=t: nc.sync.dma_start(
            out=y_d[128 * t:128 * (t + 1), :], in_=x2_sb[:, t * D:(t + 1) * D])
            .then_inc(sems["st"], 16))
        P.bump("st", 16)

    _finalize(nc, P, sems, es)
    return nc


def _finalize(nc, P, sems, es):
    for semn in ("st", "tap", "g", "goa", "gob", "xa", "xb"):
        if P.cnt[semn]:
            P.wait("sync", semn, P.cnt[semn])
    with nc.Block() as block:
        @block.tensor
        def _(eng):
            for f in P.q["tensor"]:
                f()

        @block.vector
        def _(eng):
            for f in P.q["vector"]:
                f()

        @block.scalar
        def _(eng):
            for f in P.q["scalar"]:
                f()

        @block.sync
        def _(eng):
            for f in P.q["sync"]:
                f()

        @block.gpsimd
        def _(eng):
            for f in P.q["gpsimd"]:
                f()
    es.close()


def _rot_idx(qi):
    own = list(range(4 * qi, 4 * qi + 4))
    rest = [c for c in range(16) if c not in own]
    idx = np.concatenate([np.arange(c * CHK, (c + 1) * CHK) for c in own + rest])
    return idx


def make_in_maps(inputs):
    x = np.ascontiguousarray(np.asarray(inputs["L0_attn_hidden_states"]))
    eye = np.eye(128, dtype=np.float32)
    eyeb = np.eye(128, dtype=ml_dtypes.bfloat16)
    shared = {
        "wq": np.ascontiguousarray(inputs["L0_attn_q_weight"]),
        "wk": np.ascontiguousarray(inputs["L0_attn_k_weight"]),
        "wv": np.ascontiguousarray(inputs["L0_attn_v_weight"]),
        "wo": np.ascontiguousarray(inputs["L0_attn_o_weight"]),
        "bq": np.ascontiguousarray(inputs["L0_attn_q_bias"]),
        "bk": np.ascontiguousarray(inputs["L0_attn_k_bias"]),
        "bv": np.ascontiguousarray(inputs["L0_attn_v_bias"]),
        "bo": np.ascontiguousarray(inputs["L0_attn_o_bias"]),
        "gate": np.ascontiguousarray(inputs["L0_ffn_gate"]),
        "ebias": np.ascontiguousarray(inputs["L0_ffn_expert_bias"]),
        "w1": np.ascontiguousarray(inputs["L0_ffn_w1"]),
        "w2": np.ascontiguousarray(inputs["L0_ffn_w2"]),
        "w3": np.ascontiguousarray(inputs["L0_ffn_w3"]),
        "sw1": np.ascontiguousarray(inputs["L0_ffn_shared_w1"]),
        "sw2": np.ascontiguousarray(inputs["L0_ffn_shared_w2"]),
        "sw3": np.ascontiguousarray(inputs["L0_ffn_shared_w3"]),
        "wmod": np.ascontiguousarray(inputs["L0_wmod"]),
        "eyef": eye, "eyeb": eyeb,
    }
    cfull = np.asarray(inputs["L0_c"])
    in_maps = []
    for c in range(8):
        b, qi = c // 4, c % 4
        m = dict(shared)
        m["x"] = np.ascontiguousarray(x[b][_rot_idx(qi)])
        m["cvec"] = np.ascontiguousarray(cfull[b])
        in_maps.append(m)
    return in_maps


def kernel(**inputs):
    nc = build("full")
    in_maps = make_in_maps(inputs)
    res = run_bass_kernel_spmd(nc, in_maps, core_ids=list(range(8)))
    out = np.zeros((2, 4096, 2048), dtype=np.float32)
    for c in range(8):
        b, qi = c // 4, c % 4
        out[b, qi * NO:(qi + 1) * NO] = np.asarray(res.results[c]["y"])
    return out


# version 03
# speedup vs baseline: 1.1764x; 1.1764x over previous
import numpy as np
import ml_dtypes
from contextlib import ExitStack

import concourse.bass as bass
import concourse.mybir as mybir
from concourse.alu_op_type import AluOpType as OP
from concourse.bass_utils import run_bass_kernel_spmd

F32 = mybir.dt.float32
BF16 = mybir.dt.bfloat16
AF = mybir.ActivationFunctionType
AX = mybir.AxisListType

D = 2048
NT = 4096
NO = 1024
CHK = 256
NCH = 16
NCH_OWN = 4
EPS = 1e-5
C7 = float(np.float32(1.0) / np.float32(7.0))
INV2048 = float(np.float32(1.0) / np.float32(2048.0))

ENGINES = ("tensor", "vector", "scalar", "sync", "gpsimd")
KB = 1024


class Prog:
    def __init__(self, nc, sems):
        self.nc = nc
        self.S = sems
        self.q = {e: [] for e in ENGINES}
        self.cnt = {k: 0 for k in sems}
        self.autogap = False
        self.gap_fn = None

    def emit(self, eng, fn):
        self.q[eng].append(fn)
        if eng == "vector" and self.gap_fn is not None:
            self.q[eng].append(self.gap_fn)

    def bump(self, sem, by=1):
        self.cnt[sem] += by
        return self.cnt[sem]

    def wait(self, eng, sem, val):
        nc, h = self.nc, self.S[sem]
        self.emit(eng, lambda: getattr(nc, eng).wait_ge(h, val))


def build(upto="full", taps=()):
    nc = bass.Bass("TRN2", target_bir_lowering=False)

    def dri(n, s, dt=F32):
        return nc.dram_tensor(n, s, dt, kind="ExternalInput").ap()

    x_d = dri("x", [NT, D])
    w_qkvo = {"q": dri("wq", [D, D]), "k": dri("wk", [D, D]),
              "v": dri("wv", [D, D]), "o": dri("wo", [D, D])}
    b_qkvo = {"q": dri("bq", [D]), "k": dri("bk", [D]),
              "v": dri("bv", [D]), "o": dri("bo", [D])}
    gate_d = dri("gate", [8, D])
    ebias_d = dri("ebias", [8])
    w1_d = dri("w1", [8, 1024, D])
    w2_d = dri("w2", [8, D, 1024])
    w3_d = dri("w3", [8, 1024, D])
    sw1_d = dri("sw1", [1024, D])
    sw2_d = dri("sw2", [D, 1024])
    sw3_d = dri("sw3", [1024, D])
    cvec_d = dri("cvec", [D])
    wmod_d = dri("wmod", [D, 6 * D])
    eye_d = dri("eyef", [128, 128])
    eyeb_d = nc.dram_tensor("eyeb", [128, 128], BF16, kind="ExternalInput").ap()

    y_d = nc.dram_tensor("y", [NO, D], F32, kind="ExternalOutput").ap()

    K_dram = nc.dram_tensor("K_dram", [D, NT], F32).ap()
    Vt_dram = nc.dram_tensor("Vt_dram", [NT, D], F32).ap()
    Q_dram = nc.dram_tensor("Q_dram", [D, NO], F32).ap()
    oe_dram = nc.dram_tensor("oe_dram", [8, D, NO], F32).ap()
    x2_dram = nc.dram_tensor("x2_dram", [NO, D], F32).ap()

    tap_d = {}
    for tname, tshape, tdt in taps:
        tap_d[tname] = nc.dram_tensor("tap_" + tname, list(tshape), tdt,
                                      kind="ExternalOutput").ap()

    AR = 192 * KB
    arena = nc.alloc_sbuf_tensor("arena", [128, AR], mybir.dt.uint8)
    base = nc.lookup_mloc(arena).addr

    def at(name, shape, dt, off):
        assert off + shape[1] * (4 if dt == F32 else 2 if dt == BF16 else 1) <= AR
        return nc.alloc_sbuf_tensor_at(name, shape, dt, offset=base + off).ap()

    R1, R2, R3 = 0, 64 * KB, 128 * KB
    Ip = at("Ip", [128, 16 * NT], BF16, R1)
    ctx_fm = at("ctx_fm", [128, 16 * NO], F32, R1)
    x2_sb = at("x2", [128, 8 * D], F32, R1)
    xb_sb = at("xb", [128, 16 * NO], BF16, R1)
    wexp = at("wexp", [128, 16 * KB], BF16, R1 + 32 * KB)
    ssl = at("ssl", [128, D], F32, R1)
    swt = at("swt", [128, D], F32, R1 + 8 * KB)
    s1s_sl = at("s1s", [128, 1024], F32, R1 + 16 * KB)
    hh_s = at("hhs", [128, 8 * KB], F32, R1 + 20 * KB)
    q4 = [at(f"q4_{i}", [128, 1024], F32, R2 + i * 4 * KB) for i in range(2)]
    kcb = [at(f"kcb{i}", [128, 256], F32, R2 + 8 * KB + i * KB) for i in range(2)]
    v65 = [at(f"v65{i}", [128, 260], F32, R2 + 10 * KB + i * 1056) for i in range(2)]
    probs = [at(f"prb{i}", [128, 512], F32, R2 + 13 * KB + i * 2 * KB)
             for i in range(8)]
    rsum65 = at("rsum65", [65, 4 * 512], F32, R2 + 29 * KB)
    rsum0 = at("rsum0", [1, 4 * 512], F32, R2 + 37 * KB)
    rq0 = at("rq0", [1, 4 * 512], F32, R2 + 45 * KB)
    rsb = at("rsb", [64, 512], F32, R2 + 53 * KB)
    ostg = at("ostg", [64, 512], F32, R2 + 55 * KB)
    I2 = at("I2", [128, 16 * NO], BF16, R2)
    pd_scr = at("pd_scr", [128, NO], F32, R2 + 32 * KB)
    pd_scr2 = at("pd_scr2", [128, NO], F32, R2 + 36 * KB)
    h2_fm = at("h2f", [128, 16 * NO], F32, R2)
    oe_buf = [at(f"oeb{i}", [128, NO], F32, R2 + i * 4 * KB) for i in range(2)]
    fin_scr = at("fin_scr", [128, NO], F32, R2 + 8 * KB)
    wcol = [at(f"wcol{i}", [128, D], F32, R3 + i * 8 * KB) for i in range(2)]
    qw_buf = at("qw_buf", [128, D], F32, R3 + 8 * KB)
    iwtT = at("iwtT", [128, D], BF16, R3 + 24 * KB)
    xt_buf = [at(f"xt{i}", [128, D], F32, R3 + 16 * KB + i * 8 * KB) for i in range(2)]
    qchunk = at("qchunk", [128, 16 * CHK], F32, R3 + 16 * KB)
    hfm = at("hfm", [128, 16 * CHK], F32, R3 + 32 * KB)
    lad_thr = at("lad_thr", [128, D], F32, R3 + 44 * KB)
    lad_bb = at("lad_bb", [128, D], F32, R3 + 52 * KB)
    scr2k = at("scr2k", [128, D], F32, R3 + 52 * KB)
    wts_f = at("wts_f", [128, D], F32, R3 + 16 * KB)
    iwt_sl = at("iwt", [128, D], BF16, R3 + 24 * KB)
    krow = at("krow", [128, NT], F32, R3 + 28 * KB)
    mw_sl = at("mw_sl", [128, D], F32, R3 + 36 * KB)
    s1_sb = at("s1", [128, 8 * KB], BF16, R3)
    hh_sb = at("hh", [128, 8 * KB], BF16, R3 + 16 * KB)
    oestg = at("oestg", [128, NO], F32, R3 + 32 * KB)
    out_moe = at("om", [128, 16 * NO], F32, R3)

    def sb(n, s, dt=F32):
        return nc.alloc_sbuf_tensor(n, s, dt).ap()

    eye = sb("eye", [128, 128])
    eye16 = sb("eye16", [128, 128], BF16)
    ones1 = sb("ones1", [1, 128])
    cvec = sb("cvecs", [128, 16])
    scv = sb("scv", [128, 16])
    modv = sb("modv", [128, 96])
    ca_v = sb("cav", [128, 16])
    ga_v = sb("gav", [128, 16])
    cb_v = sb("cbv", [128, 16])
    gb_v = sb("gbv", [128, 16])
    t5a = sb("t5a", [128, 16]); t5b = sb("t5b", [128, 16]); t5c = sb("t5c", [128, 16])
    chb = sb("chb", [128, 16 * NCH])
    tm896 = sb("tm896", [128, NCH])
    chb2 = sb("chb2", [128, 16 * 16])
    tm896o = sb("tm896o", [128, NCH_OWN])
    mxb = sb("mxb", [128, 16]); mnb = sb("mnb", [128, 16])
    cmaxb = sb("cmaxb", [128, 16]); p2b = sb("p2b", [128, 16])
    rscaleb = sb("rscaleb", [128, 16])
    tmcol = sb("tmcol", [128, 1]); tmrow = sb("tmrow", [1, 128])
    tmsc = sb("tmsc", [1, 1]); tmb = sb("tmb", [128, 1])
    thrj = sb("thrj", [128, 1]); tm7b = sb("tm7b", [128, 1])
    lnsm = sb("lnsm", [128, 1]); lnm = sb("lnm", [128, 1])
    lnv2 = sb("lnv2", [128, 1]); lna = sb("lna", [128, 1])
    lnsq = sb("lnsq", [128, 1]); lnr0 = sb("lnr0", [128, 1])
    lnt = sb("lnt", [128, 1]); lnr = sb("lnr", [128, 1])
    swv = sb("swv", [128, 1]); rswv = sb("rswv", [128, 1])
    swcv = sb("swcv", [128, 16]); cbbv = sb("cbbv", [128, 16])
    bias_sl = sb("bias_sl", [128, 1])
    gate_sb = at("gate_sb", [8, D], F32, R3 + 44 * KB)
    gtT = sb("gtT", [128, 16 * 8])
    scoresT = at("scoresT", [8, 1024], F32, R3 + 52 * KB)
    sc_tok = sb("sc_tok", [128, 8 * 8])
    bs_tok = sb("bs_tok", [128, 8])
    ebias_r = sb("ebias_r", [1, 8])
    ebias_b = sb("ebias_b", [128, 8])
    cnt1 = sb("cnt1", [128, 1]); cnt2 = sb("cnt2", [128, 1])
    cmp8 = sb("cmp8", [128, 8])
    selc = sb("selc", [128, 1])
    wt_tok = sb("wt_tok", [128, 8 * 8])
    wt_fm = at("wt_fm", [8, 1024], F32, R3 + 56 * KB)
    wtrow0 = at("wtrow0", [1, 1024], F32, R3 + 60 * KB)
    h1b_st = sb("h1b_st", [128, 1024], BF16)
    h3b_st = sb("h3b_st", [128, 1024], BF16)

    psum = nc.alloc_psum_tensor("psum", [128, 4096], F32).ap()
    B = [psum[:, 512 * i: 512 * (i + 1)] for i in range(8)]
    psum16 = psum.bitcast(BF16)
    B16 = [psum16[:, 1024 * i: 1024 * (i + 1)] for i in range(8)]

    vg_a = sb("vg_a", [128, 64])
    vg_b = sb("vg_b", [128, 64])

    sem_names = ["c0", "wa", "wb", "bs", "xa", "xb", "ka", "kb", "qa", "qb",
                 "gg", "sa", "st", "t", "v", "a", "g", "gw", "goa", "gob", "wt", "mw", "tap"]
    es = ExitStack()
    sems = {n: es.enter_context(nc.semaphore(n)) for n in sem_names}
    P = Prog(nc, sems)
    P.gap_fn = lambda: nc.vector.drain()

    def tap(name, src_ap):
        if name not in tap_d:
            return
        dst = tap_d[name]
        if P.cnt["tap"]:
            P.wait("gpsimd", "tap", P.cnt["tap"])
        P.emit("gpsimd", lambda: nc.gpsimd.dma_start(out=dst, in_=src_ap)
               .then_inc(sems["tap"], 16))
        c = P.bump("tap", 16)
        P.wait("vector", "tap", c)
        P.wait("scalar", "tap", c)

    def tap_wait(eng_sem_pairs):
        for semn, val in eng_sem_pairs:
            if tap_d:
                P.wait("gpsimd", semn, val)

    def emit_ladder(buf, width):
        v = nc.vector
        thr, bb = lad_thr[:, 0:width], lad_bb[:, 0:width]
        P.emit("vector", lambda: v.tensor_scalar(bb, buf, -0.5, None, OP.is_ge))
        P.emit("vector", lambda: v.tensor_scalar(thr, bb, 8.0, -4.5, OP.mult, OP.add))
        P.emit("vector", lambda: v.tensor_tensor(bb, buf, thr, OP.is_ge))
        P.emit("vector", lambda: v.scalar_tensor_tensor(thr, bb, 4.0, thr, OP.mult, OP.add))
        P.emit("vector", lambda: v.scalar_tensor_tensor(bb, thr, -2.0, buf, OP.add, OP.is_le))
        P.emit("vector", lambda: v.scalar_tensor_tensor(thr, bb, 2.0, thr, OP.mult, OP.add))
        P.emit("vector", lambda: v.scalar_tensor_tensor(bb, thr, -3.0, buf, OP.add, OP.is_le))
        P.emit("vector", lambda: v.scalar_tensor_tensor(buf, thr, -3.5, bb, OP.add, OP.add))

    for dst, src in [(eye, eye_d), (eye16, eyeb_d)]:
        P.emit("sync", lambda dst=dst, src=src:
               nc.sync.dma_start(out=dst, in_=src).then_inc(sems["c0"], 16))
        P.bump("c0", 16)
    P.emit("vector", lambda: nc.vector.memset(ones1, 1.0)
           .then_inc(sems["v"], 1))
    v_const = P.bump("v")

    P.emit("sync", lambda: nc.sync.dma_start(
        out=cvec, in_=cvec_d.rearrange("(j p) -> p j", p=128),
        allow_slow_non_contiguous=True).then_inc(sems["c0"], 16))
    v_ld = P.bump("c0", 16)
    P.wait("scalar", "c0", v_ld)
    P.emit("scalar", lambda: nc.scalar.preload_act_table(AF.Silu))
    P.emit("scalar", lambda: nc.scalar.activation(scv, cvec, AF.Silu)
           .then_inc(sems["a"], 1))
    v_scv = P.bump("a")

    mod_mm_done = {}
    for m in range(96):
        buf = wcol[m % 2]
        wsem = "wa" if m % 2 == 0 else "wb"
        src = wmod_d[:, 128 * m:128 * (m + 1)].rearrange("(k p) j -> p k j", p=128)
        dst3 = buf.rearrange("p (k j) -> p k j", j=128)
        if m >= 2:
            P.wait("sync", "t", mod_mm_done[m - 2])
        P.emit("sync", lambda dst3=dst3, src=src, wsem=wsem:
               nc.sync.dma_start(out=dst3, in_=src).then_inc(sems[wsem], 16))
        v_ld = P.bump(wsem, 16)
        P.wait("tensor", wsem, v_ld)
        if m == 0:
            P.wait("tensor", "a", v_scv)
        bufr = buf.rearrange("p (k j) -> p k j", j=128)
        for kc in range(16):
            lhsT, rhs = bufr[:, kc], scv[:, kc:kc + 1]
            if kc == 15:
                P.emit("tensor", lambda lhsT=lhsT, rhs=rhs, m=m:
                       nc.tensor.matmul(B[7][:, m:m + 1], lhsT, rhs,
                                        start=False, stop=True).then_inc(sems["t"], 1))
                mod_mm_done[m] = P.bump("t")
            else:
                P.emit("tensor", lambda lhsT=lhsT, rhs=rhs, m=m, kc=kc:
                       nc.tensor.matmul(B[7][:, m:m + 1], lhsT, rhs,
                                        start=(kc == 0), stop=False))
    P.wait("vector", "t", mod_mm_done[31])
    P.emit("vector", lambda: nc.vector.tensor_copy(modv[:, 0:16], B[7][:, 0:16]))
    P.emit("vector", lambda: nc.vector.tensor_scalar(
        ca_v, B[7][:, 16:32], 1.0, None, OP.add).then_inc(sems["v"], 1))
    v_mod1 = P.bump("v")

    def emit_clamp(src_cols, dst):
        v = nc.vector
        P.autogap = True
        P.emit("vector", lambda: v.tensor_scalar(t5a, src_cols, 0.25, None, OP.max))
        P.emit("vector", lambda: v.tensor_scalar(t5b, src_cols, -0.25, None, OP.min))
        P.emit("vector", lambda: v.tensor_scalar(t5c, src_cols, 0.0, None, OP.is_ge))
        P.emit("vector", lambda: v.tensor_tensor(t5a, t5a, t5b, OP.subtract))
        P.emit("vector", lambda: v.tensor_tensor(t5a, t5c, t5a, OP.mult))
        P.emit("vector", lambda: v.tensor_tensor(dst, t5a, t5b, OP.add))
        P.autogap = False

    def emit_mod_part2():
        P.wait("vector", "t", mod_mm_done[95])
        P.emit("vector", lambda: nc.vector.tensor_copy(modv[:, 16:96], B[7][:, 16:96]))
        emit_clamp(modv[:, 32:48], ga_v)
        P.emit("vector", lambda: nc.vector.tensor_scalar(
            cb_v, modv[:, 64:80], 1.0, None, OP.add))
        emit_clamp(modv[:, 80:96], gb_v)
        P.emit("vector", lambda: nc.vector.engine_nop().then_inc(sems["v"], 1))
        return P.bump("v")

    if upto == "s1":
        v_mod2 = emit_mod_part2()
        if tap_d:
            tap_wait([("v", v_mod2)])
            tap("modv", modv)
            tap("cav", ca_v)
            tap("gav", ga_v)
            tap("cbv", cb_v)
            tap("gbv", gb_v)
        _finalize(nc, P, sems, es)
        return nc

    P.emit("scalar", lambda: nc.scalar.preload_act_table(AF.Sqrt))

    def emit_ln_tile(xin):
        v = nc.vector
        P.autogap = True
        P.emit("vector", lambda: v.tensor_reduce(lnsm, xin, axis=AX.X, op=OP.add))
        P.emit("vector", lambda: v.tensor_scalar(lnm, lnsm, INV2048, None, OP.mult))
        P.emit("vector", lambda: v.tensor_scalar(xin, xin, lnm, None, OP.subtract))
        P.emit("vector", lambda: v.tensor_tensor(scr2k, xin, xin, OP.mult))
        P.emit("vector", lambda: v.tensor_reduce(lnv2, scr2k, axis=AX.X, op=OP.add))
        P.emit("vector", lambda: v.tensor_scalar(
            lna, lnv2, INV2048, EPS, OP.mult, OP.add).then_inc(sems["v"], 1))
        vv = P.bump("v")
        P.wait("scalar", "v", vv)
        P.emit("scalar", lambda: nc.scalar.activation(lnsq, lna, AF.Sqrt)
               .then_inc(sems["a"], 1))
        aa = P.bump("a")
        P.wait("vector", "a", aa)
        P.emit("vector", lambda: v.reciprocal(lnr0, lnsq))
        P.emit("vector", lambda: v.tensor_tensor(lnt, lnr0, lnr0, OP.mult))
        P.emit("vector", lambda: v.tensor_tensor(lnt, lnt, lna, OP.mult))
        P.emit("vector", lambda: v.tensor_scalar(lnt, lnt, -0.5, 1.5, OP.mult, OP.add))
        P.emit("vector", lambda: v.tensor_tensor(lnr, lnr0, lnt, OP.mult))
        P.emit("vector", lambda: v.tensor_scalar(xin, xin, lnr, None, OP.mult))
        P.autogap = False

    def emit_quant_chunk(src_fm, chb_dst, tm_dst, ip_dst):
        v = nc.vector
        P.autogap = True
        for pc in range(16):
            s = src_fm(pc)
            P.emit("vector", lambda s=s, pc=pc: v.tensor_reduce(
                mxb[:, pc:pc + 1], s, axis=AX.X, op=OP.max))
            P.emit("vector", lambda s=s, pc=pc: v.tensor_reduce(
                mnb[:, pc:pc + 1], s, axis=AX.X, op=OP.min))
        P.emit("vector", lambda: v.tensor_tensor(t5a, mxb, mnb, OP.add))
        P.emit("vector", lambda: v.tensor_scalar(t5b, t5a, 0.5, None, OP.mult))
        for pc in range(16):
            d = chb_dst(pc)
            P.emit("vector", lambda d=d, pc=pc: v.tensor_copy(d, t5b[:, pc:pc + 1]))
        P.emit("vector", lambda: v.tensor_tensor(t5a, mxb, t5b, OP.subtract))
        P.emit("vector", lambda: v.tensor_tensor(t5c, t5b, mnb, OP.subtract))
        P.emit("vector", lambda: v.tensor_tensor(cmaxb, t5a, t5c, OP.max))
        P.emit("vector", lambda: v.tensor_reduce(
            tmcol, cmaxb, axis=AX.X, op=OP.max).then_inc(sems["v"], 1))
        vv = P.bump("v")
        P.wait("tensor", "v", vv)
        P.emit("tensor", lambda: nc.tensor.transpose(B[6][0:1, 0:128], tmcol, eye)
               .then_inc(sems["t"], 1))
        tt_ = P.bump("t")
        P.wait("vector", "t", tt_)
        P.emit("vector", lambda: v.tensor_copy(tmrow, B[6][0:1, 0:128]))
        P.emit("vector", lambda: v.tensor_reduce(
            tmsc, tmrow, axis=AX.X, op=OP.max).then_inc(sems["v"], 1))
        vv = P.bump("v")
        P.wait("tensor", "v", vv)
        P.emit("tensor", lambda: nc.tensor.matmul(
            B[6][:, 256:257], ones1, tmsc, start=True, stop=True)
            .then_inc(sems["t"], 1))
        tt_ = P.bump("t")
        P.wait("vector", "t", tt_)
        P.emit("vector", lambda: v.tensor_copy(tmb, B[6][:, 256:257]))
        P.emit("vector", lambda: v.tensor_scalar(tm7b, tmb, C7, None, OP.mult))
        P.emit("vector", lambda: v.tensor_scalar(tm_dst, tm7b, 2.0 ** -7, None, OP.mult))
        for j in range(7):
            P.emit("vector", lambda j=j: v.tensor_scalar(
                thrj, tmb, 2.0 ** (j - 7), None, OP.mult))
            P.emit("vector", lambda: v.tensor_scalar(t5a, cmaxb, thrj, None, OP.is_gt))
            if j == 0:
                P.emit("vector", lambda: v.tensor_scalar(p2b, t5a, 1.0, None, OP.add))
            else:
                P.emit("vector", lambda: v.tensor_scalar(t5a, t5a, 1.0, None, OP.add))
                P.emit("vector", lambda: v.tensor_tensor(p2b, p2b, t5a, OP.mult))
        P.emit("vector", lambda: v.tensor_scalar(t5a, p2b, tm7b, 2.0 ** -7,
                                                 OP.mult, OP.mult))
        P.emit("vector", lambda: v.tensor_scalar(t5a, t5a, 1e-9, None, OP.max))
        P.emit("vector", lambda: v.reciprocal(rscaleb, t5a))
        P.autogap = False
        for pc in range(16):
            s = src_fm(pc)
            cb_ = chb_dst(pc)
            P.emit("vector", lambda s=s, cb_=cb_, pc=pc: v.tensor_scalar(
                qchunk[:, pc * CHK:(pc + 1) * CHK], s, cb_, rscaleb[:, pc:pc + 1],
                OP.subtract, OP.mult))
        emit_ladder(qchunk[:, 0:2048], 2048)
        emit_ladder(qchunk[:, 2048:4096], 2048)
        for pc in range(16):
            d = ip_dst(pc)
            P.emit("vector", lambda d=d, pc=pc: v.tensor_scalar(
                d, qchunk[:, pc * CHK:(pc + 1) * CHK], p2b[:, pc:pc + 1], None,
                OP.mult))
        P.emit("vector", lambda: nc.vector.engine_nop().then_inc(sems["v"], 1))
        return P.bump("v")

    v_qdone = {}
    xtr_done = {}
    for c in range(NCH):
        for half in range(2):
            tix = 2 * c + half
            buf = xt_buf[half]
            xsem = "xa" if half == 0 else "xb"
            if c >= 1 and half == 0:
                P.wait("scalar", "v", v_qdone[c - 1])
            P.emit("scalar", lambda buf=buf, tix=tix, xsem=xsem:
                   nc.scalar.dma_start(
                       out=buf, in_=x_d[128 * tix:128 * (tix + 1), :])
                   .then_inc(sems[xsem], 16))
            v_x = P.bump(xsem, 16)
            P.wait("vector", xsem, v_x)
            emit_ln_tile(buf)
            P.emit("vector", lambda: nc.vector.engine_nop().then_inc(sems["v"], 1))
            v_xh = P.bump("v")
            P.wait("tensor", "v", v_xh)
            if c == 0 and half == 0:
                P.wait("tensor", "v", v_mod1)
            for pcg in range(4):
                for pc in range(4 * pcg, 4 * pcg + 4):
                    bank = B[pc % 4]
                    fin = pc % 4 == 3
                    if fin:
                        P.emit("tensor", lambda pc=pc, bank=bank, buf=buf:
                               nc.tensor.transpose(bank[:, 0:128],
                                                   buf[:, 128 * pc:128 * (pc + 1)],
                                                   eye).then_inc(sems["t"], 1))
                        t_tr = P.bump("t")
                    else:
                        P.emit("tensor", lambda pc=pc, bank=bank, buf=buf:
                               nc.tensor.transpose(bank[:, 0:128],
                                                   buf[:, 128 * pc:128 * (pc + 1)],
                                                   eye))
                P.wait("vector", "t", t_tr)
                for pc in range(4 * pcg, 4 * pcg + 4):
                    bank = B[pc % 4]
                    dst = hfm[:, pc * CHK + half * 128: pc * CHK + half * 128 + 128]
                    P.emit("vector", lambda pc=pc, bank=bank, dst=dst:
                           nc.vector.tensor_scalar(
                               dst, bank[:, 0:128], ca_v[:, pc:pc + 1],
                               modv[:, pc:pc + 1], OP.mult, OP.add))
                P.emit("vector", lambda: nc.vector.engine_nop().then_inc(sems["v"], 1))
                v_ev = P.bump("v")
                P.wait("tensor", "v", v_ev)
        v_qdone[c] = emit_quant_chunk(
            src_fm=lambda pc: hfm[:, pc * CHK:(pc + 1) * CHK],
            chb_dst=lambda pc, c=c: chb[:, pc * 16 + c: pc * 16 + c + 1],
            tm_dst=tm896[:, c:c + 1],
            ip_dst=lambda pc, c=c: Ip[:, pc * NT + c * CHK: pc * NT + (c + 1) * CHK],
        )
        if tap_d and c == 0:
            tap_wait([("v", v_qdone[0])])
            tap("hfm0", hfm)
            tap("qchunk0", qchunk)

    v_quant_all = v_qdone[NCH - 1]
    v_mod2 = emit_mod_part2()
    if tap_d:
        tap_wait([("v", v_quant_all), ("v", v_mod2)])
        tap("modv", modv)
        tap("gav", ga_v)
        tap("chb", chb)
        tap("tm896", tm896)
        tap("Ip0", Ip[:, 0:NT])
    if upto == "s2":
        _finalize(nc, P, sems, es)
        return nc

    wstate = {
        "wcol_free_t": None,
        "bias_free_a": None,
        "proj_bank_a": {0: None, 1: None},
        "tr_bank_a": {0: None, 1: None},
        "itr_bank_a": {0: None, 1: None},
        "cbb_bank_a": None,
        "iwt_free_t": None,
        "qw_free_t": None,
        "krow_free": None,
    }

    def emit_wstream(kind, ip_src, ip_blk, tm_src, chb_src, nch, ntok):
        w_da = w_qkvo[kind.lower()]
        b_da = b_qkvo[kind.lower()]
        ntt = ntok // 512
        for jt in range(16):
            wsem = "wa"
            if wstate["wcol_free_t"] is not None:
                P.wait("sync", "t", wstate["wcol_free_t"])
            if wstate.get("wcol_free_v") is not None:
                P.wait("sync", "v", wstate["wcol_free_v"])
            P.emit("sync", lambda jt=jt, w_da=w_da: nc.sync.dma_start(
                out=wcol[0], in_=w_da[128 * jt:128 * (jt + 1), :])
                .then_inc(sems["wa"], 16))
            v_w = P.bump("wa", 16)
            if wstate["bias_free_a"] is not None:
                P.wait("sync", "a", wstate["bias_free_a"])
            P.emit("sync", lambda jt=jt, b_da=b_da: nc.sync.dma_start(
                out=bias_sl,
                in_=b_da.rearrange("(a p) -> p a", p=128)[:, jt:jt + 1],
                allow_slow_non_contiguous=True).then_inc(sems["bs"], 16))
            v_b = P.bump("bs", 16)
            v = nc.vector
            wbuf = wcol[0]
            P.wait("tensor", "wa", v_w)
            if wstate.get("wts_free_t") is not None:
                P.wait("scalar", "t", wstate["wts_free_t"])
            for g in range(4):
                bk = 4 + g % 2
                if wstate["tr_bank_a"][g % 2] is not None:
                    P.wait("tensor", "a", wstate["tr_bank_a"][g % 2])
                for kc in range(4 * g, 4 * g + 4):
                    fin = kc % 4 == 3
                    if fin:
                        P.emit("tensor", lambda kc=kc, bk=bk, wbuf=wbuf:
                               nc.tensor.transpose(
                                   B[bk][:, 128 * (kc % 4):128 * (kc % 4) + 128],
                                   wbuf[:, 128 * kc:128 * (kc + 1)], eye)
                               .then_inc(sems["t"], 1))
                        t_f = P.bump("t")
                    else:
                        P.emit("tensor", lambda kc=kc, bk=bk, wbuf=wbuf:
                               nc.tensor.transpose(
                                   B[bk][:, 128 * (kc % 4):128 * (kc % 4) + 128],
                                   wbuf[:, 128 * kc:128 * (kc + 1)], eye))
                if g == 3:
                    wstate["wcol_free_t"] = t_f
                P.wait("scalar", "t", t_f)
                P.emit("scalar", lambda g=g, bk=bk: nc.scalar.activation(
                    wts_f[:, 512 * g:512 * (g + 1)], B[bk], AF.Identity)
                    .then_inc(sems["a"], 1))
                wstate["tr_bank_a"][g % 2] = P.bump("a")
            a_wts = P.cnt["a"]
            P.wait("tensor", "a", a_wts)
            if wstate["cbb_bank_a"] is not None:
                P.wait("tensor", "v", wstate["cbb_bank_a"])
            for kc in range(16):
                lhsT = wts_f[:, 128 * kc:128 * (kc + 1)]
                rhs = chb_src[:, kc * 16: kc * 16 + nch]
                if kc == 15:
                    P.emit("tensor", lambda lhsT=lhsT, rhs=rhs, nch=nch:
                           nc.tensor.matmul(B[6][:, 0:nch], lhsT, rhs,
                                            start=False, stop=True)
                           .then_inc(sems["t"], 1))
                    t_cbb = P.bump("t")
                    wstate["wts_free_t"] = t_cbb
                else:
                    P.emit("tensor", lambda lhsT=lhsT, rhs=rhs, kc=kc, nch=nch:
                           nc.tensor.matmul(B[6][:, 0:nch], lhsT, rhs,
                                            start=(kc == 0), stop=False))
            P.wait("vector", "wa", v_w)
            P.autogap = True
            P.emit("vector", lambda wbuf=wbuf: v.tensor_reduce(
                swv, wbuf, axis=AX.X, op=OP.max, apply_absolute_value=True))
            P.emit("vector", lambda: v.tensor_scalar(swv, swv, C7, 1e-9, OP.mult, OP.max))
            P.emit("vector", lambda: v.reciprocal(rswv, swv))
            P.autogap = False
            if wstate["iwt_free_t"] is not None:
                P.wait("vector", "t", wstate["iwt_free_t"])
            P.emit("vector", lambda wbuf=wbuf: v.tensor_scalar(
                qw_buf, wbuf, rswv, None, OP.mult))
            thr, bb = lad_thr[:, 0:D], lad_bb[:, 0:D]
            P.emit("vector", lambda: v.tensor_scalar(bb, qw_buf, -0.5, None, OP.is_ge))
            P.emit("vector", lambda: v.tensor_scalar(thr, bb, 8.0, -4.5, OP.mult, OP.add))
            P.emit("vector", lambda: v.tensor_tensor(bb, qw_buf, thr, OP.is_ge))
            P.emit("vector", lambda: v.scalar_tensor_tensor(thr, bb, 4.0, thr, OP.mult, OP.add))
            P.emit("vector", lambda: v.scalar_tensor_tensor(bb, thr, -2.0, qw_buf, OP.add, OP.is_le))
            P.emit("vector", lambda: v.scalar_tensor_tensor(thr, bb, 2.0, thr, OP.mult, OP.add))
            P.emit("vector", lambda: v.scalar_tensor_tensor(bb, thr, -3.0, qw_buf, OP.add, OP.is_le))
            if wstate.get("qw_free_t") is not None:
                P.wait("vector", "t", wstate["qw_free_t"])
            P.emit("vector", lambda: v.scalar_tensor_tensor(qw_buf, thr, -3.5, bb, OP.add, OP.add)
                   .then_inc(sems["v"], 1))
            v_iw = P.bump("v")
            wstate["wcol_free_v"] = v_iw
            P.wait("tensor", "v", v_iw)
            for g in range(4):
                bk = 2 + g % 2
                if wstate["itr_bank_a"][g % 2] is not None:
                    P.wait("tensor", "a", wstate["itr_bank_a"][g % 2])
                for kc in range(4 * g, 4 * g + 4):
                    if kc % 4 == 3:
                        P.emit("tensor", lambda kc=kc, bk=bk:
                               nc.tensor.transpose(
                                   B[bk][:, 128 * (kc % 4):128 * (kc % 4) + 128],
                                   qw_buf[:, 128 * kc:128 * (kc + 1)], eye)
                               .then_inc(sems["t"], 1))
                        t_i = P.bump("t")
                    else:
                        P.emit("tensor", lambda kc=kc, bk=bk:
                               nc.tensor.transpose(
                                   B[bk][:, 128 * (kc % 4):128 * (kc % 4) + 128],
                                   qw_buf[:, 128 * kc:128 * (kc + 1)], eye))
                if g == 3:
                    wstate["qw_free_t"] = t_i
                P.wait("scalar", "t", t_i)
                if wstate["iwt_free_t"] is not None and g == 0:
                    P.wait("scalar", "t", wstate["iwt_free_t"])
                P.emit("scalar", lambda g=g, bk=bk: nc.scalar.activation(
                    iwtT[:, 512 * g:512 * (g + 1)], B[bk], AF.Identity)
                    .then_inc(sems["a"], 1))
                wstate["itr_bank_a"][g % 2] = P.bump("a")
            wt_iw = P.cnt["a"]
            P.wait("vector", "t", t_cbb)
            P.wait("vector", "bs", v_b)
            P.autogap = True
            if kind == "Q":
                P.emit("vector", lambda nch=nch: v.tensor_scalar(
                    cbbv[:, 0:nch], B[6][:, 0:nch], bias_sl, 0.125, OP.add, OP.mult))
                P.emit("vector", lambda nch=nch: v.tensor_scalar(
                    swcv[:, 0:nch], tm_src[:, 0:nch], swv, 0.125, OP.mult, OP.mult))
            else:
                P.emit("vector", lambda nch=nch: v.tensor_scalar(
                    cbbv[:, 0:nch], B[6][:, 0:nch], bias_sl, None, OP.add))
                P.emit("vector", lambda nch=nch: v.tensor_scalar(
                    swcv[:, 0:nch], tm_src[:, 0:nch], swv, None, OP.mult))
            P.autogap = False
            P.emit("vector", lambda: nc.vector.engine_nop().then_inc(sems["v"], 1))
            v_sc = P.bump("v")
            wstate["cbb_bank_a"] = v_sc
            P.wait("tensor", "a", wt_iw)
            P.wait("scalar", "v", v_sc)
            for tt in range(ntt):
                bank = tt % 2
                if wstate["proj_bank_a"][bank] is not None:
                    P.wait("tensor", "a", wstate["proj_bank_a"][bank])
                for kc in range(16):
                    lhsT = iwtT[:, 128 * kc:128 * (kc + 1)]
                    rhs = ip_src[:, kc * ip_blk + tt * 512: kc * ip_blk + tt * 512 + 512]
                    if kc == 15:
                        P.emit("tensor", lambda lhsT=lhsT, rhs=rhs, bank=bank:
                               nc.tensor.matmul(B[bank], lhsT, rhs,
                                                start=False, stop=True)
                               .then_inc(sems["t"], 1))
                        t_p = P.bump("t")
                    else:
                        P.emit("tensor", lambda lhsT=lhsT, rhs=rhs, kc=kc, bank=bank:
                               nc.tensor.matmul(B[bank], lhsT, rhs,
                                                start=(kc == 0), stop=False))
                if tt == ntt - 1:
                    wstate["iwt_free_t"] = t_p
                P.wait("scalar", "t", t_p)
                if wstate["krow_free"] is not None and tt == 0:
                    P.wait("scalar", wstate["krow_free"][0], wstate["krow_free"][1])
                for cc in range(2):
                    ci = tt * 2 + cc
                    dst = krow[:, tt * 512 + cc * 256: tt * 512 + cc * 256 + 256]
                    if cc == 1:
                        P.emit("scalar", lambda dst=dst, cc=cc, ci=ci, bank=bank:
                               nc.scalar.activation(
                                   dst, B[bank][:, cc * 256:(cc + 1) * 256],
                                   AF.Identity, bias=cbbv[:, ci:ci + 1],
                                   scale=swcv[:, ci:ci + 1]).then_inc(sems["a"], 1))
                    else:
                        P.emit("scalar", lambda dst=dst, cc=cc, ci=ci, bank=bank:
                               nc.scalar.activation(
                                   dst, B[bank][:, cc * 256:(cc + 1) * 256],
                                   AF.Identity, bias=cbbv[:, ci:ci + 1],
                                   scale=swcv[:, ci:ci + 1]))
                wstate["proj_bank_a"][bank] = P.bump("a")
            a_row = P.cnt["a"]
            wstate["bias_free_a"] = a_row
            if kind == "K":
                P.wait("sync", "a", a_row)
                P.emit("sync", lambda jt=jt: nc.sync.dma_start(
                    out=K_dram[128 * jt:128 * (jt + 1), :], in_=krow)
                    .then_inc(sems["st"], 16))
                st_c = P.bump("st", 16)
                wstate["krow_free"] = ("st", st_c)
            elif kind == "Q":
                P.wait("sync", "a", a_row)
                P.emit("sync", lambda jt=jt: nc.sync.dma_start(
                    out=Q_dram[128 * jt:128 * (jt + 1), :], in_=krow[:, 0:NO])
                    .then_inc(sems["st"], 16))
                st_c = P.bump("st", 16)
                wstate["krow_free"] = ("st", st_c)
            elif kind == "O":
                _pd_consume(jt, a_row)
            elif kind == "V":
                P.wait("tensor", "a", a_row)
                for kc in range(32):
                    bk = 2 + (kc % 2)
                    P.emit("tensor", lambda kc=kc, bk=bk:
                           nc.tensor.transpose(B[bk][:, 0:128],
                                               krow[:, 128 * kc:128 * (kc + 1)], eye)
                           .then_inc(sems["t"], 1))
                    t_v = P.bump("t")
                    P.wait("vector", "t", t_v)
                    P.emit("vector", lambda kc=kc, bk=bk:
                           v.tensor_copy(krow[:, 128 * kc:128 * (kc + 1)],
                                         B[bk][:, 0:128]).then_inc(sems["v"], 1))
                    v_vt = P.bump("v")
                    P.wait("tensor", "v", v_vt)
                P.wait("sync", "v", P.cnt["v"])
                dstap = Vt_dram.rearrange("(c p) (a j) -> p c a j", p=128, j=128)[:, :, jt]
                srcap = krow.rearrange("p (c j) -> p c j", j=128)
                P.emit("sync", lambda dstap=dstap, srcap=srcap: nc.sync.dma_start(
                    out=dstap, in_=srcap).then_inc(sems["st"], 16))
                st_c = P.bump("st", 16)
                wstate["krow_free"] = ("st", st_c)
        return

    wstate["wcol_free_t"] = mod_mm_done[95]
    P.wait("scalar", "v", v_quant_all)
    P.wait("tensor", "v", v_quant_all)
    emit_wstream("K", Ip, NT, tm896, chb, NCH, NT)
    emit_wstream("V", Ip, NT, tm896, chb, NCH, NT)
    emit_wstream("Q", Ip, NT, tm896, chb, NCH_OWN, NO)
    v_kvq_store = P.cnt["st"]

    if tap_d:
        tap_wait([("st", v_kvq_store)])
        tap("Krow0", K_dram[0:128, :])
        tap("Vt0", Vt_dram[0:128, :])
        tap("Q0", Q_dram[0:128, :])
    if upto == "qkv":
        _finalize(nc, P, sems, es)
        return nc

    P.emit("scalar", lambda: nc.scalar.preload_act_table(AF.Exp))
    P.wait("tensor", "st", v_kvq_store)
    P.wait("sync", "st", v_kvq_store)
    P.wait("vector", "st", v_kvq_store)
    for i in range(2):
        vv = v65[i].rearrange("p (i j) -> p i j", j=65)[:, :, 64:65]
        P.emit("vector", lambda vv=vv: nc.vector.memset(vv, 1.0))
    P.emit("vector", lambda: nc.vector.engine_nop().then_inc(sems["v"], 1))
    v_ones_pc = P.bump("v")

    t_pv = {}
    t_qk = {}
    a_exp = {}
    t_last_qk_of_grp = {}
    v_epi = {}
    for qh in range(2):
        for grp in range(8):
            gi = qh * 8 + grp
            qb = q4[gi % 2]
            qsem = "qa" if gi % 2 == 0 else "qb"
            if gi >= 2:
                P.wait("sync", "t", t_last_qk_of_grp[gi - 2])
            P.emit("sync", lambda qb=qb, qh=qh, grp=grp, qsem=qsem:
                   nc.sync.dma_start(
                       out=qb.rearrange("p (a t) -> p a t", a=2),
                       in_=Q_dram[256 * grp:256 * (grp + 1), 512 * qh:512 * (qh + 1)]
                       .rearrange("(a p) t -> p a t", p=128))
                   .then_inc(sems[qsem], 16))
            v_q = P.bump(qsem, 16)
            for kc in range(32):
                kb, vb = kcb[kc % 2], v65[kc % 2]
                if kc >= 2:
                    P.wait("sync", "t", t_pv[(gi, kc - 2)])
                elif gi >= 1:
                    P.wait("sync", "t", t_pv[(gi - 1, 30 + kc)])
                ksem = "ka" if kc % 2 == 0 else "kb"
                P.emit("sync", lambda kb=kb, grp=grp, kc=kc, ksem=ksem:
                       nc.sync.dma_start(
                           out=kb.rearrange("p (a t) -> p a t", a=2),
                           in_=K_dram[256 * grp:256 * (grp + 1), 128 * kc:128 * (kc + 1)]
                           .rearrange("(a p) t -> p a t", p=128))
                       .then_inc(sems[ksem], 16))
                P.bump(ksem, 16)
                P.emit("sync", lambda vb=vb, grp=grp, kc=kc, ksem=ksem:
                       nc.sync.dma_start(
                           out=vb.rearrange("p (i j) -> p i j", j=65)[:, :, 0:64],
                           in_=Vt_dram[128 * kc:128 * (kc + 1), 256 * grp:256 * (grp + 1)]
                           .rearrange("p (i j) -> p i j", j=64))
                       .then_inc(sems[ksem], 16))
                v_v = P.bump(ksem, 16)
                P.wait("tensor", ksem, v_v)
                if kc == 0:
                    P.wait("tensor", qsem, v_q)
                    if gi == 0:
                        P.wait("tensor", "v", v_ones_pc)
                    if gi >= 1:
                        P.wait("tensor", "v", v_epi[gi - 1])
                if kc >= 1:
                    P.wait("tensor", "a", a_exp[(gi, kc - 1)])
                for pr in range(2):
                    for o in range(2):
                        h_i = 2 * pr + o
                        bank = B[h_i]
                        lhsT = kb[64 * o:64 * o + 64, 128 * pr:128 * (pr + 1)]
                        rhs = qb[64 * o:64 * o + 64, 512 * pr:512 * (pr + 1)]
                        fin = h_i == 3
                        if fin:
                            P.emit("tensor", lambda bank=bank, lhsT=lhsT, rhs=rhs, o=o:
                                   nc.tensor.matmul(bank, lhsT, rhs, start=True,
                                                    stop=True,
                                                    tile_position=(64 * o, 0))
                                   .then_inc(sems["t"], 1))
                            t_qk[(gi, kc)] = P.bump("t")
                        else:
                            P.emit("tensor", lambda bank=bank, lhsT=lhsT, rhs=rhs, o=o:
                                   nc.tensor.matmul(bank, lhsT, rhs, start=True,
                                                    stop=True,
                                                    tile_position=(64 * o, 0)))
                if kc == 31:
                    t_last_qk_of_grp[gi] = t_qk[(gi, kc)]
                if kc >= 1:
                    for h_i in range(4):
                        pb = probs[((kc - 1) % 2) * 4 + h_i]
                        lhsT = v65[(kc - 1) % 2][:, 65 * h_i:65 * h_i + 65]
                        fin = h_i == 3
                        if fin:
                            P.emit("tensor", lambda lhsT=lhsT, pb=pb, h_i=h_i, kc=kc:
                                   nc.tensor.matmul(B[4 + h_i][0:65, :], lhsT, pb,
                                                    start=(kc - 1 == 0), stop=(kc - 1 == 31))
                                   .then_inc(sems["t"], 1))
                            t_pv[(gi, kc - 1)] = P.bump("t")
                        else:
                            P.emit("tensor", lambda lhsT=lhsT, pb=pb, h_i=h_i, kc=kc:
                                   nc.tensor.matmul(B[4 + h_i][0:65, :], lhsT, pb,
                                                    start=(kc - 1 == 0), stop=(kc - 1 == 31)))
                P.wait("scalar", "t", t_qk[(gi, kc)])
                if kc >= 2:
                    P.wait("scalar", "t", t_pv[(gi, kc - 2)])
                for h_i in range(4):
                    pb = probs[(kc % 2) * 4 + h_i]
                    fin = h_i == 3
                    if fin:
                        P.emit("scalar", lambda pb=pb, h_i=h_i: nc.scalar.activation(
                            pb, B[h_i], AF.Exp).then_inc(sems["a"], 1))
                        a_exp[(gi, kc)] = P.bump("a")
                    else:
                        P.emit("scalar", lambda pb=pb, h_i=h_i: nc.scalar.activation(
                            pb, B[h_i], AF.Exp))
            P.wait("tensor", "a", a_exp[(gi, 31)])
            for h_i in range(4):
                pb = probs[(31 % 2) * 4 + h_i]
                lhsT = v65[31 % 2][:, 65 * h_i:65 * h_i + 65]
                if h_i == 3:
                    P.emit("tensor", lambda lhsT=lhsT, pb=pb, h_i=h_i:
                           nc.tensor.matmul(B[4 + h_i][0:65, :], lhsT, pb,
                                            start=False, stop=True)
                           .then_inc(sems["t"], 1))
                    t_pv[(gi, 31)] = P.bump("t")
                else:
                    P.emit("tensor", lambda lhsT=lhsT, pb=pb, h_i=h_i:
                           nc.tensor.matmul(B[4 + h_i][0:65, :], lhsT, pb,
                                            start=False, stop=True))
            P.wait("vector", "t", t_pv[(gi, 31)])
            for h_i in range(4):
                P.emit("vector", lambda h_i=h_i: nc.vector.tensor_copy(
                    rsum65[64:65, 512 * h_i:512 * (h_i + 1)], B[4 + h_i][64:65, :]))
            P.emit("vector", lambda: nc.vector.engine_nop().then_inc(sems["v"], 1))
            v_s4 = P.bump("v")
            P.wait("gpsimd", "v", v_s4)
            P.emit("gpsimd", lambda: nc.gpsimd.dma_start(
                out=rsum0, in_=rsum65[64:65, :]).then_inc(sems["g"], 16))
            g_r = P.bump("g", 16)
            P.wait("vector", "g", g_r)
            P.emit("vector", lambda: nc.vector.reciprocal(rq0, rsum0)
                   .then_inc(sems["v"], 1))
            v_rq = P.bump("v")
            P.wait("tensor", "v", v_rq)
            for h_i in range(4):
                H = 4 * grp + h_i
                pcH, half = H // 2, H % 2
                P.emit("tensor", lambda h_i=h_i: nc.tensor.matmul(
                    B[h_i][0:64, :], ones1[0:1, 0:64],
                    rq0[0:1, 512 * h_i:512 * (h_i + 1)], start=True,
                    stop=True).then_inc(sems["t"], 1))
                t_b = P.bump("t")
                P.wait("vector", "t", t_b)
                P.emit("vector", lambda h_i=h_i: nc.vector.tensor_copy(
                    rsb, B[h_i][0:64, :]))
                if half == 0:
                    dst = ctx_fm[0:64, pcH * NO + qh * 512: pcH * NO + qh * 512 + 512]
                    P.emit("vector", lambda dst=dst, h_i=h_i: nc.vector.tensor_tensor(
                        dst, B[4 + h_i][0:64, :], rsb, OP.mult)
                        .then_inc(sems["v"], 1))
                    P.bump("v")
                else:
                    P.emit("vector", lambda h_i=h_i: nc.vector.tensor_tensor(
                        ostg, B[4 + h_i][0:64, :], rsb, OP.mult)
                        .then_inc(sems["v"], 1))
                    v_o = P.bump("v")
                    P.wait("gpsimd", "v", v_o)
                    dst = ctx_fm[64:128, pcH * NO + qh * 512: pcH * NO + qh * 512 + 512]
                    P.emit("gpsimd", lambda dst=dst: nc.gpsimd.dma_start(
                        out=dst, in_=ostg).then_inc(sems["g"], 16))
                    g_c = P.bump("g", 16)
                    P.wait("vector", "g", g_c)
            P.emit("vector", lambda: nc.vector.engine_nop().then_inc(sems["v"], 1))
            v_epi[gi] = P.bump("v")
    v_attn_done_v = P.cnt["v"]
    g_attn_done = P.cnt["g"]

    if tap_d:
        tap_wait([("v", v_attn_done_v), ("g", g_attn_done)])
        tap("ctx", ctx_fm)
    if upto == "attn":
        _finalize(nc, P, sems, es)
        return nc

    P.wait("vector", "g", g_attn_done)
    for c in range(NCH_OWN):
        v_qdone2 = emit_quant_chunk(
            src_fm=lambda pc, c=c: ctx_fm[:, pc * NO + c * CHK: pc * NO + (c + 1) * CHK],
            chb_dst=lambda pc, c=c: chb2[:, pc * 16 + c: pc * 16 + c + 1],
            tm_dst=tm896o[:, c:c + 1],
            ip_dst=lambda pc, c=c: I2[:, pc * NO + c * CHK: pc * NO + (c + 1) * CHK],
        )
    if tap_d:
        tap_wait([("v", v_qdone2)])
        tap("I2", I2[:, 0:NO])
        tap("chb2", chb2)
    P.wait("scalar", "v", v_qdone2)
    for t in range(8):
        P.emit("scalar", lambda t=t: nc.scalar.dma_start(
            out=x2_sb[:, t * D:(t + 1) * D], in_=x_d[128 * t:128 * (t + 1), :])
            .then_inc(sems["xa"], 16))
        P.bump("xa", 16)
    v_x2ld = P.cnt["xa"]
    P.wait("vector", "xa", v_x2ld)
    P.wait("tensor", "xa", v_x2ld)

    def _pd_consume(jt, a_row):
        v = nc.vector
        P.wait("vector", "a", a_row)
        P.emit("vector", lambda jt=jt: v.tensor_scalar(
            pd_scr, krow[:, 0:NO], ga_v[:, jt:jt + 1], None, OP.mult)
            .then_inc(sems["v"], 1))
        v_g = P.bump("v")
        P.wait("tensor", "v", v_g)
        wstate["krow_free"] = ("v", P.cnt["v"])
        for tt in range(8):
            bk = 2 + tt % 2
            P.emit("tensor", lambda tt=tt, bk=bk: nc.tensor.transpose(
                B[bk][:, 0:128], pd_scr[:, 128 * tt:128 * (tt + 1)], eye)
                .then_inc(sems["t"], 1))
            t_t = P.bump("t")
            P.wait("vector", "t", t_t)
            dst = x2_sb[:, tt * D + jt * 128: tt * D + jt * 128 + 128]
            P.emit("vector", lambda dst=dst, bk=bk: v.tensor_tensor(
                dst, B[bk][:, 0:128], dst, OP.add).then_inc(sems["v"], 1))
            v_a2 = P.bump("v")
            P.wait("tensor", "v", v_a2)

    P.wait("scalar", "v", v_qdone2)
    P.wait("tensor", "v", v_qdone2)
    emit_wstream("O", I2, NO, tm896o, chb2, NCH_OWN, NO)
    v_x2_done = P.cnt["v"]
    if tap_d:
        tap_wait([("v", v_x2_done)])
        tap("x2", x2_sb)
    if upto == "x2":
        _finalize(nc, P, sems, es)
        return nc

    P.wait("sync", "v", v_x2_done)
    for t in range(8):
        P.emit("sync", lambda t=t: nc.sync.dma_start(
            out=x2_dram[128 * t:128 * (t + 1), :], in_=x2_sb[:, t * D:(t + 1) * D])
            .then_inc(sems["st"], 16))
        P.bump("st", 16)
    st_x2 = P.cnt["st"]
    P.emit("scalar", lambda: nc.scalar.preload_act_table(AF.Sqrt))
    P.wait("vector", "v", v_x2_done)
    for t in range(8):
        xt = wcol[0]
        P.emit("vector", lambda t=t, xt=xt: nc.vector.tensor_copy(
            xt, x2_sb[:, t * D:(t + 1) * D]))
        emit_ln_tile(xt)
        P.emit("vector", lambda: nc.vector.engine_nop().then_inc(sems["v"], 1))
        v_xh = P.bump("v")
        P.wait("tensor", "v", v_xh)
        for pcg in range(4):
            for pc in range(4 * pcg, 4 * pcg + 4):
                bank = B[pc % 4]
                fin = pc % 4 == 3
                if fin:
                    P.emit("tensor", lambda pc=pc, bank=bank, xt=xt:
                           nc.tensor.transpose(bank[:, 0:128],
                                               xt[:, 128 * pc:128 * (pc + 1)], eye)
                           .then_inc(sems["t"], 1))
                    t_tr = P.bump("t")
                else:
                    P.emit("tensor", lambda pc=pc, bank=bank, xt=xt:
                           nc.tensor.transpose(bank[:, 0:128],
                                               xt[:, 128 * pc:128 * (pc + 1)], eye))
            P.wait("vector", "t", t_tr)
            for pc in range(4 * pcg, 4 * pcg + 4):
                bank = B[pc % 4]
                dst = h2_fm[:, pc * NO + t * 128: pc * NO + t * 128 + 128]
                P.emit("vector", lambda pc=pc, bank=bank, dst=dst:
                       nc.vector.tensor_scalar(
                           dst, bank[:, 0:128], cb_v[:, pc:pc + 1],
                           modv[:, 48 + pc:48 + pc + 1], OP.mult, OP.add))
            P.emit("vector", lambda: nc.vector.engine_nop().then_inc(sems["v"], 1))
            v_ev = P.bump("v")
            P.wait("tensor", "v", v_ev)
    v_h2 = P.cnt["v"]
    if tap_d:
        tap_wait([("v", v_h2)])
        tap("h2", h2_fm)
    if upto == "h2":
        _finalize(nc, P, sems, es)
        return nc

    P.emit("sync", lambda: nc.sync.dma_start(out=gate_sb, in_=gate_d)
           .then_inc(sems["gg"], 16))
    P.emit("sync", lambda: nc.sync.dma_start(
        out=ebias_r, in_=ebias_d.rearrange("(a e) -> a e", a=1))
        .then_inc(sems["gg"], 16))
    v_g8 = P.bump("gg", 32)
    P.wait("tensor", "gg", v_g8)
    P.wait("tensor", "v", v_h2)
    for kc in range(16):
        bk = 2 + kc % 2
        P.emit("tensor", lambda kc=kc, bk=bk: nc.tensor.transpose(
            B[bk][:, 0:8], gate_sb[:, 128 * kc:128 * (kc + 1)], eye[0:8, 0:8])
            .then_inc(sems["t"], 1))
        t_g = P.bump("t")
        P.wait("vector", "t", t_g)
        P.emit("vector", lambda kc=kc, bk=bk: nc.vector.tensor_copy(
            gtT[:, 8 * kc:8 * (kc + 1)], B[bk][:, 0:8])
            .then_inc(sems["v"], 1))
        v_gt = P.bump("v")
        P.wait("tensor", "v", v_gt)
    P.emit("tensor", lambda: nc.tensor.matmul(
        B[6][:, 0:8], ones1, ebias_r, start=True, stop=True).then_inc(sems["t"], 1))
    t_eb = P.bump("t")
    P.wait("vector", "t", t_eb)
    P.emit("vector", lambda: nc.vector.tensor_copy(ebias_b, B[6][:, 0:8]))
    P.emit("scalar", lambda: nc.scalar.preload_act_table(AF.Sigmoid))
    for tt in range(2):
        bank = tt % 2
        for kc in range(16):
            lhsT = gtT[:, 8 * kc:8 * (kc + 1)]
            rhs = h2_fm[:, kc * NO + tt * 512: kc * NO + tt * 512 + 512]
            if kc == 15:
                P.emit("tensor", lambda lhsT=lhsT, rhs=rhs, bank=bank:
                       nc.tensor.matmul(B[bank][0:8, :], lhsT, rhs,
                                        start=False, stop=True)
                       .then_inc(sems["t"], 1))
                t_gm = P.bump("t")
            else:
                P.emit("tensor", lambda lhsT=lhsT, rhs=rhs, kc=kc, bank=bank:
                       nc.tensor.matmul(B[bank][0:8, :], lhsT, rhs,
                                        start=(kc == 0), stop=False))
        P.wait("scalar", "t", t_gm)
        P.emit("scalar", lambda tt=tt, bank=bank: nc.scalar.activation(
            scoresT[:, 512 * tt:512 * (tt + 1)], B[bank][0:8, :], AF.Sigmoid)
            .then_inc(sems["a"], 1))
        P.bump("a")
    a_sc = P.cnt["a"]
    P.wait("tensor", "a", a_sc)
    for tti in range(8):
        bk = 2 + tti % 2
        P.emit("tensor", lambda tti=tti, bk=bk: nc.tensor.transpose(
            B[bk][:, 0:8], scoresT[:, 128 * tti:128 * (tti + 1)], eye[0:8, 0:8])
            .then_inc(sems["t"], 1))
        t_s = P.bump("t")
        P.wait("vector", "t", t_s)
        P.emit("vector", lambda tti=tti, bk=bk: nc.vector.tensor_copy(
            sc_tok[:, 8 * tti:8 * (tti + 1)], B[bk][:, 0:8]))
        P.emit("vector", lambda tti=tti: nc.vector.tensor_tensor(
            bs_tok, sc_tok[:, 8 * tti:8 * (tti + 1)], ebias_b, OP.add))
        P.emit("vector", lambda: nc.vector.engine_nop().then_inc(sems["v"], 1))
        v_bs = P.bump("v")
        P.wait("tensor", "v", v_bs)
        v = nc.vector
        P.autogap = True
        for e in range(8):
            be = bs_tok[:, e:e + 1]
            P.emit("vector", lambda be=be: v.tensor_scalar(
                cmp8, bs_tok, be, None, OP.is_gt))
            P.emit("vector", lambda: v.tensor_reduce(cnt1, cmp8, axis=AX.X, op=OP.add))
            if e > 0:
                P.emit("vector", lambda be=be: v.tensor_scalar(
                    cmp8, bs_tok, be, None, OP.is_equal))
                P.emit("vector", lambda e=e: v.tensor_reduce(
                    cnt2, cmp8[:, 0:e], axis=AX.X, op=OP.add))
                P.emit("vector", lambda: v.tensor_tensor(cnt1, cnt1, cnt2, OP.add))
            P.emit("vector", lambda: v.tensor_scalar(selc, cnt1, 1.5, None, OP.is_lt))
            P.emit("vector", lambda tti=tti, e=e: v.tensor_tensor(
                wt_tok[:, 8 * tti + e: 8 * tti + e + 1], selc,
                sc_tok[:, 8 * tti + e: 8 * tti + e + 1], OP.mult))
        P.autogap = False
        P.emit("vector", lambda: nc.vector.engine_nop().then_inc(sems["v"], 1))
        v_wt = P.bump("v")
        P.wait("tensor", "v", v_wt)
        P.emit("tensor", lambda tti=tti, bk=bk: nc.tensor.transpose(
            B[bk][0:8, 0:128], wt_tok[:, 8 * tti:8 * (tti + 1)], eye)
            .then_inc(sems["t"], 1))
        t_w = P.bump("t")
        P.wait("vector", "t", t_w)
        P.emit("vector", lambda tti=tti, bk=bk: nc.vector.tensor_copy(
            wt_fm[:, 128 * tti:128 * (tti + 1)], B[bk][0:8, 0:128])
            .then_inc(sems["v"], 1))
        v_wf = P.bump("v")
        P.wait("tensor", "v", v_wf)
    v_masks = P.cnt["v"]
    if tap_d:
        tap_wait([("v", v_masks)])
        tap("wt_fm", wt_fm)

    P.emit("scalar", lambda: nc.scalar.preload_act_table(AF.Silu))

    estate = {"bank_a": {0: None, 1: None, 2: None, 3: None}, "mw_free_t": None}

    def stream_expert_w(w_slice_src, n_slices, dst_col, kc_count, first_wait=None):
        inner = 128 * kc_count
        for hs in range(n_slices):
            if estate["mw_free_t"] is not None:
                P.wait("sync", "t", estate["mw_free_t"])
            P.emit("sync", lambda hs=hs, inner=inner: nc.sync.dma_start(
                out=mw_sl[:, 0:inner], in_=w_slice_src(hs)).then_inc(sems["mw"], 16))
            g_w = P.bump("mw", 16)
            P.wait("tensor", "mw", g_w)
            if first_wait is not None and hs == 0:
                P.wait("tensor", first_wait[0], first_wait[1])
            ngr = kc_count // 4
            for g in range(ngr):
                bk = 2 + g % 4
                if estate["bank_a"][g % 4] is not None:
                    P.wait("tensor", "a", estate["bank_a"][g % 4])
                for kc in range(4 * g, 4 * g + 4):
                    if kc % 4 == 3:
                        P.emit("tensor", lambda kc=kc, bk=bk:
                               nc.tensor.transpose(
                                   B[bk][:, 128 * (kc % 4):128 * (kc % 4) + 128],
                                   mw_sl[:, 128 * kc:128 * (kc + 1)], eye)
                               .then_inc(sems["t"], 1))
                        t_i = P.bump("t")
                    else:
                        P.emit("tensor", lambda kc=kc, bk=bk:
                               nc.tensor.transpose(
                                   B[bk][:, 128 * (kc % 4):128 * (kc % 4) + 128],
                                   mw_sl[:, 128 * kc:128 * (kc + 1)], eye))
                if g == ngr - 1:
                    estate["mw_free_t"] = t_i
                P.wait("scalar", "t", t_i)
                d0 = dst_col(4 * g, hs)
                for q in range(4):
                    dq = dst_col(4 * g + q, hs)
                    if q == 3:
                        P.emit("scalar", lambda dq=dq, bk=bk, q=q:
                               nc.scalar.activation(dq, B[bk][:, 128 * q:128 * (q + 1)],
                                                    AF.Identity).then_inc(sems["a"], 1))
                    else:
                        P.emit("scalar", lambda dq=dq, bk=bk, q=q:
                               nc.scalar.activation(dq, B[bk][:, 128 * q:128 * (q + 1)],
                                                    AF.Identity))
                estate["bank_a"][g % 4] = P.bump("a")
        return P.cnt["a"]

    P.wait("vector", "st", st_x2)
    P.wait("scalar", "st", st_x2)
    P.wait("tensor", "st", st_x2)
    v_xb_prev = None
    t_wtb_prev = None
    v_oe_prev = None
    for e in range(8):
        P.wait("gpsimd", "v", v_masks)
        if v_xb_prev is not None:
            P.wait("gpsimd", "v", v_xb_prev)
        P.emit("gpsimd", lambda e=e: nc.gpsimd.dma_start(
            out=wtrow0, in_=wt_fm[e:e + 1, :]).then_inc(sems["gw"], 16))
        g_wr = P.bump("gw", 16)
        P.wait("tensor", "gw", g_wr)
        P.wait("tensor", "v", v_masks)
        if v_xb_prev is not None:
            P.wait("tensor", "v", v_xb_prev)
        if v_oe_prev is not None:
            P.wait("tensor", "v", v_oe_prev)
        for tt in range(2):
            if tt == 1:
                P.emit("tensor", lambda e=e, tt=tt: nc.tensor.matmul(
                    B[6 + tt][:, 0:512], ones1, wtrow0[0:1, 512 * tt:512 * (tt + 1)],
                    start=True, stop=True).then_inc(sems["t"], 1))
                t_wb = P.bump("t")
            else:
                P.emit("tensor", lambda e=e, tt=tt: nc.tensor.matmul(
                    B[6 + tt][:, 0:512], ones1, wtrow0[0:1, 512 * tt:512 * (tt + 1)],
                    start=True, stop=True))
        t_wtb_prev = t_wb
        P.wait("vector", "t", t_wb)
        for pc in range(16):
            for tt in range(2):
                P.emit("vector", lambda pc=pc, tt=tt: nc.vector.tensor_tensor(
                    xb_sb[:, pc * NO + 512 * tt: pc * NO + 512 * tt + 512],
                    h2_fm[:, pc * NO + 512 * tt: pc * NO + 512 * tt + 512],
                    B[6 + tt][:, 0:512], OP.mult))
        P.emit("vector", lambda: nc.vector.engine_nop().then_inc(sems["v"], 1))
        v_xb = P.bump("v")
        v_xb_prev = v_xb
        v_w1 = stream_expert_w(
            lambda hs, e=e: w1_d[e, 128 * hs:128 * (hs + 1), :],
            8, lambda kc, hs: wexp[:, kc * KB + hs * 128: kc * KB + hs * 128 + 128],
            16, first_wait=(("v", v_oe_prev) if v_oe_prev is not None else None))
        P.wait("tensor", "a", v_w1)
        P.wait("tensor", "v", v_xb)
        a_s1_last = None
        for hs in range(8):
            for tt in range(2):
                bank = tt % 2
                for kc in range(16):
                    lhsT = wexp[:, kc * KB + hs * 128: kc * KB + hs * 128 + 128]
                    rhs = xb_sb[:, kc * NO + tt * 512: kc * NO + tt * 512 + 512]
                    if kc == 15:
                        P.emit("tensor", lambda lhsT=lhsT, rhs=rhs, bank=bank:
                               nc.tensor.matmul(B[bank], lhsT, rhs, start=False,
                                                stop=True).then_inc(sems["t"], 1))
                        t_h1 = P.bump("t")
                    else:
                        P.emit("tensor", lambda lhsT=lhsT, rhs=rhs, kc=kc, bank=bank:
                               nc.tensor.matmul(B[bank], lhsT, rhs,
                                                start=(kc == 0), stop=False))
                P.wait("vector", "t", t_h1)
                if a_s1_last is not None and tt == 0:
                    P.wait("vector", "a", a_s1_last)
                P.emit("vector", lambda tt=tt, bank=bank: nc.vector.tensor_copy(
                    h1b_st[:, 512 * tt:512 * (tt + 1)], B[bank]).then_inc(sems["v"], 1))
                v_h1b = P.bump("v")
                P.wait("tensor", "v", v_h1b)
            P.wait("scalar", "v", v_h1b)
            P.emit("scalar", lambda hs=hs: nc.scalar.activation(
                s1_sb[:, hs * KB:(hs + 1) * KB], h1b_st, AF.Silu)
                .then_inc(sems["a"], 1))
            a_s1_last = P.bump("a")
        a_s1 = P.cnt["a"]
        v_w3 = stream_expert_w(
            lambda hs, e=e: w3_d[e, 128 * hs:128 * (hs + 1), :],
            8, lambda kc, hs: wexp[:, kc * KB + hs * 128: kc * KB + hs * 128 + 128],
            16, first_wait=("v", P.cnt["v"]))
        P.wait("tensor", "a", v_w3)
        P.wait("vector", "a", a_s1)
        for hs in range(8):
            for tt in range(2):
                bank = tt % 2
                for kc in range(16):
                    lhsT = wexp[:, kc * KB + hs * 128: kc * KB + hs * 128 + 128]
                    rhs = xb_sb[:, kc * NO + tt * 512: kc * NO + tt * 512 + 512]
                    if kc == 15:
                        P.emit("tensor", lambda lhsT=lhsT, rhs=rhs, bank=bank:
                               nc.tensor.matmul(B[bank], lhsT, rhs, start=False,
                                                stop=True).then_inc(sems["t"], 1))
                        t_h3 = P.bump("t")
                    else:
                        P.emit("tensor", lambda lhsT=lhsT, rhs=rhs, kc=kc, bank=bank:
                               nc.tensor.matmul(B[bank], lhsT, rhs,
                                                start=(kc == 0), stop=False))
                P.wait("vector", "t", t_h3)
                P.emit("vector", lambda tt=tt, bank=bank: nc.vector.tensor_copy(
                    h3b_st[:, 512 * tt:512 * (tt + 1)], B[bank]))
            P.emit("vector", lambda hs=hs: nc.vector.tensor_tensor(
                hh_sb[:, hs * KB:(hs + 1) * KB], s1_sb[:, hs * KB:(hs + 1) * KB],
                h3b_st, OP.mult).then_inc(sems["v"], 1))
            v_hh = P.bump("v")
            P.wait("tensor", "v", v_hh)
        v_w2 = stream_expert_w(
            lambda hs, e=e: w2_d[e, 128 * hs:128 * (hs + 1), :],
            16, lambda kc, hs: wexp[:, kc * 2 * KB + hs * 128: kc * 2 * KB + hs * 128 + 128],
            8, first_wait=("v", P.cnt["v"]))
        P.wait("tensor", "a", v_w2)
        for dt_ in range(16):
            for tt in range(2):
                bank = tt % 2
                for hc in range(8):
                    lhsT = wexp[:, hc * 2 * KB + dt_ * 128: hc * 2 * KB + dt_ * 128 + 128]
                    rhs = hh_sb[:, hc * KB + tt * 512: hc * KB + tt * 512 + 512]
                    if hc == 7:
                        P.emit("tensor", lambda lhsT=lhsT, rhs=rhs, bank=bank:
                               nc.tensor.matmul(B[bank], lhsT, rhs, start=False,
                                                stop=True).then_inc(sems["t"], 1))
                        t_oe = P.bump("t")
                    else:
                        P.emit("tensor", lambda lhsT=lhsT, rhs=rhs, hc=hc, bank=bank:
                               nc.tensor.matmul(B[bank], lhsT, rhs,
                                                start=(hc == 0), stop=False))
                P.wait("vector", "t", t_oe)
                P.emit("vector", lambda tt=tt, bank=bank: nc.vector.tensor_copy(
                    oestg[:, 512 * tt:512 * (tt + 1)], B[bank]).then_inc(sems["v"], 1))
                v_oe = P.bump("v")
                P.wait("tensor", "v", v_oe)
            P.wait("sync", "v", P.cnt["v"])
            P.emit("sync", lambda e=e, dt_=dt_: nc.sync.dma_start(
                out=oe_dram[e, 128 * dt_:128 * (dt_ + 1), :], in_=oestg)
                .then_inc(sems["st"], 16))
            st_oe = P.bump("st", 16)
            P.wait("vector", "st", st_oe)
        v_oe_prev = P.cnt["v"]
    st_experts = P.cnt["st"]

    if upto == "experts":
        _finalize(nc, P, sems, es)
        return nc

    P.emit("scalar", lambda: nc.scalar.preload_act_table(AF.Silu))
    a_w1_done = None
    for phase in ("w1", "w3"):
        if phase == "w3":
            a_w1_done = P.cnt["a"]
            P.wait("vector", "a", a_w1_done)
        for hs in range(8):
            wd = sw1_d if phase == "w1" else sw3_d
            P.emit("sync", lambda wd=wd, hs=hs: nc.sync.dma_start(
                out=ssl, in_=wd[128 * hs:128 * (hs + 1), :]).then_inc(sems["sa"], 16))
            v_s = P.bump("sa", 16)
            P.wait("tensor", "sa", v_s)
            for kc in range(16):
                bk = 6 + kc % 2
                P.emit("tensor", lambda kc=kc, bk=bk: nc.tensor.transpose(
                    B[bk][:, 0:128], ssl[:, 128 * kc:128 * (kc + 1)], eye)
                    .then_inc(sems["t"], 1))
                t_t = P.bump("t")
                P.wait("vector", "t", t_t)
                P.emit("vector", lambda kc=kc, bk=bk: nc.vector.tensor_copy(
                    swt[:, 128 * kc:128 * (kc + 1)], B[bk][:, 0:128])
                    .then_inc(sems["v"], 1))
                v_t = P.bump("v")
                P.wait("tensor", "v", v_t)
            P.wait("sync", "t", P.cnt["t"])
            for tt in range(2):
                bank = tt % 2
                for kc in range(16):
                    lhsT = swt[:, 128 * kc:128 * (kc + 1)]
                    rhs = h2_fm[:, kc * NO + tt * 512: kc * NO + tt * 512 + 512]
                    if kc == 15:
                        P.emit("tensor", lambda lhsT=lhsT, rhs=rhs, bank=bank:
                               nc.tensor.matmul(B[bank], lhsT, rhs, start=False,
                                                stop=True).then_inc(sems["t"], 1))
                        t_h = P.bump("t")
                    else:
                        P.emit("tensor", lambda lhsT=lhsT, rhs=rhs, kc=kc, bank=bank:
                               nc.tensor.matmul(B[bank], lhsT, rhs,
                                                start=(kc == 0), stop=False))
                if phase == "w1":
                    P.wait("scalar", "t", t_h)
                    P.emit("scalar", lambda hs=hs, tt=tt, bank=bank:
                           nc.scalar.activation(
                               hh_s[:, hs * KB + 512 * tt: hs * KB + 512 * tt + 512],
                               B[bank], AF.Silu).then_inc(sems["a"], 1))
                    a_h = P.bump("a")
                    P.wait("tensor", "a", a_h)
                else:
                    P.wait("vector", "t", t_h)
                    dst = hh_s[:, hs * KB + 512 * tt: hs * KB + 512 * tt + 512]
                    P.emit("vector", lambda dst=dst, bank=bank:
                           nc.vector.tensor_tensor(dst, B[bank], dst, OP.mult)
                           .then_inc(sems["v"], 1))
                    v_hh = P.bump("v")
                    P.wait("tensor", "v", v_hh)

    for ds in range(16):
        P.emit("sync", lambda ds=ds: nc.sync.dma_start(
            out=ssl[:, 0:1024], in_=sw2_d[128 * ds:128 * (ds + 1), :])
            .then_inc(sems["sa"], 16))
        v_s = P.bump("sa", 16)
        P.wait("tensor", "sa", v_s)
        for hc in range(8):
            bk = 6 + hc % 2
            P.emit("tensor", lambda hc=hc, bk=bk: nc.tensor.transpose(
                B[bk][:, 0:128], ssl[:, 128 * hc:128 * (hc + 1)], eye)
                .then_inc(sems["t"], 1))
            t_t = P.bump("t")
            P.wait("vector", "t", t_t)
            P.emit("vector", lambda hc=hc, bk=bk: nc.vector.tensor_copy(
                swt[:, 128 * hc:128 * (hc + 1)], B[bk][:, 0:128])
                .then_inc(sems["v"], 1))
            v_t = P.bump("v")
            P.wait("tensor", "v", v_t)
        P.wait("sync", "t", P.cnt["t"])
        for tt in range(2):
            bank = tt % 2
            for hc in range(8):
                lhsT = swt[:, 128 * hc:128 * (hc + 1)]
                rhs = hh_s[:, hc * KB + tt * 512: hc * KB + tt * 512 + 512]
                if hc == 7:
                    P.emit("tensor", lambda lhsT=lhsT, rhs=rhs, bank=bank:
                           nc.tensor.matmul(B[bank], lhsT, rhs, start=False,
                                            stop=True).then_inc(sems["t"], 1))
                    t_o = P.bump("t")
                else:
                    P.emit("tensor", lambda lhsT=lhsT, rhs=rhs, hc=hc, bank=bank:
                           nc.tensor.matmul(B[bank], lhsT, rhs,
                                            start=(hc == 0), stop=False))
            P.wait("vector", "t", t_o)
            dst = out_moe[:, ds * NO + 512 * tt: ds * NO + 512 * tt + 512]
            P.emit("vector", lambda dst=dst, bank=bank: nc.vector.tensor_copy(
                dst, B[bank]).then_inc(sems["v"], 1))
            v_o = P.bump("v")
            P.wait("tensor", "v", v_o)
    v_shared = P.cnt["v"]

    P.wait("scalar", "v", v_shared)
    for t in range(8):
        P.emit("scalar", lambda t=t: nc.scalar.dma_start(
            out=x2_sb[:, t * D:(t + 1) * D], in_=x2_dram[128 * t:128 * (t + 1), :])
            .then_inc(sems["xa"], 16))
        P.bump("xa", 16)
    v_x2r = P.cnt["xa"]
    P.wait("gpsimd", "st", st_experts)
    P.wait("gpsimd", "v", v_shared)
    g_oe = {}
    for e in range(8):
        for jt in range(16):
            i = e * 16 + jt
            ob = oe_buf[i % 2]
            osem = "goa" if i % 2 == 0 else "gob"
            if i >= 2:
                P.wait("gpsimd", "v", g_oe[i - 2])
            P.emit("gpsimd", lambda e=e, jt=jt, ob=ob, osem=osem:
                   nc.gpsimd.dma_start(
                       out=ob, in_=oe_dram[e, 128 * jt:128 * (jt + 1), :])
                   .then_inc(sems[osem], 16))
            g_l = P.bump(osem, 16)
            P.wait("vector", osem, g_l)
            dst = out_moe[:, jt * NO:(jt + 1) * NO]
            P.emit("vector", lambda dst=dst, ob=ob: nc.vector.tensor_tensor(
                dst, dst, ob, OP.add).then_inc(sems["v"], 1))
            g_oe[i] = P.bump("v")
    v_moe = P.cnt["v"]

    P.wait("vector", "xa", v_x2r)
    P.wait("tensor", "xa", v_x2r)
    for jt in range(16):
        P.emit("vector", lambda jt=jt: nc.vector.tensor_scalar(
            fin_scr, out_moe[:, jt * NO:(jt + 1) * NO], gb_v[:, jt:jt + 1],
            None, OP.mult).then_inc(sems["v"], 1))
        v_f = P.bump("v")
        P.wait("tensor", "v", v_f)
        for tt in range(8):
            bk = 2 + tt % 2
            P.emit("tensor", lambda tt=tt, bk=bk: nc.tensor.transpose(
                B[bk][:, 0:128], fin_scr[:, 128 * tt:128 * (tt + 1)], eye)
                .then_inc(sems["t"], 1))
            t_t = P.bump("t")
            P.wait("vector", "t", t_t)
            dst = x2_sb[:, tt * D + jt * 128: tt * D + jt * 128 + 128]
            P.emit("vector", lambda dst=dst, bk=bk: nc.vector.tensor_tensor(
                dst, B[bk][:, 0:128], dst, OP.add).then_inc(sems["v"], 1))
            v_a = P.bump("v")
            P.wait("tensor", "v", v_a)
    v_y = P.cnt["v"]
    P.wait("sync", "v", v_y)
    for t in range(8):
        P.emit("sync", lambda t=t: nc.sync.dma_start(
            out=y_d[128 * t:128 * (t + 1), :], in_=x2_sb[:, t * D:(t + 1) * D])
            .then_inc(sems["st"], 16))
        P.bump("st", 16)

    _finalize(nc, P, sems, es)
    return nc


def _finalize(nc, P, sems, es):
    for semn in ("st", "tap", "g", "goa", "gob", "xa", "xb"):
        if P.cnt[semn]:
            P.wait("sync", semn, P.cnt[semn])
    with nc.Block() as block:
        @block.tensor
        def _(eng):
            for f in P.q["tensor"]:
                f()

        @block.vector
        def _(eng):
            for f in P.q["vector"]:
                f()

        @block.scalar
        def _(eng):
            for f in P.q["scalar"]:
                f()

        @block.sync
        def _(eng):
            for f in P.q["sync"]:
                f()

        @block.gpsimd
        def _(eng):
            for f in P.q["gpsimd"]:
                f()
    es.close()


def _rot_idx(qi):
    own = list(range(4 * qi, 4 * qi + 4))
    rest = [c for c in range(16) if c not in own]
    idx = np.concatenate([np.arange(c * CHK, (c + 1) * CHK) for c in own + rest])
    return idx


def make_in_maps(inputs):
    x = np.ascontiguousarray(np.asarray(inputs["L0_attn_hidden_states"]))
    eye = np.eye(128, dtype=np.float32)
    eyeb = np.eye(128, dtype=ml_dtypes.bfloat16)
    shared = {
        "wq": np.ascontiguousarray(inputs["L0_attn_q_weight"]),
        "wk": np.ascontiguousarray(inputs["L0_attn_k_weight"]),
        "wv": np.ascontiguousarray(inputs["L0_attn_v_weight"]),
        "wo": np.ascontiguousarray(inputs["L0_attn_o_weight"]),
        "bq": np.ascontiguousarray(inputs["L0_attn_q_bias"]),
        "bk": np.ascontiguousarray(inputs["L0_attn_k_bias"]),
        "bv": np.ascontiguousarray(inputs["L0_attn_v_bias"]),
        "bo": np.ascontiguousarray(inputs["L0_attn_o_bias"]),
        "gate": np.ascontiguousarray(inputs["L0_ffn_gate"]),
        "ebias": np.ascontiguousarray(inputs["L0_ffn_expert_bias"]),
        "w1": np.ascontiguousarray(inputs["L0_ffn_w1"]),
        "w2": np.ascontiguousarray(inputs["L0_ffn_w2"]),
        "w3": np.ascontiguousarray(inputs["L0_ffn_w3"]),
        "sw1": np.ascontiguousarray(inputs["L0_ffn_shared_w1"]),
        "sw2": np.ascontiguousarray(inputs["L0_ffn_shared_w2"]),
        "sw3": np.ascontiguousarray(inputs["L0_ffn_shared_w3"]),
        "wmod": np.ascontiguousarray(inputs["L0_wmod"]),
        "eyef": eye, "eyeb": eyeb,
    }
    cfull = np.asarray(inputs["L0_c"])
    in_maps = []
    for c in range(8):
        b, qi = c // 4, c % 4
        m = dict(shared)
        m["x"] = np.ascontiguousarray(x[b][_rot_idx(qi)])
        m["cvec"] = np.ascontiguousarray(cfull[b])
        in_maps.append(m)
    return in_maps


def kernel(**inputs):
    nc = build("full")
    in_maps = make_in_maps(inputs)
    res = run_bass_kernel_spmd(nc, in_maps, core_ids=list(range(8)))
    out = np.zeros((2, 4096, 2048), dtype=np.float32)
    for c in range(8):
        b, qi = c // 4, c % 4
        out[b, qi * NO:(qi + 1) * NO] = np.asarray(res.results[c]["y"])
    return out


# version 04
# speedup vs baseline: 1.3852x; 1.1775x over previous
import numpy as np
import ml_dtypes
from contextlib import ExitStack

import concourse.bass as bass
import concourse.mybir as mybir
from concourse.alu_op_type import AluOpType as OP
from concourse.bass_utils import run_bass_kernel_spmd

F32 = mybir.dt.float32
BF16 = mybir.dt.bfloat16
AF = mybir.ActivationFunctionType
AX = mybir.AxisListType

D = 2048
NT = 4096
NO = 1024
CHK = 256
NCH = 16
NCH_OWN = 4
EPS = 1e-5
C7 = float(np.float32(1.0) / np.float32(7.0))
INV2048 = float(np.float32(1.0) / np.float32(2048.0))

ENGINES = ("tensor", "vector", "scalar", "sync", "gpsimd")
KB = 1024


class Prog:
    def __init__(self, nc, sems):
        self.nc = nc
        self.S = sems
        self.q = {e: [] for e in ENGINES}
        self.cnt = {k: 0 for k in sems}
        self.autogap = False
        self.gap_fn = None

    def emit(self, eng, fn):
        self.q[eng].append(fn)
        if eng == "vector" and self.gap_fn is not None:
            self.q[eng].append(self.gap_fn)

    def bump(self, sem, by=1):
        self.cnt[sem] += by
        return self.cnt[sem]

    def wait(self, eng, sem, val):
        nc, h = self.nc, self.S[sem]
        self.emit(eng, lambda: getattr(nc, eng).wait_ge(h, val))


def build(upto="full", taps=()):
    nc = bass.Bass("TRN2", target_bir_lowering=False)

    def dri(n, s, dt=F32):
        return nc.dram_tensor(n, s, dt, kind="ExternalInput").ap()

    x_d = dri("x", [NT, D])
    w_qkvo = {"q": dri("wq", [D, D]), "k": dri("wk", [D, D]),
              "v": dri("wv", [D, D]), "o": dri("wo", [D, D])}
    b_qkvo = {"q": dri("bq", [D]), "k": dri("bk", [D]),
              "v": dri("bv", [D]), "o": dri("bo", [D])}
    gate_d = dri("gate", [8, D])
    ebias_d = dri("ebias", [8])
    w1_d = dri("w1", [8, 1024, D])
    w2_d = dri("w2", [8, D, 1024])
    w3_d = dri("w3", [8, 1024, D])
    sw1_d = dri("sw1", [1024, D])
    sw2_d = dri("sw2", [D, 1024])
    sw3_d = dri("sw3", [1024, D])
    cvec_d = dri("cvec", [D])
    wmod_d = dri("wmod", [D, 6 * D])
    eye_d = dri("eyef", [128, 128])
    eyeb_d = nc.dram_tensor("eyeb", [128, 128], BF16, kind="ExternalInput").ap()

    y_d = nc.dram_tensor("y", [NO, D], F32, kind="ExternalOutput").ap()

    K_dram = nc.dram_tensor("K_dram", [D, NT], F32).ap()
    Vt_dram = nc.dram_tensor("Vt_dram", [NT, D], F32).ap()
    Q_dram = nc.dram_tensor("Q_dram", [D, NO], F32).ap()
    oe_dram = nc.dram_tensor("oe_dram", [8, D, NO], F32).ap()
    x2_dram = nc.dram_tensor("x2_dram", [NO, D], F32).ap()

    tap_d = {}
    for tname, tshape, tdt in taps:
        tap_d[tname] = nc.dram_tensor("tap_" + tname, list(tshape), tdt,
                                      kind="ExternalOutput").ap()

    AR = 192 * KB
    arena = nc.alloc_sbuf_tensor("arena", [128, AR], mybir.dt.uint8)
    base = nc.lookup_mloc(arena).addr

    def at(name, shape, dt, off):
        assert off + shape[1] * (4 if dt == F32 else 2 if dt == BF16 else 1) <= AR
        return nc.alloc_sbuf_tensor_at(name, shape, dt, offset=base + off).ap()

    R1, R2, R3 = 0, 64 * KB, 128 * KB
    Ip = at("Ip", [128, 16 * NT], BF16, R1)
    ctx_fm = at("ctx_fm", [128, 16 * NO], F32, R1)
    x2_sb = at("x2", [128, 8 * D], F32, R1)
    xb_sb = at("xb", [128, 16 * NO], BF16, R1)
    wexp = at("wexp", [128, 16 * KB], BF16, R1 + 32 * KB)
    ssl = at("ssl", [128, D], F32, R1)
    swt = at("swt", [128, D], F32, R1 + 8 * KB)
    s1s_sl = at("s1s", [128, 1024], F32, R1 + 16 * KB)
    hh_s = at("hhs", [128, 8 * KB], F32, R1 + 20 * KB)
    q4 = [at(f"q4_{i}", [128, 1024], F32, R2 + i * 4 * KB) for i in range(2)]
    kcb = [at(f"kcb{i}", [128, 256], F32, R2 + 37 * KB + i * KB) for i in range(8)]
    v65 = [at(f"v65{i}", [128, 260], F32, R2 + 45 * KB + i * 1056) for i in range(8)]
    probs = [at(f"prb{i}", [128, 512], F32, R2 + 13 * KB + i * 2 * KB)
             for i in range(8)]
    rsum65 = at("rsum65", [65, 4 * 512], F32, R2 + 29 * KB)
    rsum0 = at("rsum0", [1, 4 * 512], F32, R2 + 54 * KB)
    rq0 = rsum0
    rsb = at("rsb", [64, 512], F32, R2 + 8 * KB)
    ostg = at("ostg", [64, 512], F32, R2 + 10 * KB)
    I2 = at("I2", [128, 16 * NO], BF16, R2)
    pd_scr = at("pd_scr", [128, NO], F32, R2 + 32 * KB)
    pd_scr2 = at("pd_scr2", [128, NO], F32, R2 + 36 * KB)
    h2_fm = at("h2f", [128, 16 * NO], F32, R2)
    oe_buf = [at(f"oeb{i}", [128, NO], F32, R2 + i * 4 * KB) for i in range(2)]
    fin_scr = at("fin_scr", [128, NO], F32, R2 + 8 * KB)
    wcol = [at(f"wcol{i}", [128, D], F32, R3 + i * 8 * KB) for i in range(2)]
    qw_buf = at("qw_buf", [128, D], F32, R3 + 8 * KB)
    iwtT = at("iwtT", [128, D], BF16, R3 + 24 * KB)
    xt_buf = [at(f"xt{i}", [128, D], F32, R3 + 16 * KB + i * 8 * KB) for i in range(2)]
    qchunk = at("qchunk", [128, 16 * CHK], F32, R3 + 16 * KB)
    hfm = at("hfm", [128, 16 * CHK], F32, R3 + 32 * KB)
    lad_thr = at("lad_thr", [128, D], F32, R3 + 44 * KB)
    lad_bb = at("lad_bb", [128, D], F32, R3 + 52 * KB)
    scr2k = at("scr2k", [128, D], F32, R3 + 52 * KB)
    wts_f = at("wts_f", [128, D], F32, R3 + 16 * KB)
    iwt_sl = at("iwt", [128, D], BF16, R3 + 24 * KB)
    krow = at("krow", [128, NT], F32, R3 + 28 * KB)
    mw_sl2 = [at("mw_sl", [128, D], F32, R3 + 36 * KB),
              at("mw_slb", [128, D], F32, R3 + 44 * KB)]
    s1_sb = at("s1", [128, 8 * KB], BF16, R3)
    hh_sb = at("hh", [128, 8 * KB], BF16, R3 + 16 * KB)
    oestg = at("oestg", [128, NO], F32, R3 + 32 * KB)
    out_moe = at("om", [128, 16 * NO], F32, R3)

    def sb(n, s, dt=F32):
        return nc.alloc_sbuf_tensor(n, s, dt).ap()

    eye = sb("eye", [128, 128])
    eye16 = sb("eye16", [128, 128], BF16)
    ones1 = sb("ones1", [1, 128])
    cvec = sb("cvecs", [128, 16])
    scv = sb("scv", [128, 16])
    modv = sb("modv", [128, 96])
    ca_v = sb("cav", [128, 16])
    ga_v = sb("gav", [128, 16])
    cb_v = sb("cbv", [128, 16])
    gb_v = sb("gbv", [128, 16])
    t5a = sb("t5a", [128, 16]); t5b = sb("t5b", [128, 16]); t5c = sb("t5c", [128, 16])
    chb = sb("chb", [128, 16 * NCH])
    tm896 = sb("tm896", [128, NCH])
    chb2 = sb("chb2", [128, 16 * 16])
    tm896o = sb("tm896o", [128, NCH_OWN])
    mxb = sb("mxb", [128, 16]); mnb = sb("mnb", [128, 16])
    cmaxb = sb("cmaxb", [128, 16]); p2b = sb("p2b", [128, 16])
    rscaleb = sb("rscaleb", [128, 16])
    tmcol = sb("tmcol", [128, 1]); tmrow = sb("tmrow", [1, 128])
    tmsc = sb("tmsc", [1, 1]); tmb = sb("tmb", [128, 1])
    thrj = sb("thrj", [128, 1]); tm7b = sb("tm7b", [128, 1])
    lnsm = sb("lnsm", [128, 1]); lnm = sb("lnm", [128, 1])
    lnv2 = sb("lnv2", [128, 1]); lna = sb("lna", [128, 1])
    lnsq = sb("lnsq", [128, 1]); lnr0 = sb("lnr0", [128, 1])
    lnt = sb("lnt", [128, 1]); lnr = sb("lnr", [128, 1])
    swv = sb("swv", [128, 1]); rswv = sb("rswv", [128, 1])
    swcv = sb("swcv", [128, 16]); cbbv = sb("cbbv", [128, 16])
    bias_sl = sb("bias_sl", [128, 1])
    gate_sb = at("gate_sb", [8, D], F32, R3 + 44 * KB)
    gtT = sb("gtT", [128, 16 * 8])
    scoresT = at("scoresT", [8, 1024], F32, R3 + 52 * KB)
    sc_tok = sb("sc_tok", [128, 8 * 8])
    bs_tok = sb("bs_tok", [128, 8])
    ebias_r = sb("ebias_r", [1, 8])
    ebias_b = sb("ebias_b", [128, 8])
    cnt1 = sb("cnt1", [128, 1]); cnt2 = sb("cnt2", [128, 1])
    cmp8 = sb("cmp8", [128, 8])
    selc = sb("selc", [128, 1])
    wt_tok = sb("wt_tok", [128, 8 * 8])
    wt_fm = at("wt_fm", [8, 1024], F32, R3 + 56 * KB)
    wtrow0 = at("wtrow0", [1, 1024], F32, R3 + 60 * KB)
    h1b_st = sb("h1b_st", [128, 1024], BF16)
    h3b_st = sb("h3b_st", [128, 1024], BF16)

    psum = nc.alloc_psum_tensor("psum", [128, 4096], F32).ap()
    B = [psum[:, 512 * i: 512 * (i + 1)] for i in range(8)]
    psum16 = psum.bitcast(BF16)
    B16 = [psum16[:, 1024 * i: 1024 * (i + 1)] for i in range(8)]

    vg_a = sb("vg_a", [128, 64])
    vg_b = sb("vg_b", [128, 64])

    sem_names = ["c0", "wa", "wb", "bs", "xa", "xb", "ka", "kb", "qa", "qb",
                 "gg", "sa", "st", "t", "v", "a", "g", "gw", "goa", "gob", "wt", "mw",
                 "k0", "k1", "k2", "k3", "k4", "k5", "k6", "k7", "tap"]
    es = ExitStack()
    sems = {n: es.enter_context(nc.semaphore(n)) for n in sem_names}
    P = Prog(nc, sems)
    P.gap_fn = lambda: nc.vector.drain()

    def tap(name, src_ap):
        if name not in tap_d:
            return
        dst = tap_d[name]
        if P.cnt["tap"]:
            P.wait("gpsimd", "tap", P.cnt["tap"])
        P.emit("gpsimd", lambda: nc.gpsimd.dma_start(out=dst, in_=src_ap)
               .then_inc(sems["tap"], 16))
        c = P.bump("tap", 16)
        P.wait("vector", "tap", c)
        P.wait("scalar", "tap", c)

    def tap_wait(eng_sem_pairs):
        for semn, val in eng_sem_pairs:
            if tap_d:
                P.wait("gpsimd", semn, val)

    def emit_ladder(buf, width):
        v = nc.vector
        thr, bb = lad_thr[:, 0:width], lad_bb[:, 0:width]
        P.emit("vector", lambda: v.tensor_scalar(bb, buf, -0.5, None, OP.is_ge))
        P.emit("vector", lambda: v.tensor_scalar(thr, bb, 8.0, -4.5, OP.mult, OP.add))
        P.emit("vector", lambda: v.tensor_tensor(bb, buf, thr, OP.is_ge))
        P.emit("vector", lambda: v.scalar_tensor_tensor(thr, bb, 4.0, thr, OP.mult, OP.add))
        P.emit("vector", lambda: v.scalar_tensor_tensor(bb, thr, -2.0, buf, OP.add, OP.is_le))
        P.emit("vector", lambda: v.scalar_tensor_tensor(thr, bb, 2.0, thr, OP.mult, OP.add))
        P.emit("vector", lambda: v.scalar_tensor_tensor(bb, thr, -3.0, buf, OP.add, OP.is_le))
        P.emit("vector", lambda: v.scalar_tensor_tensor(buf, thr, -3.5, bb, OP.add, OP.add))

    for dst, src in [(eye, eye_d), (eye16, eyeb_d)]:
        P.emit("sync", lambda dst=dst, src=src:
               nc.sync.dma_start(out=dst, in_=src).then_inc(sems["c0"], 16))
        P.bump("c0", 16)
    P.emit("vector", lambda: nc.vector.memset(ones1, 1.0)
           .then_inc(sems["v"], 1))
    v_const = P.bump("v")

    P.emit("sync", lambda: nc.sync.dma_start(
        out=cvec, in_=cvec_d.rearrange("(j p) -> p j", p=128),
        allow_slow_non_contiguous=True).then_inc(sems["c0"], 16))
    v_ld = P.bump("c0", 16)
    P.wait("scalar", "c0", v_ld)
    P.emit("scalar", lambda: nc.scalar.preload_act_table(AF.Silu))
    P.emit("scalar", lambda: nc.scalar.activation(scv, cvec, AF.Silu)
           .then_inc(sems["a"], 1))
    v_scv = P.bump("a")

    mod_mm_done = {}
    for m in range(96):
        buf = wcol[m % 2]
        wsem = "wa" if m % 2 == 0 else "wb"
        src = wmod_d[:, 128 * m:128 * (m + 1)].rearrange("(k p) j -> p k j", p=128)
        dst3 = buf.rearrange("p (k j) -> p k j", j=128)
        if m >= 2:
            P.wait("sync", "t", mod_mm_done[m - 2])
        P.emit("sync", lambda dst3=dst3, src=src, wsem=wsem:
               nc.sync.dma_start(out=dst3, in_=src).then_inc(sems[wsem], 16))
        v_ld = P.bump(wsem, 16)
        P.wait("tensor", wsem, v_ld)
        if m == 0:
            P.wait("tensor", "a", v_scv)
        bufr = buf.rearrange("p (k j) -> p k j", j=128)
        for kc in range(16):
            lhsT, rhs = bufr[:, kc], scv[:, kc:kc + 1]
            if kc == 15:
                P.emit("tensor", lambda lhsT=lhsT, rhs=rhs, m=m:
                       nc.tensor.matmul(B[7][:, m:m + 1], lhsT, rhs,
                                        start=False, stop=True).then_inc(sems["t"], 1))
                mod_mm_done[m] = P.bump("t")
            else:
                P.emit("tensor", lambda lhsT=lhsT, rhs=rhs, m=m, kc=kc:
                       nc.tensor.matmul(B[7][:, m:m + 1], lhsT, rhs,
                                        start=(kc == 0), stop=False))
    P.wait("vector", "t", mod_mm_done[31])
    P.emit("vector", lambda: nc.vector.tensor_copy(modv[:, 0:16], B[7][:, 0:16]))
    P.emit("vector", lambda: nc.vector.tensor_scalar(
        ca_v, B[7][:, 16:32], 1.0, None, OP.add).then_inc(sems["v"], 1))
    v_mod1 = P.bump("v")

    def emit_clamp(src_cols, dst):
        v = nc.vector
        P.autogap = True
        P.emit("vector", lambda: v.tensor_scalar(t5a, src_cols, 0.25, None, OP.max))
        P.emit("vector", lambda: v.tensor_scalar(t5b, src_cols, -0.25, None, OP.min))
        P.emit("vector", lambda: v.tensor_scalar(t5c, src_cols, 0.0, None, OP.is_ge))
        P.emit("vector", lambda: v.tensor_tensor(t5a, t5a, t5b, OP.subtract))
        P.emit("vector", lambda: v.tensor_tensor(t5a, t5c, t5a, OP.mult))
        P.emit("vector", lambda: v.tensor_tensor(dst, t5a, t5b, OP.add))
        P.autogap = False

    def emit_mod_part2():
        P.wait("vector", "t", mod_mm_done[95])
        P.emit("vector", lambda: nc.vector.tensor_copy(modv[:, 16:96], B[7][:, 16:96]))
        emit_clamp(modv[:, 32:48], ga_v)
        P.emit("vector", lambda: nc.vector.tensor_scalar(
            cb_v, modv[:, 64:80], 1.0, None, OP.add))
        emit_clamp(modv[:, 80:96], gb_v)
        P.emit("vector", lambda: nc.vector.engine_nop().then_inc(sems["v"], 1))
        return P.bump("v")

    if upto == "s1":
        v_mod2 = emit_mod_part2()
        if tap_d:
            tap_wait([("v", v_mod2)])
            tap("modv", modv)
            tap("cav", ca_v)
            tap("gav", ga_v)
            tap("cbv", cb_v)
            tap("gbv", gb_v)
        _finalize(nc, P, sems, es)
        return nc

    P.emit("scalar", lambda: nc.scalar.preload_act_table(AF.Sqrt))


    tfm_state = {"bank_done": {}}

    def emit_tok2fm(xt, dst_fn, sc_v, bi_v, half_or_t, blk):
        for g in range(4):
            pair = g % 2
            bk0 = 2 * pair
            key = (blk, pair)
            prev = tfm_state["bank_done"].get(pair)
            if prev is not None:
                P.wait("tensor", prev[0], prev[1])
            for i, pc in enumerate(range(4 * g, 4 * g + 4)):
                bank = B[bk0 + i // 2]
                col = 128 * (i % 2)
                if i == 3:
                    P.emit("tensor", lambda pc=pc, bank=bank, col=col, xt=xt:
                           nc.tensor.transpose(bank[:, col:col + 128],
                                               xt[:, 128 * pc:128 * (pc + 1)], eye)
                           .then_inc(sems["t"], 1))
                    t_tr = P.bump("t")
                else:
                    P.emit("tensor", lambda pc=pc, bank=bank, col=col, xt=xt:
                           nc.tensor.transpose(bank[:, col:col + 128],
                                               xt[:, 128 * pc:128 * (pc + 1)], eye))
            if pair == 0:
                P.wait("vector", "t", t_tr)
                for i, pc in enumerate(range(4 * g, 4 * g + 4)):
                    bank = B[bk0 + i // 2]
                    col = 128 * (i % 2)
                    dst = dst_fn(pc)
                    fin = i == 3
                    if fin:
                        P.emit("vector", lambda pc=pc, bank=bank, col=col, dst=dst:
                               nc.vector.tensor_scalar(
                                   dst, bank[:, col:col + 128], sc_v[:, pc:pc + 1],
                                   bi_v[:, pc:pc + 1], OP.mult, OP.add)
                               .then_inc(sems["v"], 1))
                    else:
                        P.emit("vector", lambda pc=pc, bank=bank, col=col, dst=dst:
                               nc.vector.tensor_scalar(
                                   dst, bank[:, col:col + 128], sc_v[:, pc:pc + 1],
                                   bi_v[:, pc:pc + 1], OP.mult, OP.add))
                tfm_state["bank_done"][pair] = ("v", P.bump("v"))
            else:
                P.wait("scalar", "t", t_tr)
                for i, pc in enumerate(range(4 * g, 4 * g + 4)):
                    bank = B[bk0 + i // 2]
                    col = 128 * (i % 2)
                    dst = dst_fn(pc)
                    fin = i == 3
                    if fin:
                        P.emit("scalar", lambda pc=pc, bank=bank, col=col, dst=dst:
                               nc.scalar.activation(
                                   dst, bank[:, col:col + 128], AF.Identity,
                                   bias=bi_v[:, pc:pc + 1], scale=sc_v[:, pc:pc + 1])
                               .then_inc(sems["a"], 1))
                    else:
                        P.emit("scalar", lambda pc=pc, bank=bank, col=col, dst=dst:
                               nc.scalar.activation(
                                   dst, bank[:, col:col + 128], AF.Identity,
                                   bias=bi_v[:, pc:pc + 1], scale=sc_v[:, pc:pc + 1]))
                tfm_state["bank_done"][pair] = ("a", P.bump("a"))
        return tfm_state["bank_done"][0][1], tfm_state["bank_done"][1][1]

    def emit_ln_tile(xin):
        v = nc.vector
        P.autogap = True
        P.emit("vector", lambda: v.tensor_reduce(lnsm, xin, axis=AX.X, op=OP.add))
        P.emit("vector", lambda: v.tensor_scalar(lnm, lnsm, INV2048, None, OP.mult))
        P.emit("vector", lambda: v.tensor_scalar(xin, xin, lnm, None, OP.subtract))
        P.emit("vector", lambda: v.tensor_tensor(scr2k, xin, xin, OP.mult))
        P.emit("vector", lambda: v.tensor_reduce(lnv2, scr2k, axis=AX.X, op=OP.add))
        P.emit("vector", lambda: v.tensor_scalar(
            lna, lnv2, INV2048, EPS, OP.mult, OP.add).then_inc(sems["v"], 1))
        vv = P.bump("v")
        P.wait("scalar", "v", vv)
        P.emit("scalar", lambda: nc.scalar.activation(lnsq, lna, AF.Sqrt)
               .then_inc(sems["a"], 1))
        aa = P.bump("a")
        P.wait("vector", "a", aa)
        P.emit("vector", lambda: v.reciprocal(lnr0, lnsq))
        P.emit("vector", lambda: v.tensor_tensor(lnt, lnr0, lnr0, OP.mult))
        P.emit("vector", lambda: v.tensor_tensor(lnt, lnt, lna, OP.mult))
        P.emit("vector", lambda: v.tensor_scalar(lnt, lnt, -0.5, 1.5, OP.mult, OP.add))
        P.emit("vector", lambda: v.tensor_tensor(lnr, lnr0, lnt, OP.mult))
        P.emit("vector", lambda: v.tensor_scalar(xin, xin, lnr, None, OP.mult))
        P.autogap = False

    def emit_quant_chunk(src_fm, chb_dst, tm_dst, ip_dst, src3_fn):
        v = nc.vector
        P.autogap = True
        src3 = src3_fn()
        P.emit("vector", lambda src3=src3: v.tensor_reduce(
            mxb, src3, axis=AX.X, op=OP.max))
        P.emit("vector", lambda src3=src3: v.tensor_reduce(
            mnb, src3, axis=AX.X, op=OP.min))
        P.emit("vector", lambda: v.tensor_tensor(t5a, mxb, mnb, OP.add))
        P.emit("vector", lambda: v.tensor_scalar(t5b, t5a, 0.5, None, OP.mult))
        for pc in range(16):
            d = chb_dst(pc)
            P.emit("vector", lambda d=d, pc=pc: v.tensor_copy(d, t5b[:, pc:pc + 1]))
        P.emit("vector", lambda: v.tensor_tensor(t5a, mxb, t5b, OP.subtract))
        P.emit("vector", lambda: v.tensor_tensor(t5c, t5b, mnb, OP.subtract))
        P.emit("vector", lambda: v.tensor_tensor(cmaxb, t5a, t5c, OP.max))
        P.emit("vector", lambda: v.tensor_reduce(
            tmcol, cmaxb, axis=AX.X, op=OP.max).then_inc(sems["v"], 1))
        vv = P.bump("v")
        P.wait("tensor", "v", vv)
        P.emit("tensor", lambda: nc.tensor.transpose(B[6][0:1, 0:128], tmcol, eye)
               .then_inc(sems["t"], 1))
        tt_ = P.bump("t")
        P.wait("vector", "t", tt_)
        P.emit("vector", lambda: v.tensor_copy(tmrow, B[6][0:1, 0:128]))
        P.emit("vector", lambda: v.tensor_reduce(
            tmsc, tmrow, axis=AX.X, op=OP.max).then_inc(sems["v"], 1))
        vv = P.bump("v")
        P.wait("tensor", "v", vv)
        P.emit("tensor", lambda: nc.tensor.matmul(
            B[6][:, 256:257], ones1, tmsc, start=True, stop=True)
            .then_inc(sems["t"], 1))
        tt_ = P.bump("t")
        P.wait("vector", "t", tt_)
        P.emit("vector", lambda: v.tensor_copy(tmb, B[6][:, 256:257]))
        P.emit("vector", lambda: v.tensor_scalar(tm7b, tmb, C7, None, OP.mult))
        P.emit("vector", lambda: v.tensor_scalar(tm_dst, tm7b, 2.0 ** -7, None, OP.mult))
        for j in range(7):
            P.emit("vector", lambda j=j: v.tensor_scalar(
                thrj, tmb, 2.0 ** (j - 7), None, OP.mult))
            P.emit("vector", lambda: v.tensor_scalar(t5a, cmaxb, thrj, None, OP.is_gt))
            if j == 0:
                P.emit("vector", lambda: v.tensor_scalar(p2b, t5a, 1.0, None, OP.add))
            else:
                P.emit("vector", lambda: v.tensor_scalar(t5a, t5a, 1.0, None, OP.add))
                P.emit("vector", lambda: v.tensor_tensor(p2b, p2b, t5a, OP.mult))
        P.emit("vector", lambda: v.tensor_scalar(t5a, p2b, tm7b, 2.0 ** -7,
                                                 OP.mult, OP.mult))
        P.emit("vector", lambda: v.tensor_scalar(t5a, t5a, 1e-9, None, OP.max))
        P.emit("vector", lambda: v.reciprocal(rscaleb, t5a))
        P.autogap = False
        for pc in range(16):
            s = src_fm(pc)
            cb_ = chb_dst(pc)
            P.emit("vector", lambda s=s, cb_=cb_, pc=pc: v.tensor_scalar(
                qchunk[:, pc * CHK:(pc + 1) * CHK], s, cb_, rscaleb[:, pc:pc + 1],
                OP.subtract, OP.mult))
        emit_ladder(qchunk[:, 0:2048], 2048)
        emit_ladder(qchunk[:, 2048:4096], 2048)
        for pc in range(16):
            d = ip_dst(pc)
            P.emit("vector", lambda d=d, pc=pc: v.tensor_scalar(
                d, qchunk[:, pc * CHK:(pc + 1) * CHK], p2b[:, pc:pc + 1], None,
                OP.mult))
        P.emit("vector", lambda: nc.vector.engine_nop().then_inc(sems["v"], 1))
        return P.bump("v")

    v_qdone = {}
    xtr_done = {}
    for c in range(NCH):
        for half in range(2):
            tix = 2 * c + half
            buf = xt_buf[half]
            xsem = "xa" if half == 0 else "xb"
            if c >= 1 and half == 0:
                P.wait("scalar", "v", v_qdone[c - 1])
            P.emit("scalar", lambda buf=buf, tix=tix, xsem=xsem:
                   nc.scalar.dma_start(
                       out=buf, in_=x_d[128 * tix:128 * (tix + 1), :])
                   .then_inc(sems[xsem], 16))
            v_x = P.bump(xsem, 16)
            P.wait("vector", xsem, v_x)
            emit_ln_tile(buf)
            P.emit("vector", lambda: nc.vector.engine_nop().then_inc(sems["v"], 1))
            v_xh = P.bump("v")
            P.wait("tensor", "v", v_xh)
            if c == 0 and half == 0:
                P.wait("tensor", "v", v_mod1)
            vv, aa = emit_tok2fm(
                buf,
                lambda pc, half=half: hfm[:, pc * CHK + half * 128: pc * CHK + half * 128 + 128],
                ca_v, modv, half, "s2")
            xtr_done[tix] = P.cnt["t"]
        v_h = P.cnt["v"]
        P.wait("vector", "a", P.cnt["a"])
        v_qdone[c] = emit_quant_chunk(
            src_fm=lambda pc: hfm[:, pc * CHK:(pc + 1) * CHK],
            chb_dst=lambda pc, c=c: chb[:, pc * 16 + c: pc * 16 + c + 1],
            tm_dst=tm896[:, c:c + 1],
            ip_dst=lambda pc, c=c: Ip[:, pc * NT + c * CHK: pc * NT + (c + 1) * CHK],
            src3_fn=lambda: hfm.rearrange("p (c t) -> p c t", t=CHK),
        )
        if tap_d and c == 0:
            tap_wait([("v", v_qdone[0])])
            tap("hfm0", hfm)
            tap("qchunk0", qchunk)

    v_quant_all = v_qdone[NCH - 1]
    v_mod2 = emit_mod_part2()
    if tap_d:
        tap_wait([("v", v_quant_all), ("v", v_mod2)])
        tap("modv", modv)
        tap("gav", ga_v)
        tap("chb", chb)
        tap("tm896", tm896)
        tap("Ip0", Ip[:, 0:NT])
    if upto == "s2":
        _finalize(nc, P, sems, es)
        return nc

    wstate = {
        "wcol_free_t": None,
        "bias_free_a": None,
        "proj_bank_a": {0: None, 1: None},
        "tr_bank_a": {0: None, 1: None},
        "itr_bank_a": {0: None, 1: None},
        "cbb_bank_a": None,
        "iwt_free_t": None,
        "qw_free_t": None,
        "krow_free": None,
    }

    def emit_wstream(kind, ip_src, ip_blk, tm_src, chb_src, nch, ntok):
        w_da = w_qkvo[kind.lower()]
        b_da = b_qkvo[kind.lower()]
        ntt = ntok // 512
        for jt in range(16):
            wsem = "wa"
            if wstate["wcol_free_t"] is not None:
                P.wait("sync", "t", wstate["wcol_free_t"])
            if wstate.get("wcol_free_v") is not None:
                P.wait("sync", "v", wstate["wcol_free_v"])
            P.emit("sync", lambda jt=jt, w_da=w_da: nc.sync.dma_start(
                out=wcol[0], in_=w_da[128 * jt:128 * (jt + 1), :])
                .then_inc(sems["wa"], 16))
            v_w = P.bump("wa", 16)
            if wstate["bias_free_a"] is not None:
                P.wait("sync", "a", wstate["bias_free_a"])
            P.emit("sync", lambda jt=jt, b_da=b_da: nc.sync.dma_start(
                out=bias_sl,
                in_=b_da.rearrange("(a p) -> p a", p=128)[:, jt:jt + 1],
                allow_slow_non_contiguous=True).then_inc(sems["bs"], 16))
            v_b = P.bump("bs", 16)
            v = nc.vector
            wbuf = wcol[0]
            P.wait("tensor", "wa", v_w)
            if wstate.get("wts_free_t") is not None:
                P.wait("scalar", "t", wstate["wts_free_t"])
            for g in range(4):
                bk = 4 + g % 2
                if wstate["tr_bank_a"][g % 2] is not None:
                    P.wait("tensor", "a", wstate["tr_bank_a"][g % 2])
                for kc in range(4 * g, 4 * g + 4):
                    fin = kc % 4 == 3
                    if fin:
                        P.emit("tensor", lambda kc=kc, bk=bk, wbuf=wbuf:
                               nc.tensor.transpose(
                                   B[bk][:, 128 * (kc % 4):128 * (kc % 4) + 128],
                                   wbuf[:, 128 * kc:128 * (kc + 1)], eye)
                               .then_inc(sems["t"], 1))
                        t_f = P.bump("t")
                    else:
                        P.emit("tensor", lambda kc=kc, bk=bk, wbuf=wbuf:
                               nc.tensor.transpose(
                                   B[bk][:, 128 * (kc % 4):128 * (kc % 4) + 128],
                                   wbuf[:, 128 * kc:128 * (kc + 1)], eye))
                if g == 3:
                    wstate["wcol_free_t"] = t_f
                P.wait("scalar", "t", t_f)
                P.emit("scalar", lambda g=g, bk=bk: nc.scalar.activation(
                    wts_f[:, 512 * g:512 * (g + 1)], B[bk], AF.Identity)
                    .then_inc(sems["a"], 1))
                wstate["tr_bank_a"][g % 2] = P.bump("a")
            a_wts = P.cnt["a"]
            P.wait("tensor", "a", a_wts)
            if wstate["cbb_bank_a"] is not None:
                P.wait("tensor", "v", wstate["cbb_bank_a"])
            for kc in range(16):
                lhsT = wts_f[:, 128 * kc:128 * (kc + 1)]
                rhs = chb_src[:, kc * 16: kc * 16 + nch]
                if kc == 15:
                    P.emit("tensor", lambda lhsT=lhsT, rhs=rhs, nch=nch:
                           nc.tensor.matmul(B[6][:, 0:nch], lhsT, rhs,
                                            start=False, stop=True)
                           .then_inc(sems["t"], 1))
                    t_cbb = P.bump("t")
                    wstate["wts_free_t"] = t_cbb
                else:
                    P.emit("tensor", lambda lhsT=lhsT, rhs=rhs, kc=kc, nch=nch:
                           nc.tensor.matmul(B[6][:, 0:nch], lhsT, rhs,
                                            start=(kc == 0), stop=False))
            P.wait("vector", "wa", v_w)
            P.autogap = True
            P.emit("vector", lambda wbuf=wbuf: v.tensor_reduce(
                swv, wbuf, axis=AX.X, op=OP.max, apply_absolute_value=True))
            P.emit("vector", lambda: v.tensor_scalar(swv, swv, C7, 1e-9, OP.mult, OP.max))
            P.emit("vector", lambda: v.reciprocal(rswv, swv))
            P.autogap = False
            if wstate["iwt_free_t"] is not None:
                P.wait("vector", "t", wstate["iwt_free_t"])
            P.emit("vector", lambda wbuf=wbuf: v.tensor_scalar(
                qw_buf, wbuf, rswv, None, OP.mult))
            thr, bb = lad_thr[:, 0:D], lad_bb[:, 0:D]
            P.emit("vector", lambda: v.tensor_scalar(bb, qw_buf, -0.5, None, OP.is_ge))
            P.emit("vector", lambda: v.tensor_scalar(thr, bb, 8.0, -4.5, OP.mult, OP.add))
            P.emit("vector", lambda: v.tensor_tensor(bb, qw_buf, thr, OP.is_ge))
            P.emit("vector", lambda: v.scalar_tensor_tensor(thr, bb, 4.0, thr, OP.mult, OP.add))
            P.emit("vector", lambda: v.scalar_tensor_tensor(bb, thr, -2.0, qw_buf, OP.add, OP.is_le))
            P.emit("vector", lambda: v.scalar_tensor_tensor(thr, bb, 2.0, thr, OP.mult, OP.add))
            P.emit("vector", lambda: v.scalar_tensor_tensor(bb, thr, -3.0, qw_buf, OP.add, OP.is_le))
            if wstate.get("qw_free_t") is not None:
                P.wait("vector", "t", wstate["qw_free_t"])
            P.emit("vector", lambda: v.scalar_tensor_tensor(qw_buf, thr, -3.5, bb, OP.add, OP.add)
                   .then_inc(sems["v"], 1))
            v_iw = P.bump("v")
            wstate["wcol_free_v"] = v_iw
            P.wait("tensor", "v", v_iw)
            for g in range(4):
                bk = 2 + g % 2
                if wstate["itr_bank_a"][g % 2] is not None:
                    P.wait("tensor", "a", wstate["itr_bank_a"][g % 2])
                for kc in range(4 * g, 4 * g + 4):
                    if kc % 4 == 3:
                        P.emit("tensor", lambda kc=kc, bk=bk:
                               nc.tensor.transpose(
                                   B[bk][:, 128 * (kc % 4):128 * (kc % 4) + 128],
                                   qw_buf[:, 128 * kc:128 * (kc + 1)], eye)
                               .then_inc(sems["t"], 1))
                        t_i = P.bump("t")
                    else:
                        P.emit("tensor", lambda kc=kc, bk=bk:
                               nc.tensor.transpose(
                                   B[bk][:, 128 * (kc % 4):128 * (kc % 4) + 128],
                                   qw_buf[:, 128 * kc:128 * (kc + 1)], eye))
                if g == 3:
                    wstate["qw_free_t"] = t_i
                P.wait("scalar", "t", t_i)
                if wstate["iwt_free_t"] is not None and g == 0:
                    P.wait("scalar", "t", wstate["iwt_free_t"])
                P.emit("scalar", lambda g=g, bk=bk: nc.scalar.activation(
                    iwtT[:, 512 * g:512 * (g + 1)], B[bk], AF.Identity)
                    .then_inc(sems["a"], 1))
                wstate["itr_bank_a"][g % 2] = P.bump("a")
            wt_iw = P.cnt["a"]
            P.wait("vector", "t", t_cbb)
            P.wait("vector", "bs", v_b)
            P.autogap = True
            if kind == "Q":
                P.emit("vector", lambda nch=nch: v.tensor_scalar(
                    cbbv[:, 0:nch], B[6][:, 0:nch], bias_sl, 0.125, OP.add, OP.mult))
                P.emit("vector", lambda nch=nch: v.tensor_scalar(
                    swcv[:, 0:nch], tm_src[:, 0:nch], swv, 0.125, OP.mult, OP.mult))
            else:
                P.emit("vector", lambda nch=nch: v.tensor_scalar(
                    cbbv[:, 0:nch], B[6][:, 0:nch], bias_sl, None, OP.add))
                P.emit("vector", lambda nch=nch: v.tensor_scalar(
                    swcv[:, 0:nch], tm_src[:, 0:nch], swv, None, OP.mult))
            P.autogap = False
            P.emit("vector", lambda: nc.vector.engine_nop().then_inc(sems["v"], 1))
            v_sc = P.bump("v")
            wstate["cbb_bank_a"] = v_sc
            P.wait("tensor", "a", wt_iw)
            P.wait("scalar", "v", v_sc)
            for tt in range(ntt):
                bank = tt % 2
                if wstate["proj_bank_a"][bank] is not None:
                    P.wait("tensor", "a", wstate["proj_bank_a"][bank])
                for kc in range(16):
                    lhsT = iwtT[:, 128 * kc:128 * (kc + 1)]
                    rhs = ip_src[:, kc * ip_blk + tt * 512: kc * ip_blk + tt * 512 + 512]
                    if kc == 15:
                        P.emit("tensor", lambda lhsT=lhsT, rhs=rhs, bank=bank:
                               nc.tensor.matmul(B[bank], lhsT, rhs,
                                                start=False, stop=True)
                               .then_inc(sems["t"], 1))
                        t_p = P.bump("t")
                    else:
                        P.emit("tensor", lambda lhsT=lhsT, rhs=rhs, kc=kc, bank=bank:
                               nc.tensor.matmul(B[bank], lhsT, rhs,
                                                start=(kc == 0), stop=False))
                if tt == ntt - 1:
                    wstate["iwt_free_t"] = t_p
                P.wait("scalar", "t", t_p)
                if wstate["krow_free"] is not None and tt == 0:
                    P.wait("scalar", wstate["krow_free"][0], wstate["krow_free"][1])
                for cc in range(2):
                    ci = tt * 2 + cc
                    dst = krow[:, tt * 512 + cc * 256: tt * 512 + cc * 256 + 256]
                    if cc == 1:
                        P.emit("scalar", lambda dst=dst, cc=cc, ci=ci, bank=bank:
                               nc.scalar.activation(
                                   dst, B[bank][:, cc * 256:(cc + 1) * 256],
                                   AF.Identity, bias=cbbv[:, ci:ci + 1],
                                   scale=swcv[:, ci:ci + 1]).then_inc(sems["a"], 1))
                    else:
                        P.emit("scalar", lambda dst=dst, cc=cc, ci=ci, bank=bank:
                               nc.scalar.activation(
                                   dst, B[bank][:, cc * 256:(cc + 1) * 256],
                                   AF.Identity, bias=cbbv[:, ci:ci + 1],
                                   scale=swcv[:, ci:ci + 1]))
                wstate["proj_bank_a"][bank] = P.bump("a")
            a_row = P.cnt["a"]
            wstate["bias_free_a"] = a_row
            if kind == "K":
                P.wait("sync", "a", a_row)
                P.emit("sync", lambda jt=jt: nc.sync.dma_start(
                    out=K_dram[128 * jt:128 * (jt + 1), :], in_=krow)
                    .then_inc(sems["st"], 16))
                st_c = P.bump("st", 16)
                wstate["krow_free"] = ("st", st_c)
            elif kind == "Q":
                P.wait("sync", "a", a_row)
                P.emit("sync", lambda jt=jt: nc.sync.dma_start(
                    out=Q_dram[128 * jt:128 * (jt + 1), :], in_=krow[:, 0:NO])
                    .then_inc(sems["st"], 16))
                st_c = P.bump("st", 16)
                wstate["krow_free"] = ("st", st_c)
            elif kind == "O":
                _pd_consume(jt, a_row)
            elif kind == "V":
                P.wait("tensor", "a", a_row)
                for kc in range(32):
                    bk = 2 + (kc % 2)
                    P.emit("tensor", lambda kc=kc, bk=bk:
                           nc.tensor.transpose(B[bk][:, 0:128],
                                               krow[:, 128 * kc:128 * (kc + 1)], eye)
                           .then_inc(sems["t"], 1))
                    t_v = P.bump("t")
                    P.wait("vector", "t", t_v)
                    P.emit("vector", lambda kc=kc, bk=bk:
                           v.tensor_copy(krow[:, 128 * kc:128 * (kc + 1)],
                                         B[bk][:, 0:128]).then_inc(sems["v"], 1))
                    v_vt = P.bump("v")
                    P.wait("tensor", "v", v_vt)
                P.wait("sync", "v", P.cnt["v"])
                dstap = Vt_dram.rearrange("(c p) (a j) -> p c a j", p=128, j=128)[:, :, jt]
                srcap = krow.rearrange("p (c j) -> p c j", j=128)
                P.emit("sync", lambda dstap=dstap, srcap=srcap: nc.sync.dma_start(
                    out=dstap, in_=srcap).then_inc(sems["st"], 16))
                st_c = P.bump("st", 16)
                wstate["krow_free"] = ("st", st_c)
        return

    wstate["wcol_free_t"] = mod_mm_done[95]
    P.wait("scalar", "v", v_quant_all)
    P.wait("tensor", "v", v_quant_all)
    emit_wstream("K", Ip, NT, tm896, chb, NCH, NT)
    emit_wstream("V", Ip, NT, tm896, chb, NCH, NT)
    emit_wstream("Q", Ip, NT, tm896, chb, NCH_OWN, NO)
    v_kvq_store = P.cnt["st"]

    if tap_d:
        tap_wait([("st", v_kvq_store)])
        tap("Krow0", K_dram[0:128, :])
        tap("Vt0", Vt_dram[0:128, :])
        tap("Q0", Q_dram[0:128, :])
    if upto == "qkv":
        _finalize(nc, P, sems, es)
        return nc

    P.emit("scalar", lambda: nc.scalar.preload_act_table(AF.Exp))
    P.wait("tensor", "st", v_kvq_store)
    P.wait("sync", "st", v_kvq_store)
    P.wait("vector", "st", v_kvq_store)
    for i in range(8):
        vv = v65[i].rearrange("p (i j) -> p i j", j=65)[:, :, 64:65]
        P.emit("vector", lambda vv=vv: nc.vector.memset(vv, 1.0))
    P.emit("vector", lambda: nc.vector.engine_nop().then_inc(sems["v"], 1))
    v_ones_pc = P.bump("v")

    t_pv = {}
    t_qk = {}
    a_exp = {}
    t_last_qk_of_grp = {}
    v_epi = {}
    for qh in range(2):
        for grp in range(8):
            gi = qh * 8 + grp
            qb = q4[gi % 2]
            qsem = "qa" if gi % 2 == 0 else "qb"
            if gi >= 2:
                P.wait("sync", "t", t_last_qk_of_grp[gi - 2])
            P.emit("sync", lambda qb=qb, qh=qh, grp=grp, qsem=qsem:
                   nc.sync.dma_start(
                       out=qb.rearrange("p (a t) -> p a t", a=2),
                       in_=Q_dram[256 * grp:256 * (grp + 1), 512 * qh:512 * (qh + 1)]
                       .rearrange("(a p) t -> p a t", p=128))
                   .then_inc(sems[qsem], 16))
            v_q = P.bump(qsem, 16)
            for kc in range(32):
                kb, vb = kcb[kc % 8], v65[kc % 8]
                if kc >= 8:
                    P.wait("sync", "t", t_pv[(gi, kc - 8)])
                elif gi >= 1:
                    P.wait("sync", "t", t_pv[(gi - 1, 24 + kc)])
                ksem = f"k{kc % 8}"
                P.emit("sync", lambda kb=kb, grp=grp, kc=kc, ksem=ksem:
                       nc.sync.dma_start(
                           out=kb.rearrange("p (a t) -> p a t", a=2),
                           in_=K_dram[256 * grp:256 * (grp + 1), 128 * kc:128 * (kc + 1)]
                           .rearrange("(a p) t -> p a t", p=128))
                       .then_inc(sems[ksem], 16))
                P.bump(ksem, 16)
                P.emit("sync", lambda vb=vb, grp=grp, kc=kc, ksem=ksem:
                       nc.sync.dma_start(
                           out=vb.rearrange("p (i j) -> p i j", j=65)[:, :, 0:64],
                           in_=Vt_dram[128 * kc:128 * (kc + 1), 256 * grp:256 * (grp + 1)]
                           .rearrange("p (i j) -> p i j", j=64))
                       .then_inc(sems[ksem], 16))
                v_v = P.bump(ksem, 16)
                P.wait("tensor", ksem, v_v)
                if kc == 0:
                    P.wait("tensor", qsem, v_q)
                    if gi == 0:
                        P.wait("tensor", "v", v_ones_pc)
                    if gi >= 1:
                        P.wait("tensor", "v", v_epi[gi - 1])
                if kc >= 1:
                    P.wait("tensor", "a", a_exp[(gi, kc - 1)])
                for pr in range(2):
                    for o in range(2):
                        h_i = 2 * pr + o
                        bank = B[h_i]
                        lhsT = kb[64 * o:64 * o + 64, 128 * pr:128 * (pr + 1)]
                        rhs = qb[64 * o:64 * o + 64, 512 * pr:512 * (pr + 1)]
                        fin = h_i == 3
                        if fin:
                            P.emit("tensor", lambda bank=bank, lhsT=lhsT, rhs=rhs, o=o:
                                   nc.tensor.matmul(bank, lhsT, rhs, start=True,
                                                    stop=True,
                                                    tile_position=(64 * o, 0))
                                   .then_inc(sems["t"], 1))
                            t_qk[(gi, kc)] = P.bump("t")
                        else:
                            P.emit("tensor", lambda bank=bank, lhsT=lhsT, rhs=rhs, o=o:
                                   nc.tensor.matmul(bank, lhsT, rhs, start=True,
                                                    stop=True,
                                                    tile_position=(64 * o, 0)))
                if kc == 31:
                    t_last_qk_of_grp[gi] = t_qk[(gi, kc)]
                if kc >= 1:
                    for h_i in range(4):
                        pb = probs[((kc - 1) % 2) * 4 + h_i]
                        lhsT = v65[(kc - 1) % 8][:, 65 * h_i:65 * h_i + 65]
                        fin = h_i == 3
                        if fin:
                            P.emit("tensor", lambda lhsT=lhsT, pb=pb, h_i=h_i, kc=kc:
                                   nc.tensor.matmul(B[4 + h_i][0:65, :], lhsT, pb,
                                                    start=(kc - 1 == 0), stop=(kc - 1 == 31))
                                   .then_inc(sems["t"], 1))
                            t_pv[(gi, kc - 1)] = P.bump("t")
                        else:
                            P.emit("tensor", lambda lhsT=lhsT, pb=pb, h_i=h_i, kc=kc:
                                   nc.tensor.matmul(B[4 + h_i][0:65, :], lhsT, pb,
                                                    start=(kc - 1 == 0), stop=(kc - 1 == 31)))
                P.wait("scalar", "t", t_qk[(gi, kc)])
                if kc >= 2:
                    P.wait("scalar", "t", t_pv[(gi, kc - 2)])
                for h_i in range(4):
                    pb = probs[(kc % 2) * 4 + h_i]
                    fin = h_i == 3
                    if fin:
                        P.emit("scalar", lambda pb=pb, h_i=h_i: nc.scalar.activation(
                            pb, B[h_i], AF.Exp).then_inc(sems["a"], 1))
                        a_exp[(gi, kc)] = P.bump("a")
                    else:
                        P.emit("scalar", lambda pb=pb, h_i=h_i: nc.scalar.activation(
                            pb, B[h_i], AF.Exp))
            P.wait("tensor", "a", a_exp[(gi, 31)])
            for h_i in range(4):
                pb = probs[(31 % 2) * 4 + h_i]
                lhsT = v65[31 % 8][:, 65 * h_i:65 * h_i + 65]
                if h_i == 3:
                    P.emit("tensor", lambda lhsT=lhsT, pb=pb, h_i=h_i:
                           nc.tensor.matmul(B[4 + h_i][0:65, :], lhsT, pb,
                                            start=False, stop=True)
                           .then_inc(sems["t"], 1))
                    t_pv[(gi, 31)] = P.bump("t")
                else:
                    P.emit("tensor", lambda lhsT=lhsT, pb=pb, h_i=h_i:
                           nc.tensor.matmul(B[4 + h_i][0:65, :], lhsT, pb,
                                            start=False, stop=True))
            P.wait("vector", "t", t_pv[(gi, 31)])
            for h_i in range(4):
                P.emit("vector", lambda h_i=h_i: nc.vector.tensor_copy(
                    rsum65[64:65, 512 * h_i:512 * (h_i + 1)], B[4 + h_i][64:65, :]))
            P.emit("vector", lambda: nc.vector.engine_nop().then_inc(sems["v"], 1))
            v_s4 = P.bump("v")
            P.wait("gpsimd", "v", v_s4)
            P.emit("gpsimd", lambda: nc.gpsimd.dma_start(
                out=rsum0, in_=rsum65[64:65, :]).then_inc(sems["g"], 16))
            g_r = P.bump("g", 16)
            P.wait("vector", "g", g_r)
            P.emit("vector", lambda: nc.vector.reciprocal(rq0, rsum0)
                   .then_inc(sems["v"], 1))
            v_rq = P.bump("v")
            P.wait("tensor", "v", v_rq)
            for h_i in range(4):
                H = 4 * grp + h_i
                pcH, half = H // 2, H % 2
                P.emit("tensor", lambda h_i=h_i: nc.tensor.matmul(
                    B[h_i][0:64, :], ones1[0:1, 0:64],
                    rq0[0:1, 512 * h_i:512 * (h_i + 1)], start=True,
                    stop=True).then_inc(sems["t"], 1))
                t_b = P.bump("t")
                P.wait("vector", "t", t_b)
                P.emit("vector", lambda h_i=h_i: nc.vector.tensor_copy(
                    rsb, B[h_i][0:64, :]))
                if half == 0:
                    dst = ctx_fm[0:64, pcH * NO + qh * 512: pcH * NO + qh * 512 + 512]
                    P.emit("vector", lambda dst=dst, h_i=h_i: nc.vector.tensor_tensor(
                        dst, B[4 + h_i][0:64, :], rsb, OP.mult)
                        .then_inc(sems["v"], 1))
                    P.bump("v")
                else:
                    P.emit("vector", lambda h_i=h_i: nc.vector.tensor_tensor(
                        ostg, B[4 + h_i][0:64, :], rsb, OP.mult)
                        .then_inc(sems["v"], 1))
                    v_o = P.bump("v")
                    P.wait("gpsimd", "v", v_o)
                    dst = ctx_fm[64:128, pcH * NO + qh * 512: pcH * NO + qh * 512 + 512]
                    P.emit("gpsimd", lambda dst=dst: nc.gpsimd.dma_start(
                        out=dst, in_=ostg).then_inc(sems["g"], 16))
                    g_c = P.bump("g", 16)
                    P.wait("vector", "g", g_c)
            P.emit("vector", lambda: nc.vector.engine_nop().then_inc(sems["v"], 1))
            v_epi[gi] = P.bump("v")
    v_attn_done_v = P.cnt["v"]
    g_attn_done = P.cnt["g"]

    if tap_d:
        tap_wait([("v", v_attn_done_v), ("g", g_attn_done)])
        tap("ctx", ctx_fm)
    if upto == "attn":
        _finalize(nc, P, sems, es)
        return nc

    P.wait("vector", "g", g_attn_done)
    for c in range(NCH_OWN):
        v_qdone2 = emit_quant_chunk(
            src_fm=lambda pc, c=c: ctx_fm[:, pc * NO + c * CHK: pc * NO + (c + 1) * CHK],
            chb_dst=lambda pc, c=c: chb2[:, pc * 16 + c: pc * 16 + c + 1],
            tm_dst=tm896o[:, c:c + 1],
            ip_dst=lambda pc, c=c: I2[:, pc * NO + c * CHK: pc * NO + (c + 1) * CHK],
            src3_fn=lambda c=c: ctx_fm.rearrange("p (q t) -> p q t", t=CHK)[:, :, :]
            if False else ctx_fm.rearrange("p (q ch) -> p q ch", q=16)[:, :, c * CHK:(c + 1) * CHK],
        )
    if tap_d:
        tap_wait([("v", v_qdone2)])
        tap("I2", I2[:, 0:NO])
        tap("chb2", chb2)
    P.wait("scalar", "v", v_qdone2)
    for t in range(8):
        P.emit("scalar", lambda t=t: nc.scalar.dma_start(
            out=x2_sb[:, t * D:(t + 1) * D], in_=x_d[128 * t:128 * (t + 1), :])
            .then_inc(sems["xa"], 16))
        P.bump("xa", 16)
    v_x2ld = P.cnt["xa"]
    P.wait("vector", "xa", v_x2ld)
    P.wait("tensor", "xa", v_x2ld)

    def _pd_consume(jt, a_row):
        v = nc.vector
        P.wait("vector", "a", a_row)
        P.emit("vector", lambda jt=jt: v.tensor_scalar(
            pd_scr, krow[:, 0:NO], ga_v[:, jt:jt + 1], None, OP.mult)
            .then_inc(sems["v"], 1))
        v_g = P.bump("v")
        P.wait("tensor", "v", v_g)
        wstate["krow_free"] = ("v", P.cnt["v"])
        for tt in range(8):
            bk = 2 + tt % 2
            P.emit("tensor", lambda tt=tt, bk=bk: nc.tensor.transpose(
                B[bk][:, 0:128], pd_scr[:, 128 * tt:128 * (tt + 1)], eye)
                .then_inc(sems["t"], 1))
            t_t = P.bump("t")
            P.wait("vector", "t", t_t)
            dst = x2_sb[:, tt * D + jt * 128: tt * D + jt * 128 + 128]
            P.emit("vector", lambda dst=dst, bk=bk: v.tensor_tensor(
                dst, B[bk][:, 0:128], dst, OP.add).then_inc(sems["v"], 1))
            v_a2 = P.bump("v")
            P.wait("tensor", "v", v_a2)

    P.wait("scalar", "v", v_qdone2)
    P.wait("tensor", "v", v_qdone2)
    emit_wstream("O", I2, NO, tm896o, chb2, NCH_OWN, NO)
    v_x2_done = P.cnt["v"]
    if tap_d:
        tap_wait([("v", v_x2_done)])
        tap("x2", x2_sb)
    if upto == "x2":
        _finalize(nc, P, sems, es)
        return nc

    P.wait("sync", "v", v_x2_done)
    for t in range(8):
        P.emit("sync", lambda t=t: nc.sync.dma_start(
            out=x2_dram[128 * t:128 * (t + 1), :], in_=x2_sb[:, t * D:(t + 1) * D])
            .then_inc(sems["st"], 16))
        P.bump("st", 16)
    st_x2 = P.cnt["st"]
    P.emit("scalar", lambda: nc.scalar.preload_act_table(AF.Sqrt))
    P.wait("vector", "v", v_x2_done)
    pe_xt_free = [None]
    for t in range(8):
        xt = wcol[0]
        if pe_xt_free[0] is not None:
            P.wait("vector", "t", pe_xt_free[0])
        P.emit("vector", lambda t=t, xt=xt: nc.vector.tensor_copy(
            xt, x2_sb[:, t * D:(t + 1) * D]))
        emit_ln_tile(xt)
        P.emit("vector", lambda: nc.vector.engine_nop().then_inc(sems["v"], 1))
        v_xh = P.bump("v")
        P.wait("tensor", "v", v_xh)
        emit_tok2fm(
            xt,
            lambda pc, t=t: h2_fm[:, pc * NO + t * 128: pc * NO + t * 128 + 128],
            cb_v, modv[:, 48:64], t, "pe")
        pe_xt_free[0] = P.cnt["t"]
    P.wait("vector", "a", P.cnt["a"])
    v_h2 = P.cnt["v"]
    if tap_d:
        tap_wait([("v", v_h2)])
        tap("h2", h2_fm)
    if upto == "h2":
        _finalize(nc, P, sems, es)
        return nc

    P.emit("sync", lambda: nc.sync.dma_start(out=gate_sb, in_=gate_d)
           .then_inc(sems["gg"], 16))
    P.emit("sync", lambda: nc.sync.dma_start(
        out=ebias_r, in_=ebias_d.rearrange("(a e) -> a e", a=1))
        .then_inc(sems["gg"], 16))
    v_g8 = P.bump("gg", 32)
    P.wait("tensor", "gg", v_g8)
    P.wait("tensor", "v", v_h2)
    P.wait("tensor", "a", P.cnt["a"])
    for kc in range(16):
        bk = 2 + kc % 2
        P.emit("tensor", lambda kc=kc, bk=bk: nc.tensor.transpose(
            B[bk][:, 0:8], gate_sb[:, 128 * kc:128 * (kc + 1)], eye[0:8, 0:8])
            .then_inc(sems["t"], 1))
        t_g = P.bump("t")
        P.wait("vector", "t", t_g)
        P.emit("vector", lambda kc=kc, bk=bk: nc.vector.tensor_copy(
            gtT[:, 8 * kc:8 * (kc + 1)], B[bk][:, 0:8])
            .then_inc(sems["v"], 1))
        v_gt = P.bump("v")
        P.wait("tensor", "v", v_gt)
    P.emit("tensor", lambda: nc.tensor.matmul(
        B[6][:, 0:8], ones1, ebias_r, start=True, stop=True).then_inc(sems["t"], 1))
    t_eb = P.bump("t")
    P.wait("vector", "t", t_eb)
    P.emit("vector", lambda: nc.vector.tensor_copy(ebias_b, B[6][:, 0:8]))
    P.emit("scalar", lambda: nc.scalar.preload_act_table(AF.Sigmoid))
    for tt in range(2):
        bank = tt % 2
        for kc in range(16):
            lhsT = gtT[:, 8 * kc:8 * (kc + 1)]
            rhs = h2_fm[:, kc * NO + tt * 512: kc * NO + tt * 512 + 512]
            if kc == 15:
                P.emit("tensor", lambda lhsT=lhsT, rhs=rhs, bank=bank:
                       nc.tensor.matmul(B[bank][0:8, :], lhsT, rhs,
                                        start=False, stop=True)
                       .then_inc(sems["t"], 1))
                t_gm = P.bump("t")
            else:
                P.emit("tensor", lambda lhsT=lhsT, rhs=rhs, kc=kc, bank=bank:
                       nc.tensor.matmul(B[bank][0:8, :], lhsT, rhs,
                                        start=(kc == 0), stop=False))
        P.wait("scalar", "t", t_gm)
        P.emit("scalar", lambda tt=tt, bank=bank: nc.scalar.activation(
            scoresT[:, 512 * tt:512 * (tt + 1)], B[bank][0:8, :], AF.Sigmoid)
            .then_inc(sems["a"], 1))
        P.bump("a")
    a_sc = P.cnt["a"]
    P.wait("tensor", "a", a_sc)
    for tti in range(8):
        bk = 2 + tti % 2
        P.emit("tensor", lambda tti=tti, bk=bk: nc.tensor.transpose(
            B[bk][:, 0:8], scoresT[:, 128 * tti:128 * (tti + 1)], eye[0:8, 0:8])
            .then_inc(sems["t"], 1))
        t_s = P.bump("t")
        P.wait("vector", "t", t_s)
        P.emit("vector", lambda tti=tti, bk=bk: nc.vector.tensor_copy(
            sc_tok[:, 8 * tti:8 * (tti + 1)], B[bk][:, 0:8]))
        P.emit("vector", lambda tti=tti: nc.vector.tensor_tensor(
            bs_tok, sc_tok[:, 8 * tti:8 * (tti + 1)], ebias_b, OP.add))
        P.emit("vector", lambda: nc.vector.engine_nop().then_inc(sems["v"], 1))
        v_bs = P.bump("v")
        P.wait("tensor", "v", v_bs)
        v = nc.vector
        P.autogap = True
        for e in range(8):
            be = bs_tok[:, e:e + 1]
            P.emit("vector", lambda be=be: v.tensor_scalar(
                cmp8, bs_tok, be, None, OP.is_gt))
            P.emit("vector", lambda: v.tensor_reduce(cnt1, cmp8, axis=AX.X, op=OP.add))
            if e > 0:
                P.emit("vector", lambda be=be: v.tensor_scalar(
                    cmp8, bs_tok, be, None, OP.is_equal))
                P.emit("vector", lambda e=e: v.tensor_reduce(
                    cnt2, cmp8[:, 0:e], axis=AX.X, op=OP.add))
                P.emit("vector", lambda: v.tensor_tensor(cnt1, cnt1, cnt2, OP.add))
            P.emit("vector", lambda: v.tensor_scalar(selc, cnt1, 1.5, None, OP.is_lt))
            P.emit("vector", lambda tti=tti, e=e: v.tensor_tensor(
                wt_tok[:, 8 * tti + e: 8 * tti + e + 1], selc,
                sc_tok[:, 8 * tti + e: 8 * tti + e + 1], OP.mult))
        P.autogap = False
        P.emit("vector", lambda: nc.vector.engine_nop().then_inc(sems["v"], 1))
        v_wt = P.bump("v")
        P.wait("tensor", "v", v_wt)
        P.emit("tensor", lambda tti=tti, bk=bk: nc.tensor.transpose(
            B[bk][0:8, 0:128], wt_tok[:, 8 * tti:8 * (tti + 1)], eye)
            .then_inc(sems["t"], 1))
        t_w = P.bump("t")
        P.wait("vector", "t", t_w)
        P.emit("vector", lambda tti=tti, bk=bk: nc.vector.tensor_copy(
            wt_fm[:, 128 * tti:128 * (tti + 1)], B[bk][0:8, 0:128])
            .then_inc(sems["v"], 1))
        v_wf = P.bump("v")
        P.wait("tensor", "v", v_wf)
    v_masks = P.cnt["v"]
    if tap_d:
        tap_wait([("v", v_masks)])
        tap("wt_fm", wt_fm)

    P.emit("scalar", lambda: nc.scalar.preload_act_table(AF.Silu))

    estate = {"bank_a": {0: None, 1: None, 2: None, 3: None},
              "mw_free_t": {0: None, 1: None}}

    def stream_expert_w(w_slice_src, n_slices, dst_col, kc_count,
                        first_wait=None, mm_emitter=None):
        for hs in range(n_slices):
            mw = mw_sl2[hs % 2]
            msem = "mw" if hs % 2 == 0 else "wt"
            if estate["mw_free_t"][hs % 2] is not None:
                P.wait("sync", "t", estate["mw_free_t"][hs % 2])
            P.emit("sync", lambda hs=hs, mw=mw, kc_count=kc_count, msem=msem:
                   nc.sync.dma_start(out=mw[:, 0:128 * kc_count],
                                     in_=w_slice_src(hs)).then_inc(sems[msem], 16))
            g_w = P.bump(msem, 16)
            P.wait("tensor", msem, g_w)
            if first_wait is not None and hs == 0:
                P.wait("tensor", first_wait[0], first_wait[1])
            ngr = kc_count // 4
            for g in range(ngr):
                bk = 2 + g % 4
                if estate["bank_a"][g % 4] is not None:
                    P.wait("tensor", "a", estate["bank_a"][g % 4])
                for kc in range(4 * g, 4 * g + 4):
                    if kc % 4 == 3:
                        P.emit("tensor", lambda kc=kc, bk=bk, mw=mw:
                               nc.tensor.transpose(
                                   B[bk][:, 128 * (kc % 4):128 * (kc % 4) + 128],
                                   mw[:, 128 * kc:128 * (kc + 1)], eye)
                               .then_inc(sems["t"], 1))
                        t_i = P.bump("t")
                    else:
                        P.emit("tensor", lambda kc=kc, bk=bk, mw=mw:
                               nc.tensor.transpose(
                                   B[bk][:, 128 * (kc % 4):128 * (kc % 4) + 128],
                                   mw[:, 128 * kc:128 * (kc + 1)], eye))
                if g == ngr - 1:
                    estate["mw_free_t"][hs % 2] = t_i
                P.wait("scalar", "t", t_i)
                for q in range(4):
                    dq = dst_col(4 * g + q, hs)
                    if q == 3:
                        P.emit("scalar", lambda dq=dq, bk=bk, q=q:
                               nc.scalar.activation(dq, B[bk][:, 128 * q:128 * (q + 1)],
                                                    AF.Identity).then_inc(sems["a"], 1))
                    else:
                        P.emit("scalar", lambda dq=dq, bk=bk, q=q:
                               nc.scalar.activation(dq, B[bk][:, 128 * q:128 * (q + 1)],
                                                    AF.Identity))
                estate["bank_a"][g % 4] = P.bump("a")
            if mm_emitter is not None:
                mm_emitter(hs, P.cnt["a"])
        return P.cnt["a"]

    P.wait("vector", "st", st_x2)
    P.wait("scalar", "st", st_x2)
    P.wait("tensor", "st", st_x2)
    estate["mw_free_t"][0] = P.cnt["t"]
    estate["mw_free_t"][1] = P.cnt["t"]
    P.wait("sync", "v", v_x2_done)
    P.wait("sync", "a", P.cnt["a"])
    v_xb_prev = None
    t_wtb_prev = None
    v_oe_prev = None
    for e in range(8):
        P.wait("gpsimd", "v", v_masks)
        if v_xb_prev is not None:
            P.wait("gpsimd", "v", v_xb_prev)
        P.emit("gpsimd", lambda e=e: nc.gpsimd.dma_start(
            out=wtrow0, in_=wt_fm[e:e + 1, :]).then_inc(sems["gw"], 16))
        g_wr = P.bump("gw", 16)
        P.wait("tensor", "gw", g_wr)
        P.wait("tensor", "v", v_masks)
        if v_xb_prev is not None:
            P.wait("tensor", "v", v_xb_prev)
        if v_oe_prev is not None:
            P.wait("tensor", "v", v_oe_prev)
        for tt in range(2):
            if tt == 1:
                P.emit("tensor", lambda e=e, tt=tt: nc.tensor.matmul(
                    B[6 + tt][:, 0:512], ones1, wtrow0[0:1, 512 * tt:512 * (tt + 1)],
                    start=True, stop=True).then_inc(sems["t"], 1))
                t_wb = P.bump("t")
            else:
                P.emit("tensor", lambda e=e, tt=tt: nc.tensor.matmul(
                    B[6 + tt][:, 0:512], ones1, wtrow0[0:1, 512 * tt:512 * (tt + 1)],
                    start=True, stop=True))
        t_wtb_prev = t_wb
        P.wait("vector", "t", t_wb)
        for pc in range(16):
            for tt in range(2):
                P.emit("vector", lambda pc=pc, tt=tt: nc.vector.tensor_tensor(
                    xb_sb[:, pc * NO + 512 * tt: pc * NO + 512 * tt + 512],
                    h2_fm[:, pc * NO + 512 * tt: pc * NO + 512 * tt + 512],
                    B[6 + tt][:, 0:512], OP.mult))
        P.emit("vector", lambda: nc.vector.engine_nop().then_inc(sems["v"], 1))
        v_xb = P.bump("v")
        v_xb_prev = v_xb
        exp_state = {"a_s1_last": None, "v_h1b": None}

        def mm_h1(hs, a_w):
            P.wait("tensor", "a", a_w)
            if hs == 0:
                P.wait("tensor", "v", v_xb)
            for tt in range(2):
                bank = tt % 2
                for kc in range(16):
                    lhsT = wexp[:, kc * KB + hs * 128: kc * KB + hs * 128 + 128]
                    rhs = xb_sb[:, kc * NO + tt * 512: kc * NO + tt * 512 + 512]
                    if kc == 15:
                        P.emit("tensor", lambda lhsT=lhsT, rhs=rhs, bank=bank:
                               nc.tensor.matmul(B[bank], lhsT, rhs, start=False,
                                                stop=True).then_inc(sems["t"], 1))
                        t_h1 = P.bump("t")
                    else:
                        P.emit("tensor", lambda lhsT=lhsT, rhs=rhs, kc=kc, bank=bank:
                               nc.tensor.matmul(B[bank], lhsT, rhs,
                                                start=(kc == 0), stop=False))
                P.wait("vector", "t", t_h1)
                if exp_state["a_s1_last"] is not None and tt == 0:
                    P.wait("vector", "a", exp_state["a_s1_last"])
                P.emit("vector", lambda tt=tt, bank=bank: nc.vector.tensor_copy(
                    h1b_st[:, 512 * tt:512 * (tt + 1)], B[bank]).then_inc(sems["v"], 1))
                v_h1b = P.bump("v")
                P.wait("tensor", "v", v_h1b)
                exp_state["v_h1b"] = v_h1b
            P.wait("scalar", "v", exp_state["v_h1b"])
            P.emit("scalar", lambda hs=hs: nc.scalar.activation(
                s1_sb[:, hs * KB:(hs + 1) * KB], h1b_st, AF.Silu)
                .then_inc(sems["a"], 1))
            exp_state["a_s1_last"] = P.bump("a")

        stream_expert_w(
            lambda hs, e=e: w1_d[e, 128 * hs:128 * (hs + 1), :],
            8, lambda kc, hs: wexp[:, kc * KB + hs * 128: kc * KB + hs * 128 + 128],
            16, first_wait=(("v", v_oe_prev) if v_oe_prev is not None else None),
            mm_emitter=mm_h1)
        a_s1 = P.cnt["a"]

        def mm_h3(hs, a_w):
            P.wait("tensor", "a", a_w)
            for tt in range(2):
                bank = tt % 2
                for kc in range(16):
                    lhsT = wexp[:, kc * KB + hs * 128: kc * KB + hs * 128 + 128]
                    rhs = xb_sb[:, kc * NO + tt * 512: kc * NO + tt * 512 + 512]
                    if kc == 15:
                        P.emit("tensor", lambda lhsT=lhsT, rhs=rhs, bank=bank:
                               nc.tensor.matmul(B[bank], lhsT, rhs, start=False,
                                                stop=True).then_inc(sems["t"], 1))
                        t_h3 = P.bump("t")
                    else:
                        P.emit("tensor", lambda lhsT=lhsT, rhs=rhs, kc=kc, bank=bank:
                               nc.tensor.matmul(B[bank], lhsT, rhs,
                                                start=(kc == 0), stop=False))
                P.wait("vector", "t", t_h3)
                P.emit("vector", lambda tt=tt, bank=bank: nc.vector.tensor_copy(
                    h3b_st[:, 512 * tt:512 * (tt + 1)], B[bank]))
            P.emit("vector", lambda hs=hs: nc.vector.tensor_tensor(
                hh_sb[:, hs * KB:(hs + 1) * KB], s1_sb[:, hs * KB:(hs + 1) * KB],
                h3b_st, OP.mult).then_inc(sems["v"], 1))
            v_hh = P.bump("v")
            P.wait("tensor", "v", v_hh)

        stream_expert_w(
            lambda hs, e=e: w3_d[e, 128 * hs:128 * (hs + 1), :],
            8, lambda kc, hs: wexp[:, kc * KB + hs * 128: kc * KB + hs * 128 + 128],
            16, mm_emitter=mm_h3)
        v_hh_all = P.cnt["v"]

        def mm_oe(dt_, a_w):
            P.wait("tensor", "a", a_w)
            if dt_ == 0:
                P.wait("tensor", "v", v_hh_all)
            for tt in range(2):
                bank = tt % 2
                for hc in range(8):
                    lhsT = wexp[:, hc * 2 * KB + dt_ * 128: hc * 2 * KB + dt_ * 128 + 128]
                    rhs = hh_sb[:, hc * KB + tt * 512: hc * KB + tt * 512 + 512]
                    if hc == 7:
                        P.emit("tensor", lambda lhsT=lhsT, rhs=rhs, bank=bank:
                               nc.tensor.matmul(B[bank], lhsT, rhs, start=False,
                                                stop=True).then_inc(sems["t"], 1))
                        t_oe = P.bump("t")
                    else:
                        P.emit("tensor", lambda lhsT=lhsT, rhs=rhs, hc=hc, bank=bank:
                               nc.tensor.matmul(B[bank], lhsT, rhs,
                                                start=(hc == 0), stop=False))
                P.wait("vector", "t", t_oe)
                P.emit("vector", lambda tt=tt, bank=bank: nc.vector.tensor_copy(
                    oestg[:, 512 * tt:512 * (tt + 1)], B[bank]).then_inc(sems["v"], 1))
                v_oe = P.bump("v")
                P.wait("tensor", "v", v_oe)
            P.wait("sync", "v", P.cnt["v"])
            P.emit("sync", lambda e=e, dt_=dt_: nc.sync.dma_start(
                out=oe_dram[e, 128 * dt_:128 * (dt_ + 1), :], in_=oestg)
                .then_inc(sems["st"], 16))
            st_oe = P.bump("st", 16)
            P.wait("vector", "st", st_oe)

        stream_expert_w(
            lambda hs, e=e: w2_d[e, 128 * hs:128 * (hs + 1), :],
            16, lambda kc, hs: wexp[:, kc * 2 * KB + hs * 128: kc * 2 * KB + hs * 128 + 128],
            8, mm_emitter=mm_oe)
        v_oe_prev = P.cnt["v"]
    st_experts = P.cnt["st"]

    if upto == "experts":
        _finalize(nc, P, sems, es)
        return nc

    P.emit("scalar", lambda: nc.scalar.preload_act_table(AF.Silu))
    a_w1_done = None
    for phase in ("w1", "w3"):
        if phase == "w3":
            a_w1_done = P.cnt["a"]
            P.wait("vector", "a", a_w1_done)
        for hs in range(8):
            wd = sw1_d if phase == "w1" else sw3_d
            P.emit("sync", lambda wd=wd, hs=hs: nc.sync.dma_start(
                out=ssl, in_=wd[128 * hs:128 * (hs + 1), :]).then_inc(sems["sa"], 16))
            v_s = P.bump("sa", 16)
            P.wait("tensor", "sa", v_s)
            for kc in range(16):
                bk = 6 + kc % 2
                P.emit("tensor", lambda kc=kc, bk=bk: nc.tensor.transpose(
                    B[bk][:, 0:128], ssl[:, 128 * kc:128 * (kc + 1)], eye)
                    .then_inc(sems["t"], 1))
                t_t = P.bump("t")
                P.wait("vector", "t", t_t)
                P.emit("vector", lambda kc=kc, bk=bk: nc.vector.tensor_copy(
                    swt[:, 128 * kc:128 * (kc + 1)], B[bk][:, 0:128])
                    .then_inc(sems["v"], 1))
                v_t = P.bump("v")
                P.wait("tensor", "v", v_t)
            P.wait("sync", "t", P.cnt["t"])
            for tt in range(2):
                bank = tt % 2
                for kc in range(16):
                    lhsT = swt[:, 128 * kc:128 * (kc + 1)]
                    rhs = h2_fm[:, kc * NO + tt * 512: kc * NO + tt * 512 + 512]
                    if kc == 15:
                        P.emit("tensor", lambda lhsT=lhsT, rhs=rhs, bank=bank:
                               nc.tensor.matmul(B[bank], lhsT, rhs, start=False,
                                                stop=True).then_inc(sems["t"], 1))
                        t_h = P.bump("t")
                    else:
                        P.emit("tensor", lambda lhsT=lhsT, rhs=rhs, kc=kc, bank=bank:
                               nc.tensor.matmul(B[bank], lhsT, rhs,
                                                start=(kc == 0), stop=False))
                if phase == "w1":
                    P.wait("scalar", "t", t_h)
                    P.emit("scalar", lambda hs=hs, tt=tt, bank=bank:
                           nc.scalar.activation(
                               hh_s[:, hs * KB + 512 * tt: hs * KB + 512 * tt + 512],
                               B[bank], AF.Silu).then_inc(sems["a"], 1))
                    a_h = P.bump("a")
                    P.wait("tensor", "a", a_h)
                else:
                    P.wait("vector", "t", t_h)
                    dst = hh_s[:, hs * KB + 512 * tt: hs * KB + 512 * tt + 512]
                    P.emit("vector", lambda dst=dst, bank=bank:
                           nc.vector.tensor_tensor(dst, B[bank], dst, OP.mult)
                           .then_inc(sems["v"], 1))
                    v_hh = P.bump("v")
                    P.wait("tensor", "v", v_hh)

    for ds in range(16):
        P.emit("sync", lambda ds=ds: nc.sync.dma_start(
            out=ssl[:, 0:1024], in_=sw2_d[128 * ds:128 * (ds + 1), :])
            .then_inc(sems["sa"], 16))
        v_s = P.bump("sa", 16)
        P.wait("tensor", "sa", v_s)
        for hc in range(8):
            bk = 6 + hc % 2
            P.emit("tensor", lambda hc=hc, bk=bk: nc.tensor.transpose(
                B[bk][:, 0:128], ssl[:, 128 * hc:128 * (hc + 1)], eye)
                .then_inc(sems["t"], 1))
            t_t = P.bump("t")
            P.wait("vector", "t", t_t)
            P.emit("vector", lambda hc=hc, bk=bk: nc.vector.tensor_copy(
                swt[:, 128 * hc:128 * (hc + 1)], B[bk][:, 0:128])
                .then_inc(sems["v"], 1))
            v_t = P.bump("v")
            P.wait("tensor", "v", v_t)
        P.wait("sync", "t", P.cnt["t"])
        for tt in range(2):
            bank = tt % 2
            for hc in range(8):
                lhsT = swt[:, 128 * hc:128 * (hc + 1)]
                rhs = hh_s[:, hc * KB + tt * 512: hc * KB + tt * 512 + 512]
                if hc == 7:
                    P.emit("tensor", lambda lhsT=lhsT, rhs=rhs, bank=bank:
                           nc.tensor.matmul(B[bank], lhsT, rhs, start=False,
                                            stop=True).then_inc(sems["t"], 1))
                    t_o = P.bump("t")
                else:
                    P.emit("tensor", lambda lhsT=lhsT, rhs=rhs, hc=hc, bank=bank:
                           nc.tensor.matmul(B[bank], lhsT, rhs,
                                            start=(hc == 0), stop=False))
            P.wait("vector", "t", t_o)
            dst = out_moe[:, ds * NO + 512 * tt: ds * NO + 512 * tt + 512]
            P.emit("vector", lambda dst=dst, bank=bank: nc.vector.tensor_copy(
                dst, B[bank]).then_inc(sems["v"], 1))
            v_o = P.bump("v")
            P.wait("tensor", "v", v_o)
    v_shared = P.cnt["v"]

    P.wait("scalar", "v", v_shared)
    for t in range(8):
        P.emit("scalar", lambda t=t: nc.scalar.dma_start(
            out=x2_sb[:, t * D:(t + 1) * D], in_=x2_dram[128 * t:128 * (t + 1), :])
            .then_inc(sems["xa"], 16))
        P.bump("xa", 16)
    v_x2r = P.cnt["xa"]
    P.wait("gpsimd", "st", st_experts)
    P.wait("gpsimd", "v", v_shared)
    g_oe = {}
    for e in range(8):
        for jt in range(16):
            i = e * 16 + jt
            ob = oe_buf[i % 2]
            osem = "goa" if i % 2 == 0 else "gob"
            if i >= 2:
                P.wait("gpsimd", "v", g_oe[i - 2])
            P.emit("gpsimd", lambda e=e, jt=jt, ob=ob, osem=osem:
                   nc.gpsimd.dma_start(
                       out=ob, in_=oe_dram[e, 128 * jt:128 * (jt + 1), :])
                   .then_inc(sems[osem], 16))
            g_l = P.bump(osem, 16)
            P.wait("vector", osem, g_l)
            dst = out_moe[:, jt * NO:(jt + 1) * NO]
            P.emit("vector", lambda dst=dst, ob=ob: nc.vector.tensor_tensor(
                dst, dst, ob, OP.add).then_inc(sems["v"], 1))
            g_oe[i] = P.bump("v")
    v_moe = P.cnt["v"]

    P.wait("vector", "xa", v_x2r)
    P.wait("tensor", "xa", v_x2r)
    for jt in range(16):
        P.emit("vector", lambda jt=jt: nc.vector.tensor_scalar(
            fin_scr, out_moe[:, jt * NO:(jt + 1) * NO], gb_v[:, jt:jt + 1],
            None, OP.mult).then_inc(sems["v"], 1))
        v_f = P.bump("v")
        P.wait("tensor", "v", v_f)
        for tt in range(8):
            bk = 2 + tt % 2
            P.emit("tensor", lambda tt=tt, bk=bk: nc.tensor.transpose(
                B[bk][:, 0:128], fin_scr[:, 128 * tt:128 * (tt + 1)], eye)
                .then_inc(sems["t"], 1))
            t_t = P.bump("t")
            P.wait("vector", "t", t_t)
            dst = x2_sb[:, tt * D + jt * 128: tt * D + jt * 128 + 128]
            P.emit("vector", lambda dst=dst, bk=bk: nc.vector.tensor_tensor(
                dst, B[bk][:, 0:128], dst, OP.add).then_inc(sems["v"], 1))
            v_a = P.bump("v")
            P.wait("tensor", "v", v_a)
    v_y = P.cnt["v"]
    P.wait("sync", "v", v_y)
    for t in range(8):
        P.emit("sync", lambda t=t: nc.sync.dma_start(
            out=y_d[128 * t:128 * (t + 1), :], in_=x2_sb[:, t * D:(t + 1) * D])
            .then_inc(sems["st"], 16))
        P.bump("st", 16)

    _finalize(nc, P, sems, es)
    return nc


def _finalize(nc, P, sems, es):
    for semn in ("st", "tap", "g", "goa", "gob", "xa", "xb"):
        if P.cnt[semn]:
            P.wait("sync", semn, P.cnt[semn])
    with nc.Block() as block:
        @block.tensor
        def _(eng):
            for f in P.q["tensor"]:
                f()

        @block.vector
        def _(eng):
            for f in P.q["vector"]:
                f()

        @block.scalar
        def _(eng):
            for f in P.q["scalar"]:
                f()

        @block.sync
        def _(eng):
            for f in P.q["sync"]:
                f()

        @block.gpsimd
        def _(eng):
            for f in P.q["gpsimd"]:
                f()
    es.close()


def _rot_idx(qi):
    own = list(range(4 * qi, 4 * qi + 4))
    rest = [c for c in range(16) if c not in own]
    idx = np.concatenate([np.arange(c * CHK, (c + 1) * CHK) for c in own + rest])
    return idx


def make_in_maps(inputs):
    x = np.ascontiguousarray(np.asarray(inputs["L0_attn_hidden_states"]))
    eye = np.eye(128, dtype=np.float32)
    eyeb = np.eye(128, dtype=ml_dtypes.bfloat16)
    shared = {
        "wq": np.ascontiguousarray(inputs["L0_attn_q_weight"]),
        "wk": np.ascontiguousarray(inputs["L0_attn_k_weight"]),
        "wv": np.ascontiguousarray(inputs["L0_attn_v_weight"]),
        "wo": np.ascontiguousarray(inputs["L0_attn_o_weight"]),
        "bq": np.ascontiguousarray(inputs["L0_attn_q_bias"]),
        "bk": np.ascontiguousarray(inputs["L0_attn_k_bias"]),
        "bv": np.ascontiguousarray(inputs["L0_attn_v_bias"]),
        "bo": np.ascontiguousarray(inputs["L0_attn_o_bias"]),
        "gate": np.ascontiguousarray(inputs["L0_ffn_gate"]),
        "ebias": np.ascontiguousarray(inputs["L0_ffn_expert_bias"]),
        "w1": np.ascontiguousarray(inputs["L0_ffn_w1"]),
        "w2": np.ascontiguousarray(inputs["L0_ffn_w2"]),
        "w3": np.ascontiguousarray(inputs["L0_ffn_w3"]),
        "sw1": np.ascontiguousarray(inputs["L0_ffn_shared_w1"]),
        "sw2": np.ascontiguousarray(inputs["L0_ffn_shared_w2"]),
        "sw3": np.ascontiguousarray(inputs["L0_ffn_shared_w3"]),
        "wmod": np.ascontiguousarray(inputs["L0_wmod"]),
        "eyef": eye, "eyeb": eyeb,
    }
    cfull = np.asarray(inputs["L0_c"])
    in_maps = []
    for c in range(8):
        b, qi = c // 4, c % 4
        m = dict(shared)
        m["x"] = np.ascontiguousarray(x[b][_rot_idx(qi)])
        m["cvec"] = np.ascontiguousarray(cfull[b])
        in_maps.append(m)
    return in_maps


def kernel(**inputs):
    nc = build("full")
    in_maps = make_in_maps(inputs)
    res = run_bass_kernel_spmd(nc, in_maps, core_ids=list(range(8)))
    out = np.zeros((2, 4096, 2048), dtype=np.float32)
    for c in range(8):
        b, qi = c // 4, c % 4
        out[b, qi * NO:(qi + 1) * NO] = np.asarray(res.results[c]["y"])
    return out


# version 06
# speedup vs baseline: 1.4055x; 1.0147x over previous
import numpy as np
import ml_dtypes
from contextlib import ExitStack

import concourse.bass as bass
import concourse.mybir as mybir
from concourse.alu_op_type import AluOpType as OP
from concourse.bass_utils import run_bass_kernel_spmd

F32 = mybir.dt.float32
BF16 = mybir.dt.bfloat16
AF = mybir.ActivationFunctionType
AX = mybir.AxisListType

D = 2048
NT = 4096
NO = 1024
CHK = 256
NCH = 16
NCH_OWN = 4
EPS = 1e-5
C7 = 1.0 / 7.0
INV2048 = 1.0 / 2048.0

ENGINES = ("tensor", "vector", "scalar", "sync", "gpsimd")
KB = 1024


class Prog:
    def __init__(self, nc, sems):
        self.nc = nc
        self.S = sems
        self.q = {e: [] for e in ENGINES}
        self.cnt = {k: 0 for k in sems}
        self.autogap = False
        self.gap_fn = None

    def emit(self, eng, fn):
        self.q[eng].append(fn)
        if eng == "vector" and self.gap_fn is not None:
            self.q[eng].append(self.gap_fn)

    def bump(self, sem, by=1):
        self.cnt[sem] += by
        return self.cnt[sem]

    def wait(self, eng, sem, val):
        nc, h = self.nc, self.S[sem]
        self.emit(eng, lambda: getattr(nc, eng).wait_ge(h, val))


def build(upto="full", taps=()):
    nc = bass.Bass("TRN2", target_bir_lowering=False)

    def dri(n, s, dt=F32):
        return nc.dram_tensor(n, s, dt, kind="ExternalInput").ap()

    x_d = dri("x", [NT, D])
    w_qkvo = {"q": dri("wq", [D, D]), "k": dri("wk", [D, D]),
              "v": dri("wv", [D, D]), "o": dri("wo", [D, D])}
    b_qkvo = {"q": dri("bq", [D]), "k": dri("bk", [D]),
              "v": dri("bv", [D]), "o": dri("bo", [D])}
    gate_d = dri("gate", [8, D])
    ebias_d = dri("ebias", [8])
    w1_d = dri("w1", [8, 1024, D])
    w2_d = dri("w2", [8, D, 1024])
    w3_d = dri("w3", [8, 1024, D])
    sw1_d = dri("sw1", [1024, D])
    sw2_d = dri("sw2", [D, 1024])
    sw3_d = dri("sw3", [1024, D])
    sw1t_d = dri("sw1t", [D, 1024])
    sw2t_d = dri("sw2t", [1024, D])
    sw3t_d = dri("sw3t", [D, 1024])
    cvec_d = dri("cvec", [D])
    wmod_d = dri("wmod", [D, 6 * D])
    eye_d = dri("eyef", [128, 128])
    eyeb_d = nc.dram_tensor("eyeb", [128, 128], BF16, kind="ExternalInput").ap()

    y_d = nc.dram_tensor("y", [NO, D], F32, kind="ExternalOutput").ap()

    K_dram = nc.dram_tensor("K_dram", [D, NT], F32).ap()
    Vt_dram = nc.dram_tensor("Vt_dram", [NT, D], F32).ap()
    Q_dram = nc.dram_tensor("Q_dram", [D, NO], F32).ap()
    oe_dram = nc.dram_tensor("oe_dram", [8, D, NO], F32).ap()
    x2_dram = nc.dram_tensor("x2_dram", [NO, D], F32).ap()

    tap_d = {}
    for tname, tshape, tdt in taps:
        tap_d[tname] = nc.dram_tensor("tap_" + tname, list(tshape), tdt,
                                      kind="ExternalOutput").ap()

    AR = 192 * KB
    arena = nc.alloc_sbuf_tensor("arena", [128, AR], mybir.dt.uint8)
    base = nc.lookup_mloc(arena).addr

    def at(name, shape, dt, off):
        assert off + shape[1] * (4 if dt == F32 else 2 if dt == BF16 else 1) <= AR
        return nc.alloc_sbuf_tensor_at(name, shape, dt, offset=base + off).ap()

    R1, R2, R3 = 0, 64 * KB, 128 * KB
    Ip = at("Ip", [128, 16 * NT], BF16, R1)
    ctx_fm = at("ctx_fm", [128, 16 * NO], F32, R1)
    x2_sb = at("x2", [128, 8 * D], F32, R1)
    xb_sb = at("xb", [128, 16 * NO], BF16, R1)
    wexp = at("wexp", [128, 16 * KB], BF16, R1 + 32 * KB)
    swt2 = [at("swtA", [128, D], F32, R1),
            at("swtB", [128, D], F32, R1 + 8 * KB)]
    s1s_sl = at("s1s", [128, 1024], F32, R1 + 16 * KB)
    hh_s = at("hhs", [128, 8 * KB], F32, R1 + 20 * KB)
    q4 = [at(f"q4_{i}", [128, 1024], F32, R2 + i * 4 * KB) for i in range(2)]
    kcb = [at(f"kcb{i}", [128, 256], F32, R2 + 37 * KB + i * KB) for i in range(8)]
    v65 = [at(f"v65{i}", [128, 260], F32, R2 + 45 * KB + i * 1056) for i in range(8)]
    probs = [at(f"prb{i}", [128, 512], F32, R2 + 13 * KB + i * 2 * KB)
             for i in range(8)]
    rsum65 = at("rsum65", [65, 4 * 512], F32, R2 + 29 * KB)
    rsum0 = at("rsum0", [1, 4 * 512], F32, R2 + 54 * KB)
    rq0 = rsum0
    rsb = at("rsb", [64, 512], F32, R2 + 8 * KB)
    ostg = at("ostg", [64, 512], F32, R2 + 10 * KB)
    I2 = at("I2", [128, 16 * NO], BF16, R2)
    pd_scr = at("pd_scr", [128, NO], F32, R2 + 32 * KB)
    pd_scr2 = at("pd_scr2", [128, NO], F32, R2 + 36 * KB)
    h2_fm = at("h2f", [128, 16 * NO], F32, R2)
    oe_buf = [at(f"oeb{i}", [128, NO], F32, R2 + i * 4 * KB) for i in range(2)]
    fin_scr = at("fin_scr", [128, NO], F32, R2 + 8 * KB)
    wcol = [at(f"wcol{i}", [128, D], F32, R3 + i * 8 * KB) for i in range(2)]
    qw_buf = at("qw_buf", [128, D], F32, R3 + 8 * KB)
    iwtT = at("iwtT", [128, D], BF16, R3 + 24 * KB)
    xt_buf = [at(f"xt{i}", [128, D], F32, R3 + 16 * KB + i * 8 * KB) for i in range(2)]
    qchunk = at("qchunk", [128, 16 * CHK], F32, R3 + 16 * KB)
    hfm = at("hfm", [128, 16 * CHK], F32, R3 + 32 * KB)
    lad_thr = at("lad_thr", [128, D], F32, R3 + 44 * KB)
    lad_bb = at("lad_bb", [128, D], F32, R3 + 52 * KB)
    scr2k = at("scr2k", [128, D], F32, R3 + 52 * KB)
    wts_f = at("wts_f", [128, D], F32, R3 + 16 * KB)
    iwt_sl = at("iwt", [128, D], BF16, R3 + 24 * KB)
    krow = at("krow", [128, NT], F32, R3 + 28 * KB)
    mw_sl2 = [at("mw_sl", [128, D], F32, R3 + 36 * KB),
              at("mw_slb", [128, D], F32, R3 + 44 * KB)]
    s1_sb = at("s1", [128, 8 * KB], BF16, R3)
    hh_sb = at("hh", [128, 8 * KB], BF16, R3 + 16 * KB)
    oestg = at("oestg", [128, NO], F32, R3 + 32 * KB)
    out_moe = at("om", [128, 16 * NO], F32, R3)

    def sb(n, s, dt=F32):
        return nc.alloc_sbuf_tensor(n, s, dt).ap()

    eye = sb("eye", [128, 128])
    eye16 = sb("eye16", [128, 128], BF16)
    ones1 = sb("ones1", [1, 128])
    cvec = sb("cvecs", [128, 16])
    scv = sb("scv", [128, 16])
    modv = sb("modv", [128, 96])
    ca_v = sb("cav", [128, 16])
    ga_v = sb("gav", [128, 16])
    cb_v = sb("cbv", [128, 16])
    gb_v = sb("gbv", [128, 16])
    t5a = sb("t5a", [128, 16]); t5b = sb("t5b", [128, 16]); t5c = sb("t5c", [128, 16])
    chb = sb("chb", [128, 16 * NCH])
    tm896 = sb("tm896", [128, NCH])
    chb2 = sb("chb2", [128, 16 * 16])
    tm896o = sb("tm896o", [128, NCH_OWN])
    mxb = sb("mxb", [128, 16]); mnb = sb("mnb", [128, 16])
    cmaxb = sb("cmaxb", [128, 16]); p2b = sb("p2b", [128, 16])
    rscaleb = sb("rscaleb", [128, 16])
    tmcol = sb("tmcol", [128, 1]); tmrow = sb("tmrow", [1, 128])
    tmsc = sb("tmsc", [1, 1]); tmb = sb("tmb", [128, 1])
    thrj = sb("thrj", [128, 1]); tm7b = sb("tm7b", [128, 1])
    lnsm = sb("lnsm", [128, 1]); lnm = sb("lnm", [128, 1])
    lnv2 = sb("lnv2", [128, 1]); lna = sb("lna", [128, 1])
    lnsq = sb("lnsq", [128, 1]); lnr0 = sb("lnr0", [128, 1])
    lnt = sb("lnt", [128, 1]); lnr = sb("lnr", [128, 1])
    swv = sb("swv", [128, 1]); rswv = sb("rswv", [128, 1])
    swcv = sb("swcv", [128, 16]); cbbv = sb("cbbv", [128, 16])
    bias_sl = sb("bias_sl", [128, 1])
    gate_sb = at("gate_sb", [8, D], F32, R3 + 44 * KB)
    gtT = sb("gtT", [128, 16 * 8])
    scoresT = at("scoresT", [8, 1024], F32, R3 + 52 * KB)
    sc_tok = sb("sc_tok", [128, 8 * 8])
    bs_tok = sb("bs_tok", [128, 8])
    ebias_r = sb("ebias_r", [1, 8])
    ebias_b = sb("ebias_b", [128, 8])
    cnt1 = sb("cnt1", [128, 1]); cnt2 = sb("cnt2", [128, 1])
    cmp8 = sb("cmp8", [128, 8])
    selc = sb("selc", [128, 1])
    wt_tok = sb("wt_tok", [128, 8 * 8])
    wt_fm = at("wt_fm", [8, 1024], F32, R3 + 56 * KB)
    wtrow0 = at("wtrow0", [1, 1024], F32, R3 + 60 * KB)
    h1b_st = sb("h1b_st", [128, 1024], BF16)
    h3b_st = sb("h3b_st", [128, 1024], BF16)

    psum = nc.alloc_psum_tensor("psum", [128, 4096], F32).ap()
    B = [psum[:, 512 * i: 512 * (i + 1)] for i in range(8)]
    psum16 = psum.bitcast(BF16)
    B16 = [psum16[:, 1024 * i: 1024 * (i + 1)] for i in range(8)]

    vg_a = sb("vg_a", [128, 64])
    vg_b = sb("vg_b", [128, 64])

    sem_names = ["c0", "wa", "wb", "bs", "xa", "xb", "ka", "kb", "qa", "qb",
                 "gg", "sa", "st", "t", "v", "a", "g", "gw", "goa", "gob", "wt", "mw",
                 "k0", "k1", "k2", "k3", "k4", "k5", "k6", "k7", "tap"]
    es = ExitStack()
    sems = {n: es.enter_context(nc.semaphore(n)) for n in sem_names}
    P = Prog(nc, sems)
    P.gap_fn = lambda: nc.vector.drain()

    def tap(name, src_ap):
        if name not in tap_d:
            return
        dst = tap_d[name]
        if P.cnt["tap"]:
            P.wait("gpsimd", "tap", P.cnt["tap"])
        P.emit("gpsimd", lambda: nc.gpsimd.dma_start(out=dst, in_=src_ap)
               .then_inc(sems["tap"], 16))
        c = P.bump("tap", 16)
        P.wait("vector", "tap", c)
        P.wait("scalar", "tap", c)

    def tap_wait(eng_sem_pairs):
        for semn, val in eng_sem_pairs:
            if tap_d:
                P.wait("gpsimd", semn, val)

    def emit_ladder(buf, width):
        v = nc.vector
        thr, bb = lad_thr[:, 0:width], lad_bb[:, 0:width]
        P.emit("vector", lambda: v.tensor_scalar(bb, buf, -0.5, None, OP.is_ge))
        P.emit("vector", lambda: v.tensor_scalar(thr, bb, 8.0, -4.5, OP.mult, OP.add))
        P.emit("vector", lambda: v.tensor_tensor(bb, buf, thr, OP.is_ge))
        P.emit("vector", lambda: v.scalar_tensor_tensor(thr, bb, 4.0, thr, OP.mult, OP.add))
        P.emit("vector", lambda: v.scalar_tensor_tensor(bb, thr, -2.0, buf, OP.add, OP.is_le))
        P.emit("vector", lambda: v.scalar_tensor_tensor(thr, bb, 2.0, thr, OP.mult, OP.add))
        P.emit("vector", lambda: v.scalar_tensor_tensor(bb, thr, -3.0, buf, OP.add, OP.is_le))
        P.emit("vector", lambda: v.scalar_tensor_tensor(buf, thr, -3.5, bb, OP.add, OP.add))

    for dst, src in [(eye, eye_d), (eye16, eyeb_d)]:
        P.emit("sync", lambda dst=dst, src=src:
               nc.sync.dma_start(out=dst, in_=src).then_inc(sems["c0"], 16))
        P.bump("c0", 16)
    P.emit("vector", lambda: nc.vector.memset(ones1, 1.0)
           .then_inc(sems["v"], 1))
    v_const = P.bump("v")

    P.emit("sync", lambda: nc.sync.dma_start(
        out=cvec, in_=cvec_d.rearrange("(j p) -> p j", p=128),
        allow_slow_non_contiguous=True).then_inc(sems["c0"], 16))
    v_ld = P.bump("c0", 16)
    P.wait("scalar", "c0", v_ld)
    P.emit("scalar", lambda: nc.scalar.preload_act_table(AF.Silu))
    P.emit("scalar", lambda: nc.scalar.activation(scv, cvec, AF.Silu)
           .then_inc(sems["a"], 1))
    v_scv = P.bump("a")

    mod_mm_done = {}
    for m in range(96):
        buf = wcol[m % 2]
        wsem = "wa" if m % 2 == 0 else "wb"
        src = wmod_d[:, 128 * m:128 * (m + 1)].rearrange("(k p) j -> p k j", p=128)
        dst3 = buf.rearrange("p (k j) -> p k j", j=128)
        if m >= 2:
            P.wait("sync", "t", mod_mm_done[m - 2])
        P.emit("sync", lambda dst3=dst3, src=src, wsem=wsem:
               nc.sync.dma_start(out=dst3, in_=src).then_inc(sems[wsem], 16))
        v_ld = P.bump(wsem, 16)
        P.wait("tensor", wsem, v_ld)
        if m == 0:
            P.wait("tensor", "a", v_scv)
        bufr = buf.rearrange("p (k j) -> p k j", j=128)
        for kc in range(16):
            lhsT, rhs = bufr[:, kc], scv[:, kc:kc + 1]
            if kc == 15:
                P.emit("tensor", lambda lhsT=lhsT, rhs=rhs, m=m:
                       nc.tensor.matmul(B[7][:, m:m + 1], lhsT, rhs,
                                        start=False, stop=True).then_inc(sems["t"], 1))
                mod_mm_done[m] = P.bump("t")
            else:
                P.emit("tensor", lambda lhsT=lhsT, rhs=rhs, m=m, kc=kc:
                       nc.tensor.matmul(B[7][:, m:m + 1], lhsT, rhs,
                                        start=(kc == 0), stop=False))
    P.wait("vector", "t", mod_mm_done[31])
    P.emit("vector", lambda: nc.vector.tensor_copy(modv[:, 0:16], B[7][:, 0:16]))
    P.emit("vector", lambda: nc.vector.tensor_scalar(
        ca_v, B[7][:, 16:32], 1.0, None, OP.add).then_inc(sems["v"], 1))
    v_mod1 = P.bump("v")

    def emit_clamp(src_cols, dst):
        v = nc.vector
        P.autogap = True
        P.emit("vector", lambda: v.tensor_scalar(t5a, src_cols, 0.25, None, OP.max))
        P.emit("vector", lambda: v.tensor_scalar(t5b, src_cols, -0.25, None, OP.min))
        P.emit("vector", lambda: v.tensor_scalar(t5c, src_cols, 0.0, None, OP.is_ge))
        P.emit("vector", lambda: v.tensor_tensor(t5a, t5a, t5b, OP.subtract))
        P.emit("vector", lambda: v.tensor_tensor(t5a, t5c, t5a, OP.mult))
        P.emit("vector", lambda: v.tensor_tensor(dst, t5a, t5b, OP.add))
        P.autogap = False

    def emit_mod_part2():
        P.wait("vector", "t", mod_mm_done[95])
        P.emit("vector", lambda: nc.vector.tensor_copy(modv[:, 16:96], B[7][:, 16:96]))
        emit_clamp(modv[:, 32:48], ga_v)
        P.emit("vector", lambda: nc.vector.tensor_scalar(
            cb_v, modv[:, 64:80], 1.0, None, OP.add))
        emit_clamp(modv[:, 80:96], gb_v)
        P.emit("vector", lambda: nc.vector.engine_nop().then_inc(sems["v"], 1))
        return P.bump("v")

    if upto == "s1":
        v_mod2 = emit_mod_part2()
        if tap_d:
            tap_wait([("v", v_mod2)])
            tap("modv", modv)
            tap("cav", ca_v)
            tap("gav", ga_v)
            tap("cbv", cb_v)
            tap("gbv", gb_v)
        _finalize(nc, P, sems, es)
        return nc

    P.emit("scalar", lambda: nc.scalar.preload_act_table(AF.Sqrt))


    tfm_state = {"bank_done": {}}

    def emit_tok2fm(xt, dst_fn, sc_v, bi_v, half_or_t, blk):
        for g in range(4):
            pair = g % 2
            bk0 = 2 * pair
            key = (blk, pair)
            prev = tfm_state["bank_done"].get(pair)
            if prev is not None:
                P.wait("tensor", prev[0], prev[1])
            for i, pc in enumerate(range(4 * g, 4 * g + 4)):
                bank = B[bk0 + i // 2]
                col = 128 * (i % 2)
                if i == 3:
                    P.emit("tensor", lambda pc=pc, bank=bank, col=col, xt=xt:
                           nc.tensor.transpose(bank[:, col:col + 128],
                                               xt[:, 128 * pc:128 * (pc + 1)], eye)
                           .then_inc(sems["t"], 1))
                    t_tr = P.bump("t")
                else:
                    P.emit("tensor", lambda pc=pc, bank=bank, col=col, xt=xt:
                           nc.tensor.transpose(bank[:, col:col + 128],
                                               xt[:, 128 * pc:128 * (pc + 1)], eye))
            if pair == 0:
                P.wait("vector", "t", t_tr)
                for i, pc in enumerate(range(4 * g, 4 * g + 4)):
                    bank = B[bk0 + i // 2]
                    col = 128 * (i % 2)
                    dst = dst_fn(pc)
                    fin = i == 3
                    if fin:
                        P.emit("vector", lambda pc=pc, bank=bank, col=col, dst=dst:
                               nc.vector.tensor_scalar(
                                   dst, bank[:, col:col + 128], sc_v[:, pc:pc + 1],
                                   bi_v[:, pc:pc + 1], OP.mult, OP.add)
                               .then_inc(sems["v"], 1))
                    else:
                        P.emit("vector", lambda pc=pc, bank=bank, col=col, dst=dst:
                               nc.vector.tensor_scalar(
                                   dst, bank[:, col:col + 128], sc_v[:, pc:pc + 1],
                                   bi_v[:, pc:pc + 1], OP.mult, OP.add))
                tfm_state["bank_done"][pair] = ("v", P.bump("v"))
            else:
                P.wait("scalar", "t", t_tr)
                for i, pc in enumerate(range(4 * g, 4 * g + 4)):
                    bank = B[bk0 + i // 2]
                    col = 128 * (i % 2)
                    dst = dst_fn(pc)
                    fin = i == 3
                    if fin:
                        P.emit("scalar", lambda pc=pc, bank=bank, col=col, dst=dst:
                               nc.scalar.activation(
                                   dst, bank[:, col:col + 128], AF.Identity,
                                   bias=bi_v[:, pc:pc + 1], scale=sc_v[:, pc:pc + 1])
                               .then_inc(sems["a"], 1))
                    else:
                        P.emit("scalar", lambda pc=pc, bank=bank, col=col, dst=dst:
                               nc.scalar.activation(
                                   dst, bank[:, col:col + 128], AF.Identity,
                                   bias=bi_v[:, pc:pc + 1], scale=sc_v[:, pc:pc + 1]))
                tfm_state["bank_done"][pair] = ("a", P.bump("a"))
        return tfm_state["bank_done"][0][1], tfm_state["bank_done"][1][1]

    def emit_ln_tile(xin):
        v = nc.vector
        P.autogap = True
        P.emit("vector", lambda: v.tensor_reduce(lnsm, xin, axis=AX.X, op=OP.add))
        P.emit("vector", lambda: v.tensor_scalar(lnm, lnsm, INV2048, None, OP.mult))
        P.emit("vector", lambda: v.tensor_scalar(xin, xin, lnm, None, OP.subtract))
        P.emit("vector", lambda: v.tensor_tensor(scr2k, xin, xin, OP.mult))
        P.emit("vector", lambda: v.tensor_reduce(lnv2, scr2k, axis=AX.X, op=OP.add))
        P.emit("vector", lambda: v.tensor_scalar(
            lna, lnv2, INV2048, EPS, OP.mult, OP.add).then_inc(sems["v"], 1))
        vv = P.bump("v")
        P.wait("scalar", "v", vv)
        P.emit("scalar", lambda: nc.scalar.activation(lnsq, lna, AF.Sqrt)
               .then_inc(sems["a"], 1))
        aa = P.bump("a")
        P.wait("vector", "a", aa)
        P.emit("vector", lambda: v.reciprocal(lnr0, lnsq))
        P.emit("vector", lambda: v.tensor_tensor(lnt, lnr0, lnr0, OP.mult))
        P.emit("vector", lambda: v.tensor_tensor(lnt, lnt, lna, OP.mult))
        P.emit("vector", lambda: v.tensor_scalar(lnt, lnt, -0.5, 1.5, OP.mult, OP.add))
        P.emit("vector", lambda: v.tensor_tensor(lnr, lnr0, lnt, OP.mult))
        P.emit("vector", lambda: v.tensor_scalar(xin, xin, lnr, None, OP.mult))
        P.autogap = False

    def emit_quant_chunk(src_fm, chb_dst, tm_dst, ip_dst, src3_fn):
        v = nc.vector
        P.autogap = True
        src3 = src3_fn()
        P.emit("vector", lambda src3=src3: v.tensor_reduce(
            mxb, src3, axis=AX.X, op=OP.max))
        P.emit("vector", lambda src3=src3: v.tensor_reduce(
            mnb, src3, axis=AX.X, op=OP.min))
        P.emit("vector", lambda: v.tensor_tensor(t5a, mxb, mnb, OP.add))
        P.emit("vector", lambda: v.tensor_scalar(t5b, t5a, 0.5, None, OP.mult))
        for pc in range(16):
            d = chb_dst(pc)
            P.emit("vector", lambda d=d, pc=pc: v.tensor_copy(d, t5b[:, pc:pc + 1]))
        P.emit("vector", lambda: v.tensor_tensor(t5a, mxb, t5b, OP.subtract))
        P.emit("vector", lambda: v.tensor_tensor(t5c, t5b, mnb, OP.subtract))
        P.emit("vector", lambda: v.tensor_tensor(cmaxb, t5a, t5c, OP.max))
        P.emit("vector", lambda: v.tensor_reduce(
            tmcol, cmaxb, axis=AX.X, op=OP.max).then_inc(sems["v"], 1))
        vv = P.bump("v")
        P.wait("tensor", "v", vv)
        P.emit("tensor", lambda: nc.tensor.transpose(B[6][0:1, 0:128], tmcol, eye)
               .then_inc(sems["t"], 1))
        tt_ = P.bump("t")
        P.wait("vector", "t", tt_)
        P.emit("vector", lambda: v.tensor_copy(tmrow, B[6][0:1, 0:128]))
        P.emit("vector", lambda: v.tensor_reduce(
            tmsc, tmrow, axis=AX.X, op=OP.max).then_inc(sems["v"], 1))
        vv = P.bump("v")
        P.wait("tensor", "v", vv)
        P.emit("tensor", lambda: nc.tensor.matmul(
            B[6][:, 256:257], ones1, tmsc, start=True, stop=True)
            .then_inc(sems["t"], 1))
        tt_ = P.bump("t")
        P.wait("vector", "t", tt_)
        P.emit("vector", lambda: v.tensor_copy(tmb, B[6][:, 256:257]))
        P.emit("vector", lambda: v.tensor_scalar(tm7b, tmb, C7, None, OP.mult))
        P.emit("vector", lambda: v.tensor_scalar(tm_dst, tm7b, 2.0 ** -7, None, OP.mult))
        for j in range(7):
            P.emit("vector", lambda j=j: v.tensor_scalar(
                thrj, tmb, 2.0 ** (j - 7), None, OP.mult))
            P.emit("vector", lambda: v.tensor_scalar(t5a, cmaxb, thrj, None, OP.is_gt))
            if j == 0:
                P.emit("vector", lambda: v.tensor_scalar(p2b, t5a, 1.0, None, OP.add))
            else:
                P.emit("vector", lambda: v.tensor_scalar(t5a, t5a, 1.0, None, OP.add))
                P.emit("vector", lambda: v.tensor_tensor(p2b, p2b, t5a, OP.mult))
        P.emit("vector", lambda: v.tensor_scalar(t5a, p2b, tm7b, 2.0 ** -7,
                                                 OP.mult, OP.mult))
        P.emit("vector", lambda: v.tensor_scalar(t5a, t5a, 1e-9, None, OP.max))
        P.emit("vector", lambda: v.reciprocal(rscaleb, t5a))
        P.autogap = False
        for pc in range(16):
            s = src_fm(pc)
            cb_ = chb_dst(pc)
            P.emit("vector", lambda s=s, cb_=cb_, pc=pc: v.tensor_scalar(
                qchunk[:, pc * CHK:(pc + 1) * CHK], s, cb_, rscaleb[:, pc:pc + 1],
                OP.subtract, OP.mult))
        emit_ladder(qchunk[:, 0:2048], 2048)
        emit_ladder(qchunk[:, 2048:4096], 2048)
        for pc in range(16):
            d = ip_dst(pc)
            P.emit("vector", lambda d=d, pc=pc: v.tensor_scalar(
                d, qchunk[:, pc * CHK:(pc + 1) * CHK], p2b[:, pc:pc + 1], None,
                OP.mult))
        P.emit("vector", lambda: nc.vector.engine_nop().then_inc(sems["v"], 1))
        return P.bump("v")

    v_qdone = {}
    xtr_done = {}
    for c in range(NCH):
        for half in range(2):
            tix = 2 * c + half
            buf = xt_buf[half]
            xsem = "xa" if half == 0 else "xb"
            if c >= 1 and half == 0:
                P.wait("scalar", "v", v_qdone[c - 1])
            P.emit("scalar", lambda buf=buf, tix=tix, xsem=xsem:
                   nc.scalar.dma_start(
                       out=buf, in_=x_d[128 * tix:128 * (tix + 1), :])
                   .then_inc(sems[xsem], 16))
            v_x = P.bump(xsem, 16)
            P.wait("vector", xsem, v_x)
            emit_ln_tile(buf)
            P.emit("vector", lambda: nc.vector.engine_nop().then_inc(sems["v"], 1))
            v_xh = P.bump("v")
            P.wait("tensor", "v", v_xh)
            if c == 0 and half == 0:
                P.wait("tensor", "v", v_mod1)
            vv, aa = emit_tok2fm(
                buf,
                lambda pc, half=half: hfm[:, pc * CHK + half * 128: pc * CHK + half * 128 + 128],
                ca_v, modv, half, "s2")
            xtr_done[tix] = P.cnt["t"]
        v_h = P.cnt["v"]
        P.wait("vector", "a", P.cnt["a"])
        v_qdone[c] = emit_quant_chunk(
            src_fm=lambda pc: hfm[:, pc * CHK:(pc + 1) * CHK],
            chb_dst=lambda pc, c=c: chb[:, pc * 16 + c: pc * 16 + c + 1],
            tm_dst=tm896[:, c:c + 1],
            ip_dst=lambda pc, c=c: Ip[:, pc * NT + c * CHK: pc * NT + (c + 1) * CHK],
            src3_fn=lambda: hfm.rearrange("p (c t) -> p c t", t=CHK),
        )
        if tap_d and c == 0:
            tap_wait([("v", v_qdone[0])])
            tap("hfm0", hfm)
            tap("qchunk0", qchunk)

    v_quant_all = v_qdone[NCH - 1]
    v_mod2 = emit_mod_part2()
    if tap_d:
        tap_wait([("v", v_quant_all), ("v", v_mod2)])
        tap("modv", modv)
        tap("gav", ga_v)
        tap("chb", chb)
        tap("tm896", tm896)
        tap("Ip0", Ip[:, 0:NT])
    if upto == "s2":
        _finalize(nc, P, sems, es)
        return nc

    wstate = {
        "wcol_free_t": None,
        "bias_free_a": None,
        "proj_bank_a": {0: None, 1: None},
        "tr_bank_a": {0: None, 1: None},
        "itr_bank_a": {0: None, 1: None},
        "cbb_bank_a": None,
        "iwt_free_t": None,
        "qw_free_t": None,
        "krow_free": None,
    }

    def emit_wstream(kind, ip_src, ip_blk, tm_src, chb_src, nch, ntok):
        w_da = w_qkvo[kind.lower()]
        b_da = b_qkvo[kind.lower()]
        ntt = ntok // 512
        for jt in range(16):
            wsem = "wa"
            if wstate["wcol_free_t"] is not None:
                P.wait("sync", "t", wstate["wcol_free_t"])
            if wstate.get("wcol_free_v") is not None:
                P.wait("sync", "v", wstate["wcol_free_v"])
            P.emit("sync", lambda jt=jt, w_da=w_da: nc.sync.dma_start(
                out=wcol[0], in_=w_da[128 * jt:128 * (jt + 1), :])
                .then_inc(sems["wa"], 16))
            v_w = P.bump("wa", 16)
            if wstate["bias_free_a"] is not None:
                P.wait("sync", "a", wstate["bias_free_a"])
            P.emit("sync", lambda jt=jt, b_da=b_da: nc.sync.dma_start(
                out=bias_sl,
                in_=b_da.rearrange("(a p) -> p a", p=128)[:, jt:jt + 1],
                allow_slow_non_contiguous=True).then_inc(sems["bs"], 16))
            v_b = P.bump("bs", 16)
            v = nc.vector
            wbuf = wcol[0]
            P.wait("tensor", "wa", v_w)
            if wstate.get("wts_free_t") is not None:
                P.wait("scalar", "t", wstate["wts_free_t"])
            for g in range(4):
                bk = 4 + g % 2
                if wstate["tr_bank_a"][g % 2] is not None:
                    P.wait("tensor", "a", wstate["tr_bank_a"][g % 2])
                for kc in range(4 * g, 4 * g + 4):
                    fin = kc % 4 == 3
                    if fin:
                        P.emit("tensor", lambda kc=kc, bk=bk, wbuf=wbuf:
                               nc.tensor.transpose(
                                   B[bk][:, 128 * (kc % 4):128 * (kc % 4) + 128],
                                   wbuf[:, 128 * kc:128 * (kc + 1)], eye)
                               .then_inc(sems["t"], 1))
                        t_f = P.bump("t")
                    else:
                        P.emit("tensor", lambda kc=kc, bk=bk, wbuf=wbuf:
                               nc.tensor.transpose(
                                   B[bk][:, 128 * (kc % 4):128 * (kc % 4) + 128],
                                   wbuf[:, 128 * kc:128 * (kc + 1)], eye))
                if g == 3:
                    wstate["wcol_free_t"] = t_f
                P.wait("scalar", "t", t_f)
                P.emit("scalar", lambda g=g, bk=bk: nc.scalar.activation(
                    wts_f[:, 512 * g:512 * (g + 1)], B[bk], AF.Identity)
                    .then_inc(sems["a"], 1))
                wstate["tr_bank_a"][g % 2] = P.bump("a")
            a_wts = P.cnt["a"]
            P.wait("tensor", "a", a_wts)
            if wstate["cbb_bank_a"] is not None:
                P.wait("tensor", "v", wstate["cbb_bank_a"])
            for kc in range(16):
                lhsT = wts_f[:, 128 * kc:128 * (kc + 1)]
                rhs = chb_src[:, kc * 16: kc * 16 + nch]
                if kc == 15:
                    P.emit("tensor", lambda lhsT=lhsT, rhs=rhs, nch=nch:
                           nc.tensor.matmul(B[6][:, 0:nch], lhsT, rhs,
                                            start=False, stop=True)
                           .then_inc(sems["t"], 1))
                    t_cbb = P.bump("t")
                    wstate["wts_free_t"] = t_cbb
                else:
                    P.emit("tensor", lambda lhsT=lhsT, rhs=rhs, kc=kc, nch=nch:
                           nc.tensor.matmul(B[6][:, 0:nch], lhsT, rhs,
                                            start=(kc == 0), stop=False))
            P.wait("vector", "wa", v_w)
            P.autogap = True
            P.emit("vector", lambda wbuf=wbuf: v.tensor_reduce(
                swv, wbuf, axis=AX.X, op=OP.max, apply_absolute_value=True))
            P.emit("vector", lambda: v.tensor_scalar(swv, swv, C7, 1e-9, OP.mult, OP.max))
            P.emit("vector", lambda: v.reciprocal(rswv, swv))
            P.autogap = False
            if wstate["iwt_free_t"] is not None:
                P.wait("vector", "t", wstate["iwt_free_t"])
            P.emit("vector", lambda wbuf=wbuf: v.tensor_scalar(
                qw_buf, wbuf, rswv, None, OP.mult))
            thr, bb = lad_thr[:, 0:D], lad_bb[:, 0:D]
            P.emit("vector", lambda: v.tensor_scalar(bb, qw_buf, -0.5, None, OP.is_ge))
            P.emit("vector", lambda: v.tensor_scalar(thr, bb, 8.0, -4.5, OP.mult, OP.add))
            P.emit("vector", lambda: v.tensor_tensor(bb, qw_buf, thr, OP.is_ge))
            P.emit("vector", lambda: v.scalar_tensor_tensor(thr, bb, 4.0, thr, OP.mult, OP.add))
            P.emit("vector", lambda: v.scalar_tensor_tensor(bb, thr, -2.0, qw_buf, OP.add, OP.is_le))
            P.emit("vector", lambda: v.scalar_tensor_tensor(thr, bb, 2.0, thr, OP.mult, OP.add))
            P.emit("vector", lambda: v.scalar_tensor_tensor(bb, thr, -3.0, qw_buf, OP.add, OP.is_le))
            if wstate.get("qw_free_t") is not None:
                P.wait("vector", "t", wstate["qw_free_t"])
            P.emit("vector", lambda: v.scalar_tensor_tensor(qw_buf, thr, -3.5, bb, OP.add, OP.add)
                   .then_inc(sems["v"], 1))
            v_iw = P.bump("v")
            wstate["wcol_free_v"] = v_iw
            P.wait("tensor", "v", v_iw)
            for g in range(4):
                bk = 2 + g % 2
                if wstate["itr_bank_a"][g % 2] is not None:
                    P.wait("tensor", "a", wstate["itr_bank_a"][g % 2])
                for kc in range(4 * g, 4 * g + 4):
                    if kc % 4 == 3:
                        P.emit("tensor", lambda kc=kc, bk=bk:
                               nc.tensor.transpose(
                                   B[bk][:, 128 * (kc % 4):128 * (kc % 4) + 128],
                                   qw_buf[:, 128 * kc:128 * (kc + 1)], eye)
                               .then_inc(sems["t"], 1))
                        t_i = P.bump("t")
                    else:
                        P.emit("tensor", lambda kc=kc, bk=bk:
                               nc.tensor.transpose(
                                   B[bk][:, 128 * (kc % 4):128 * (kc % 4) + 128],
                                   qw_buf[:, 128 * kc:128 * (kc + 1)], eye))
                if g == 3:
                    wstate["qw_free_t"] = t_i
                P.wait("scalar", "t", t_i)
                if wstate["iwt_free_t"] is not None and g == 0:
                    P.wait("scalar", "t", wstate["iwt_free_t"])
                P.emit("scalar", lambda g=g, bk=bk: nc.scalar.activation(
                    iwtT[:, 512 * g:512 * (g + 1)], B[bk], AF.Identity)
                    .then_inc(sems["a"], 1))
                wstate["itr_bank_a"][g % 2] = P.bump("a")
            wt_iw = P.cnt["a"]
            P.wait("vector", "t", t_cbb)
            P.wait("vector", "bs", v_b)
            P.autogap = True
            if kind == "Q":
                P.emit("vector", lambda nch=nch: v.tensor_scalar(
                    cbbv[:, 0:nch], B[6][:, 0:nch], bias_sl, 0.125, OP.add, OP.mult))
                P.emit("vector", lambda nch=nch: v.tensor_scalar(
                    swcv[:, 0:nch], tm_src[:, 0:nch], swv, 0.125, OP.mult, OP.mult))
            else:
                P.emit("vector", lambda nch=nch: v.tensor_scalar(
                    cbbv[:, 0:nch], B[6][:, 0:nch], bias_sl, None, OP.add))
                P.emit("vector", lambda nch=nch: v.tensor_scalar(
                    swcv[:, 0:nch], tm_src[:, 0:nch], swv, None, OP.mult))
            P.autogap = False
            P.emit("vector", lambda: nc.vector.engine_nop().then_inc(sems["v"], 1))
            v_sc = P.bump("v")
            wstate["cbb_bank_a"] = v_sc
            P.wait("tensor", "a", wt_iw)
            P.wait("scalar", "v", v_sc)
            for tt in range(ntt):
                bank = tt % 2
                if wstate["proj_bank_a"][bank] is not None:
                    P.wait("tensor", "a", wstate["proj_bank_a"][bank])
                for kc in range(16):
                    lhsT = iwtT[:, 128 * kc:128 * (kc + 1)]
                    rhs = ip_src[:, kc * ip_blk + tt * 512: kc * ip_blk + tt * 512 + 512]
                    if kc == 15:
                        P.emit("tensor", lambda lhsT=lhsT, rhs=rhs, bank=bank:
                               nc.tensor.matmul(B[bank], lhsT, rhs,
                                                start=False, stop=True)
                               .then_inc(sems["t"], 1))
                        t_p = P.bump("t")
                    else:
                        P.emit("tensor", lambda lhsT=lhsT, rhs=rhs, kc=kc, bank=bank:
                               nc.tensor.matmul(B[bank], lhsT, rhs,
                                                start=(kc == 0), stop=False))
                if tt == ntt - 1:
                    wstate["iwt_free_t"] = t_p
                P.wait("scalar", "t", t_p)
                if wstate["krow_free"] is not None and tt == 0:
                    P.wait("scalar", wstate["krow_free"][0], wstate["krow_free"][1])
                for cc in range(2):
                    ci = tt * 2 + cc
                    dst = krow[:, tt * 512 + cc * 256: tt * 512 + cc * 256 + 256]
                    if cc == 1:
                        P.emit("scalar", lambda dst=dst, cc=cc, ci=ci, bank=bank:
                               nc.scalar.activation(
                                   dst, B[bank][:, cc * 256:(cc + 1) * 256],
                                   AF.Identity, bias=cbbv[:, ci:ci + 1],
                                   scale=swcv[:, ci:ci + 1]).then_inc(sems["a"], 1))
                    else:
                        P.emit("scalar", lambda dst=dst, cc=cc, ci=ci, bank=bank:
                               nc.scalar.activation(
                                   dst, B[bank][:, cc * 256:(cc + 1) * 256],
                                   AF.Identity, bias=cbbv[:, ci:ci + 1],
                                   scale=swcv[:, ci:ci + 1]))
                wstate["proj_bank_a"][bank] = P.bump("a")
            a_row = P.cnt["a"]
            wstate["bias_free_a"] = a_row
            if kind == "K":
                P.wait("sync", "a", a_row)
                P.emit("sync", lambda jt=jt: nc.sync.dma_start(
                    out=K_dram[128 * jt:128 * (jt + 1), :], in_=krow)
                    .then_inc(sems["st"], 16))
                st_c = P.bump("st", 16)
                wstate["krow_free"] = ("st", st_c)
            elif kind == "Q":
                P.wait("sync", "a", a_row)
                P.emit("sync", lambda jt=jt: nc.sync.dma_start(
                    out=Q_dram[128 * jt:128 * (jt + 1), :], in_=krow[:, 0:NO])
                    .then_inc(sems["st"], 16))
                st_c = P.bump("st", 16)
                wstate["krow_free"] = ("st", st_c)
            elif kind == "O":
                _pd_consume(jt, a_row)
            elif kind == "V":
                P.wait("tensor", "a", a_row)
                for kc in range(32):
                    bk = 2 + (kc % 2)
                    P.emit("tensor", lambda kc=kc, bk=bk:
                           nc.tensor.transpose(B[bk][:, 0:128],
                                               krow[:, 128 * kc:128 * (kc + 1)], eye)
                           .then_inc(sems["t"], 1))
                    t_v = P.bump("t")
                    P.wait("vector", "t", t_v)
                    P.emit("vector", lambda kc=kc, bk=bk:
                           v.tensor_copy(krow[:, 128 * kc:128 * (kc + 1)],
                                         B[bk][:, 0:128]).then_inc(sems["v"], 1))
                    v_vt = P.bump("v")
                    P.wait("tensor", "v", v_vt)
                P.wait("sync", "v", P.cnt["v"])
                dstap = Vt_dram.rearrange("(c p) (a j) -> p c a j", p=128, j=128)[:, :, jt]
                srcap = krow.rearrange("p (c j) -> p c j", j=128)
                P.emit("sync", lambda dstap=dstap, srcap=srcap: nc.sync.dma_start(
                    out=dstap, in_=srcap).then_inc(sems["st"], 16))
                st_c = P.bump("st", 16)
                wstate["krow_free"] = ("st", st_c)
        return

    wstate["wcol_free_t"] = mod_mm_done[95]
    P.wait("scalar", "v", v_quant_all)
    P.wait("tensor", "v", v_quant_all)
    emit_wstream("K", Ip, NT, tm896, chb, NCH, NT)
    emit_wstream("V", Ip, NT, tm896, chb, NCH, NT)
    emit_wstream("Q", Ip, NT, tm896, chb, NCH_OWN, NO)
    v_kvq_store = P.cnt["st"]

    if tap_d:
        tap_wait([("st", v_kvq_store)])
        tap("Krow0", K_dram[0:128, :])
        tap("Vt0", Vt_dram[0:128, :])
        tap("Q0", Q_dram[0:128, :])
    if upto == "qkv":
        _finalize(nc, P, sems, es)
        return nc

    P.emit("scalar", lambda: nc.scalar.preload_act_table(AF.Exp))
    P.wait("tensor", "st", v_kvq_store)
    P.wait("sync", "st", v_kvq_store)
    P.wait("vector", "st", v_kvq_store)
    for i in range(8):
        vv = v65[i].rearrange("p (i j) -> p i j", j=65)[:, :, 64:65]
        P.emit("vector", lambda vv=vv: nc.vector.memset(vv, 1.0))
    P.emit("vector", lambda: nc.vector.engine_nop().then_inc(sems["v"], 1))
    v_ones_pc = P.bump("v")

    t_pv = {}
    t_qk = {}
    a_exp = {}
    t_last_qk_of_grp = {}
    v_epi = {}
    for qh in range(2):
        for grp in range(8):
            gi = qh * 8 + grp
            qb = q4[gi % 2]
            qsem = "qa" if gi % 2 == 0 else "qb"
            if gi >= 2:
                P.wait("sync", "t", t_last_qk_of_grp[gi - 2])
            P.emit("sync", lambda qb=qb, qh=qh, grp=grp, qsem=qsem:
                   nc.sync.dma_start(
                       out=qb.rearrange("p (a t) -> p a t", a=2),
                       in_=Q_dram[256 * grp:256 * (grp + 1), 512 * qh:512 * (qh + 1)]
                       .rearrange("(a p) t -> p a t", p=128))
                   .then_inc(sems[qsem], 16))
            v_q = P.bump(qsem, 16)
            for kc in range(32):
                kb, vb = kcb[kc % 8], v65[kc % 8]
                if kc >= 8:
                    P.wait("sync", "t", t_pv[(gi, kc - 8)])
                elif gi >= 1:
                    P.wait("sync", "t", t_pv[(gi - 1, 24 + kc)])
                ksem = f"k{kc % 8}"
                P.emit("sync", lambda kb=kb, grp=grp, kc=kc, ksem=ksem:
                       nc.sync.dma_start(
                           out=kb.rearrange("p (a t) -> p a t", a=2),
                           in_=K_dram[256 * grp:256 * (grp + 1), 128 * kc:128 * (kc + 1)]
                           .rearrange("(a p) t -> p a t", p=128))
                       .then_inc(sems[ksem], 16))
                P.bump(ksem, 16)
                P.emit("sync", lambda vb=vb, grp=grp, kc=kc, ksem=ksem:
                       nc.sync.dma_start(
                           out=vb.rearrange("p (i j) -> p i j", j=65)[:, :, 0:64],
                           in_=Vt_dram[128 * kc:128 * (kc + 1), 256 * grp:256 * (grp + 1)]
                           .rearrange("p (i j) -> p i j", j=64))
                       .then_inc(sems[ksem], 16))
                v_v = P.bump(ksem, 16)
                P.wait("tensor", ksem, v_v)
                if kc == 0:
                    P.wait("tensor", qsem, v_q)
                    if gi == 0:
                        P.wait("tensor", "v", v_ones_pc)
                    if gi >= 1:
                        P.wait("tensor", "v", v_epi[gi - 1])
                if kc >= 1:
                    P.wait("tensor", "a", a_exp[(gi, kc - 1)])
                for pr in range(2):
                    for o in range(2):
                        h_i = 2 * pr + o
                        bank = B[h_i]
                        lhsT = kb[64 * o:64 * o + 64, 128 * pr:128 * (pr + 1)]
                        rhs = qb[64 * o:64 * o + 64, 512 * pr:512 * (pr + 1)]
                        fin = h_i == 3
                        if fin:
                            P.emit("tensor", lambda bank=bank, lhsT=lhsT, rhs=rhs, o=o:
                                   nc.tensor.matmul(bank, lhsT, rhs, start=True,
                                                    stop=True,
                                                    tile_position=(64 * o, 0))
                                   .then_inc(sems["t"], 1))
                            t_qk[(gi, kc)] = P.bump("t")
                        else:
                            P.emit("tensor", lambda bank=bank, lhsT=lhsT, rhs=rhs, o=o:
                                   nc.tensor.matmul(bank, lhsT, rhs, start=True,
                                                    stop=True,
                                                    tile_position=(64 * o, 0)))
                if kc == 31:
                    t_last_qk_of_grp[gi] = t_qk[(gi, kc)]
                if kc >= 1:
                    for h_i in range(4):
                        pb = probs[((kc - 1) % 2) * 4 + h_i]
                        lhsT = v65[(kc - 1) % 8][:, 65 * h_i:65 * h_i + 65]
                        fin = h_i == 3
                        if fin:
                            P.emit("tensor", lambda lhsT=lhsT, pb=pb, h_i=h_i, kc=kc:
                                   nc.tensor.matmul(B[4 + h_i][0:65, :], lhsT, pb,
                                                    start=(kc - 1 == 0), stop=(kc - 1 == 31))
                                   .then_inc(sems["t"], 1))
                            t_pv[(gi, kc - 1)] = P.bump("t")
                        else:
                            P.emit("tensor", lambda lhsT=lhsT, pb=pb, h_i=h_i, kc=kc:
                                   nc.tensor.matmul(B[4 + h_i][0:65, :], lhsT, pb,
                                                    start=(kc - 1 == 0), stop=(kc - 1 == 31)))
                P.wait("scalar", "t", t_qk[(gi, kc)])
                if kc >= 2:
                    P.wait("scalar", "t", t_pv[(gi, kc - 2)])
                for h_i in range(4):
                    pb = probs[(kc % 2) * 4 + h_i]
                    fin = h_i == 3
                    if fin:
                        P.emit("scalar", lambda pb=pb, h_i=h_i: nc.scalar.activation(
                            pb, B[h_i], AF.Exp).then_inc(sems["a"], 1))
                        a_exp[(gi, kc)] = P.bump("a")
                    else:
                        P.emit("scalar", lambda pb=pb, h_i=h_i: nc.scalar.activation(
                            pb, B[h_i], AF.Exp))
            P.wait("tensor", "a", a_exp[(gi, 31)])
            for h_i in range(4):
                pb = probs[(31 % 2) * 4 + h_i]
                lhsT = v65[31 % 8][:, 65 * h_i:65 * h_i + 65]
                if h_i == 3:
                    P.emit("tensor", lambda lhsT=lhsT, pb=pb, h_i=h_i:
                           nc.tensor.matmul(B[4 + h_i][0:65, :], lhsT, pb,
                                            start=False, stop=True)
                           .then_inc(sems["t"], 1))
                    t_pv[(gi, 31)] = P.bump("t")
                else:
                    P.emit("tensor", lambda lhsT=lhsT, pb=pb, h_i=h_i:
                           nc.tensor.matmul(B[4 + h_i][0:65, :], lhsT, pb,
                                            start=False, stop=True))
            P.wait("vector", "t", t_pv[(gi, 31)])
            for h_i in range(4):
                P.emit("vector", lambda h_i=h_i: nc.vector.tensor_copy(
                    rsum65[64:65, 512 * h_i:512 * (h_i + 1)], B[4 + h_i][64:65, :]))
            P.emit("vector", lambda: nc.vector.engine_nop().then_inc(sems["v"], 1))
            v_s4 = P.bump("v")
            P.wait("gpsimd", "v", v_s4)
            P.emit("gpsimd", lambda: nc.gpsimd.dma_start(
                out=rsum0, in_=rsum65[64:65, :]).then_inc(sems["g"], 16))
            g_r = P.bump("g", 16)
            P.wait("vector", "g", g_r)
            P.emit("vector", lambda: nc.vector.reciprocal(rq0, rsum0)
                   .then_inc(sems["v"], 1))
            v_rq = P.bump("v")
            P.wait("tensor", "v", v_rq)
            for h_i in range(4):
                H = 4 * grp + h_i
                pcH, half = H // 2, H % 2
                P.emit("tensor", lambda h_i=h_i: nc.tensor.matmul(
                    B[h_i][0:64, :], ones1[0:1, 0:64],
                    rq0[0:1, 512 * h_i:512 * (h_i + 1)], start=True,
                    stop=True).then_inc(sems["t"], 1))
                t_b = P.bump("t")
                P.wait("vector", "t", t_b)
                P.emit("vector", lambda h_i=h_i: nc.vector.tensor_copy(
                    rsb, B[h_i][0:64, :]))
                if half == 0:
                    dst = ctx_fm[0:64, pcH * NO + qh * 512: pcH * NO + qh * 512 + 512]
                    P.emit("vector", lambda dst=dst, h_i=h_i: nc.vector.tensor_tensor(
                        dst, B[4 + h_i][0:64, :], rsb, OP.mult)
                        .then_inc(sems["v"], 1))
                    P.bump("v")
                else:
                    P.emit("vector", lambda h_i=h_i: nc.vector.tensor_tensor(
                        ostg, B[4 + h_i][0:64, :], rsb, OP.mult)
                        .then_inc(sems["v"], 1))
                    v_o = P.bump("v")
                    P.wait("gpsimd", "v", v_o)
                    dst = ctx_fm[64:128, pcH * NO + qh * 512: pcH * NO + qh * 512 + 512]
                    P.emit("gpsimd", lambda dst=dst: nc.gpsimd.dma_start(
                        out=dst, in_=ostg).then_inc(sems["g"], 16))
                    g_c = P.bump("g", 16)
                    P.wait("vector", "g", g_c)
            P.emit("vector", lambda: nc.vector.engine_nop().then_inc(sems["v"], 1))
            v_epi[gi] = P.bump("v")
    v_attn_done_v = P.cnt["v"]
    g_attn_done = P.cnt["g"]

    if tap_d:
        tap_wait([("v", v_attn_done_v), ("g", g_attn_done)])
        tap("ctx", ctx_fm)
    if upto == "attn":
        _finalize(nc, P, sems, es)
        return nc

    P.wait("vector", "g", g_attn_done)
    for c in range(NCH_OWN):
        v_qdone2 = emit_quant_chunk(
            src_fm=lambda pc, c=c: ctx_fm[:, pc * NO + c * CHK: pc * NO + (c + 1) * CHK],
            chb_dst=lambda pc, c=c: chb2[:, pc * 16 + c: pc * 16 + c + 1],
            tm_dst=tm896o[:, c:c + 1],
            ip_dst=lambda pc, c=c: I2[:, pc * NO + c * CHK: pc * NO + (c + 1) * CHK],
            src3_fn=lambda c=c: ctx_fm.rearrange("p (q t) -> p q t", t=CHK)[:, :, :]
            if False else ctx_fm.rearrange("p (q ch) -> p q ch", q=16)[:, :, c * CHK:(c + 1) * CHK],
        )
    if tap_d:
        tap_wait([("v", v_qdone2)])
        tap("I2", I2[:, 0:NO])
        tap("chb2", chb2)
    P.wait("scalar", "v", v_qdone2)
    for t in range(8):
        P.emit("scalar", lambda t=t: nc.scalar.dma_start(
            out=x2_sb[:, t * D:(t + 1) * D], in_=x_d[128 * t:128 * (t + 1), :])
            .then_inc(sems["xa"], 16))
        P.bump("xa", 16)
    v_x2ld = P.cnt["xa"]
    P.wait("vector", "xa", v_x2ld)
    P.wait("tensor", "xa", v_x2ld)

    def _pd_consume(jt, a_row):
        v = nc.vector
        P.wait("vector", "a", a_row)
        P.emit("vector", lambda jt=jt: v.tensor_scalar(
            pd_scr, krow[:, 0:NO], ga_v[:, jt:jt + 1], None, OP.mult)
            .then_inc(sems["v"], 1))
        v_g = P.bump("v")
        P.wait("tensor", "v", v_g)
        wstate["krow_free"] = ("v", P.cnt["v"])
        for tt in range(8):
            bk = 2 + tt % 2
            P.emit("tensor", lambda tt=tt, bk=bk: nc.tensor.transpose(
                B[bk][:, 0:128], pd_scr[:, 128 * tt:128 * (tt + 1)], eye)
                .then_inc(sems["t"], 1))
            t_t = P.bump("t")
            P.wait("vector", "t", t_t)
            dst = x2_sb[:, tt * D + jt * 128: tt * D + jt * 128 + 128]
            P.emit("vector", lambda dst=dst, bk=bk: v.tensor_tensor(
                dst, B[bk][:, 0:128], dst, OP.add).then_inc(sems["v"], 1))
            v_a2 = P.bump("v")
            P.wait("tensor", "v", v_a2)

    P.wait("scalar", "v", v_qdone2)
    P.wait("tensor", "v", v_qdone2)
    emit_wstream("O", I2, NO, tm896o, chb2, NCH_OWN, NO)
    v_x2_done = P.cnt["v"]
    if tap_d:
        tap_wait([("v", v_x2_done)])
        tap("x2", x2_sb)
    if upto == "x2":
        _finalize(nc, P, sems, es)
        return nc

    P.wait("sync", "v", v_x2_done)
    for t in range(8):
        P.emit("sync", lambda t=t: nc.sync.dma_start(
            out=x2_dram[128 * t:128 * (t + 1), :], in_=x2_sb[:, t * D:(t + 1) * D])
            .then_inc(sems["st"], 16))
        P.bump("st", 16)
    st_x2 = P.cnt["st"]
    P.emit("scalar", lambda: nc.scalar.preload_act_table(AF.Sqrt))
    P.wait("vector", "v", v_x2_done)
    pe_xt_free = [None]
    for t in range(8):
        xt = wcol[0]
        if pe_xt_free[0] is not None:
            P.wait("vector", "t", pe_xt_free[0])
        P.emit("vector", lambda t=t, xt=xt: nc.vector.tensor_copy(
            xt, x2_sb[:, t * D:(t + 1) * D]))
        emit_ln_tile(xt)
        P.emit("vector", lambda: nc.vector.engine_nop().then_inc(sems["v"], 1))
        v_xh = P.bump("v")
        P.wait("tensor", "v", v_xh)
        emit_tok2fm(
            xt,
            lambda pc, t=t: h2_fm[:, pc * NO + t * 128: pc * NO + t * 128 + 128],
            cb_v, modv[:, 48:64], t, "pe")
        pe_xt_free[0] = P.cnt["t"]
    P.wait("vector", "a", P.cnt["a"])
    v_h2 = P.cnt["v"]
    if tap_d:
        tap_wait([("v", v_h2)])
        tap("h2", h2_fm)
    if upto == "h2":
        _finalize(nc, P, sems, es)
        return nc

    P.emit("sync", lambda: nc.sync.dma_start(out=gate_sb, in_=gate_d)
           .then_inc(sems["gg"], 16))
    P.emit("sync", lambda: nc.sync.dma_start(
        out=ebias_r, in_=ebias_d.rearrange("(a e) -> a e", a=1))
        .then_inc(sems["gg"], 16))
    v_g8 = P.bump("gg", 32)
    P.wait("tensor", "gg", v_g8)
    P.wait("tensor", "v", v_h2)
    P.wait("tensor", "a", P.cnt["a"])
    for kc in range(16):
        bk = 2 + kc % 2
        P.emit("tensor", lambda kc=kc, bk=bk: nc.tensor.transpose(
            B[bk][:, 0:8], gate_sb[:, 128 * kc:128 * (kc + 1)], eye[0:8, 0:8])
            .then_inc(sems["t"], 1))
        t_g = P.bump("t")
        P.wait("vector", "t", t_g)
        P.emit("vector", lambda kc=kc, bk=bk: nc.vector.tensor_copy(
            gtT[:, 8 * kc:8 * (kc + 1)], B[bk][:, 0:8])
            .then_inc(sems["v"], 1))
        v_gt = P.bump("v")
        P.wait("tensor", "v", v_gt)
    P.emit("tensor", lambda: nc.tensor.matmul(
        B[6][:, 0:8], ones1, ebias_r, start=True, stop=True).then_inc(sems["t"], 1))
    t_eb = P.bump("t")
    P.wait("vector", "t", t_eb)
    P.emit("vector", lambda: nc.vector.tensor_copy(ebias_b, B[6][:, 0:8]))
    P.emit("scalar", lambda: nc.scalar.preload_act_table(AF.Sigmoid))
    for tt in range(2):
        bank = tt % 2
        for kc in range(16):
            lhsT = gtT[:, 8 * kc:8 * (kc + 1)]
            rhs = h2_fm[:, kc * NO + tt * 512: kc * NO + tt * 512 + 512]
            if kc == 15:
                P.emit("tensor", lambda lhsT=lhsT, rhs=rhs, bank=bank:
                       nc.tensor.matmul(B[bank][0:8, :], lhsT, rhs,
                                        start=False, stop=True)
                       .then_inc(sems["t"], 1))
                t_gm = P.bump("t")
            else:
                P.emit("tensor", lambda lhsT=lhsT, rhs=rhs, kc=kc, bank=bank:
                       nc.tensor.matmul(B[bank][0:8, :], lhsT, rhs,
                                        start=(kc == 0), stop=False))
        P.wait("scalar", "t", t_gm)
        P.emit("scalar", lambda tt=tt, bank=bank: nc.scalar.activation(
            scoresT[:, 512 * tt:512 * (tt + 1)], B[bank][0:8, :], AF.Sigmoid)
            .then_inc(sems["a"], 1))
        P.bump("a")
    a_sc = P.cnt["a"]
    P.wait("tensor", "a", a_sc)
    for tti in range(8):
        bk = 2 + tti % 2
        P.emit("tensor", lambda tti=tti, bk=bk: nc.tensor.transpose(
            B[bk][:, 0:8], scoresT[:, 128 * tti:128 * (tti + 1)], eye[0:8, 0:8])
            .then_inc(sems["t"], 1))
        t_s = P.bump("t")
        P.wait("vector", "t", t_s)
        P.emit("vector", lambda tti=tti, bk=bk: nc.vector.tensor_copy(
            sc_tok[:, 8 * tti:8 * (tti + 1)], B[bk][:, 0:8]))
        P.emit("vector", lambda tti=tti: nc.vector.tensor_tensor(
            bs_tok, sc_tok[:, 8 * tti:8 * (tti + 1)], ebias_b, OP.add))
        P.emit("vector", lambda: nc.vector.engine_nop().then_inc(sems["v"], 1))
        v_bs = P.bump("v")
        P.wait("tensor", "v", v_bs)
        v = nc.vector
        P.autogap = True
        for e in range(8):
            be = bs_tok[:, e:e + 1]
            P.emit("vector", lambda be=be: v.tensor_scalar(
                cmp8, bs_tok, be, None, OP.is_gt))
            P.emit("vector", lambda: v.tensor_reduce(cnt1, cmp8, axis=AX.X, op=OP.add))
            if e > 0:
                P.emit("vector", lambda be=be: v.tensor_scalar(
                    cmp8, bs_tok, be, None, OP.is_equal))
                P.emit("vector", lambda e=e: v.tensor_reduce(
                    cnt2, cmp8[:, 0:e], axis=AX.X, op=OP.add))
                P.emit("vector", lambda: v.tensor_tensor(cnt1, cnt1, cnt2, OP.add))
            P.emit("vector", lambda: v.tensor_scalar(selc, cnt1, 1.5, None, OP.is_lt))
            P.emit("vector", lambda tti=tti, e=e: v.tensor_tensor(
                wt_tok[:, 8 * tti + e: 8 * tti + e + 1], selc,
                sc_tok[:, 8 * tti + e: 8 * tti + e + 1], OP.mult))
        P.autogap = False
        P.emit("vector", lambda: nc.vector.engine_nop().then_inc(sems["v"], 1))
        v_wt = P.bump("v")
        P.wait("tensor", "v", v_wt)
        P.emit("tensor", lambda tti=tti, bk=bk: nc.tensor.transpose(
            B[bk][0:8, 0:128], wt_tok[:, 8 * tti:8 * (tti + 1)], eye)
            .then_inc(sems["t"], 1))
        t_w = P.bump("t")
        P.wait("vector", "t", t_w)
        P.emit("vector", lambda tti=tti, bk=bk: nc.vector.tensor_copy(
            wt_fm[:, 128 * tti:128 * (tti + 1)], B[bk][0:8, 0:128])
            .then_inc(sems["v"], 1))
        v_wf = P.bump("v")
        P.wait("tensor", "v", v_wf)
    v_masks = P.cnt["v"]
    if tap_d:
        tap_wait([("v", v_masks)])
        tap("wt_fm", wt_fm)

    P.emit("scalar", lambda: nc.scalar.preload_act_table(AF.Silu))

    estate = {"bank_a": {0: None, 1: None, 2: None, 3: None},
              "mw_free_t": {0: None, 1: None}}

    def stream_expert_w(w_slice_src, n_slices, dst_col, kc_count,
                        first_wait=None, mm_emitter=None):
        for hs in range(n_slices):
            mw = mw_sl2[hs % 2]
            msem = "mw" if hs % 2 == 0 else "wt"
            if estate["mw_free_t"][hs % 2] is not None:
                P.wait("sync", "t", estate["mw_free_t"][hs % 2])
            P.emit("sync", lambda hs=hs, mw=mw, kc_count=kc_count, msem=msem:
                   nc.sync.dma_start(out=mw[:, 0:128 * kc_count],
                                     in_=w_slice_src(hs)).then_inc(sems[msem], 16))
            g_w = P.bump(msem, 16)
            P.wait("tensor", msem, g_w)
            if first_wait is not None and hs == 0:
                P.wait("tensor", first_wait[0], first_wait[1])
            ngr = kc_count // 4
            for g in range(ngr):
                bk = 2 + g % 4
                if estate["bank_a"][g % 4] is not None:
                    P.wait("tensor", "a", estate["bank_a"][g % 4])
                for kc in range(4 * g, 4 * g + 4):
                    if kc % 4 == 3:
                        P.emit("tensor", lambda kc=kc, bk=bk, mw=mw:
                               nc.tensor.transpose(
                                   B[bk][:, 128 * (kc % 4):128 * (kc % 4) + 128],
                                   mw[:, 128 * kc:128 * (kc + 1)], eye)
                               .then_inc(sems["t"], 1))
                        t_i = P.bump("t")
                    else:
                        P.emit("tensor", lambda kc=kc, bk=bk, mw=mw:
                               nc.tensor.transpose(
                                   B[bk][:, 128 * (kc % 4):128 * (kc % 4) + 128],
                                   mw[:, 128 * kc:128 * (kc + 1)], eye))
                if g == ngr - 1:
                    estate["mw_free_t"][hs % 2] = t_i
                P.wait("scalar", "t", t_i)
                for q in range(4):
                    dq = dst_col(4 * g + q, hs)
                    if q == 3:
                        P.emit("scalar", lambda dq=dq, bk=bk, q=q:
                               nc.scalar.activation(dq, B[bk][:, 128 * q:128 * (q + 1)],
                                                    AF.Identity).then_inc(sems["a"], 1))
                    else:
                        P.emit("scalar", lambda dq=dq, bk=bk, q=q:
                               nc.scalar.activation(dq, B[bk][:, 128 * q:128 * (q + 1)],
                                                    AF.Identity))
                estate["bank_a"][g % 4] = P.bump("a")
            if mm_emitter is not None:
                mm_emitter(hs, P.cnt["a"])
        return P.cnt["a"]

    P.wait("vector", "st", st_x2)
    P.wait("scalar", "st", st_x2)
    P.wait("tensor", "st", st_x2)
    estate["mw_free_t"][0] = P.cnt["t"]
    estate["mw_free_t"][1] = P.cnt["t"]
    P.wait("sync", "v", v_x2_done)
    P.wait("sync", "a", P.cnt["a"])
    v_xb_prev = None
    t_wtb_prev = None
    v_oe_prev = None
    for e in range(8):
        P.wait("gpsimd", "v", v_masks)
        if v_xb_prev is not None:
            P.wait("gpsimd", "v", v_xb_prev)
        P.emit("gpsimd", lambda e=e: nc.gpsimd.dma_start(
            out=wtrow0, in_=wt_fm[e:e + 1, :]).then_inc(sems["gw"], 16))
        g_wr = P.bump("gw", 16)
        P.wait("tensor", "gw", g_wr)
        P.wait("tensor", "v", v_masks)
        if v_xb_prev is not None:
            P.wait("tensor", "v", v_xb_prev)
        if v_oe_prev is not None:
            P.wait("tensor", "v", v_oe_prev)
        for tt in range(2):
            if tt == 1:
                P.emit("tensor", lambda e=e, tt=tt: nc.tensor.matmul(
                    B[6 + tt][:, 0:512], ones1, wtrow0[0:1, 512 * tt:512 * (tt + 1)],
                    start=True, stop=True).then_inc(sems["t"], 1))
                t_wb = P.bump("t")
            else:
                P.emit("tensor", lambda e=e, tt=tt: nc.tensor.matmul(
                    B[6 + tt][:, 0:512], ones1, wtrow0[0:1, 512 * tt:512 * (tt + 1)],
                    start=True, stop=True))
        t_wtb_prev = t_wb
        P.wait("vector", "t", t_wb)
        for pc in range(16):
            for tt in range(2):
                P.emit("vector", lambda pc=pc, tt=tt: nc.vector.tensor_tensor(
                    xb_sb[:, pc * NO + 512 * tt: pc * NO + 512 * tt + 512],
                    h2_fm[:, pc * NO + 512 * tt: pc * NO + 512 * tt + 512],
                    B[6 + tt][:, 0:512], OP.mult))
        P.emit("vector", lambda: nc.vector.engine_nop().then_inc(sems["v"], 1))
        v_xb = P.bump("v")
        v_xb_prev = v_xb
        exp_state = {"a_s1_last": None, "v_h1b": None}

        def mm_h1(hs, a_w):
            P.wait("tensor", "a", a_w)
            if hs == 0:
                P.wait("tensor", "v", v_xb)
            for tt in range(2):
                bank = tt % 2
                for kc in range(16):
                    lhsT = wexp[:, kc * KB + hs * 128: kc * KB + hs * 128 + 128]
                    rhs = xb_sb[:, kc * NO + tt * 512: kc * NO + tt * 512 + 512]
                    if kc == 15:
                        P.emit("tensor", lambda lhsT=lhsT, rhs=rhs, bank=bank:
                               nc.tensor.matmul(B[bank], lhsT, rhs, start=False,
                                                stop=True).then_inc(sems["t"], 1))
                        t_h1 = P.bump("t")
                    else:
                        P.emit("tensor", lambda lhsT=lhsT, rhs=rhs, kc=kc, bank=bank:
                               nc.tensor.matmul(B[bank], lhsT, rhs,
                                                start=(kc == 0), stop=False))
                P.wait("vector", "t", t_h1)
                if exp_state["a_s1_last"] is not None and tt == 0:
                    P.wait("vector", "a", exp_state["a_s1_last"])
                P.emit("vector", lambda tt=tt, bank=bank: nc.vector.tensor_copy(
                    h1b_st[:, 512 * tt:512 * (tt + 1)], B[bank]).then_inc(sems["v"], 1))
                v_h1b = P.bump("v")
                P.wait("tensor", "v", v_h1b)
                exp_state["v_h1b"] = v_h1b
            P.wait("scalar", "v", exp_state["v_h1b"])
            P.emit("scalar", lambda hs=hs: nc.scalar.activation(
                s1_sb[:, hs * KB:(hs + 1) * KB], h1b_st, AF.Silu)
                .then_inc(sems["a"], 1))
            exp_state["a_s1_last"] = P.bump("a")

        stream_expert_w(
            lambda hs, e=e: w1_d[e, 128 * hs:128 * (hs + 1), :],
            8, lambda kc, hs: wexp[:, kc * KB + hs * 128: kc * KB + hs * 128 + 128],
            16, first_wait=(("v", v_oe_prev) if v_oe_prev is not None else None),
            mm_emitter=mm_h1)
        a_s1 = P.cnt["a"]

        def mm_h3(hs, a_w):
            P.wait("tensor", "a", a_w)
            for tt in range(2):
                bank = tt % 2
                for kc in range(16):
                    lhsT = wexp[:, kc * KB + hs * 128: kc * KB + hs * 128 + 128]
                    rhs = xb_sb[:, kc * NO + tt * 512: kc * NO + tt * 512 + 512]
                    if kc == 15:
                        P.emit("tensor", lambda lhsT=lhsT, rhs=rhs, bank=bank:
                               nc.tensor.matmul(B[bank], lhsT, rhs, start=False,
                                                stop=True).then_inc(sems["t"], 1))
                        t_h3 = P.bump("t")
                    else:
                        P.emit("tensor", lambda lhsT=lhsT, rhs=rhs, kc=kc, bank=bank:
                               nc.tensor.matmul(B[bank], lhsT, rhs,
                                                start=(kc == 0), stop=False))
                P.wait("vector", "t", t_h3)
                P.emit("vector", lambda tt=tt, bank=bank: nc.vector.tensor_copy(
                    h3b_st[:, 512 * tt:512 * (tt + 1)], B[bank]))
            P.emit("vector", lambda hs=hs: nc.vector.tensor_tensor(
                hh_sb[:, hs * KB:(hs + 1) * KB], s1_sb[:, hs * KB:(hs + 1) * KB],
                h3b_st, OP.mult).then_inc(sems["v"], 1))
            v_hh = P.bump("v")
            P.wait("tensor", "v", v_hh)

        stream_expert_w(
            lambda hs, e=e: w3_d[e, 128 * hs:128 * (hs + 1), :],
            8, lambda kc, hs: wexp[:, kc * KB + hs * 128: kc * KB + hs * 128 + 128],
            16, mm_emitter=mm_h3)
        v_hh_all = P.cnt["v"]

        def mm_oe(dt_, a_w):
            P.wait("tensor", "a", a_w)
            if dt_ == 0:
                P.wait("tensor", "v", v_hh_all)
            for tt in range(2):
                bank = tt % 2
                for hc in range(8):
                    lhsT = wexp[:, hc * 2 * KB + dt_ * 128: hc * 2 * KB + dt_ * 128 + 128]
                    rhs = hh_sb[:, hc * KB + tt * 512: hc * KB + tt * 512 + 512]
                    if hc == 7:
                        P.emit("tensor", lambda lhsT=lhsT, rhs=rhs, bank=bank:
                               nc.tensor.matmul(B[bank], lhsT, rhs, start=False,
                                                stop=True).then_inc(sems["t"], 1))
                        t_oe = P.bump("t")
                    else:
                        P.emit("tensor", lambda lhsT=lhsT, rhs=rhs, hc=hc, bank=bank:
                               nc.tensor.matmul(B[bank], lhsT, rhs,
                                                start=(hc == 0), stop=False))
                P.wait("vector", "t", t_oe)
                P.emit("vector", lambda tt=tt, bank=bank: nc.vector.tensor_copy(
                    oestg[:, 512 * tt:512 * (tt + 1)], B[bank]).then_inc(sems["v"], 1))
                v_oe = P.bump("v")
                P.wait("tensor", "v", v_oe)
            P.wait("sync", "v", P.cnt["v"])
            P.emit("sync", lambda e=e, dt_=dt_: nc.sync.dma_start(
                out=oe_dram[e, 128 * dt_:128 * (dt_ + 1), :], in_=oestg)
                .then_inc(sems["st"], 16))
            st_oe = P.bump("st", 16)
            P.wait("vector", "st", st_oe)

        stream_expert_w(
            lambda hs, e=e: w2_d[e, 128 * hs:128 * (hs + 1), :],
            16, lambda kc, hs: wexp[:, kc * 2 * KB + hs * 128: kc * 2 * KB + hs * 128 + 128],
            8, mm_emitter=mm_oe)
        v_oe_prev = P.cnt["v"]
    st_experts = P.cnt["st"]

    if upto == "experts":
        _finalize(nc, P, sems, es)
        return nc

    P.emit("scalar", lambda: nc.scalar.preload_act_table(AF.Silu))
    t_pg0 = P.cnt["t"]
    P.wait("sync", "t", t_pg0)
    swt_free = [None, None]
    ib_sw = [0]
    a_w1_done = None
    for phase in ("w1", "w3"):
        if phase == "w3":
            a_w1_done = P.cnt["a"]
            P.wait("vector", "a", a_w1_done)
        for hs in range(8):
            wtd = sw1t_d if phase == "w1" else sw3t_d
            swt = swt2[ib_sw[0] % 2]
            ssem = "sa" if ib_sw[0] % 2 == 0 else "gg"
            if swt_free[ib_sw[0] % 2] is not None:
                P.wait("sync", "t", swt_free[ib_sw[0] % 2])
            P.emit("sync", lambda wtd=wtd, hs=hs, swt=swt, ssem=ssem:
                   nc.sync.dma_start(
                       out=swt.rearrange("p (kc j) -> p kc j", j=128),
                       in_=wtd[:, 128 * hs:128 * (hs + 1)]
                       .rearrange("(kc p) j -> p kc j", p=128))
                   .then_inc(sems[ssem], 16))
            v_s = P.bump(ssem, 16)
            P.wait("tensor", ssem, v_s)
            for tt in range(2):
                bank = tt % 2
                for kc in range(16):
                    lhsT = swt[:, 128 * kc:128 * (kc + 1)]
                    rhs = h2_fm[:, kc * NO + tt * 512: kc * NO + tt * 512 + 512]
                    if kc == 15:
                        P.emit("tensor", lambda lhsT=lhsT, rhs=rhs, bank=bank:
                               nc.tensor.matmul(B[bank], lhsT, rhs, start=False,
                                                stop=True).then_inc(sems["t"], 1))
                        t_h = P.bump("t")
                    else:
                        P.emit("tensor", lambda lhsT=lhsT, rhs=rhs, kc=kc, bank=bank:
                               nc.tensor.matmul(B[bank], lhsT, rhs,
                                                start=(kc == 0), stop=False))
                if phase == "w1":
                    P.wait("scalar", "t", t_h)
                    P.emit("scalar", lambda hs=hs, tt=tt, bank=bank:
                           nc.scalar.activation(
                               hh_s[:, hs * KB + 512 * tt: hs * KB + 512 * tt + 512],
                               B[bank], AF.Silu).then_inc(sems["a"], 1))
                    a_h = P.bump("a")
                    P.wait("tensor", "a", a_h)
                else:
                    P.wait("vector", "t", t_h)
                    dst = hh_s[:, hs * KB + 512 * tt: hs * KB + 512 * tt + 512]
                    P.emit("vector", lambda dst=dst, bank=bank:
                           nc.vector.tensor_tensor(dst, B[bank], dst, OP.mult)
                           .then_inc(sems["v"], 1))
                    v_hh = P.bump("v")
                    P.wait("tensor", "v", v_hh)
            swt_free[ib_sw[0] % 2] = P.cnt["t"]
            ib_sw[0] += 1

    for ds in range(16):
        swt = swt2[ib_sw[0] % 2]
        ssem = "sa" if ib_sw[0] % 2 == 0 else "gg"
        if swt_free[ib_sw[0] % 2] is not None:
            P.wait("sync", "t", swt_free[ib_sw[0] % 2])
        P.emit("sync", lambda ds=ds, swt=swt, ssem=ssem: nc.sync.dma_start(
            out=swt[:, 0:1024].rearrange("p (hc j) -> p hc j", j=128),
            in_=sw2t_d[:, 128 * ds:128 * (ds + 1)]
            .rearrange("(hc p) j -> p hc j", p=128)).then_inc(sems[ssem], 16))
        v_s = P.bump(ssem, 16)
        P.wait("tensor", ssem, v_s)
        for tt in range(2):
            bank = tt % 2
            for hc in range(8):
                lhsT = swt[:, 128 * hc:128 * (hc + 1)]
                rhs = hh_s[:, hc * KB + tt * 512: hc * KB + tt * 512 + 512]
                if hc == 7:
                    P.emit("tensor", lambda lhsT=lhsT, rhs=rhs, bank=bank:
                           nc.tensor.matmul(B[bank], lhsT, rhs, start=False,
                                            stop=True).then_inc(sems["t"], 1))
                    t_o = P.bump("t")
                else:
                    P.emit("tensor", lambda lhsT=lhsT, rhs=rhs, hc=hc, bank=bank:
                           nc.tensor.matmul(B[bank], lhsT, rhs,
                                            start=(hc == 0), stop=False))
            P.wait("vector", "t", t_o)
            dst = out_moe[:, ds * NO + 512 * tt: ds * NO + 512 * tt + 512]
            P.emit("vector", lambda dst=dst, bank=bank: nc.vector.tensor_copy(
                dst, B[bank]).then_inc(sems["v"], 1))
            v_o = P.bump("v")
            P.wait("tensor", "v", v_o)
        swt_free[ib_sw[0] % 2] = P.cnt["t"]
        ib_sw[0] += 1
    v_shared = P.cnt["v"]

    P.wait("scalar", "v", v_shared)
    for t in range(8):
        P.emit("scalar", lambda t=t: nc.scalar.dma_start(
            out=x2_sb[:, t * D:(t + 1) * D], in_=x2_dram[128 * t:128 * (t + 1), :])
            .then_inc(sems["xa"], 16))
        P.bump("xa", 16)
    v_x2r = P.cnt["xa"]
    P.wait("gpsimd", "st", st_experts)
    P.wait("gpsimd", "v", v_shared)
    g_oe = {}
    for e in range(8):
        for jt in range(16):
            i = e * 16 + jt
            ob = oe_buf[i % 2]
            osem = "goa" if i % 2 == 0 else "gob"
            if i >= 2:
                P.wait("gpsimd", "v", g_oe[i - 2])
            P.emit("gpsimd", lambda e=e, jt=jt, ob=ob, osem=osem:
                   nc.gpsimd.dma_start(
                       out=ob, in_=oe_dram[e, 128 * jt:128 * (jt + 1), :])
                   .then_inc(sems[osem], 16))
            g_l = P.bump(osem, 16)
            P.wait("vector", osem, g_l)
            dst = out_moe[:, jt * NO:(jt + 1) * NO]
            P.emit("vector", lambda dst=dst, ob=ob: nc.vector.tensor_tensor(
                dst, dst, ob, OP.add).then_inc(sems["v"], 1))
            g_oe[i] = P.bump("v")
    v_moe = P.cnt["v"]

    P.wait("vector", "xa", v_x2r)
    P.wait("tensor", "xa", v_x2r)
    for jt in range(16):
        P.emit("vector", lambda jt=jt: nc.vector.tensor_scalar(
            fin_scr, out_moe[:, jt * NO:(jt + 1) * NO], gb_v[:, jt:jt + 1],
            None, OP.mult).then_inc(sems["v"], 1))
        v_f = P.bump("v")
        P.wait("tensor", "v", v_f)
        for tt in range(8):
            bk = 2 + tt % 2
            P.emit("tensor", lambda tt=tt, bk=bk: nc.tensor.transpose(
                B[bk][:, 0:128], fin_scr[:, 128 * tt:128 * (tt + 1)], eye)
                .then_inc(sems["t"], 1))
            t_t = P.bump("t")
            P.wait("vector", "t", t_t)
            dst = x2_sb[:, tt * D + jt * 128: tt * D + jt * 128 + 128]
            P.emit("vector", lambda dst=dst, bk=bk: nc.vector.tensor_tensor(
                dst, B[bk][:, 0:128], dst, OP.add).then_inc(sems["v"], 1))
            v_a = P.bump("v")
            P.wait("tensor", "v", v_a)
    v_y = P.cnt["v"]
    P.wait("sync", "v", v_y)
    for t in range(8):
        P.emit("sync", lambda t=t: nc.sync.dma_start(
            out=y_d[128 * t:128 * (t + 1), :], in_=x2_sb[:, t * D:(t + 1) * D])
            .then_inc(sems["st"], 16))
        P.bump("st", 16)

    _finalize(nc, P, sems, es)
    return nc


def _finalize(nc, P, sems, es):
    for semn in ("st", "tap", "g", "goa", "gob", "xa", "xb"):
        if P.cnt[semn]:
            P.wait("sync", semn, P.cnt[semn])
    with nc.Block() as block:
        @block.tensor
        def _(eng):
            for f in P.q["tensor"]:
                f()

        @block.vector
        def _(eng):
            for f in P.q["vector"]:
                f()

        @block.scalar
        def _(eng):
            for f in P.q["scalar"]:
                f()

        @block.sync
        def _(eng):
            for f in P.q["sync"]:
                f()

        @block.gpsimd
        def _(eng):
            for f in P.q["gpsimd"]:
                f()
    es.close()


def _rot_idx(qi):
    own = list(range(4 * qi, 4 * qi + 4))
    rest = [c for c in range(16) if c not in own]
    idx = np.concatenate([np.arange(c * CHK, (c + 1) * CHK) for c in own + rest])
    return idx


def make_in_maps(inputs):
    x = np.ascontiguousarray(np.asarray(inputs["L0_attn_hidden_states"]))
    eye = np.eye(128, dtype=np.float32)
    eyeb = np.eye(128, dtype=ml_dtypes.bfloat16)
    shared = {
        "wq": np.ascontiguousarray(inputs["L0_attn_q_weight"]),
        "wk": np.ascontiguousarray(inputs["L0_attn_k_weight"]),
        "wv": np.ascontiguousarray(inputs["L0_attn_v_weight"]),
        "wo": np.ascontiguousarray(inputs["L0_attn_o_weight"]),
        "bq": np.ascontiguousarray(inputs["L0_attn_q_bias"]),
        "bk": np.ascontiguousarray(inputs["L0_attn_k_bias"]),
        "bv": np.ascontiguousarray(inputs["L0_attn_v_bias"]),
        "bo": np.ascontiguousarray(inputs["L0_attn_o_bias"]),
        "gate": np.ascontiguousarray(inputs["L0_ffn_gate"]),
        "ebias": np.ascontiguousarray(inputs["L0_ffn_expert_bias"]),
        "w1": np.ascontiguousarray(inputs["L0_ffn_w1"]),
        "w2": np.ascontiguousarray(inputs["L0_ffn_w2"]),
        "w3": np.ascontiguousarray(inputs["L0_ffn_w3"]),
        "sw1": np.ascontiguousarray(inputs["L0_ffn_shared_w1"]),
        "sw2": np.ascontiguousarray(inputs["L0_ffn_shared_w2"]),
        "sw3": np.ascontiguousarray(inputs["L0_ffn_shared_w3"]),
        "sw1t": np.ascontiguousarray(np.asarray(inputs["L0_ffn_shared_w1"]).T),
        "sw2t": np.ascontiguousarray(np.asarray(inputs["L0_ffn_shared_w2"]).T),
        "sw3t": np.ascontiguousarray(np.asarray(inputs["L0_ffn_shared_w3"]).T),
        "wmod": np.ascontiguousarray(inputs["L0_wmod"]),
        "eyef": eye, "eyeb": eyeb,
    }
    cfull = np.asarray(inputs["L0_c"])
    in_maps = []
    for c in range(8):
        b, qi = c // 4, c % 4
        m = dict(shared)
        m["x"] = np.ascontiguousarray(x[b][_rot_idx(qi)])
        m["cvec"] = np.ascontiguousarray(cfull[b])
        in_maps.append(m)
    return in_maps


def kernel(**inputs):
    nc = build("full")
    in_maps = make_in_maps(inputs)
    res = run_bass_kernel_spmd(nc, in_maps, core_ids=list(range(8)))
    out = np.zeros((2, 4096, 2048), dtype=np.float32)
    for c in range(8):
        b, qi = c // 4, c % 4
        out[b, qi * NO:(qi + 1) * NO] = np.asarray(res.results[c]["y"])
    return out


# version 07
# speedup vs baseline: 1.5660x; 1.1142x over previous
import numpy as np
import ml_dtypes
from contextlib import ExitStack

import concourse.bass as bass
import concourse.mybir as mybir
from concourse.alu_op_type import AluOpType as OP
from concourse.bass_utils import run_bass_kernel_spmd

F32 = mybir.dt.float32
BF16 = mybir.dt.bfloat16
AF = mybir.ActivationFunctionType
AX = mybir.AxisListType

D = 2048
NT = 4096
NO = 1024
CHK = 256
NCH = 16
NCH_OWN = 4
EPS = 1e-5
C7 = 1.0 / 7.0
INV2048 = 1.0 / 2048.0

ENGINES = ("tensor", "vector", "scalar", "sync", "gpsimd")
KB = 1024


class Prog:
    def __init__(self, nc, sems):
        self.nc = nc
        self.S = sems
        self.q = {e: [] for e in ENGINES}
        self.cnt = {k: 0 for k in sems}
        self.autogap = False
        self.gap_fn = None

    def emit(self, eng, fn):
        self.q[eng].append(fn)
        if eng == "vector" and self.gap_fn is not None:
            self.q[eng].append(self.gap_fn)

    def bump(self, sem, by=1):
        self.cnt[sem] += by
        return self.cnt[sem]

    def wait(self, eng, sem, val):
        nc, h = self.nc, self.S[sem]
        self.emit(eng, lambda: getattr(nc, eng).wait_ge(h, val))


def build(upto="full", taps=()):
    nc = bass.Bass("TRN2", target_bir_lowering=False)

    def dri(n, s, dt=F32):
        return nc.dram_tensor(n, s, dt, kind="ExternalInput").ap()

    x_d = dri("x", [NT, D])
    w_qkvo = {"q": dri("wq", [D, D]), "k": dri("wk", [D, D]),
              "v": dri("wv", [D, D]), "o": dri("wo", [D, D])}
    b_qkvo = {"q": dri("bq", [D]), "k": dri("bk", [D]),
              "v": dri("bv", [D]), "o": dri("bo", [D])}
    gate_d = dri("gate", [8, D])
    ebias_d = dri("ebias", [8])
    w1_d = dri("w1", [8, 1024, D])
    w2_d = dri("w2", [8, D, 1024])
    w3_d = dri("w3", [8, 1024, D])
    w1t_d = dri("w1t", [8, D, 1024])
    w2t_d = dri("w2t", [8, 1024, D])
    w3t_d = dri("w3t", [8, D, 1024])
    sw1_d = dri("sw1", [1024, D])
    sw2_d = dri("sw2", [D, 1024])
    sw3_d = dri("sw3", [1024, D])
    sw1t_d = dri("sw1t", [D, 1024])
    sw2t_d = dri("sw2t", [1024, D])
    sw3t_d = dri("sw3t", [D, 1024])
    cvec_d = dri("cvec", [D])
    wmod_d = dri("wmod", [D, 6 * D])
    eye_d = dri("eyef", [128, 128])
    eyeb_d = nc.dram_tensor("eyeb", [128, 128], BF16, kind="ExternalInput").ap()

    y_d = nc.dram_tensor("y", [NO, D], F32, kind="ExternalOutput").ap()

    K_dram = nc.dram_tensor("K_dram", [D, NT], F32).ap()
    Vt_dram = nc.dram_tensor("Vt_dram", [NT, D], F32).ap()
    Q_dram = nc.dram_tensor("Q_dram", [D, NO], F32).ap()
    oe_dram = nc.dram_tensor("oe_dram", [8, D, NO], F32).ap()
    x2_dram = nc.dram_tensor("x2_dram", [NO, D], F32).ap()

    tap_d = {}
    for tname, tshape, tdt in taps:
        tap_d[tname] = nc.dram_tensor("tap_" + tname, list(tshape), tdt,
                                      kind="ExternalOutput").ap()

    AR = 192 * KB
    arena = nc.alloc_sbuf_tensor("arena", [128, AR], mybir.dt.uint8)
    base = nc.lookup_mloc(arena).addr

    def at(name, shape, dt, off):
        assert off + shape[1] * (4 if dt == F32 else 2 if dt == BF16 else 1) <= AR
        return nc.alloc_sbuf_tensor_at(name, shape, dt, offset=base + off).ap()

    R1, R2, R3 = 0, 64 * KB, 128 * KB
    Ip = at("Ip", [128, 16 * NT], BF16, R1)
    ctx_fm = at("ctx_fm", [128, 16 * NO], F32, R1)
    x2_sb = at("x2", [128, 8 * D], F32, R1)
    xb_sb = at("xb", [128, 16 * NO], BF16, R1)
    wexp = at("wexp", [128, 16 * KB], BF16, R1 + 32 * KB)
    swt2 = [at("swtA", [128, D], F32, R1),
            at("swtB", [128, D], F32, R1 + 8 * KB)]
    s1s_sl = at("s1s", [128, 1024], F32, R1 + 16 * KB)
    hh_s = at("hhs", [128, 8 * KB], F32, R1 + 20 * KB)
    q4 = [at(f"q4_{i}", [128, 1024], F32, R2 + i * 4 * KB) for i in range(2)]
    kcb = [at(f"kcb{i}", [128, 256], F32, R2 + 37 * KB + i * KB) for i in range(8)]
    v65 = [at(f"v65{i}", [128, 260], F32, R2 + 45 * KB + i * 1056) for i in range(8)]
    probs = [at(f"prb{i}", [128, 512], F32, R2 + 13 * KB + i * 2 * KB)
             for i in range(8)]
    rsum65 = at("rsum65", [65, 4 * 512], F32, R2 + 29 * KB)
    rsum0 = at("rsum0", [1, 4 * 512], F32, R2 + 54 * KB)
    rq0 = rsum0
    rsb = at("rsb", [64, 512], F32, R2 + 8 * KB)
    ostg = at("ostg", [64, 512], F32, R2 + 10 * KB)
    I2 = at("I2", [128, 16 * NO], BF16, R2)
    pd_scr = at("pd_scr", [128, NO], F32, R2 + 32 * KB)
    pd_scr2 = at("pd_scr2", [128, NO], F32, R2 + 36 * KB)
    h2_fm = at("h2f", [128, 16 * NO], F32, R2)
    oe_buf = [at(f"oeb{i}", [128, NO], F32, R2 + i * 4 * KB) for i in range(2)]
    fin_scr = at("fin_scr", [128, NO], F32, R2 + 8 * KB)
    wcol = [at(f"wcol{i}", [128, D], F32, R3 + i * 8 * KB) for i in range(2)]
    qw_buf = at("qw_buf", [128, D], F32, R3 + 8 * KB)
    iwtT = at("iwtT", [128, D], BF16, R3 + 24 * KB)
    xt_buf = [at(f"xt{i}", [128, D], F32, R3 + 16 * KB + i * 8 * KB) for i in range(2)]
    qchunk = at("qchunk", [128, 16 * CHK], F32, R3 + 16 * KB)
    hfm = at("hfm", [128, 16 * CHK], F32, R3 + 32 * KB)
    lad_thr = at("lad_thr", [128, D], F32, R3 + 44 * KB)
    lad_bb = at("lad_bb", [128, D], F32, R3 + 52 * KB)
    scr2k = at("scr2k", [128, D], F32, R3 + 52 * KB)
    wts_f = at("wts_f", [128, D], F32, R3 + 16 * KB)
    iwt_sl = at("iwt", [128, D], BF16, R3 + 24 * KB)
    krow = at("krow", [128, NT], F32, R3 + 28 * KB)
    mw_sl2 = [at("mw_sl", [128, D], F32, R3 + 36 * KB),
              at("mw_slb", [128, D], F32, R3 + 44 * KB)]
    s1_sb = at("s1", [128, 8 * KB], BF16, R3)
    hh_sb = at("hh", [128, 8 * KB], BF16, R3 + 16 * KB)
    oestg = at("oestg", [128, NO], F32, R3 + 32 * KB)
    out_moe = at("om", [128, 16 * NO], F32, R3)

    def sb(n, s, dt=F32):
        return nc.alloc_sbuf_tensor(n, s, dt).ap()

    eye = sb("eye", [128, 128])
    eye16 = sb("eye16", [128, 128], BF16)
    ones1 = sb("ones1", [1, 128])
    cvec = sb("cvecs", [128, 16])
    scv = sb("scv", [128, 16])
    modv = sb("modv", [128, 96])
    ca_v = sb("cav", [128, 16])
    ga_v = sb("gav", [128, 16])
    cb_v = sb("cbv", [128, 16])
    gb_v = sb("gbv", [128, 16])
    t5a = sb("t5a", [128, 16]); t5b = sb("t5b", [128, 16]); t5c = sb("t5c", [128, 16])
    chb = sb("chb", [128, 16 * NCH])
    tm896 = sb("tm896", [128, NCH])
    chb2 = sb("chb2", [128, 16 * 16])
    tm896o = sb("tm896o", [128, NCH_OWN])
    mxb = sb("mxb", [128, 16]); mnb = sb("mnb", [128, 16])
    cmaxb = sb("cmaxb", [128, 16]); p2b = sb("p2b", [128, 16])
    rscaleb = sb("rscaleb", [128, 16])
    tmcol = sb("tmcol", [128, 1]); tmrow = sb("tmrow", [1, 128])
    tmsc = sb("tmsc", [1, 1]); tmb = sb("tmb", [128, 1])
    thrj = sb("thrj", [128, 1]); tm7b = sb("tm7b", [128, 1])
    lnsm = sb("lnsm", [128, 1]); lnm = sb("lnm", [128, 1])
    lnv2 = sb("lnv2", [128, 1]); lna = sb("lna", [128, 1])
    lnsq = sb("lnsq", [128, 1]); lnr0 = sb("lnr0", [128, 1])
    lnt = sb("lnt", [128, 1]); lnr = sb("lnr", [128, 1])
    swv = sb("swv", [128, 1]); rswv = sb("rswv", [128, 1])
    swcv = sb("swcv", [128, 16]); cbbv = sb("cbbv", [128, 16])
    bias_sl = sb("bias_sl", [128, 1])
    gate_sb = at("gate_sb", [8, D], F32, R3 + 44 * KB)
    gtT = sb("gtT", [128, 16 * 8])
    scoresT = at("scoresT", [8, 1024], F32, R3 + 52 * KB)
    sc_tok = sb("sc_tok", [128, 8 * 8])
    bs_tok = sb("bs_tok", [128, 8])
    ebias_r = sb("ebias_r", [1, 8])
    ebias_b = sb("ebias_b", [128, 8])
    cnt1 = sb("cnt1", [128, 1]); cnt2 = sb("cnt2", [128, 1])
    cmp8 = sb("cmp8", [128, 8])
    selc = sb("selc", [128, 1])
    wt_tok = sb("wt_tok", [128, 8 * 8])
    wt_fm = at("wt_fm", [8, 1024], F32, R3 + 56 * KB)
    wtrow0 = at("wtrow0", [1, 1024], F32, R3 + 60 * KB)
    h1b_st = sb("h1b_st", [128, 1024], BF16)
    h3b_st = sb("h3b_st", [128, 1024], BF16)

    psum = nc.alloc_psum_tensor("psum", [128, 4096], F32).ap()
    B = [psum[:, 512 * i: 512 * (i + 1)] for i in range(8)]
    psum16 = psum.bitcast(BF16)
    B16 = [psum16[:, 1024 * i: 1024 * (i + 1)] for i in range(8)]

    vg_a = sb("vg_a", [128, 64])
    vg_b = sb("vg_b", [128, 64])

    sem_names = ["c0", "wa", "wb", "bs", "xa", "xb", "ka", "kb", "qa", "qb",
                 "gg", "sa", "st", "t", "v", "a", "g", "gw", "goa", "gob", "wt", "mw",
                 "k0", "k1", "k2", "k3", "k4", "k5", "k6", "k7", "tap"]
    es = ExitStack()
    sems = {n: es.enter_context(nc.semaphore(n)) for n in sem_names}
    P = Prog(nc, sems)
    P.gap_fn = lambda: nc.vector.drain()

    def tap(name, src_ap):
        if name not in tap_d:
            return
        dst = tap_d[name]
        if P.cnt["tap"]:
            P.wait("gpsimd", "tap", P.cnt["tap"])
        P.emit("gpsimd", lambda: nc.gpsimd.dma_start(out=dst, in_=src_ap)
               .then_inc(sems["tap"], 16))
        c = P.bump("tap", 16)
        P.wait("vector", "tap", c)
        P.wait("scalar", "tap", c)

    def tap_wait(eng_sem_pairs):
        for semn, val in eng_sem_pairs:
            if tap_d:
                P.wait("gpsimd", semn, val)

    def emit_ladder(buf, width):
        v = nc.vector
        thr, bb = lad_thr[:, 0:width], lad_bb[:, 0:width]
        P.emit("vector", lambda: v.tensor_scalar(bb, buf, -0.5, None, OP.is_ge))
        P.emit("vector", lambda: v.tensor_scalar(thr, bb, 8.0, -4.5, OP.mult, OP.add))
        P.emit("vector", lambda: v.tensor_tensor(bb, buf, thr, OP.is_ge))
        P.emit("vector", lambda: v.scalar_tensor_tensor(thr, bb, 4.0, thr, OP.mult, OP.add))
        P.emit("vector", lambda: v.scalar_tensor_tensor(bb, thr, -2.0, buf, OP.add, OP.is_le))
        P.emit("vector", lambda: v.scalar_tensor_tensor(thr, bb, 2.0, thr, OP.mult, OP.add))
        P.emit("vector", lambda: v.scalar_tensor_tensor(bb, thr, -3.0, buf, OP.add, OP.is_le))
        P.emit("vector", lambda: v.scalar_tensor_tensor(buf, thr, -3.5, bb, OP.add, OP.add))

    for dst, src in [(eye, eye_d), (eye16, eyeb_d)]:
        P.emit("sync", lambda dst=dst, src=src:
               nc.sync.dma_start(out=dst, in_=src).then_inc(sems["c0"], 16))
        P.bump("c0", 16)
    P.emit("vector", lambda: nc.vector.memset(ones1, 1.0)
           .then_inc(sems["v"], 1))
    v_const = P.bump("v")

    P.emit("sync", lambda: nc.sync.dma_start(
        out=cvec, in_=cvec_d.rearrange("(j p) -> p j", p=128),
        allow_slow_non_contiguous=True).then_inc(sems["c0"], 16))
    v_ld = P.bump("c0", 16)
    P.wait("scalar", "c0", v_ld)
    P.emit("scalar", lambda: nc.scalar.preload_act_table(AF.Silu))
    P.emit("scalar", lambda: nc.scalar.activation(scv, cvec, AF.Silu)
           .then_inc(sems["a"], 1))
    v_scv = P.bump("a")

    mod_mm_done = {}
    for m in range(96):
        buf = wcol[m % 2]
        wsem = "wa" if m % 2 == 0 else "wb"
        src = wmod_d[:, 128 * m:128 * (m + 1)].rearrange("(k p) j -> p k j", p=128)
        dst3 = buf.rearrange("p (k j) -> p k j", j=128)
        if m >= 2:
            P.wait("sync", "t", mod_mm_done[m - 2])
        P.emit("sync", lambda dst3=dst3, src=src, wsem=wsem:
               nc.sync.dma_start(out=dst3, in_=src).then_inc(sems[wsem], 16))
        v_ld = P.bump(wsem, 16)
        P.wait("tensor", wsem, v_ld)
        if m == 0:
            P.wait("tensor", "a", v_scv)
        bufr = buf.rearrange("p (k j) -> p k j", j=128)
        for kc in range(16):
            lhsT, rhs = bufr[:, kc], scv[:, kc:kc + 1]
            if kc == 15:
                P.emit("tensor", lambda lhsT=lhsT, rhs=rhs, m=m:
                       nc.tensor.matmul(B[7][:, m:m + 1], lhsT, rhs,
                                        start=False, stop=True).then_inc(sems["t"], 1))
                mod_mm_done[m] = P.bump("t")
            else:
                P.emit("tensor", lambda lhsT=lhsT, rhs=rhs, m=m, kc=kc:
                       nc.tensor.matmul(B[7][:, m:m + 1], lhsT, rhs,
                                        start=(kc == 0), stop=False))
    P.wait("vector", "t", mod_mm_done[31])
    P.emit("vector", lambda: nc.vector.tensor_copy(modv[:, 0:16], B[7][:, 0:16]))
    P.emit("vector", lambda: nc.vector.tensor_scalar(
        ca_v, B[7][:, 16:32], 1.0, None, OP.add).then_inc(sems["v"], 1))
    v_mod1 = P.bump("v")

    def emit_clamp(src_cols, dst):
        v = nc.vector
        P.autogap = True
        P.emit("vector", lambda: v.tensor_scalar(t5a, src_cols, 0.25, None, OP.max))
        P.emit("vector", lambda: v.tensor_scalar(t5b, src_cols, -0.25, None, OP.min))
        P.emit("vector", lambda: v.tensor_scalar(t5c, src_cols, 0.0, None, OP.is_ge))
        P.emit("vector", lambda: v.tensor_tensor(t5a, t5a, t5b, OP.subtract))
        P.emit("vector", lambda: v.tensor_tensor(t5a, t5c, t5a, OP.mult))
        P.emit("vector", lambda: v.tensor_tensor(dst, t5a, t5b, OP.add))
        P.autogap = False

    def emit_mod_part2():
        P.wait("vector", "t", mod_mm_done[95])
        P.emit("vector", lambda: nc.vector.tensor_copy(modv[:, 16:96], B[7][:, 16:96]))
        emit_clamp(modv[:, 32:48], ga_v)
        P.emit("vector", lambda: nc.vector.tensor_scalar(
            cb_v, modv[:, 64:80], 1.0, None, OP.add))
        emit_clamp(modv[:, 80:96], gb_v)
        P.emit("vector", lambda: nc.vector.engine_nop().then_inc(sems["v"], 1))
        return P.bump("v")

    if upto == "s1":
        v_mod2 = emit_mod_part2()
        if tap_d:
            tap_wait([("v", v_mod2)])
            tap("modv", modv)
            tap("cav", ca_v)
            tap("gav", ga_v)
            tap("cbv", cb_v)
            tap("gbv", gb_v)
        _finalize(nc, P, sems, es)
        return nc

    P.emit("scalar", lambda: nc.scalar.preload_act_table(AF.Sqrt))


    tfm_state = {"bank_done": {}}

    def emit_tok2fm(xt, dst_fn, sc_v, bi_v, half_or_t, blk):
        for g in range(4):
            pair = g % 2
            bk0 = 2 * pair
            key = (blk, pair)
            prev = tfm_state["bank_done"].get(pair)
            if prev is not None:
                P.wait("tensor", prev[0], prev[1])
            for i, pc in enumerate(range(4 * g, 4 * g + 4)):
                bank = B[bk0 + i // 2]
                col = 128 * (i % 2)
                if i == 3:
                    P.emit("tensor", lambda pc=pc, bank=bank, col=col, xt=xt:
                           nc.tensor.transpose(bank[:, col:col + 128],
                                               xt[:, 128 * pc:128 * (pc + 1)], eye)
                           .then_inc(sems["t"], 1))
                    t_tr = P.bump("t")
                else:
                    P.emit("tensor", lambda pc=pc, bank=bank, col=col, xt=xt:
                           nc.tensor.transpose(bank[:, col:col + 128],
                                               xt[:, 128 * pc:128 * (pc + 1)], eye))
            if pair == 0:
                P.wait("vector", "t", t_tr)
                for i, pc in enumerate(range(4 * g, 4 * g + 4)):
                    bank = B[bk0 + i // 2]
                    col = 128 * (i % 2)
                    dst = dst_fn(pc)
                    fin = i == 3
                    if fin:
                        P.emit("vector", lambda pc=pc, bank=bank, col=col, dst=dst:
                               nc.vector.tensor_scalar(
                                   dst, bank[:, col:col + 128], sc_v[:, pc:pc + 1],
                                   bi_v[:, pc:pc + 1], OP.mult, OP.add)
                               .then_inc(sems["v"], 1))
                    else:
                        P.emit("vector", lambda pc=pc, bank=bank, col=col, dst=dst:
                               nc.vector.tensor_scalar(
                                   dst, bank[:, col:col + 128], sc_v[:, pc:pc + 1],
                                   bi_v[:, pc:pc + 1], OP.mult, OP.add))
                tfm_state["bank_done"][pair] = ("v", P.bump("v"))
            else:
                P.wait("scalar", "t", t_tr)
                for i, pc in enumerate(range(4 * g, 4 * g + 4)):
                    bank = B[bk0 + i // 2]
                    col = 128 * (i % 2)
                    dst = dst_fn(pc)
                    fin = i == 3
                    if fin:
                        P.emit("scalar", lambda pc=pc, bank=bank, col=col, dst=dst:
                               nc.scalar.activation(
                                   dst, bank[:, col:col + 128], AF.Identity,
                                   bias=bi_v[:, pc:pc + 1], scale=sc_v[:, pc:pc + 1])
                               .then_inc(sems["a"], 1))
                    else:
                        P.emit("scalar", lambda pc=pc, bank=bank, col=col, dst=dst:
                               nc.scalar.activation(
                                   dst, bank[:, col:col + 128], AF.Identity,
                                   bias=bi_v[:, pc:pc + 1], scale=sc_v[:, pc:pc + 1]))
                tfm_state["bank_done"][pair] = ("a", P.bump("a"))
        return tfm_state["bank_done"][0][1], tfm_state["bank_done"][1][1]

    def emit_ln_tile(xin):
        v = nc.vector
        P.autogap = True
        P.emit("vector", lambda: v.tensor_reduce(lnsm, xin, axis=AX.X, op=OP.add))
        P.emit("vector", lambda: v.tensor_scalar(lnm, lnsm, INV2048, None, OP.mult))
        P.emit("vector", lambda: v.tensor_scalar(xin, xin, lnm, None, OP.subtract))
        P.emit("vector", lambda: v.tensor_tensor(scr2k, xin, xin, OP.mult))
        P.emit("vector", lambda: v.tensor_reduce(lnv2, scr2k, axis=AX.X, op=OP.add))
        P.emit("vector", lambda: v.tensor_scalar(
            lna, lnv2, INV2048, EPS, OP.mult, OP.add).then_inc(sems["v"], 1))
        vv = P.bump("v")
        P.wait("scalar", "v", vv)
        P.emit("scalar", lambda: nc.scalar.activation(lnsq, lna, AF.Sqrt)
               .then_inc(sems["a"], 1))
        aa = P.bump("a")
        P.wait("vector", "a", aa)
        P.emit("vector", lambda: v.reciprocal(lnr0, lnsq))
        P.emit("vector", lambda: v.tensor_tensor(lnt, lnr0, lnr0, OP.mult))
        P.emit("vector", lambda: v.tensor_tensor(lnt, lnt, lna, OP.mult))
        P.emit("vector", lambda: v.tensor_scalar(lnt, lnt, -0.5, 1.5, OP.mult, OP.add))
        P.emit("vector", lambda: v.tensor_tensor(lnr, lnr0, lnt, OP.mult))
        P.emit("vector", lambda: v.tensor_scalar(xin, xin, lnr, None, OP.mult))
        P.autogap = False

    def emit_quant_chunk(src_fm, chb_dst, tm_dst, ip_dst, src3_fn):
        v = nc.vector
        P.autogap = True
        src3 = src3_fn()
        P.emit("vector", lambda src3=src3: v.tensor_reduce(
            mxb, src3, axis=AX.X, op=OP.max))
        P.emit("vector", lambda src3=src3: v.tensor_reduce(
            mnb, src3, axis=AX.X, op=OP.min))
        P.emit("vector", lambda: v.tensor_tensor(t5a, mxb, mnb, OP.add))
        P.emit("vector", lambda: v.tensor_scalar(t5b, t5a, 0.5, None, OP.mult))
        for pc in range(16):
            d = chb_dst(pc)
            P.emit("vector", lambda d=d, pc=pc: v.tensor_copy(d, t5b[:, pc:pc + 1]))
        P.emit("vector", lambda: v.tensor_tensor(t5a, mxb, t5b, OP.subtract))
        P.emit("vector", lambda: v.tensor_tensor(t5c, t5b, mnb, OP.subtract))
        P.emit("vector", lambda: v.tensor_tensor(cmaxb, t5a, t5c, OP.max))
        P.emit("vector", lambda: v.tensor_reduce(
            tmcol, cmaxb, axis=AX.X, op=OP.max).then_inc(sems["v"], 1))
        vv = P.bump("v")
        P.wait("tensor", "v", vv)
        P.emit("tensor", lambda: nc.tensor.transpose(B[6][0:1, 0:128], tmcol, eye)
               .then_inc(sems["t"], 1))
        tt_ = P.bump("t")
        P.wait("vector", "t", tt_)
        P.emit("vector", lambda: v.tensor_copy(tmrow, B[6][0:1, 0:128]))
        P.emit("vector", lambda: v.tensor_reduce(
            tmsc, tmrow, axis=AX.X, op=OP.max).then_inc(sems["v"], 1))
        vv = P.bump("v")
        P.wait("tensor", "v", vv)
        P.emit("tensor", lambda: nc.tensor.matmul(
            B[6][:, 256:257], ones1, tmsc, start=True, stop=True)
            .then_inc(sems["t"], 1))
        tt_ = P.bump("t")
        P.wait("vector", "t", tt_)
        P.emit("vector", lambda: v.tensor_copy(tmb, B[6][:, 256:257]))
        P.emit("vector", lambda: v.tensor_scalar(tm7b, tmb, C7, None, OP.mult))
        P.emit("vector", lambda: v.tensor_scalar(tm_dst, tm7b, 2.0 ** -7, None, OP.mult))
        for j in range(7):
            P.emit("vector", lambda j=j: v.tensor_scalar(
                thrj, tmb, 2.0 ** (j - 7), None, OP.mult))
            P.emit("vector", lambda: v.tensor_scalar(t5a, cmaxb, thrj, None, OP.is_gt))
            if j == 0:
                P.emit("vector", lambda: v.tensor_scalar(p2b, t5a, 1.0, None, OP.add))
            else:
                P.emit("vector", lambda: v.tensor_scalar(t5a, t5a, 1.0, None, OP.add))
                P.emit("vector", lambda: v.tensor_tensor(p2b, p2b, t5a, OP.mult))
        P.emit("vector", lambda: v.tensor_scalar(t5a, p2b, tm7b, 2.0 ** -7,
                                                 OP.mult, OP.mult))
        P.emit("vector", lambda: v.tensor_scalar(t5a, t5a, 1e-9, None, OP.max))
        P.emit("vector", lambda: v.reciprocal(rscaleb, t5a))
        P.autogap = False
        for pc in range(16):
            s = src_fm(pc)
            cb_ = chb_dst(pc)
            P.emit("vector", lambda s=s, cb_=cb_, pc=pc: v.tensor_scalar(
                qchunk[:, pc * CHK:(pc + 1) * CHK], s, cb_, rscaleb[:, pc:pc + 1],
                OP.subtract, OP.mult))
        emit_ladder(qchunk[:, 0:2048], 2048)
        emit_ladder(qchunk[:, 2048:4096], 2048)
        for pc in range(16):
            d = ip_dst(pc)
            P.emit("vector", lambda d=d, pc=pc: v.tensor_scalar(
                d, qchunk[:, pc * CHK:(pc + 1) * CHK], p2b[:, pc:pc + 1], None,
                OP.mult))
        P.emit("vector", lambda: nc.vector.engine_nop().then_inc(sems["v"], 1))
        return P.bump("v")

    v_qdone = {}
    xtr_done = {}
    for c in range(NCH):
        for half in range(2):
            tix = 2 * c + half
            buf = xt_buf[half]
            xsem = "xa" if half == 0 else "xb"
            if c >= 1 and half == 0:
                P.wait("scalar", "v", v_qdone[c - 1])
            P.emit("scalar", lambda buf=buf, tix=tix, xsem=xsem:
                   nc.scalar.dma_start(
                       out=buf, in_=x_d[128 * tix:128 * (tix + 1), :])
                   .then_inc(sems[xsem], 16))
            v_x = P.bump(xsem, 16)
            P.wait("vector", xsem, v_x)
            emit_ln_tile(buf)
            P.emit("vector", lambda: nc.vector.engine_nop().then_inc(sems["v"], 1))
            v_xh = P.bump("v")
            P.wait("tensor", "v", v_xh)
            if c == 0 and half == 0:
                P.wait("tensor", "v", v_mod1)
            vv, aa = emit_tok2fm(
                buf,
                lambda pc, half=half: hfm[:, pc * CHK + half * 128: pc * CHK + half * 128 + 128],
                ca_v, modv, half, "s2")
            xtr_done[tix] = P.cnt["t"]
        v_h = P.cnt["v"]
        P.wait("vector", "a", P.cnt["a"])
        v_qdone[c] = emit_quant_chunk(
            src_fm=lambda pc: hfm[:, pc * CHK:(pc + 1) * CHK],
            chb_dst=lambda pc, c=c: chb[:, pc * 16 + c: pc * 16 + c + 1],
            tm_dst=tm896[:, c:c + 1],
            ip_dst=lambda pc, c=c: Ip[:, pc * NT + c * CHK: pc * NT + (c + 1) * CHK],
            src3_fn=lambda: hfm.rearrange("p (c t) -> p c t", t=CHK),
        )
        if tap_d and c == 0:
            tap_wait([("v", v_qdone[0])])
            tap("hfm0", hfm)
            tap("qchunk0", qchunk)

    v_quant_all = v_qdone[NCH - 1]
    v_mod2 = emit_mod_part2()
    if tap_d:
        tap_wait([("v", v_quant_all), ("v", v_mod2)])
        tap("modv", modv)
        tap("gav", ga_v)
        tap("chb", chb)
        tap("tm896", tm896)
        tap("Ip0", Ip[:, 0:NT])
    if upto == "s2":
        _finalize(nc, P, sems, es)
        return nc

    wstate = {
        "wcol_free_t": None,
        "bias_free_a": None,
        "proj_bank_a": {0: None, 1: None},
        "tr_bank_a": {0: None, 1: None},
        "itr_bank_a": {0: None, 1: None},
        "cbb_bank_a": None,
        "iwt_free_t": None,
        "qw_free_t": None,
        "krow_free": None,
    }

    def emit_wstream(kind, ip_src, ip_blk, tm_src, chb_src, nch, ntok):
        w_da = w_qkvo[kind.lower()]
        b_da = b_qkvo[kind.lower()]
        ntt = ntok // 512
        for jt in range(16):
            wsem = "wa"
            if wstate["wcol_free_t"] is not None:
                P.wait("sync", "t", wstate["wcol_free_t"])
            if wstate.get("wcol_free_v") is not None:
                P.wait("sync", "v", wstate["wcol_free_v"])
            P.emit("sync", lambda jt=jt, w_da=w_da: nc.sync.dma_start(
                out=wcol[0], in_=w_da[128 * jt:128 * (jt + 1), :])
                .then_inc(sems["wa"], 16))
            v_w = P.bump("wa", 16)
            if wstate["bias_free_a"] is not None:
                P.wait("sync", "a", wstate["bias_free_a"])
            P.emit("sync", lambda jt=jt, b_da=b_da: nc.sync.dma_start(
                out=bias_sl,
                in_=b_da.rearrange("(a p) -> p a", p=128)[:, jt:jt + 1],
                allow_slow_non_contiguous=True).then_inc(sems["bs"], 16))
            v_b = P.bump("bs", 16)
            v = nc.vector
            wbuf = wcol[0]
            P.wait("tensor", "wa", v_w)
            if wstate.get("wts_free_t") is not None:
                P.wait("scalar", "t", wstate["wts_free_t"])
            for g in range(4):
                bk = 4 + g % 2
                if wstate["tr_bank_a"][g % 2] is not None:
                    P.wait("tensor", "a", wstate["tr_bank_a"][g % 2])
                for kc in range(4 * g, 4 * g + 4):
                    fin = kc % 4 == 3
                    if fin:
                        P.emit("tensor", lambda kc=kc, bk=bk, wbuf=wbuf:
                               nc.tensor.transpose(
                                   B[bk][:, 128 * (kc % 4):128 * (kc % 4) + 128],
                                   wbuf[:, 128 * kc:128 * (kc + 1)], eye)
                               .then_inc(sems["t"], 1))
                        t_f = P.bump("t")
                    else:
                        P.emit("tensor", lambda kc=kc, bk=bk, wbuf=wbuf:
                               nc.tensor.transpose(
                                   B[bk][:, 128 * (kc % 4):128 * (kc % 4) + 128],
                                   wbuf[:, 128 * kc:128 * (kc + 1)], eye))
                if g == 3:
                    wstate["wcol_free_t"] = t_f
                P.wait("scalar", "t", t_f)
                P.emit("scalar", lambda g=g, bk=bk: nc.scalar.activation(
                    wts_f[:, 512 * g:512 * (g + 1)], B[bk], AF.Identity)
                    .then_inc(sems["a"], 1))
                wstate["tr_bank_a"][g % 2] = P.bump("a")
            a_wts = P.cnt["a"]
            P.wait("tensor", "a", a_wts)
            if wstate["cbb_bank_a"] is not None:
                P.wait("tensor", "v", wstate["cbb_bank_a"])
            for kc in range(16):
                lhsT = wts_f[:, 128 * kc:128 * (kc + 1)]
                rhs = chb_src[:, kc * 16: kc * 16 + nch]
                if kc == 15:
                    P.emit("tensor", lambda lhsT=lhsT, rhs=rhs, nch=nch:
                           nc.tensor.matmul(B[6][:, 0:nch], lhsT, rhs,
                                            start=False, stop=True)
                           .then_inc(sems["t"], 1))
                    t_cbb = P.bump("t")
                    wstate["wts_free_t"] = t_cbb
                else:
                    P.emit("tensor", lambda lhsT=lhsT, rhs=rhs, kc=kc, nch=nch:
                           nc.tensor.matmul(B[6][:, 0:nch], lhsT, rhs,
                                            start=(kc == 0), stop=False))
            P.wait("vector", "wa", v_w)
            P.autogap = True
            P.emit("vector", lambda wbuf=wbuf: v.tensor_reduce(
                swv, wbuf, axis=AX.X, op=OP.max, apply_absolute_value=True))
            P.emit("vector", lambda: v.tensor_scalar(swv, swv, C7, 1e-9, OP.mult, OP.max))
            P.emit("vector", lambda: v.reciprocal(rswv, swv))
            P.autogap = False
            if wstate["iwt_free_t"] is not None:
                P.wait("vector", "t", wstate["iwt_free_t"])
            P.emit("vector", lambda wbuf=wbuf: v.tensor_scalar(
                qw_buf, wbuf, rswv, None, OP.mult))
            thr, bb = lad_thr[:, 0:D], lad_bb[:, 0:D]
            P.emit("vector", lambda: v.tensor_scalar(bb, qw_buf, -0.5, None, OP.is_ge))
            P.emit("vector", lambda: v.tensor_scalar(thr, bb, 8.0, -4.5, OP.mult, OP.add))
            P.emit("vector", lambda: v.tensor_tensor(bb, qw_buf, thr, OP.is_ge))
            P.emit("vector", lambda: v.scalar_tensor_tensor(thr, bb, 4.0, thr, OP.mult, OP.add))
            P.emit("vector", lambda: v.scalar_tensor_tensor(bb, thr, -2.0, qw_buf, OP.add, OP.is_le))
            P.emit("vector", lambda: v.scalar_tensor_tensor(thr, bb, 2.0, thr, OP.mult, OP.add))
            P.emit("vector", lambda: v.scalar_tensor_tensor(bb, thr, -3.0, qw_buf, OP.add, OP.is_le))
            if wstate.get("qw_free_t") is not None:
                P.wait("vector", "t", wstate["qw_free_t"])
            P.emit("vector", lambda: v.scalar_tensor_tensor(qw_buf, thr, -3.5, bb, OP.add, OP.add)
                   .then_inc(sems["v"], 1))
            v_iw = P.bump("v")
            wstate["wcol_free_v"] = v_iw
            P.wait("tensor", "v", v_iw)
            for g in range(4):
                bk = 2 + g % 2
                if wstate["itr_bank_a"][g % 2] is not None:
                    P.wait("tensor", "a", wstate["itr_bank_a"][g % 2])
                for kc in range(4 * g, 4 * g + 4):
                    if kc % 4 == 3:
                        P.emit("tensor", lambda kc=kc, bk=bk:
                               nc.tensor.transpose(
                                   B[bk][:, 128 * (kc % 4):128 * (kc % 4) + 128],
                                   qw_buf[:, 128 * kc:128 * (kc + 1)], eye)
                               .then_inc(sems["t"], 1))
                        t_i = P.bump("t")
                    else:
                        P.emit("tensor", lambda kc=kc, bk=bk:
                               nc.tensor.transpose(
                                   B[bk][:, 128 * (kc % 4):128 * (kc % 4) + 128],
                                   qw_buf[:, 128 * kc:128 * (kc + 1)], eye))
                if g == 3:
                    wstate["qw_free_t"] = t_i
                P.wait("scalar", "t", t_i)
                if wstate["iwt_free_t"] is not None and g == 0:
                    P.wait("scalar", "t", wstate["iwt_free_t"])
                P.emit("scalar", lambda g=g, bk=bk: nc.scalar.activation(
                    iwtT[:, 512 * g:512 * (g + 1)], B[bk], AF.Identity)
                    .then_inc(sems["a"], 1))
                wstate["itr_bank_a"][g % 2] = P.bump("a")
            wt_iw = P.cnt["a"]
            P.wait("vector", "t", t_cbb)
            P.wait("vector", "bs", v_b)
            P.autogap = True
            if kind == "Q":
                P.emit("vector", lambda nch=nch: v.tensor_scalar(
                    cbbv[:, 0:nch], B[6][:, 0:nch], bias_sl, 0.125, OP.add, OP.mult))
                P.emit("vector", lambda nch=nch: v.tensor_scalar(
                    swcv[:, 0:nch], tm_src[:, 0:nch], swv, 0.125, OP.mult, OP.mult))
            else:
                P.emit("vector", lambda nch=nch: v.tensor_scalar(
                    cbbv[:, 0:nch], B[6][:, 0:nch], bias_sl, None, OP.add))
                P.emit("vector", lambda nch=nch: v.tensor_scalar(
                    swcv[:, 0:nch], tm_src[:, 0:nch], swv, None, OP.mult))
            P.autogap = False
            P.emit("vector", lambda: nc.vector.engine_nop().then_inc(sems["v"], 1))
            v_sc = P.bump("v")
            wstate["cbb_bank_a"] = v_sc
            P.wait("tensor", "a", wt_iw)
            P.wait("scalar", "v", v_sc)
            for tt in range(ntt):
                bank = tt % 2
                if wstate["proj_bank_a"][bank] is not None:
                    P.wait("tensor", "a", wstate["proj_bank_a"][bank])
                for kc in range(16):
                    lhsT = iwtT[:, 128 * kc:128 * (kc + 1)]
                    rhs = ip_src[:, kc * ip_blk + tt * 512: kc * ip_blk + tt * 512 + 512]
                    if kc == 15:
                        P.emit("tensor", lambda lhsT=lhsT, rhs=rhs, bank=bank:
                               nc.tensor.matmul(B[bank], lhsT, rhs,
                                                start=False, stop=True)
                               .then_inc(sems["t"], 1))
                        t_p = P.bump("t")
                    else:
                        P.emit("tensor", lambda lhsT=lhsT, rhs=rhs, kc=kc, bank=bank:
                               nc.tensor.matmul(B[bank], lhsT, rhs,
                                                start=(kc == 0), stop=False))
                if tt == ntt - 1:
                    wstate["iwt_free_t"] = t_p
                P.wait("scalar", "t", t_p)
                if wstate["krow_free"] is not None and tt == 0:
                    P.wait("scalar", wstate["krow_free"][0], wstate["krow_free"][1])
                for cc in range(2):
                    ci = tt * 2 + cc
                    dst = krow[:, tt * 512 + cc * 256: tt * 512 + cc * 256 + 256]
                    if cc == 1:
                        P.emit("scalar", lambda dst=dst, cc=cc, ci=ci, bank=bank:
                               nc.scalar.activation(
                                   dst, B[bank][:, cc * 256:(cc + 1) * 256],
                                   AF.Identity, bias=cbbv[:, ci:ci + 1],
                                   scale=swcv[:, ci:ci + 1]).then_inc(sems["a"], 1))
                    else:
                        P.emit("scalar", lambda dst=dst, cc=cc, ci=ci, bank=bank:
                               nc.scalar.activation(
                                   dst, B[bank][:, cc * 256:(cc + 1) * 256],
                                   AF.Identity, bias=cbbv[:, ci:ci + 1],
                                   scale=swcv[:, ci:ci + 1]))
                wstate["proj_bank_a"][bank] = P.bump("a")
            a_row = P.cnt["a"]
            wstate["bias_free_a"] = a_row
            if kind == "K":
                P.wait("sync", "a", a_row)
                P.emit("sync", lambda jt=jt: nc.sync.dma_start(
                    out=K_dram[128 * jt:128 * (jt + 1), :], in_=krow)
                    .then_inc(sems["st"], 16))
                st_c = P.bump("st", 16)
                wstate["krow_free"] = ("st", st_c)
            elif kind == "Q":
                P.wait("sync", "a", a_row)
                P.emit("sync", lambda jt=jt: nc.sync.dma_start(
                    out=Q_dram[128 * jt:128 * (jt + 1), :], in_=krow[:, 0:NO])
                    .then_inc(sems["st"], 16))
                st_c = P.bump("st", 16)
                wstate["krow_free"] = ("st", st_c)
            elif kind == "O":
                _pd_consume(jt, a_row)
            elif kind == "V":
                P.wait("tensor", "a", a_row)
                for kc in range(32):
                    bk = 2 + (kc % 2)
                    P.emit("tensor", lambda kc=kc, bk=bk:
                           nc.tensor.transpose(B[bk][:, 0:128],
                                               krow[:, 128 * kc:128 * (kc + 1)], eye)
                           .then_inc(sems["t"], 1))
                    t_v = P.bump("t")
                    P.wait("vector", "t", t_v)
                    P.emit("vector", lambda kc=kc, bk=bk:
                           v.tensor_copy(krow[:, 128 * kc:128 * (kc + 1)],
                                         B[bk][:, 0:128]).then_inc(sems["v"], 1))
                    v_vt = P.bump("v")
                    P.wait("tensor", "v", v_vt)
                P.wait("sync", "v", P.cnt["v"])
                dstap = Vt_dram.rearrange("(c p) (a j) -> p c a j", p=128, j=128)[:, :, jt]
                srcap = krow.rearrange("p (c j) -> p c j", j=128)
                P.emit("sync", lambda dstap=dstap, srcap=srcap: nc.sync.dma_start(
                    out=dstap, in_=srcap).then_inc(sems["st"], 16))
                st_c = P.bump("st", 16)
                wstate["krow_free"] = ("st", st_c)
        return

    wstate["wcol_free_t"] = mod_mm_done[95]
    P.wait("scalar", "v", v_quant_all)
    P.wait("tensor", "v", v_quant_all)
    emit_wstream("K", Ip, NT, tm896, chb, NCH, NT)
    emit_wstream("V", Ip, NT, tm896, chb, NCH, NT)
    emit_wstream("Q", Ip, NT, tm896, chb, NCH_OWN, NO)
    v_kvq_store = P.cnt["st"]

    if tap_d:
        tap_wait([("st", v_kvq_store)])
        tap("Krow0", K_dram[0:128, :])
        tap("Vt0", Vt_dram[0:128, :])
        tap("Q0", Q_dram[0:128, :])
    if upto == "qkv":
        _finalize(nc, P, sems, es)
        return nc

    P.emit("scalar", lambda: nc.scalar.preload_act_table(AF.Exp))
    P.wait("tensor", "st", v_kvq_store)
    P.wait("sync", "st", v_kvq_store)
    P.wait("vector", "st", v_kvq_store)
    for i in range(8):
        vv = v65[i].rearrange("p (i j) -> p i j", j=65)[:, :, 64:65]
        P.emit("vector", lambda vv=vv: nc.vector.memset(vv, 1.0))
    P.emit("vector", lambda: nc.vector.engine_nop().then_inc(sems["v"], 1))
    v_ones_pc = P.bump("v")

    t_pv = {}
    t_qk = {}
    a_exp = {}
    t_last_qk_of_grp = {}
    v_epi = {}
    for qh in range(2):
        for grp in range(8):
            gi = qh * 8 + grp
            qb = q4[gi % 2]
            qsem = "qa" if gi % 2 == 0 else "qb"
            if gi >= 2:
                P.wait("sync", "t", t_last_qk_of_grp[gi - 2])
            P.emit("sync", lambda qb=qb, qh=qh, grp=grp, qsem=qsem:
                   nc.sync.dma_start(
                       out=qb.rearrange("p (a t) -> p a t", a=2),
                       in_=Q_dram[256 * grp:256 * (grp + 1), 512 * qh:512 * (qh + 1)]
                       .rearrange("(a p) t -> p a t", p=128))
                   .then_inc(sems[qsem], 16))
            v_q = P.bump(qsem, 16)
            for kc in range(32):
                kb, vb = kcb[kc % 8], v65[kc % 8]
                if kc >= 8:
                    P.wait("sync", "t", t_pv[(gi, kc - 8)])
                elif gi >= 1:
                    P.wait("sync", "t", t_pv[(gi - 1, 24 + kc)])
                ksem = f"k{kc % 8}"
                P.emit("sync", lambda kb=kb, grp=grp, kc=kc, ksem=ksem:
                       nc.sync.dma_start(
                           out=kb.rearrange("p (a t) -> p a t", a=2),
                           in_=K_dram[256 * grp:256 * (grp + 1), 128 * kc:128 * (kc + 1)]
                           .rearrange("(a p) t -> p a t", p=128))
                       .then_inc(sems[ksem], 16))
                P.bump(ksem, 16)
                P.emit("sync", lambda vb=vb, grp=grp, kc=kc, ksem=ksem:
                       nc.sync.dma_start(
                           out=vb.rearrange("p (i j) -> p i j", j=65)[:, :, 0:64],
                           in_=Vt_dram[128 * kc:128 * (kc + 1), 256 * grp:256 * (grp + 1)]
                           .rearrange("p (i j) -> p i j", j=64))
                       .then_inc(sems[ksem], 16))
                v_v = P.bump(ksem, 16)
                P.wait("tensor", ksem, v_v)
                if kc == 0:
                    P.wait("tensor", qsem, v_q)
                    if gi == 0:
                        P.wait("tensor", "v", v_ones_pc)
                    if gi >= 1:
                        P.wait("tensor", "v", v_epi[gi - 1])
                if kc >= 1:
                    P.wait("tensor", "a", a_exp[(gi, kc - 1)])
                for pr in range(2):
                    for o in range(2):
                        h_i = 2 * pr + o
                        bank = B[h_i]
                        lhsT = kb[64 * o:64 * o + 64, 128 * pr:128 * (pr + 1)]
                        rhs = qb[64 * o:64 * o + 64, 512 * pr:512 * (pr + 1)]
                        fin = h_i == 3
                        if fin:
                            P.emit("tensor", lambda bank=bank, lhsT=lhsT, rhs=rhs, o=o:
                                   nc.tensor.matmul(bank, lhsT, rhs, start=True,
                                                    stop=True,
                                                    tile_position=(64 * o, 0))
                                   .then_inc(sems["t"], 1))
                            t_qk[(gi, kc)] = P.bump("t")
                        else:
                            P.emit("tensor", lambda bank=bank, lhsT=lhsT, rhs=rhs, o=o:
                                   nc.tensor.matmul(bank, lhsT, rhs, start=True,
                                                    stop=True,
                                                    tile_position=(64 * o, 0)))
                if kc == 31:
                    t_last_qk_of_grp[gi] = t_qk[(gi, kc)]
                if kc >= 1:
                    for h_i in range(4):
                        pb = probs[((kc - 1) % 2) * 4 + h_i]
                        lhsT = v65[(kc - 1) % 8][:, 65 * h_i:65 * h_i + 65]
                        fin = h_i == 3
                        if fin:
                            P.emit("tensor", lambda lhsT=lhsT, pb=pb, h_i=h_i, kc=kc:
                                   nc.tensor.matmul(B[4 + h_i][0:65, :], lhsT, pb,
                                                    start=(kc - 1 == 0), stop=(kc - 1 == 31))
                                   .then_inc(sems["t"], 1))
                            t_pv[(gi, kc - 1)] = P.bump("t")
                        else:
                            P.emit("tensor", lambda lhsT=lhsT, pb=pb, h_i=h_i, kc=kc:
                                   nc.tensor.matmul(B[4 + h_i][0:65, :], lhsT, pb,
                                                    start=(kc - 1 == 0), stop=(kc - 1 == 31)))
                P.wait("scalar", "t", t_qk[(gi, kc)])
                if kc >= 2:
                    P.wait("scalar", "t", t_pv[(gi, kc - 2)])
                for h_i in range(4):
                    pb = probs[(kc % 2) * 4 + h_i]
                    fin = h_i == 3
                    if fin:
                        P.emit("scalar", lambda pb=pb, h_i=h_i: nc.scalar.activation(
                            pb, B[h_i], AF.Exp).then_inc(sems["a"], 1))
                        a_exp[(gi, kc)] = P.bump("a")
                    else:
                        P.emit("scalar", lambda pb=pb, h_i=h_i: nc.scalar.activation(
                            pb, B[h_i], AF.Exp))
            P.wait("tensor", "a", a_exp[(gi, 31)])
            for h_i in range(4):
                pb = probs[(31 % 2) * 4 + h_i]
                lhsT = v65[31 % 8][:, 65 * h_i:65 * h_i + 65]
                if h_i == 3:
                    P.emit("tensor", lambda lhsT=lhsT, pb=pb, h_i=h_i:
                           nc.tensor.matmul(B[4 + h_i][0:65, :], lhsT, pb,
                                            start=False, stop=True)
                           .then_inc(sems["t"], 1))
                    t_pv[(gi, 31)] = P.bump("t")
                else:
                    P.emit("tensor", lambda lhsT=lhsT, pb=pb, h_i=h_i:
                           nc.tensor.matmul(B[4 + h_i][0:65, :], lhsT, pb,
                                            start=False, stop=True))
            P.wait("vector", "t", t_pv[(gi, 31)])
            for h_i in range(4):
                P.emit("vector", lambda h_i=h_i: nc.vector.tensor_copy(
                    rsum65[64:65, 512 * h_i:512 * (h_i + 1)], B[4 + h_i][64:65, :]))
            P.emit("vector", lambda: nc.vector.engine_nop().then_inc(sems["v"], 1))
            v_s4 = P.bump("v")
            P.wait("gpsimd", "v", v_s4)
            P.emit("gpsimd", lambda: nc.gpsimd.dma_start(
                out=rsum0, in_=rsum65[64:65, :]).then_inc(sems["g"], 16))
            g_r = P.bump("g", 16)
            P.wait("vector", "g", g_r)
            P.emit("vector", lambda: nc.vector.reciprocal(rq0, rsum0)
                   .then_inc(sems["v"], 1))
            v_rq = P.bump("v")
            P.wait("tensor", "v", v_rq)
            for h_i in range(4):
                H = 4 * grp + h_i
                pcH, half = H // 2, H % 2
                P.emit("tensor", lambda h_i=h_i: nc.tensor.matmul(
                    B[h_i][0:64, :], ones1[0:1, 0:64],
                    rq0[0:1, 512 * h_i:512 * (h_i + 1)], start=True,
                    stop=True).then_inc(sems["t"], 1))
                t_b = P.bump("t")
                P.wait("vector", "t", t_b)
                P.emit("vector", lambda h_i=h_i: nc.vector.tensor_copy(
                    rsb, B[h_i][0:64, :]))
                if half == 0:
                    dst = ctx_fm[0:64, pcH * NO + qh * 512: pcH * NO + qh * 512 + 512]
                    P.emit("vector", lambda dst=dst, h_i=h_i: nc.vector.tensor_tensor(
                        dst, B[4 + h_i][0:64, :], rsb, OP.mult)
                        .then_inc(sems["v"], 1))
                    P.bump("v")
                else:
                    P.emit("vector", lambda h_i=h_i: nc.vector.tensor_tensor(
                        ostg, B[4 + h_i][0:64, :], rsb, OP.mult)
                        .then_inc(sems["v"], 1))
                    v_o = P.bump("v")
                    P.wait("gpsimd", "v", v_o)
                    dst = ctx_fm[64:128, pcH * NO + qh * 512: pcH * NO + qh * 512 + 512]
                    P.emit("gpsimd", lambda dst=dst: nc.gpsimd.dma_start(
                        out=dst, in_=ostg).then_inc(sems["g"], 16))
                    g_c = P.bump("g", 16)
                    P.wait("vector", "g", g_c)
            P.emit("vector", lambda: nc.vector.engine_nop().then_inc(sems["v"], 1))
            v_epi[gi] = P.bump("v")
    v_attn_done_v = P.cnt["v"]
    g_attn_done = P.cnt["g"]

    if tap_d:
        tap_wait([("v", v_attn_done_v), ("g", g_attn_done)])
        tap("ctx", ctx_fm)
    if upto == "attn":
        _finalize(nc, P, sems, es)
        return nc

    P.wait("vector", "g", g_attn_done)
    for c in range(NCH_OWN):
        v_qdone2 = emit_quant_chunk(
            src_fm=lambda pc, c=c: ctx_fm[:, pc * NO + c * CHK: pc * NO + (c + 1) * CHK],
            chb_dst=lambda pc, c=c: chb2[:, pc * 16 + c: pc * 16 + c + 1],
            tm_dst=tm896o[:, c:c + 1],
            ip_dst=lambda pc, c=c: I2[:, pc * NO + c * CHK: pc * NO + (c + 1) * CHK],
            src3_fn=lambda c=c: ctx_fm.rearrange("p (q t) -> p q t", t=CHK)[:, :, :]
            if False else ctx_fm.rearrange("p (q ch) -> p q ch", q=16)[:, :, c * CHK:(c + 1) * CHK],
        )
    if tap_d:
        tap_wait([("v", v_qdone2)])
        tap("I2", I2[:, 0:NO])
        tap("chb2", chb2)
    P.wait("scalar", "v", v_qdone2)
    for t in range(8):
        P.emit("scalar", lambda t=t: nc.scalar.dma_start(
            out=x2_sb[:, t * D:(t + 1) * D], in_=x_d[128 * t:128 * (t + 1), :])
            .then_inc(sems["xa"], 16))
        P.bump("xa", 16)
    v_x2ld = P.cnt["xa"]
    P.wait("vector", "xa", v_x2ld)
    P.wait("tensor", "xa", v_x2ld)

    def _pd_consume(jt, a_row):
        v = nc.vector
        P.wait("vector", "a", a_row)
        P.emit("vector", lambda jt=jt: v.tensor_scalar(
            pd_scr, krow[:, 0:NO], ga_v[:, jt:jt + 1], None, OP.mult)
            .then_inc(sems["v"], 1))
        v_g = P.bump("v")
        P.wait("tensor", "v", v_g)
        wstate["krow_free"] = ("v", P.cnt["v"])
        for tt in range(8):
            bk = 2 + tt % 2
            P.emit("tensor", lambda tt=tt, bk=bk: nc.tensor.transpose(
                B[bk][:, 0:128], pd_scr[:, 128 * tt:128 * (tt + 1)], eye)
                .then_inc(sems["t"], 1))
            t_t = P.bump("t")
            P.wait("vector", "t", t_t)
            dst = x2_sb[:, tt * D + jt * 128: tt * D + jt * 128 + 128]
            P.emit("vector", lambda dst=dst, bk=bk: v.tensor_tensor(
                dst, B[bk][:, 0:128], dst, OP.add).then_inc(sems["v"], 1))
            v_a2 = P.bump("v")
            P.wait("tensor", "v", v_a2)

    P.wait("scalar", "v", v_qdone2)
    P.wait("tensor", "v", v_qdone2)
    emit_wstream("O", I2, NO, tm896o, chb2, NCH_OWN, NO)
    v_x2_done = P.cnt["v"]
    if tap_d:
        tap_wait([("v", v_x2_done)])
        tap("x2", x2_sb)
    if upto == "x2":
        _finalize(nc, P, sems, es)
        return nc

    P.wait("sync", "v", v_x2_done)
    for t in range(8):
        P.emit("sync", lambda t=t: nc.sync.dma_start(
            out=x2_dram[128 * t:128 * (t + 1), :], in_=x2_sb[:, t * D:(t + 1) * D])
            .then_inc(sems["st"], 16))
        P.bump("st", 16)
    st_x2 = P.cnt["st"]
    P.emit("scalar", lambda: nc.scalar.preload_act_table(AF.Sqrt))
    P.wait("vector", "v", v_x2_done)
    pe_xt_free = [None]
    for t in range(8):
        xt = wcol[0]
        if pe_xt_free[0] is not None:
            P.wait("vector", "t", pe_xt_free[0])
        P.emit("vector", lambda t=t, xt=xt: nc.vector.tensor_copy(
            xt, x2_sb[:, t * D:(t + 1) * D]))
        emit_ln_tile(xt)
        P.emit("vector", lambda: nc.vector.engine_nop().then_inc(sems["v"], 1))
        v_xh = P.bump("v")
        P.wait("tensor", "v", v_xh)
        emit_tok2fm(
            xt,
            lambda pc, t=t: h2_fm[:, pc * NO + t * 128: pc * NO + t * 128 + 128],
            cb_v, modv[:, 48:64], t, "pe")
        pe_xt_free[0] = P.cnt["t"]
    P.wait("vector", "a", P.cnt["a"])
    v_h2 = P.cnt["v"]
    if tap_d:
        tap_wait([("v", v_h2)])
        tap("h2", h2_fm)
    if upto == "h2":
        _finalize(nc, P, sems, es)
        return nc

    P.emit("sync", lambda: nc.sync.dma_start(out=gate_sb, in_=gate_d)
           .then_inc(sems["gg"], 16))
    P.emit("sync", lambda: nc.sync.dma_start(
        out=ebias_r, in_=ebias_d.rearrange("(a e) -> a e", a=1))
        .then_inc(sems["gg"], 16))
    v_g8 = P.bump("gg", 32)
    P.wait("tensor", "gg", v_g8)
    P.wait("tensor", "v", v_h2)
    P.wait("tensor", "a", P.cnt["a"])
    for kc in range(16):
        bk = 2 + kc % 2
        P.emit("tensor", lambda kc=kc, bk=bk: nc.tensor.transpose(
            B[bk][:, 0:8], gate_sb[:, 128 * kc:128 * (kc + 1)], eye[0:8, 0:8])
            .then_inc(sems["t"], 1))
        t_g = P.bump("t")
        P.wait("vector", "t", t_g)
        P.emit("vector", lambda kc=kc, bk=bk: nc.vector.tensor_copy(
            gtT[:, 8 * kc:8 * (kc + 1)], B[bk][:, 0:8])
            .then_inc(sems["v"], 1))
        v_gt = P.bump("v")
        P.wait("tensor", "v", v_gt)
    P.emit("tensor", lambda: nc.tensor.matmul(
        B[6][:, 0:8], ones1, ebias_r, start=True, stop=True).then_inc(sems["t"], 1))
    t_eb = P.bump("t")
    P.wait("vector", "t", t_eb)
    P.emit("vector", lambda: nc.vector.tensor_copy(ebias_b, B[6][:, 0:8]))
    P.emit("scalar", lambda: nc.scalar.preload_act_table(AF.Sigmoid))
    for tt in range(2):
        bank = tt % 2
        for kc in range(16):
            lhsT = gtT[:, 8 * kc:8 * (kc + 1)]
            rhs = h2_fm[:, kc * NO + tt * 512: kc * NO + tt * 512 + 512]
            if kc == 15:
                P.emit("tensor", lambda lhsT=lhsT, rhs=rhs, bank=bank:
                       nc.tensor.matmul(B[bank][0:8, :], lhsT, rhs,
                                        start=False, stop=True)
                       .then_inc(sems["t"], 1))
                t_gm = P.bump("t")
            else:
                P.emit("tensor", lambda lhsT=lhsT, rhs=rhs, kc=kc, bank=bank:
                       nc.tensor.matmul(B[bank][0:8, :], lhsT, rhs,
                                        start=(kc == 0), stop=False))
        P.wait("scalar", "t", t_gm)
        P.emit("scalar", lambda tt=tt, bank=bank: nc.scalar.activation(
            scoresT[:, 512 * tt:512 * (tt + 1)], B[bank][0:8, :], AF.Sigmoid)
            .then_inc(sems["a"], 1))
        P.bump("a")
    a_sc = P.cnt["a"]
    P.wait("tensor", "a", a_sc)
    for tti in range(8):
        bk = 2 + tti % 2
        P.emit("tensor", lambda tti=tti, bk=bk: nc.tensor.transpose(
            B[bk][:, 0:8], scoresT[:, 128 * tti:128 * (tti + 1)], eye[0:8, 0:8])
            .then_inc(sems["t"], 1))
        t_s = P.bump("t")
        P.wait("vector", "t", t_s)
        P.emit("vector", lambda tti=tti, bk=bk: nc.vector.tensor_copy(
            sc_tok[:, 8 * tti:8 * (tti + 1)], B[bk][:, 0:8]))
        P.emit("vector", lambda tti=tti: nc.vector.tensor_tensor(
            bs_tok, sc_tok[:, 8 * tti:8 * (tti + 1)], ebias_b, OP.add))
        P.emit("vector", lambda: nc.vector.engine_nop().then_inc(sems["v"], 1))
        v_bs = P.bump("v")
        P.wait("tensor", "v", v_bs)
        v = nc.vector
        P.autogap = True
        for e in range(8):
            be = bs_tok[:, e:e + 1]
            P.emit("vector", lambda be=be: v.tensor_scalar(
                cmp8, bs_tok, be, None, OP.is_gt))
            P.emit("vector", lambda: v.tensor_reduce(cnt1, cmp8, axis=AX.X, op=OP.add))
            if e > 0:
                P.emit("vector", lambda be=be: v.tensor_scalar(
                    cmp8, bs_tok, be, None, OP.is_equal))
                P.emit("vector", lambda e=e: v.tensor_reduce(
                    cnt2, cmp8[:, 0:e], axis=AX.X, op=OP.add))
                P.emit("vector", lambda: v.tensor_tensor(cnt1, cnt1, cnt2, OP.add))
            P.emit("vector", lambda: v.tensor_scalar(selc, cnt1, 1.5, None, OP.is_lt))
            P.emit("vector", lambda tti=tti, e=e: v.tensor_tensor(
                wt_tok[:, 8 * tti + e: 8 * tti + e + 1], selc,
                sc_tok[:, 8 * tti + e: 8 * tti + e + 1], OP.mult))
        P.autogap = False
        P.emit("vector", lambda: nc.vector.engine_nop().then_inc(sems["v"], 1))
        v_wt = P.bump("v")
        P.wait("tensor", "v", v_wt)
        P.emit("tensor", lambda tti=tti, bk=bk: nc.tensor.transpose(
            B[bk][0:8, 0:128], wt_tok[:, 8 * tti:8 * (tti + 1)], eye)
            .then_inc(sems["t"], 1))
        t_w = P.bump("t")
        P.wait("vector", "t", t_w)
        P.emit("vector", lambda tti=tti, bk=bk: nc.vector.tensor_copy(
            wt_fm[:, 128 * tti:128 * (tti + 1)], B[bk][0:8, 0:128])
            .then_inc(sems["v"], 1))
        v_wf = P.bump("v")
        P.wait("tensor", "v", v_wf)
    v_masks = P.cnt["v"]
    if tap_d:
        tap_wait([("v", v_masks)])
        tap("wt_fm", wt_fm)

    P.emit("scalar", lambda: nc.scalar.preload_act_table(AF.Silu))

    mm_last_t = [0]
    stg_cast_a = [None, None]

    def load_matrix_T(src_fn, n_sl, ncol, dst_off):
        for i in range(n_sl):
            qn = ("sync", "scalar", "gpsimd")[i % 3]
            if qn == "gpsimd":
                dst = wexp[:, dst_off(i): dst_off(i) + ncol]
                P.wait("gpsimd", "t", mm_last_t[0])
                P.emit("gpsimd", lambda src=src_fn(i), dst=dst:
                       nc.gpsimd.dma_start(out=dst, in_=src)
                       .then_inc(sems["goa"], 16))
                P.bump("goa", 16)
            else:
                si = 0 if qn == "sync" else 1
                stg = mw_sl2[si]
                msem = "mw" if qn == "sync" else "wt"
                if stg_cast_a[si] is not None:
                    P.wait(qn, "a", stg_cast_a[si])
                if qn == "sync":
                    P.emit(qn, lambda src=src_fn(i), stg=stg, ncol=ncol,
                           msem=msem: nc.sync.dma_start(
                               out=stg[:, 0:ncol], in_=src)
                           .then_inc(sems[msem], 16))
                else:
                    P.emit(qn, lambda src=src_fn(i), stg=stg, ncol=ncol,
                           msem=msem: nc.scalar.dma_start(
                               out=stg[:, 0:ncol], in_=src)
                           .then_inc(sems[msem], 16))
                g_w = P.bump(msem, 16)
                P.wait("scalar", msem, g_w)
                P.wait("scalar", "t", mm_last_t[0])
                nq = ncol // 512
                for cq in range(nq):
                    dq = wexp[:, dst_off(i) + 512 * cq: dst_off(i) + 512 * (cq + 1)]
                    if cq == nq - 1:
                        P.emit("scalar", lambda dq=dq, stg=stg, cq=cq:
                               nc.scalar.activation(
                                   dq, stg[:, 512 * cq:512 * (cq + 1)],
                                   AF.Identity).then_inc(sems["a"], 1))
                        stg_cast_a[si] = P.bump("a")
                    else:
                        P.emit("scalar", lambda dq=dq, stg=stg, cq=cq:
                               nc.scalar.activation(
                                   dq, stg[:, 512 * cq:512 * (cq + 1)],
                                   AF.Identity))
        return P.cnt["a"], P.cnt["goa"]

    P.wait("vector", "st", st_x2)
    P.wait("scalar", "st", st_x2)
    P.wait("tensor", "st", st_x2)
    P.wait("gpsimd", "st", st_x2)
    mm_last_t[0] = P.cnt["t"]
    P.wait("sync", "v", v_x2_done)
    P.wait("sync", "a", P.cnt["a"])
    P.wait("sync", "t", P.cnt["t"])
    P.wait("scalar", "v", v_x2_done)
    P.wait("scalar", "t", P.cnt["t"])
    v_xb_prev = None
    t_wtb_prev = None
    v_oe_prev = None
    for e in range(8):
        P.wait("gpsimd", "v", v_masks)
        if v_xb_prev is not None:
            P.wait("gpsimd", "v", v_xb_prev)
        P.emit("gpsimd", lambda e=e: nc.gpsimd.dma_start(
            out=wtrow0, in_=wt_fm[e:e + 1, :]).then_inc(sems["gw"], 16))
        g_wr = P.bump("gw", 16)
        P.wait("tensor", "gw", g_wr)
        P.wait("tensor", "v", v_masks)
        if v_xb_prev is not None:
            P.wait("tensor", "v", v_xb_prev)
        if v_oe_prev is not None:
            P.wait("tensor", "v", v_oe_prev)
        for tt in range(2):
            if tt == 1:
                P.emit("tensor", lambda e=e, tt=tt: nc.tensor.matmul(
                    B[6 + tt][:, 0:512], ones1, wtrow0[0:1, 512 * tt:512 * (tt + 1)],
                    start=True, stop=True).then_inc(sems["t"], 1))
                t_wb = P.bump("t")
            else:
                P.emit("tensor", lambda e=e, tt=tt: nc.tensor.matmul(
                    B[6 + tt][:, 0:512], ones1, wtrow0[0:1, 512 * tt:512 * (tt + 1)],
                    start=True, stop=True))
        t_wtb_prev = t_wb
        P.wait("vector", "t", t_wb)
        for pc in range(16):
            for tt in range(2):
                P.emit("vector", lambda pc=pc, tt=tt: nc.vector.tensor_tensor(
                    xb_sb[:, pc * NO + 512 * tt: pc * NO + 512 * tt + 512],
                    h2_fm[:, pc * NO + 512 * tt: pc * NO + 512 * tt + 512],
                    B[6 + tt][:, 0:512], OP.mult))
        P.emit("vector", lambda: nc.vector.engine_nop().then_inc(sems["v"], 1))
        v_xb = P.bump("v")
        v_xb_prev = v_xb
        exp_state = {"a_s1_last": None, "v_h1b": None}

        def mm_h1_hs(hs):
            for tt in range(2):
                bank = tt % 2
                for kc in range(16):
                    lhsT = wexp[:, kc * KB + hs * 128: kc * KB + hs * 128 + 128]
                    rhs = xb_sb[:, kc * NO + tt * 512: kc * NO + tt * 512 + 512]
                    if kc == 15:
                        P.emit("tensor", lambda lhsT=lhsT, rhs=rhs, bank=bank:
                               nc.tensor.matmul(B[bank], lhsT, rhs, start=False,
                                                stop=True).then_inc(sems["t"], 1))
                        t_h1 = P.bump("t")
                    else:
                        P.emit("tensor", lambda lhsT=lhsT, rhs=rhs, kc=kc, bank=bank:
                               nc.tensor.matmul(B[bank], lhsT, rhs,
                                                start=(kc == 0), stop=False))
                P.wait("vector", "t", t_h1)
                if exp_state["a_s1_last"] is not None and tt == 0:
                    P.wait("vector", "a", exp_state["a_s1_last"])
                P.emit("vector", lambda tt=tt, bank=bank: nc.vector.tensor_copy(
                    h1b_st[:, 512 * tt:512 * (tt + 1)], B[bank]).then_inc(sems["v"], 1))
                v_h1b = P.bump("v")
                P.wait("tensor", "v", v_h1b)
                exp_state["v_h1b"] = v_h1b
            P.wait("scalar", "v", exp_state["v_h1b"])
            P.emit("scalar", lambda hs=hs: nc.scalar.activation(
                s1_sb[:, hs * KB:(hs + 1) * KB], h1b_st, AF.Silu)
                .then_inc(sems["a"], 1))
            exp_state["a_s1_last"] = P.bump("a")

        aw, gw_ = load_matrix_T(
            lambda i, e=e: w1t_d[e, 128 * i:128 * (i + 1), :],
            16, 1024, lambda i: i * KB)
        P.wait("tensor", "a", aw)
        P.wait("tensor", "goa", gw_)
        P.wait("tensor", "v", v_xb)
        for hs in range(8):
            mm_h1_hs(hs)
        mm_last_t[0] = P.cnt["t"]
        a_s1 = P.cnt["a"]

        def mm_h3_hs(hs):
            for tt in range(2):
                bank = tt % 2
                for kc in range(16):
                    lhsT = wexp[:, kc * KB + hs * 128: kc * KB + hs * 128 + 128]
                    rhs = xb_sb[:, kc * NO + tt * 512: kc * NO + tt * 512 + 512]
                    if kc == 15:
                        P.emit("tensor", lambda lhsT=lhsT, rhs=rhs, bank=bank:
                               nc.tensor.matmul(B[bank], lhsT, rhs, start=False,
                                                stop=True).then_inc(sems["t"], 1))
                        t_h3 = P.bump("t")
                    else:
                        P.emit("tensor", lambda lhsT=lhsT, rhs=rhs, kc=kc, bank=bank:
                               nc.tensor.matmul(B[bank], lhsT, rhs,
                                                start=(kc == 0), stop=False))
                P.wait("vector", "t", t_h3)
                P.emit("vector", lambda tt=tt, bank=bank: nc.vector.tensor_copy(
                    h3b_st[:, 512 * tt:512 * (tt + 1)], B[bank]))
            P.emit("vector", lambda hs=hs: nc.vector.tensor_tensor(
                hh_sb[:, hs * KB:(hs + 1) * KB], s1_sb[:, hs * KB:(hs + 1) * KB],
                h3b_st, OP.mult).then_inc(sems["v"], 1))
            v_hh = P.bump("v")
            P.wait("tensor", "v", v_hh)

        aw, gw_ = load_matrix_T(
            lambda i, e=e: w3t_d[e, 128 * i:128 * (i + 1), :],
            16, 1024, lambda i: i * KB)
        P.wait("tensor", "a", aw)
        P.wait("tensor", "goa", gw_)
        for hs in range(8):
            mm_h3_hs(hs)
        mm_last_t[0] = P.cnt["t"]
        v_hh_all = P.cnt["v"]

        def mm_oe_dt(dt_):
            for tt in range(2):
                bank = tt % 2
                for hc in range(8):
                    lhsT = wexp[:, hc * 2 * KB + dt_ * 128: hc * 2 * KB + dt_ * 128 + 128]
                    rhs = hh_sb[:, hc * KB + tt * 512: hc * KB + tt * 512 + 512]
                    if hc == 7:
                        P.emit("tensor", lambda lhsT=lhsT, rhs=rhs, bank=bank:
                               nc.tensor.matmul(B[bank], lhsT, rhs, start=False,
                                                stop=True).then_inc(sems["t"], 1))
                        t_oe = P.bump("t")
                    else:
                        P.emit("tensor", lambda lhsT=lhsT, rhs=rhs, hc=hc, bank=bank:
                               nc.tensor.matmul(B[bank], lhsT, rhs,
                                                start=(hc == 0), stop=False))
                P.wait("vector", "t", t_oe)
                P.emit("vector", lambda tt=tt, bank=bank: nc.vector.tensor_copy(
                    oestg[:, 512 * tt:512 * (tt + 1)], B[bank]).then_inc(sems["v"], 1))
                v_oe = P.bump("v")
                P.wait("tensor", "v", v_oe)
            P.wait("sync", "v", P.cnt["v"])
            P.emit("sync", lambda e=e, dt_=dt_: nc.sync.dma_start(
                out=oe_dram[e, 128 * dt_:128 * (dt_ + 1), :], in_=oestg)
                .then_inc(sems["st"], 16))
            st_oe = P.bump("st", 16)
            P.wait("vector", "st", st_oe)

        aw, gw_ = load_matrix_T(
            lambda i, e=e: w2t_d[e, 128 * i:128 * (i + 1), :],
            8, 2 * KB, lambda i: i * 2 * KB)
        P.wait("tensor", "a", aw)
        P.wait("tensor", "goa", gw_)
        P.wait("tensor", "v", v_hh_all)
        for dt_ in range(16):
            mm_oe_dt(dt_)
        mm_last_t[0] = P.cnt["t"]
        v_oe_prev = P.cnt["v"]
    st_experts = P.cnt["st"]

    if upto == "experts":
        _finalize(nc, P, sems, es)
        return nc

    P.emit("scalar", lambda: nc.scalar.preload_act_table(AF.Silu))
    t_pg0 = P.cnt["t"]
    P.wait("sync", "t", t_pg0)
    swt_free = [None, None]
    ib_sw = [0]
    a_w1_done = None
    for phase in ("w1", "w3"):
        if phase == "w3":
            a_w1_done = P.cnt["a"]
            P.wait("vector", "a", a_w1_done)
        for hs in range(8):
            wtd = sw1t_d if phase == "w1" else sw3t_d
            swt = swt2[ib_sw[0] % 2]
            ssem = "sa" if ib_sw[0] % 2 == 0 else "gg"
            if swt_free[ib_sw[0] % 2] is not None:
                P.wait("sync", "t", swt_free[ib_sw[0] % 2])
            P.emit("sync", lambda wtd=wtd, hs=hs, swt=swt, ssem=ssem:
                   nc.sync.dma_start(
                       out=swt.rearrange("p (kc j) -> p kc j", j=128),
                       in_=wtd[:, 128 * hs:128 * (hs + 1)]
                       .rearrange("(kc p) j -> p kc j", p=128))
                   .then_inc(sems[ssem], 16))
            v_s = P.bump(ssem, 16)
            P.wait("tensor", ssem, v_s)
            for tt in range(2):
                bank = tt % 2
                for kc in range(16):
                    lhsT = swt[:, 128 * kc:128 * (kc + 1)]
                    rhs = h2_fm[:, kc * NO + tt * 512: kc * NO + tt * 512 + 512]
                    if kc == 15:
                        P.emit("tensor", lambda lhsT=lhsT, rhs=rhs, bank=bank:
                               nc.tensor.matmul(B[bank], lhsT, rhs, start=False,
                                                stop=True).then_inc(sems["t"], 1))
                        t_h = P.bump("t")
                    else:
                        P.emit("tensor", lambda lhsT=lhsT, rhs=rhs, kc=kc, bank=bank:
                               nc.tensor.matmul(B[bank], lhsT, rhs,
                                                start=(kc == 0), stop=False))
                if phase == "w1":
                    P.wait("scalar", "t", t_h)
                    P.emit("scalar", lambda hs=hs, tt=tt, bank=bank:
                           nc.scalar.activation(
                               hh_s[:, hs * KB + 512 * tt: hs * KB + 512 * tt + 512],
                               B[bank], AF.Silu).then_inc(sems["a"], 1))
                    a_h = P.bump("a")
                    P.wait("tensor", "a", a_h)
                else:
                    P.wait("vector", "t", t_h)
                    dst = hh_s[:, hs * KB + 512 * tt: hs * KB + 512 * tt + 512]
                    P.emit("vector", lambda dst=dst, bank=bank:
                           nc.vector.tensor_tensor(dst, B[bank], dst, OP.mult)
                           .then_inc(sems["v"], 1))
                    v_hh = P.bump("v")
                    P.wait("tensor", "v", v_hh)
            swt_free[ib_sw[0] % 2] = P.cnt["t"]
            ib_sw[0] += 1

    for ds in range(16):
        swt = swt2[ib_sw[0] % 2]
        ssem = "sa" if ib_sw[0] % 2 == 0 else "gg"
        if swt_free[ib_sw[0] % 2] is not None:
            P.wait("sync", "t", swt_free[ib_sw[0] % 2])
        P.emit("sync", lambda ds=ds, swt=swt, ssem=ssem: nc.sync.dma_start(
            out=swt[:, 0:1024].rearrange("p (hc j) -> p hc j", j=128),
            in_=sw2t_d[:, 128 * ds:128 * (ds + 1)]
            .rearrange("(hc p) j -> p hc j", p=128)).then_inc(sems[ssem], 16))
        v_s = P.bump(ssem, 16)
        P.wait("tensor", ssem, v_s)
        for tt in range(2):
            bank = tt % 2
            for hc in range(8):
                lhsT = swt[:, 128 * hc:128 * (hc + 1)]
                rhs = hh_s[:, hc * KB + tt * 512: hc * KB + tt * 512 + 512]
                if hc == 7:
                    P.emit("tensor", lambda lhsT=lhsT, rhs=rhs, bank=bank:
                           nc.tensor.matmul(B[bank], lhsT, rhs, start=False,
                                            stop=True).then_inc(sems["t"], 1))
                    t_o = P.bump("t")
                else:
                    P.emit("tensor", lambda lhsT=lhsT, rhs=rhs, hc=hc, bank=bank:
                           nc.tensor.matmul(B[bank], lhsT, rhs,
                                            start=(hc == 0), stop=False))
            P.wait("vector", "t", t_o)
            dst = out_moe[:, ds * NO + 512 * tt: ds * NO + 512 * tt + 512]
            P.emit("vector", lambda dst=dst, bank=bank: nc.vector.tensor_copy(
                dst, B[bank]).then_inc(sems["v"], 1))
            v_o = P.bump("v")
            P.wait("tensor", "v", v_o)
        swt_free[ib_sw[0] % 2] = P.cnt["t"]
        ib_sw[0] += 1
    v_shared = P.cnt["v"]

    P.wait("scalar", "v", v_shared)
    for t in range(8):
        P.emit("scalar", lambda t=t: nc.scalar.dma_start(
            out=x2_sb[:, t * D:(t + 1) * D], in_=x2_dram[128 * t:128 * (t + 1), :])
            .then_inc(sems["xa"], 16))
        P.bump("xa", 16)
    v_x2r = P.cnt["xa"]
    P.wait("gpsimd", "st", st_experts)
    P.wait("gpsimd", "v", v_shared)
    g_oe = {}
    for e in range(8):
        for jt in range(16):
            i = e * 16 + jt
            ob = oe_buf[i % 2]
            osem = "goa" if i % 2 == 0 else "gob"
            if i >= 2:
                P.wait("gpsimd", "v", g_oe[i - 2])
            P.emit("gpsimd", lambda e=e, jt=jt, ob=ob, osem=osem:
                   nc.gpsimd.dma_start(
                       out=ob, in_=oe_dram[e, 128 * jt:128 * (jt + 1), :])
                   .then_inc(sems[osem], 16))
            g_l = P.bump(osem, 16)
            P.wait("vector", osem, g_l)
            dst = out_moe[:, jt * NO:(jt + 1) * NO]
            P.emit("vector", lambda dst=dst, ob=ob: nc.vector.tensor_tensor(
                dst, dst, ob, OP.add).then_inc(sems["v"], 1))
            g_oe[i] = P.bump("v")
    v_moe = P.cnt["v"]

    P.wait("vector", "xa", v_x2r)
    P.wait("tensor", "xa", v_x2r)
    for jt in range(16):
        P.emit("vector", lambda jt=jt: nc.vector.tensor_scalar(
            fin_scr, out_moe[:, jt * NO:(jt + 1) * NO], gb_v[:, jt:jt + 1],
            None, OP.mult).then_inc(sems["v"], 1))
        v_f = P.bump("v")
        P.wait("tensor", "v", v_f)
        for tt in range(8):
            bk = 2 + tt % 2
            P.emit("tensor", lambda tt=tt, bk=bk: nc.tensor.transpose(
                B[bk][:, 0:128], fin_scr[:, 128 * tt:128 * (tt + 1)], eye)
                .then_inc(sems["t"], 1))
            t_t = P.bump("t")
            P.wait("vector", "t", t_t)
            dst = x2_sb[:, tt * D + jt * 128: tt * D + jt * 128 + 128]
            P.emit("vector", lambda dst=dst, bk=bk: nc.vector.tensor_tensor(
                dst, B[bk][:, 0:128], dst, OP.add).then_inc(sems["v"], 1))
            v_a = P.bump("v")
            P.wait("tensor", "v", v_a)
    v_y = P.cnt["v"]
    P.wait("sync", "v", v_y)
    for t in range(8):
        P.emit("sync", lambda t=t: nc.sync.dma_start(
            out=y_d[128 * t:128 * (t + 1), :], in_=x2_sb[:, t * D:(t + 1) * D])
            .then_inc(sems["st"], 16))
        P.bump("st", 16)

    _finalize(nc, P, sems, es)
    return nc


def _finalize(nc, P, sems, es):
    for semn in ("st", "tap", "g", "goa", "gob", "xa", "xb"):
        if P.cnt[semn]:
            P.wait("sync", semn, P.cnt[semn])
    with nc.Block() as block:
        @block.tensor
        def _(eng):
            for f in P.q["tensor"]:
                f()

        @block.vector
        def _(eng):
            for f in P.q["vector"]:
                f()

        @block.scalar
        def _(eng):
            for f in P.q["scalar"]:
                f()

        @block.sync
        def _(eng):
            for f in P.q["sync"]:
                f()

        @block.gpsimd
        def _(eng):
            for f in P.q["gpsimd"]:
                f()
    es.close()


def _rot_idx(qi):
    own = list(range(4 * qi, 4 * qi + 4))
    rest = [c for c in range(16) if c not in own]
    idx = np.concatenate([np.arange(c * CHK, (c + 1) * CHK) for c in own + rest])
    return idx


def make_in_maps(inputs):
    x = np.ascontiguousarray(np.asarray(inputs["L0_attn_hidden_states"]))
    eye = np.eye(128, dtype=np.float32)
    eyeb = np.eye(128, dtype=ml_dtypes.bfloat16)
    shared = {
        "wq": np.ascontiguousarray(inputs["L0_attn_q_weight"]),
        "wk": np.ascontiguousarray(inputs["L0_attn_k_weight"]),
        "wv": np.ascontiguousarray(inputs["L0_attn_v_weight"]),
        "wo": np.ascontiguousarray(inputs["L0_attn_o_weight"]),
        "bq": np.ascontiguousarray(inputs["L0_attn_q_bias"]),
        "bk": np.ascontiguousarray(inputs["L0_attn_k_bias"]),
        "bv": np.ascontiguousarray(inputs["L0_attn_v_bias"]),
        "bo": np.ascontiguousarray(inputs["L0_attn_o_bias"]),
        "gate": np.ascontiguousarray(inputs["L0_ffn_gate"]),
        "ebias": np.ascontiguousarray(inputs["L0_ffn_expert_bias"]),
        "w1": np.ascontiguousarray(inputs["L0_ffn_w1"]),
        "w2": np.ascontiguousarray(inputs["L0_ffn_w2"]),
        "w3": np.ascontiguousarray(inputs["L0_ffn_w3"]),
        "w1t": np.ascontiguousarray(np.asarray(inputs["L0_ffn_w1"]).transpose(0, 2, 1)),
        "w2t": np.ascontiguousarray(np.asarray(inputs["L0_ffn_w2"]).transpose(0, 2, 1)),
        "w3t": np.ascontiguousarray(np.asarray(inputs["L0_ffn_w3"]).transpose(0, 2, 1)),
        "sw1": np.ascontiguousarray(inputs["L0_ffn_shared_w1"]),
        "sw2": np.ascontiguousarray(inputs["L0_ffn_shared_w2"]),
        "sw3": np.ascontiguousarray(inputs["L0_ffn_shared_w3"]),
        "sw1t": np.ascontiguousarray(np.asarray(inputs["L0_ffn_shared_w1"]).T),
        "sw2t": np.ascontiguousarray(np.asarray(inputs["L0_ffn_shared_w2"]).T),
        "sw3t": np.ascontiguousarray(np.asarray(inputs["L0_ffn_shared_w3"]).T),
        "wmod": np.ascontiguousarray(inputs["L0_wmod"]),
        "eyef": eye, "eyeb": eyeb,
    }
    cfull = np.asarray(inputs["L0_c"])
    in_maps = []
    for c in range(8):
        b, qi = c // 4, c % 4
        m = dict(shared)
        m["x"] = np.ascontiguousarray(x[b][_rot_idx(qi)])
        m["cvec"] = np.ascontiguousarray(cfull[b])
        in_maps.append(m)
    return in_maps


def kernel(**inputs):
    nc = build("full")
    in_maps = make_in_maps(inputs)
    res = run_bass_kernel_spmd(nc, in_maps, core_ids=list(range(8)))
    out = np.zeros((2, 4096, 2048), dtype=np.float32)
    for c in range(8):
        b, qi = c // 4, c % 4
        out[b, qi * NO:(qi + 1) * NO] = np.asarray(res.results[c]["y"])
    return out
